# version 01
# baseline (speedup 1.0000x reference)
import numpy as np
import contextlib
import ml_dtypes
import concourse.bass as bass
import concourse.bass_utils
from concourse.bass_utils import run_bass_kernel_spmd

bfd = ml_dtypes.bfloat16

import numpy as np
import concourse.mybir as mybir

F32 = mybir.dt.float32
BF16 = mybir.dt.bfloat16
AF = mybir.ActivationFunctionType
ALU = mybir.AluOpType


class Pass:
    def __init__(self, dry, sems, events, only=None):
        self.dry = dry
        self.sems = sems
        self.events = events
        self.count = {}
        self.only = only
        self.current = None

    def begin(self, name):
        self.current = name

    def _active(self):
        return not self.dry and (self.only is None or self.only == self.current)

    def emit(self, fn, inc=None, step=1, event=None):
        if inc is not None:
            self.count[inc] = self.count.get(inc, 0) + step
            if event is not None:
                self.events[(inc, event)] = self.count[inc]
        if self._active():
            ins = fn()
            if inc is not None:
                ins.then_inc(self.sems[inc], step)

    def wait(self, eng, sem, key=None, value=None):
        if not self._active():
            return
        v = value if value is not None else self.events[(sem, key)]
        eng.wait_ge(self.sems[sem], v)

    def cur(self, sem):
        return self.count.get(sem, 0)


L1_SEMS = ["x", "w", "small", "mm", "sc", "v", "st"]


def l1_alloc(nc, E, T1=2048):
    TCH = 512
    t = {}
    t["x"] = E(nc.sbuf_tensor("l1_x", [128, 8, T1 + 3], BF16))
    t["win"] = E(nc.sbuf_tensor("l1_win", [128, 8, 4096], BF16))
    t["wx"] = E(nc.sbuf_tensor("l1_wx", [128, 16, 96], BF16))
    t["wdt"] = E(nc.sbuf_tensor("l1_wdt", [64, 16, 128], BF16))
    t["cw"] = E(nc.sbuf_tensor("l1_cw", [128, 16, 4], F32))
    t["cb"] = E(nc.sbuf_tensor("l1_cb", [128, 16], F32))
    t["db"] = E(nc.sbuf_tensor("l1_db", [128, 16], F32))
    t["pe"] = [E(nc.sbuf_tensor(f"l1_pe{i}", [128, TCH + 3], F32)) for i in range(2)]
    t["tail"] = E(nc.sbuf_tensor("l1_tail", [128, 16, 3], F32))
    t["cv"] = [E(nc.sbuf_tensor(f"l1_cv{i}", [128, TCH], F32)) for i in range(2)]
    t["uc"] = E(nc.sbuf_tensor("l1_uc", [128, 16, TCH], BF16))
    t["duc"] = E(nc.sbuf_tensor("l1_duc", [128, 16, TCH], BF16))
    t["dlc"] = E(nc.sbuf_tensor("l1_dlc", [128, 16, TCH], BF16))
    t["gsc"] = E(nc.sbuf_tensor("l1_gsc", [128, 16, TCH], BF16))
    t["dtlr"] = E(nc.sbuf_tensor("l1_dtlr", [64, TCH], BF16))
    t["bcc"] = E(nc.sbuf_tensor("l1_bcc", [32, TCH], BF16))
    t["ab"] = [E(nc.sbuf_tensor(f"l1_ab{i}", [128, TCH], F32)) for i in range(2)]
    t["sm"] = [E(nc.sbuf_tensor(f"l1_sm{i}", [128, TCH], F32)) for i in range(2)]
    t["ps"] = [E(nc.psum_tensor(f"l1_ps{i}", [128, TCH], F32)) for i in range(4)]
    t["ps_pt"] = E(nc.psum_tensor("l1_ps_pt", [96, TCH], F32))
    t["ps_d"] = [E(nc.psum_tensor(f"l1_ps_d{i}", [128, TCH], F32)) for i in range(2)]
    t["ps_h"] = E(nc.psum_tensor("l1_ps_h", [128, 3], F32))
    return t


def l1_streams(p, t, d, eng, T1=2048):
    TCH = 512
    NCH = T1 // TCH
    NCT = 16
    NKT = 8
    E = eng or {k: None for k in ("sync", "gpsimd", "tensor", "scalar", "vector")}

    def stream_gpsimd(g):
        for kt in range(NKT):
            p.emit(lambda kt=kt: g.dma_start(out=t["x"][:, kt, :], in_=d["xT"][kt * 128:(kt + 1) * 128, :]),
                   inc="x", step=16)
        for kt in range(NKT):
            p.emit(lambda kt=kt: g.dma_start(out=t["win"][:, kt, :], in_=d["w_inT"][kt * 128:(kt + 1) * 128, :]),
                   inc="w", step=16)
        for kt in range(NCT):
            p.emit(lambda kt=kt: g.dma_start(out=t["wx"][:, kt, :], in_=d["w_xT"][kt * 128:(kt + 1) * 128, :]),
                   inc="w", step=16)
        p.emit(lambda: g.dma_start(out=t["wdt"][:, :, :], in_=d["w_dtT"]), inc="w", step=16)

    def stream_sync(s):
        p.emit(lambda: s.dma_start(out=t["cw"][:], in_=d["conv_w_cm"]), inc="small", step=16)
        p.emit(lambda: s.dma_start(out=t["cb"][:], in_=d["conv_b_cm"]), inc="small", step=16)
        p.emit(lambda: s.dma_start(out=t["db"][:], in_=d["dt_bias_cm"]), inc="small", step=16)
        for tc in range(NCH):
            sl = slice(tc * TCH, (tc + 1) * TCH)
            p.wait(s, "sc", ("gs", tc, NCT - 1))
            p.wait(s, "sc", ("uc", tc))
            p.wait(s, "v", ("chunk", tc))
            p.wait(s, "sc", ("bcc", tc))
            for name, o in (("uc", "o_u"), ("duc", "o_du"), ("dlc", "o_dl"), ("gsc", "o_gs")):
                p.emit(lambda name=name, o=o, sl=sl: s.dma_start(out=d[o][:, :, sl], in_=t[name][:, :, :]),
                       inc="st", step=16, event=None)
            p.emit(lambda sl=sl: s.dma_start(out=d["o_bc"][:, sl], in_=t["bcc"][:, :]),
                   inc="st", step=16, event=("stchunk", tc))
        p.wait(s, "st", value=p.cur("st"))

    def stream_tensor(te):
        p.wait(te, "x", value=NKT * 16)
        p.wait(te, "w", value=(NKT + NCT + 1) * 16)
        for ctile in range(NCT):
            if ctile >= 1:
                p.wait(te, "sc", ("tail", ctile - 1))
            for kt in range(NKT):
                p.emit(lambda ctile=ctile, kt=kt: te.matmul(
                    t["ps_h"][:, :], t["win"][:, kt, ctile * 128:(ctile + 1) * 128],
                    t["x"][:, kt, 0:3], start=(kt == 0), stop=(kt == NKT - 1)),
                    inc=None if kt < NKT - 1 else "mm",
                    event=None if kt < NKT - 1 else ("halo", ctile))
        for tc in range(NCH):
            x0 = 3 + tc * TCH
            for ctile in range(NCT):
                if (tc, ctile) not in ((0, 0), (0, 1)):
                    prev = (tc, ctile - 2) if ctile >= 2 else (tc - 1, NCT - 2 + ctile)
                    p.wait(te, "sc", ("pevac",) + prev)
                for kt in range(NKT):
                    p.emit(lambda tc=tc, ctile=ctile, kt=kt, x0=x0: te.matmul(
                        t["ps"][ctile % 2][:, :], t["win"][:, kt, ctile * 128:(ctile + 1) * 128],
                        t["x"][:, kt, x0:x0 + TCH], start=(kt == 0), stop=(kt == NKT - 1)),
                        inc=None if kt < NKT - 1 else "mm",
                        event=None if kt < NKT - 1 else ("proj", tc, ctile))
            for ctile in range(NCT):
                j = 2048 + ctile * 128
                if (tc, ctile) not in ((0, 0), (0, 1)):
                    prev = (tc, ctile - 2) if ctile >= 2 else (tc - 1, NCT - 2 + ctile)
                    p.wait(te, "sc", ("gevac",) + prev)
                for kt in range(NKT):
                    p.emit(lambda tc=tc, ctile=ctile, kt=kt, x0=x0, j=j: te.matmul(
                        t["ps"][2 + ctile % 2][:, :], t["win"][:, kt, j:j + 128],
                        t["x"][:, kt, x0:x0 + TCH], start=(kt == 0), stop=(kt == NKT - 1)),
                        inc=None if kt < NKT - 1 else "mm",
                        event=None if kt < NKT - 1 else ("gate", tc, ctile))
            p.wait(te, "sc", ("uc", tc))
            if tc >= 1:
                p.wait(te, "sc", ("ptevac", tc - 1))
            for kt in range(NCT):
                p.emit(lambda tc=tc, kt=kt: te.matmul(
                    t["ps_pt"][:, :], t["wx"][:, kt, :], t["uc"][:, kt, :],
                    start=(kt == 0), stop=(kt == NCT - 1)),
                    inc=None if kt < NCT - 1 else "mm",
                    event=None if kt < NCT - 1 else ("projt", tc))
            p.wait(te, "sc", ("dtlr", tc))
            for ctile in range(NCT):
                if (tc, ctile) not in ((0, 0), (0, 1)):
                    prev = (tc, ctile - 2) if ctile >= 2 else (tc - 1, NCT - 2 + ctile)
                    p.wait(te, "sc", ("devac",) + prev)
                p.emit(lambda tc=tc, ctile=ctile: te.matmul(
                    t["ps_d"][ctile % 2][:, :], t["wdt"][:, ctile, :], t["dtlr"][:, :],
                    start=True, stop=True),
                    inc="mm", event=("delta", tc, ctile))

    def stream_scalar(sc):
        p.wait(sc, "small", value=48)
        for ctile in range(NCT):
            p.wait(sc, "mm", ("halo", ctile))
            p.emit(lambda ctile=ctile: sc.copy(out=t["tail"][:, ctile, :], in_=t["ps_h"][:, :]),
                   inc="sc", event=("tail", ctile))
        for tc in range(NCH):
            if tc >= 1:
                p.wait(sc, "st", value=16 * 5 * tc)
                p.wait(sc, "mm", ("delta", tc - 1, NCT - 1))

            def emit_u(tc, ctile):
                p.wait(sc, "v", ("cv", tc, ctile))
                p.emit(lambda tc=tc, ctile=ctile: sc.activation(
                    out=t["uc"][:, ctile, :], in_=t["cv"][ctile % 2][:], func=AF.Silu,
                    bias=t["cb"][:, ctile:ctile + 1]),
                    inc="sc", event=("u", tc, ctile))

            for ctile in range(NCT):
                pe = t["pe"][ctile % 2]
                p.wait(sc, "mm", ("proj", tc, ctile))
                if (tc, ctile) not in ((0, 0), (0, 1)):
                    prev = (tc, ctile - 2) if ctile >= 2 else (tc - 1, NCT - 2 + ctile)
                    p.wait(sc, "v", ("convdone",) + prev)
                p.emit(lambda pe=pe, ctile=ctile: sc.copy(out=pe[:, 3:3 + TCH], in_=t["ps"][ctile % 2][:, :]),
                       inc="sc")
                if tc >= 1:
                    p.wait(sc, "v", ("tail", tc - 1, ctile))
                p.emit(lambda pe=pe, ctile=ctile: sc.copy(out=pe[:, 0:3], in_=t["tail"][:, ctile, :]),
                       inc="sc", event=("pevac", tc, ctile))
                if ctile >= 1:
                    emit_u(tc, ctile - 1)
            emit_u(tc, NCT - 1)
            p.events[("sc", ("uc", tc))] = p.cur("sc")
            for ctile in range(NCT):
                p.wait(sc, "mm", ("gate", tc, ctile))
                p.emit(lambda tc=tc, ctile=ctile: sc.activation(
                    out=t["gsc"][:, ctile, :], in_=t["ps"][2 + ctile % 2][:, :], func=AF.Silu),
                    inc="sc", event=("gevac", tc, ctile))
            p.events[("sc", ("gs", tc, NCT - 1))] = p.cur("sc")
            p.wait(sc, "mm", ("projt", tc))
            p.emit(lambda: sc.copy(out=t["dtlr"][:, :], in_=t["ps_pt"][0:64, :]),
                   inc="sc", event=("dtlr", tc))
            p.emit(lambda: sc.copy(out=t["bcc"][:, :], in_=t["ps_pt"][64:96, :]),
                   inc="sc", event=("ptevac", tc))
            p.events[("sc", ("bcc", tc))] = p.cur("sc")
            for ctile in range(NCT):
                ab = t["ab"][ctile % 2]
                sm = t["sm"][ctile % 2]
                p.wait(sc, "mm", ("delta", tc, ctile))
                if (tc, ctile) not in ((0, 0), (0, 1)):
                    prev = (tc, ctile - 2) if ctile >= 2 else (tc - 1, NCT - 2 + ctile)
                    p.wait(sc, "v", ("dl",) + prev)
                p.emit(lambda ab=ab, ctile=ctile: sc.activation(
                    out=ab[:], in_=t["ps_d"][ctile % 2][:, :], func=AF.Abs,
                    bias=t["db"][:, ctile:ctile + 1]), inc="sc")
                p.wait(sc, "sc", value=p.cur("sc"))
                p.emit(lambda ab=ab: sc.activation(out=ab[:], in_=ab[:], func=AF.Exp, scale=-1.0),
                       inc="sc")
                p.wait(sc, "sc", value=p.cur("sc"))
                p.emit(lambda ab=ab: sc.activation(out=ab[:], in_=ab[:], func=AF.Ln, bias=1.0),
                       inc="sc")
                p.emit(lambda sm=sm, ctile=ctile: sc.activation(
                    out=sm[:], in_=t["ps_d"][ctile % 2][:, :], func=AF.Relu,
                    bias=t["db"][:, ctile:ctile + 1]),
                    inc="sc", event=("dl_parts", tc, ctile))
                p.events[("sc", ("devac", tc, ctile))] = p.cur("sc")

    def stream_vector(v):
        for tc in range(NCH):
            for ctile in range(NCT):
                pe = t["pe"][ctile % 2]
                cv = t["cv"][ctile % 2]
                p.wait(v, "sc", ("pevac", tc, ctile))
                if (tc, ctile) not in ((0, 0), (0, 1)):
                    prev = (tc, ctile - 2) if ctile >= 2 else (tc - 1, NCT - 2 + ctile)
                    p.wait(v, "sc", ("u",) + prev)
                p.emit(lambda cv=cv, pe=pe, ctile=ctile: v.tensor_scalar(
                    out=cv[:], in0=pe[:, 0:TCH], scalar1=t["cw"][:, ctile, 0:1],
                    scalar2=None, op0=ALU.mult), inc="v")
                for k in (1, 2, 3):
                    p.wait(v, "v", value=p.cur("v"))
                    p.emit(lambda cv=cv, pe=pe, ctile=ctile, k=k: v.scalar_tensor_tensor(
                        out=cv[:], in0=pe[:, k:k + TCH], scalar=t["cw"][:, ctile, k:k + 1],
                        in1=cv[:], op0=ALU.mult, op1=ALU.add),
                        inc="v", event=None if k < 3 else ("cv", tc, ctile))
                if tc < NCH - 1:
                    p.emit(lambda pe=pe, ctile=ctile: v.tensor_copy(
                        out=t["tail"][:, ctile, :], in_=pe[:, TCH:TCH + 3]),
                        inc="v", event=("tail", tc, ctile))
                p.events[("v", ("convdone", tc, ctile))] = p.cur("v")
            if tc >= 1:
                p.wait(v, "st", value=16 * 5 * tc)
            for ctile in range(NCT):
                ab = t["ab"][ctile % 2]
                sm = t["sm"][ctile % 2]
                p.wait(v, "sc", ("dl_parts", tc, ctile))
                p.emit(lambda ab=ab, sm=sm, tc=tc, ctile=ctile: v.tensor_tensor(
                    out=t["dlc"][:, ctile, :], in0=sm[:], in1=ab[:], op=ALU.add),
                    inc="v", event=("dl", tc, ctile))
                p.wait(v, "v", value=p.cur("v"))
                p.wait(v, "sc", ("u", tc, ctile))
                p.emit(lambda tc=tc, ctile=ctile: v.tensor_tensor(
                    out=t["duc"][:, ctile, :], in0=t["dlc"][:, ctile, :],
                    in1=t["uc"][:, ctile, :], op=ALU.mult),
                    inc="v", event=("du", tc, ctile))
            p.events[("v", ("chunk", tc))] = p.cur("v")

    p.begin("gpsimd"); stream_gpsimd(E["gpsimd"])
    p.begin("sync"); stream_sync(E["sync"])
    p.begin("tensor"); stream_tensor(E["tensor"])
    p.begin("scalar"); stream_scalar(E["scalar"])
    p.begin("vector"); stream_vector(E["vector"])


L3_SEMS = ["ld", "w3", "mm3", "sc3", "v3", "st3", "q3"]


def l3_alloc(nc, E, T3=2048):
    TCH = 512
    t = {}
    t["wbuf"] = E(nc.sbuf_tensor("l3_wbuf", [128, 32, 1024], BF16))
    t["wq"] = E(nc.sbuf_tensor("l3_wq", [128, 4096], BF16))
    t["wf"] = [E(nc.sbuf_tensor(f"l3_wf{i}", [128, 4096], F32)) for i in range(2)]
    t["x"] = E(nc.sbuf_tensor("l3_x", [128, 8, TCH], BF16))
    t["y2"] = E(nc.sbuf_tensor("l3_y2", [128, 16, TCH], BF16))
    t["x1"] = E(nc.sbuf_tensor("l3_x1", [128, 8, TCH], BF16))
    t["rbf"] = E(nc.sbuf_tensor("l3_rbf", [128, 8, TCH], BF16))
    t["q1"] = E(nc.sbuf_tensor("l3_q1", [128, 8, TCH], BF16))
    t["g1"] = E(nc.sbuf_tensor("l3_g1", [128, 32, TCH], BF16))
    t["sq"] = E(nc.sbuf_tensor("l3_sq", [128, TCH], BF16))
    t["v"] = [E(nc.sbuf_tensor(f"l3_v{i}", [128, TCH], F32)) for i in range(2)]
    t["sg"] = [E(nc.sbuf_tensor(f"l3_sg{i}", [128, TCH], F32)) for i in range(2)]
    t["msk"] = E(nc.sbuf_tensor("l3_msk", [128, TCH], F32))
    t["qi"] = [E(nc.sbuf_tensor(f"l3_qi{i}", [128, TCH], mybir.dt.int32)) for i in range(2)]
    t["qt"] = [E(nc.sbuf_tensor(f"l3_qt{i}", [128, TCH], BF16)) for i in range(2)]
    t["r3"] = E(nc.sbuf_tensor("l3_r3", [128, 8, TCH], F32))
    t["out"] = E(nc.sbuf_tensor("l3_out", [128, 8, TCH], F32))
    t["stat"] = E(nc.sbuf_tensor("l3_stat", [1, TCH * 8], F32))
    t["smax"] = E(nc.sbuf_tensor("l3_smax", [128, 8], F32))
    t["sflat"] = E(nc.sbuf_tensor("l3_sflat", [4, 128 * 4], F32))
    t["sbc"] = E(nc.sbuf_tensor("l3_sbc", [128, TCH], F32))
    t["rbc"] = E(nc.sbuf_tensor("l3_rbc", [128, TCH], F32))
    t["ones"] = E(nc.sbuf_tensor("l3_ones", [128, 2], BF16))
    t["onesf"] = E(nc.sbuf_tensor("l3_onesf", [128, 2], F32))
    t["lng"] = E(nc.sbuf_tensor("l3_lng", [128, 8], F32))
    t["lnb"] = E(nc.sbuf_tensor("l3_lnb", [128, 8], F32))
    t["ln2g"] = E(nc.sbuf_tensor("l3_ln2g", [128, 32], F32))
    t["ln2b"] = E(nc.sbuf_tensor("l3_ln2b", [128, 32], F32))
    t["b1"] = E(nc.sbuf_tensor("l3_b1", [128, 32], F32))
    t["b2"] = E(nc.sbuf_tensor("l3_b2", [128, 8], F32))
    t["swc"] = E(nc.sbuf_tensor("l3_swc", [128, 4], F32))
    t["red"] = E(nc.sbuf_tensor("l3_red", [128, 8], F32))
    t["scl"] = E(nc.sbuf_tensor("l3_scl", [1, 64], F32))
    t["cst"] = E(nc.sbuf_tensor("l3_cst", [1, 8], F32))
    t["ps"] = [E(nc.psum_tensor(f"l3_ps{i}", [128, TCH], F32)) for i in range(8)]
    t["pss"] = E(nc.psum_tensor("l3_pss", [1, 2 * TCH], F32))
    t["psb"] = [E(nc.psum_tensor(f"l3_psb{i}", [128, TCH], F32)) for i in range(0)]
    return t



def rep0(ap, n):
    return ap.broadcast_to([ap.shape[0], n, ap.shape[1]])


def l3_streams(p, t, d, eng, T3=2048):
    TCH = 512
    NCH = T3 // TCH
    E = eng or {}
    ps = t["ps"]
    W = t["wbuf"]
    NJT, NHT = 8, 32
    g1f = t["g1"].ap().rearrange("p a b -> p (a b)")
    g1w = t["g1"].ap().rearrange("p a b -> p (a b)").bitcast(F32)
    wqa = [lambda: g1w[:, 0:4096], lambda: g1w[:, 0:4096]]
    qc1 = lambda: g1f[:, 8192:12288]
    qc2 = lambda: g1f[:, 12288:16384]
    LN_EPS = 1e-5
    w1slot = W.ap()[:, 16:32, :].rearrange("p a b -> p (a b)").rearrange("p (c e) -> p c e", c=8)
    w2slot = W.ap()[:, 16:32, :]

    def xs(tc):
        return t["x"][:, :, :]

    def y2s(tc):
        return t["y2"][:, :, :]

    def stream_gpsimd(g):
        for i in range(32):
            rnd, wi, kt = i // 16, (i % 16) // 8, i % 8
            wsrc = "w1T_s" if wi == 0 else "w2T_s"
            if i >= 1:
                p.wait(g, "v3", ("wfree", i - 1))
            p.emit(lambda wsrc=wsrc, kt=kt, i=i: g.dma_start(
                out=wqa[i % 2](), in_=d[wsrc][:, kt, :]),
                inc="q%d" % (i % 2), step=16, event=("wcast", i))
        p.emit(lambda: g.dma_start(out=W[:, 0:16, :], in_=d["w_outT_s"]),
               inc="qo", step=16, event=("wout",))
        for tc in range(NCH):
            if tc >= 1:
                p.wait(g, "v3", ("Aev", tc - 1, NJT - 1))
            p.emit(lambda tc=tc: g.dma_start(out=xs(tc), in_=d["xT_s"][:, :, tc * TCH:(tc + 1) * TCH]),
                   inc="qx", step=16, event=("xc", tc))

    def stream_sync(s):
        for nm in ("lng", "lnb", "ln2g", "ln2b", "b1", "b2", "ident", "identf", "onesb", "onesf", "onesrow", "cst"):
            p.emit(lambda nm=nm: s.dma_start(out=t[nm][:], in_=d[nm]), inc="ls", step=16)
        for i in range(16):
            wi, kt = i // 8, i % 8
            dst = "w1q_d" if wi == 0 else "w2q_d"
            p.wait(s, "v3", ("wq", i + 16))
            p.emit(lambda dst=dst, kt=kt: s.dma_start(out=d[dst][:, kt, :], in_=qc1()),
                   inc="st3", step=16, event=("wqst", i))
        for tc in range(NCH):
            if tc >= 1:
                p.wait(s, "mm3", ("Adone", tc - 1))
            p.emit(lambda tc=tc: s.dma_start(out=y2s(tc), in_=d["y2T"][:, :, tc * TCH:(tc + 1) * TCH]),
                   inc="ly", step=16, event=("y2c", tc))
            for h in range(4):
                if h == 1:
                    p.wait(s, "v3", ("sfl", tc, 1))
                    p.emit(lambda: s.dma_start(out=t["stat"][0:1, 0:2 * TCH].rearrange("p (a b) -> p a b", a=8),
                                               in_=t["sflat"][0:8, 0:128]),
                           inc="sx", step=16, event=("sxd", tc, 1))
                if h == 3:
                    p.wait(s, "v3", ("sfl", tc, 2))
                    p.emit(lambda: s.dma_start(out=t["stat"][0:1, 0:2 * TCH].rearrange("p (a b) -> p a b", a=8),
                                               in_=t["sflat"][0:8, 0:128]),
                           inc="sx", step=16, event=("sxd", tc, 2))
                p.wait(s, "st3", ("wqst", 15))
                if not (tc == 0 and h == 0):
                    p.wait(s, "mm3", ("slotfree", tc, h))
                if h < 2:
                    p.emit(lambda h=h: s.dma_start(out=w1slot, in_=d["w1q_d"][:, :, h * 2048:(h + 1) * 2048]),
                           inc="lw", step=16, event=("wslot", tc, h))
                else:
                    p.emit(lambda h=h: s.dma_start(out=w2slot, in_=d["w2q_d"].rearrange(
                        "p a b -> p (a b)").rearrange("p (c e) -> p c e", c=32)[:, (h - 2) * 16:(h - 1) * 16, :]),
                        inc="lw", step=16, event=("wslot", tc, h))
            for jt in range(NJT):
                p.wait(s, "v3", ("outj", tc, jt))
                p.emit(lambda tc=tc, jt=jt: s.dma_start(
                    out=d["outT_s"][:, jt, tc * TCH:(tc + 1) * TCH], in_=t["r3"][:, jt, :]),
                    inc="st3", step=16, event=("outst", tc, jt))
        p.wait(s, "st3", value=p.cur("st3"))

    def bc(te_, row_fn, out_ps, ev=None):
        p.emit(lambda: te_.matmul(out_ps, t["onesrow"][0:1, :], row_fn(), start=True, stop=True),
               inc="mm3", event=ev)

    def stream_tensor(te_):
        p.wait(te_, "ls", value=12 * 16)
        for wi in range(2):
            p.wait(te_, "v3", ("wred", wi * 8 + 7))
            p.emit(lambda wi=wi: te_.matmul(ps[6][0:1, wi * 8:wi * 8 + 8], t["onesf"][:, 0:1],
                                            t["red"][:, wi * 8:wi * 8 + 8], start=True, stop=True),
                   inc="mm3", event=("wtot", wi))
        p.wait(te_, "v3", ("thr",))
        bc(te_, lambda: t["scl"][0:1, 0:6], ps[7][:, 0:6], ev=("swcb",))
        for tc in range(NCH):
            p.wait(te_, "qo", ("wout",))
            p.wait(te_, "ly", ("y2c", tc))
            for jt in range(NJT):
                if not (tc == 0 and jt < 2):
                    prev = (tc, jt - 2) if jt >= 2 else (tc - 1, NJT - 2 + jt)
                    p.wait(te_, "v3", ("Aev",) + prev)
                for kt in range(16):
                    p.emit(lambda tc=tc, jt=jt, kt=kt: te_.matmul(
                        ps[jt % 2][:, :], W[:, kt, jt * 128:(jt + 1) * 128],
                        y2s(tc)[:, kt, :], start=(kt == 0), stop=(kt == 15)),
                        inc=None if kt < 15 else "mm3",
                        event=None if kt < 15 else ("A", tc, jt))
            p.events[("mm3", ("Adone", tc))] = p.cur("mm3")
            for jt in range(NJT):
                p.wait(te_, "v3", ("Aev", tc, jt))
                p.wait(te_, "sc3", ("sq0", tc, jt))
                p.emit(lambda tc=tc, jt=jt: te_.matmul(ps[2][0:1, :], t["onesb"][:, 0:1],
                                                       t["rbf"][:, jt, :], start=(jt == 0), stop=(jt == 7)),
                       inc="mm3")
                p.emit(lambda: te_.matmul(ps[3][0:1, :], t["onesb"][:, 0:1], t["sq"][:, :],
                                          start=(jt == 0), stop=(jt == 7)),
                       inc="mm3", event=("Bsum", tc, jt))
            p.wait(te_, "v3", ("stat0", tc))
            bc(te_, lambda: t["stat"][0:1, 0:TCH], ps[4][:, :], ev=("rs0b", tc))
            bc(te_, lambda: t["stat"][0:1, TCH:2 * TCH], ps[5][:, :], ev=("m20b", tc))
            p.wait(te_, "v3", ("mx1", tc))
            for tb in range(4):
                p.emit(lambda tb=tb: te_.transpose(ps[6][:, tb * 128:(tb + 1) * 128],
                                                   t["mx"][:, tb * 128:(tb + 1) * 128], t["identf"][:]),
                       inc=None if tb < 3 else "mm3", event=None if tb < 3 else ("mxT1", tc))
            p.wait(te_, "v3", ("s1c", tc))
            p.emit(lambda: te_.transpose(ps[7][0:8, 0:128], t["smax"][:, 0:8], t["identf"][:]),
                   inc="mm3", event=("sT1", tc))
            p.wait(te_, "sx", ("sxd", tc, 1))
            bc(te_, lambda: t["stat"][0:1, 0:TCH], ps[4][:, :], ev=("sb1", tc))
            bc(te_, lambda: t["stat"][0:1, TCH:2 * TCH], ps[5][:, :], ev=("rb1", tc))
            p.wait(te_, "lw", ("wslot", tc, 0))
            p.wait(te_, "v3", ("q1", tc))
            for jt2 in range(NHT):
                if jt2 == 16:
                    p.wait(te_, "lw", ("wslot", tc, 1))
                if (tc, jt2) not in ((0, 0), (0, 1)):
                    prev = (tc, jt2 - 2) if jt2 >= 2 else (tc - 1, NHT - 2 + jt2)
                    p.wait(te_, "v3", ("Cz",) + prev)
                for kt in range(NJT):
                    p.emit(lambda jt2=jt2, kt=kt: te_.matmul(
                        ps[jt2 % 2][:, :], w1slot[:, kt, (jt2 % 16) * 128:(jt2 % 16) * 128 + 128],
                        t["q1"][:, kt, :], start=(kt == 0), stop=(kt == 7)),
                        inc=None if kt < 7 else "mm3",
                        event=None if kt < 7 else ("C", tc, jt2))
                if jt2 == 15:
                    p.events[("mm3", ("slotfree", tc, 1))] = p.cur("mm3")
                if jt2 == 31:
                    p.events[("mm3", ("slotfree", tc, 2))] = p.cur("mm3")

                def dsum(j):
                    p.wait(te_, "sc3", ("gelu", tc, j))
                    p.wait(te_, "sc3", ("sq2", tc, j))
                    p.emit(lambda j=j: te_.matmul(ps[2][0:1, :], t["onesb"][:, 0:1],
                                                  t["g1"][:, j, :], start=(j == 0), stop=(j == 31)),
                           inc="mm3")
                    p.emit(lambda j=j: te_.matmul(ps[3][0:1, :], t["onesb"][:, 0:1], t["sq"][:, :],
                                                  start=(j == 0), stop=(j == 31)),
                           inc="mm3", event=("Dsum", tc, j))

                if jt2 >= 2:
                    dsum(jt2 - 2)
            dsum(NHT - 2)
            dsum(NHT - 1)
            p.wait(te_, "v3", ("stat2", tc))
            bc(te_, lambda: t["stat"][0:1, 0:TCH], ps[4][:, :], ev=("rs2b", tc))
            bc(te_, lambda: t["stat"][0:1, TCH:2 * TCH], ps[5][:, :], ev=("m22b", tc))
            p.wait(te_, "v3", ("mx2", tc))
            for tb in range(4):
                p.emit(lambda tb=tb: te_.transpose(ps[6][:, tb * 128:(tb + 1) * 128],
                                                   t["mx"][:, tb * 128:(tb + 1) * 128], t["identf"][:]),
                       inc=None if tb < 3 else "mm3", event=None if tb < 3 else ("mxT2", tc))
            p.wait(te_, "v3", ("s2c", tc))
            p.emit(lambda: te_.transpose(ps[7][0:8, 0:128], t["smax"][:, 0:8], t["identf"][:]),
                   inc="mm3", event=("sT2", tc))
            p.wait(te_, "sx", ("sxd", tc, 2))
            bc(te_, lambda: t["stat"][0:1, 0:TCH], ps[4][:, :], ev=("sb2", tc))
            bc(te_, lambda: t["stat"][0:1, TCH:2 * TCH], ps[5][:, :], ev=("rb2", tc))
            p.wait(te_, "lw", ("wslot", tc, 2))
            p.wait(te_, "v3", ("sbc2ev", tc))
            if tc >= 1:
                p.wait(te_, "v3", ("Gap", tc - 1, NJT - 1))
            for kt2 in range(NHT):
                if kt2 == 16:
                    p.wait(te_, "lw", ("wslot", tc, 3))
                p.wait(te_, "v3", ("q2", tc, kt2))
                for jt in range(NJT):
                    p.emit(lambda kt2=kt2, jt=jt: te_.matmul(
                        ps[jt][:, :], w2slot[:, kt2 % 16, jt * 128:(jt + 1) * 128],
                        t["qt"][kt2 % 2][:, :], start=(kt2 == 0), stop=(kt2 == 31)),
                        inc=None if jt < 7 else "mm3",
                        event=None if jt < 7 else ("F", tc, kt2))
                if kt2 == 15:
                    p.events[("mm3", ("slotfree", tc, 3))] = p.cur("mm3")
                if kt2 == 31:
                    p.events[("mm3", ("slotfree", tc + 1, 0))] = p.cur("mm3")
            p.wait(te_, "v3", ("Gap", tc, 5))
            for jt in range(NJT):
                p.wait(te_, "v3", ("r3bf", tc, jt))
                p.wait(te_, "sc3", ("sq3", tc, jt))
                p.emit(lambda jt=jt: te_.matmul(ps[2][0:1, :], t["onesb"][:, 0:1],
                                                t["rbf"][:, jt, :], start=(jt == 0), stop=(jt == 7)),
                       inc="mm3")
                p.emit(lambda: te_.matmul(ps[3][0:1, :], t["onesb"][:, 0:1], t["sq"][:, :],
                                          start=(jt == 0), stop=(jt == 7)),
                       inc="mm3", event=("Gsum", tc, jt))
            p.wait(te_, "v3", ("stat3", tc))
            bc(te_, lambda: t["stat"][0:1, 0:TCH], ps[4][:, :], ev=("rs3b", tc))
            bc(te_, lambda: t["stat"][0:1, TCH:2 * TCH], ps[5][:, :], ev=("m23b", tc))

    def stream_scalar(sc_):
        p.wait(sc_, "ls", value=12 * 16)
        for tc in range(NCH):
            for jt in range(NJT):
                p.wait(sc_, "v3", ("Aev", tc, jt))
                if not (tc == 0 and jt == 0):
                    prevj = ("Bsum", tc, jt - 1) if jt >= 1 else ("Bsum", tc - 1, NJT - 1)
                    p.wait(sc_, "mm3", prevj)
                p.emit(lambda jt=jt: sc_.activation(out=t["sq"][:, :], in_=t["rbf"][:, jt, :],
                                                    func=AF.Square),
                       inc="sc3", event=("sq0", tc, jt))
            p.wait(sc_, "v3", ("var0", tc))
            p.emit(lambda: sc_.activation(out=t["stat"][0:1, 2 * TCH:3 * TCH],
                                          in_=t["stat"][0:1, 3 * TCH:4 * TCH],
                                          func=AF.Sqrt, bias=t["cst"][0:1, 0:1]),
                   inc="sc3", event=("sd0", tc))
            for jt2 in range(NHT):
                p.wait(sc_, "v3", ("Cz", tc, jt2))
                p.emit(lambda jt2=jt2: sc_.activation(
                    out=t["g1"][:, jt2, :], in_=t["v"][jt2 % 2][:], func=AF.Gelu,
                    bias=t["b1"][:, jt2:jt2 + 1]),
                    inc="sc3", event=("gelu", tc, jt2))
                if not (tc == 0 and jt2 == 0):
                    prevj = ("Dsum", tc, jt2 - 1) if jt2 >= 1 else ("Dsum", tc - 1, NHT - 1)
                    p.wait(sc_, "mm3", prevj)
                p.wait(sc_, "sc3", value=p.cur("sc3"))
                p.emit(lambda jt2=jt2: sc_.activation(out=t["sq"][:, :], in_=t["g1"][:, jt2, :],
                                                      func=AF.Square),
                       inc="sc3", event=("sq2", tc, jt2))
            p.wait(sc_, "v3", ("var2", tc))
            p.emit(lambda: sc_.activation(out=t["stat"][0:1, 2 * TCH:3 * TCH],
                                          in_=t["stat"][0:1, 3 * TCH:4 * TCH],
                                          func=AF.Sqrt, bias=t["cst"][0:1, 0:1]),
                   inc="sc3", event=("sd2", tc))
            for jt in range(NJT):
                p.wait(sc_, "v3", ("r3bf", tc, jt))
                if not (tc == 0 and jt == 0):
                    prevj = ("Gsum", tc, jt - 1) if jt >= 1 else ("Gsum", tc - 1, NJT - 1)
                    p.wait(sc_, "mm3", prevj)
                p.emit(lambda jt=jt: sc_.activation(out=t["sq"][:, :], in_=t["rbf"][:, jt, :],
                                                    func=AF.Square),
                       inc="sc3", event=("sq3", tc, jt))
            p.wait(sc_, "v3", ("var3", tc))
            p.emit(lambda: sc_.activation(out=t["stat"][0:1, 2 * TCH:3 * TCH],
                                          in_=t["stat"][0:1, 3 * TCH:4 * TCH],
                                          func=AF.Sqrt, bias=t["cst"][0:1, 0:1]),
                   inc="sc3", event=("sd3", tc))

    def stream_vector(v_):
        ch = [0]

        def vw(val=None):
            p.wait(v_, "v3", value=val if val is not None else p.cur("v3"))

        for i in range(16):
            wi, kt = i // 8, i % 8
            p.wait(v_, "q%d" % (i % 2), ("wcast", i))
            p.emit(lambda i=i, wi=wi, kt=kt: v_.reduce_sum(
                out=t["red"][:, wi * 8 + kt:wi * 8 + kt + 1], in_=wqa[i % 2](),
                axis=mybir.AxisListType.X, apply_absolute_value=True),
                inc="v3", event=("wred", i) if True else None)
            p.events[("v3", ("wfree", i))] = p.cur("v3")
        for wi in range(2):
            p.wait(v_, "mm3", ("wtot", wi))
            p.emit(lambda wi=wi: v_.reduce_sum(out=t["scl"][0:1, 16 + wi:17 + wi],
                                               in_=ps[6][0:1, wi * 8:wi * 8 + 8],
                                               axis=mybir.AxisListType.X), inc="v3")
            vw()
            N = 1024 * 4096.0
            p.emit(lambda wi=wi, N=N: v_.tensor_scalar(
                out=t["scl"][0:1, 4 + wi:5 + wi], in0=t["scl"][0:1, 16 + wi:17 + wi],
                scalar1=1.0 / N, scalar2=1e-5, op0=ALU.mult, op1=ALU.max), inc="v3")
            vw()
            p.emit(lambda wi=wi: v_.tensor_scalar_mul(
                out=t["scl"][0:1, 0 + wi:1 + wi], in0=t["scl"][0:1, 4 + wi:5 + wi],
                scalar1=0.5), inc="v3")
            vw()
            p.emit(lambda wi=wi: v_.tensor_scalar_mul(
                out=t["scl"][0:1, 2 + wi:3 + wi], in0=t["scl"][0:1, 0 + wi:1 + wi],
                scalar1=-1.0), inc="v3", event=("thr",) if wi == 1 else None)
        p.wait(v_, "mm3", ("swcb",))
        p.emit(lambda: v_.tensor_copy(out=t["swc"][:, 0:6], in_=ps[7][:, 0:6]),
               inc="v3", event=("swcev",))
        for i in range(16, 32):
            wi, kt = (i - 16) // 8, (i - 16) % 8
            p.wait(v_, "q%d" % (i % 2), ("wcast", i))
            vw()
            p.emit(lambda i=i, wi=wi: v_.tensor_scalar(
                out=qc1(), in0=wqa[i % 2](), scalar1=t["swc"][:, 0 + wi:1 + wi],
                scalar2=None, op0=ALU.is_ge), inc="v3")
            vw()
            p.emit(lambda i=i, wi=wi: v_.tensor_scalar(
                out=qc2(), in0=wqa[i % 2](), scalar1=t["swc"][:, 2 + wi:3 + wi],
                scalar2=None, op0=ALU.is_le), inc="v3")
            vw()
            p.emit(lambda: v_.tensor_sub(out=qc1(), in0=qc1(), in1=qc2()),
                   inc="v3", event=("wq", i))
            p.wait(v_, "st3", ("wqst", i - 16))
            p.events[("v3", ("wfree", i))] = p.cur("v3")

        def statmath(tcev, Nf, sumev, varev, sdev, statev):
            S = TCH
            p.wait(v_, "mm3", sumev)
            p.emit(lambda: v_.tensor_scalar_mul(out=t["stat"][0:1, 4 * S:5 * S],
                                                in0=ps[2][0:1, :], scalar1=1.0 / Nf), inc="v3")
            vw()
            p.emit(lambda: v_.tensor_scalar_mul(out=t["stat"][0:1, 5 * S:6 * S],
                                                in0=ps[3][0:1, :], scalar1=1.0 / Nf), inc="v3")
            vw()
            p.emit(lambda: v_.tensor_mul(out=t["stat"][0:1, 6 * S:7 * S],
                                         in0=t["stat"][0:1, 4 * S:5 * S],
                                         in1=t["stat"][0:1, 4 * S:5 * S]), inc="v3")
            vw()
            p.emit(lambda: v_.tensor_sub(out=t["stat"][0:1, 3 * S:4 * S],
                                         in0=t["stat"][0:1, 5 * S:6 * S],
                                         in1=t["stat"][0:1, 6 * S:7 * S]),
                   inc="v3", event=varev)
            p.wait(v_, "sc3", sdev)
            p.emit(lambda: v_.reciprocal(out=t["stat"][0:1, 0:S],
                                         in_=t["stat"][0:1, 2 * S:3 * S]), inc="v3")
            vw()
            p.emit(lambda: v_.tensor_mul(out=t["stat"][0:1, S:2 * S],
                                         in0=t["stat"][0:1, 4 * S:5 * S],
                                         in1=t["stat"][0:1, 0:S]),
                   inc="v3", event=statev)

        def scalerow(tc, mxev, mxTev, sTev, scev, rowev, wi):
            p.wait(v_, "mm3", mxTev)
            p.emit(lambda: v_.reduce_max(out=t["smax"][:, 0:4],
                                         in_=ps[6].ap().rearrange("p (a b) -> p a b", a=4),
                                         axis=mybir.AxisListType.X), inc="v3")
            vw()
            p.emit(lambda: v_.tensor_scalar_max(out=t["smax"][:, 0:4], in0=t["smax"][:, 0:4],
                                                scalar1=1e-5), inc="v3")
            vw()
            p.emit(lambda wi=wi: v_.tensor_scalar(
                out=t["smax"][:, 4:8], in0=t["smax"][:, 0:4],
                scalar1=t["swc"][:, 4 + wi:5 + wi], scalar2=1.0 / 127.0,
                op0=ALU.mult, op1=ALU.mult), inc="v3")
            vw()
            p.emit(lambda: v_.reciprocal(out=t["smax"][:, 0:4], in_=t["smax"][:, 0:4]), inc="v3")
            vw()
            p.emit(lambda: v_.tensor_scalar_mul(out=t["smax"][:, 0:4], in0=t["smax"][:, 0:4],
                                                scalar1=127.0), inc="v3", event=scev)
            p.wait(v_, "mm3", sTev)
            p.emit(lambda: v_.tensor_copy(out=t["sflat"][0:8, 0:128], in_=ps[7][0:8, 0:128]),
                   inc="v3", event=rowev)

        for tc in range(NCH):
            for jt in range(NJT):
                p.wait(v_, "mm3", ("A", tc, jt))
                p.wait(v_, "qx", ("xc", tc))
                p.emit(lambda tc=tc, jt=jt: v_.tensor_tensor(
                    out=t["rbf"][:, jt, :], in0=ps[jt % 2][:, :], in1=xs(tc)[:, jt, :],
                    op=ALU.add), inc="v3", event=("Aev", tc, jt))
            statmath(tc, 1024.0, ("Bsum", tc, NJT - 1), ("var0", tc), ("sd0", tc), ("stat0", tc))
            for jt in range(NJT):
                p.wait(v_, "mm3", ("rs0b", tc))
                p.wait(v_, "mm3", ("m20b", tc))
                vw()
                p.emit(lambda jt=jt: v_.tensor_mul(out=t["v"][jt % 2][:], in0=t["rbf"][:, jt, :],
                                                   in1=ps[4][:, :]), inc="v3")
                vw()
                p.emit(lambda jt=jt: v_.tensor_sub(out=t["x1"][:, jt, :], in0=t["v"][jt % 2][:],
                                                   in1=ps[5][:, :]), inc="v3", event=("x1", tc, jt))
                vw()
                p.emit(lambda jt=jt: v_.tensor_scalar(
                    out=t["rbf"][:, jt, :], in0=t["x1"][:, jt, :],
                    scalar1=t["lng"][:, jt:jt + 1], scalar2=t["lnb"][:, jt:jt + 1],
                    op0=ALU.mult, op1=ALU.add), inc="v3", event=("l1o", tc, jt))
            p.emit(lambda: v_.scalar_tensor_tensor(out=t["mx"][:, :], in0=t["rbf"][:, 0, :],
                                                   scalar=-1.0, in1=t["rbf"][:, 0, :],
                                                   op0=ALU.mult, op1=ALU.max), inc="v3")
            for jt in range(1, NJT):
                vw()
                p.emit(lambda jt=jt: v_.tensor_tensor(out=t["mx"][:, :], in0=t["mx"][:, :],
                                                      in1=t["rbf"][:, jt, :], op=ALU.max), inc="v3")
                vw()
                p.emit(lambda jt=jt: v_.scalar_tensor_tensor(out=t["mx"][:, :], in0=t["rbf"][:, jt, :],
                                                             scalar=-1.0, in1=t["mx"][:, :],
                                                             op0=ALU.mult, op1=ALU.max),
                       inc="v3", event=("mx1", tc) if jt == NJT - 1 else None)
            scalerow(tc, None, ("mxT1", tc), ("sT1", tc), ("s1c", tc), ("sfl", tc, 1), 0)
            p.wait(v_, "mm3", ("sb1", tc))
            p.emit(lambda: v_.tensor_copy(out=t["sbc"][:, :], in_=ps[4][:, :]), inc="v3")
            p.wait(v_, "mm3", ("rb1", tc))
            vw()
            p.emit(lambda: v_.tensor_copy(out=t["rbc"][:, :], in_=ps[5][:, :]),
                   inc="v3", event=("sbc1ev", tc))
            for jt in range(NJT):
                vw()
                p.emit(lambda jt=jt: v_.tensor_mul(out=t["v"][jt % 2][:], in0=t["rbf"][:, jt, :],
                                                   in1=t["sbc"][:, :]),
                       inc="v3", event=("v1", tc, jt))
                vw()
                p.emit(lambda jt=jt: v_.tensor_scalar_add(out=t["v"][jt % 2][:], in0=t["v"][jt % 2][:],
                                                          scalar1=0.5), inc="v3")
                vw()
                p.emit(lambda jt=jt: v_.tensor_copy(out=t["qi"][jt % 2][:], in_=t["v"][jt % 2][:]),
                       inc="v3")
                vw()
                p.emit(lambda jt=jt: v_.tensor_copy(out=t["sg"][jt % 2][:], in_=t["qi"][jt % 2][:]),
                       inc="v3")
                vw()
                p.emit(lambda jt=jt: v_.tensor_tensor(out=t["msk"][:], in0=t["sg"][jt % 2][:],
                                                      in1=t["v"][jt % 2][:], op=ALU.is_gt), inc="v3")
                vw()
                p.emit(lambda jt=jt: v_.tensor_tensor(out=t["q1"][:, jt, :], in0=t["sg"][jt % 2][:],
                                                      in1=t["msk"][:], op=ALU.subtract),
                       inc="v3", event=("q1", tc) if jt == NJT - 1 else ("q1t", tc, jt))
            for jt2 in range(NHT):
                p.wait(v_, "mm3", ("C", tc, jt2))
                if jt2 >= 2:
                    p.wait(v_, "sc3", ("gelu", tc, jt2 - 2))
                p.emit(lambda jt2=jt2: v_.tensor_mul(out=t["v"][jt2 % 2][:], in0=ps[jt2 % 2][:, :],
                                                     in1=t["rbc"][:, :]),
                       inc="v3", event=("Cz", tc, jt2))
            statmath(tc, 4096.0, ("Dsum", tc, NHT - 1), ("var2", tc), ("sd2", tc), ("stat2", tc))
            for kt2 in range(NHT):
                p.wait(v_, "mm3", ("rs2b", tc))
                p.wait(v_, "mm3", ("m22b", tc))
                vw()
                p.emit(lambda kt2=kt2: v_.tensor_mul(out=t["v"][kt2 % 2][:], in0=t["g1"][:, kt2, :],
                                                     in1=ps[4][:, :]), inc="v3")
                vw()
                p.emit(lambda kt2=kt2: v_.tensor_sub(out=t["v"][kt2 % 2][:], in0=t["v"][kt2 % 2][:],
                                                     in1=ps[5][:, :]), inc="v3")
                vw()
                p.emit(lambda kt2=kt2: v_.tensor_scalar(
                    out=t["g1"][:, kt2, :], in0=t["v"][kt2 % 2][:],
                    scalar1=t["ln2g"][:, kt2:kt2 + 1], scalar2=t["ln2b"][:, kt2:kt2 + 1],
                    op0=ALU.mult, op1=ALU.add), inc="v3", event=("l2o", tc, kt2))
            p.emit(lambda: v_.scalar_tensor_tensor(out=t["mx"][:, :], in0=t["g1"][:, 0, :],
                                                   scalar=-1.0, in1=t["g1"][:, 0, :],
                                                   op0=ALU.mult, op1=ALU.max), inc="v3")
            for kt2 in range(1, NHT):
                vw()
                p.emit(lambda kt2=kt2: v_.tensor_tensor(out=t["mx"][:, :], in0=t["mx"][:, :],
                                                        in1=t["g1"][:, kt2, :], op=ALU.max), inc="v3")
                vw()
                p.emit(lambda kt2=kt2: v_.scalar_tensor_tensor(out=t["mx"][:, :], in0=t["g1"][:, kt2, :],
                                                               scalar=-1.0, in1=t["mx"][:, :],
                                                               op0=ALU.mult, op1=ALU.max),
                       inc="v3", event=("mx2", tc) if kt2 == NHT - 1 else None)
            scalerow(tc, None, ("mxT2", tc), ("sT2", tc), ("s2c", tc), ("sfl", tc, 2), 1)
            p.wait(v_, "mm3", ("sb2", tc))
            p.emit(lambda: v_.tensor_copy(out=t["sbc"][:, :], in_=ps[4][:, :]), inc="v3")
            p.wait(v_, "mm3", ("rb2", tc))
            vw()
            p.emit(lambda: v_.tensor_copy(out=t["rbc"][:, :], in_=ps[5][:, :]),
                   inc="v3", event=("sbc2ev", tc))
            for kt2 in range(NHT):
                vw()
                p.emit(lambda kt2=kt2: v_.tensor_mul(out=t["v"][kt2 % 2][:], in0=t["g1"][:, kt2, :],
                                                     in1=t["sbc"][:, :]),
                       inc="v3", event=("v2", tc, kt2))
                vw()
                p.emit(lambda kt2=kt2: v_.tensor_scalar_add(out=t["v"][kt2 % 2][:], in0=t["v"][kt2 % 2][:],
                                                            scalar1=0.5), inc="v3")
                vw()
                p.emit(lambda kt2=kt2: v_.tensor_copy(out=t["qi"][kt2 % 2][:], in_=t["v"][kt2 % 2][:]),
                       inc="v3")
                vw()
                p.emit(lambda kt2=kt2: v_.tensor_copy(out=t["sg"][kt2 % 2][:], in_=t["qi"][kt2 % 2][:]),
                       inc="v3")
                vw()
                p.emit(lambda kt2=kt2: v_.tensor_tensor(out=t["msk"][:], in0=t["sg"][kt2 % 2][:],
                                                        in1=t["v"][kt2 % 2][:], op=ALU.is_gt), inc="v3")
                vw()
                if kt2 >= 2:
                    p.wait(v_, "mm3", ("F", tc, kt2 - 2))
                p.emit(lambda kt2=kt2: v_.tensor_tensor(out=t["qt"][kt2 % 2][:], in0=t["sg"][kt2 % 2][:],
                                                        in1=t["msk"][:], op=ALU.subtract),
                       inc="v3", event=("q2", tc, kt2))
            for jt in range(NJT):
                p.wait(v_, "mm3", ("F", tc, NHT - 1))
                p.emit(lambda jt=jt: v_.tensor_mul(out=t["v"][jt % 2][:], in0=ps[jt][:, :],
                                                   in1=t["rbc"][:, :]), inc="v3")
                vw()
                p.emit(lambda jt=jt: v_.tensor_scalar(
                    out=t["v"][jt % 2][:], in0=t["v"][jt % 2][:],
                    scalar1=t["b2"][:, jt:jt + 1], scalar2=None, op0=ALU.add), inc="v3")
                vw()
                if tc >= 1:
                    p.wait(v_, "st3", ("outst", tc - 1, 7))
                p.emit(lambda jt=jt: v_.tensor_tensor(out=t["r3"][:, jt, :], in0=t["v"][jt % 2][:],
                                                      in1=t["x1"][:, jt, :], op=ALU.add),
                       inc="v3", event=("Gap", tc, jt))
                vw()
                p.emit(lambda jt=jt: v_.tensor_copy(out=t["rbf"][:, jt, :], in_=t["r3"][:, jt, :]),
                       inc="v3", event=("r3bf", tc, jt))
            statmath(tc, 1024.0, ("Gsum", tc, NJT - 1), ("var3", tc), ("sd3", tc), ("stat3", tc))
            for jt in range(NJT):
                p.wait(v_, "mm3", ("rs3b", tc))
                p.wait(v_, "mm3", ("m23b", tc))
                vw()
                p.emit(lambda jt=jt: v_.tensor_mul(out=t["v"][jt % 2][:], in0=t["r3"][:, jt, :],
                                                   in1=ps[4][:, :]), inc="v3")
                vw()
                p.emit(lambda jt=jt: v_.tensor_sub(out=t["r3"][:, jt, :], in0=t["v"][jt % 2][:],
                                                   in1=ps[5][:, :]), inc="v3", event=("outj", tc, jt))

    p.begin("gpsimd"); stream_gpsimd(E.get("gpsimd"))
    p.begin("sync"); stream_sync(E.get("sync"))
    p.begin("tensor"); stream_tensor(E.get("tensor"))
    p.begin("scalar"); stream_scalar(E.get("scalar"))
    p.begin("vector"); stream_vector(E.get("vector"))


L3_SEMS2 = ["ls", "ly", "lw", "q0", "q1", "qo", "qx", "sx", "mm3", "sc3", "v3", "st3"]


def l3_alloc2(nc, E, T3=2048):
    TCH = 512
    t = {}
    t["wbuf"] = E(nc.sbuf_tensor("l3_wbuf", [128, 32, 1024], BF16))
    t["x"] = E(nc.sbuf_tensor("l3_x", [128, 8, TCH], BF16))
    t["y2"] = E(nc.sbuf_tensor("l3_y2", [128, 16, TCH], BF16))
    t["x1"] = E(nc.sbuf_tensor("l3_x1", [128, 8, TCH], BF16))
    t["rbf"] = E(nc.sbuf_tensor("l3_rbf", [128, 8, TCH], BF16))
    t["q1"] = E(nc.sbuf_tensor("l3_q1", [128, 8, TCH], BF16))
    t["g1"] = E(nc.sbuf_tensor("l3_g1", [128, 32, TCH], BF16))
    t["sq"] = E(nc.sbuf_tensor("l3_sq", [128, TCH], BF16))
    t["mx"] = E(nc.sbuf_tensor("l3_mx", [128, TCH], F32))
    t["v"] = [E(nc.sbuf_tensor(f"l3_v{i}", [128, TCH], F32)) for i in range(2)]
    t["sg"] = [E(nc.sbuf_tensor(f"l3_sg{i}", [128, TCH], F32)) for i in range(2)]
    t["msk"] = E(nc.sbuf_tensor("l3_msk", [128, TCH], F32))
    t["qi"] = [E(nc.sbuf_tensor(f"l3_qi{i}", [128, TCH], mybir.dt.int32)) for i in range(2)]
    t["qt"] = [E(nc.sbuf_tensor(f"l3_qt{i}", [128, TCH], BF16)) for i in range(2)]
    t["r3"] = E(nc.sbuf_tensor("l3_r3", [128, 8, TCH], F32))
    t["stat"] = E(nc.sbuf_tensor("l3_stat", [1, TCH * 7], F32))
    t["smax"] = E(nc.sbuf_tensor("l3_smax", [128, 8], F32))
    t["sflat"] = E(nc.sbuf_tensor("l3_sflat", [8, 128], F32))
    t["sbc"] = E(nc.sbuf_tensor("l3_sbc", [128, TCH], F32))
    t["rbc"] = E(nc.sbuf_tensor("l3_rbc", [128, TCH], F32))
    t["onesb"] = E(nc.sbuf_tensor("l3_onesb", [128, 1], BF16))
    t["onesf"] = E(nc.sbuf_tensor("l3_onesf", [128, 1], F32))
    t["onesrow"] = E(nc.sbuf_tensor("l3_onesrow", [1, 128], F32))
    t["ident"] = E(nc.sbuf_tensor("l3_ident", [128, 128], BF16))
    t["identf"] = E(nc.sbuf_tensor("l3_identf", [128, 128], F32))
    t["lng"] = E(nc.sbuf_tensor("l3_lng", [128, 8], F32))
    t["lnb"] = E(nc.sbuf_tensor("l3_lnb", [128, 8], F32))
    t["ln2g"] = E(nc.sbuf_tensor("l3_ln2g", [128, 32], F32))
    t["ln2b"] = E(nc.sbuf_tensor("l3_ln2b", [128, 32], F32))
    t["b1"] = E(nc.sbuf_tensor("l3_b1", [128, 32], F32))
    t["b2"] = E(nc.sbuf_tensor("l3_b2", [128, 8], F32))
    t["swc"] = E(nc.sbuf_tensor("l3_swc", [128, 8], F32))
    t["red"] = E(nc.sbuf_tensor("l3_red", [128, 16], F32))
    t["scl"] = E(nc.sbuf_tensor("l3_scl", [1, 64], F32))
    t["cst"] = E(nc.sbuf_tensor("l3_cst", [1, 8], F32))
    t["ps"] = [E(nc.psum_tensor(f"l3_ps{i}", [128, TCH], F32)) for i in range(8)]
    return t


TT = 16384
TC = 2048
NCHUNK = TT // TC
NB_CHUNK = NCHUNK // 2
NG = 32
SPAN = min(512, TC)
NSPAN = TC // SPAN

def build_l2():
    nc = bass.Bass("TRN2", target_bir_lowering=False, debug=False)
    drep = nc.dram_tensor("drep", [NG, 128, TT], BF16, kind="ExternalInput").ap()
    durep = nc.dram_tensor("durep", [NG, 128, TT], BF16, kind="ExternalInput").ap()
    u_cm = nc.dram_tensor("u_cm", [2, 128, TT], BF16, kind="ExternalInput").ap()
    gs_cm = nc.dram_tensor("gs_cm", [2, 128, TT], BF16, kind="ExternalInput").ap()
    bdn = nc.dram_tensor("bdn", [128, TT], BF16, kind="ExternalInput").ap()
    cdn = nc.dram_tensor("cdn", [128, TT], BF16, kind="ExternalInput").ap()
    alog_dn = nc.dram_tensor("alog_dn", [128, NG], F32, kind="ExternalInput").ap()
    wsel = nc.dram_tensor("wsel", [128, 16, 128], BF16, kind="ExternalInput").ap()
    wdiag = nc.dram_tensor("wdiag", [128, 2, 128], F32, kind="ExternalInput").ap()
    y2 = nc.dram_tensor("y2", [2, 128, TT], BF16, kind="ExternalOutput").ap()

    iters = [(ch, ct, g16) for ch in range(NCHUNK) for ct in range(2) for g16 in range(16)]

    es = contextlib.ExitStack()
    with es:
        E = es.enter_context
        t_drep = [E(nc.sbuf_tensor(f"t_drep{i}", [128, TC], BF16)) for i in range(2)]
        t_durep = [E(nc.sbuf_tensor(f"t_durep{i}", [128, TC], BF16)) for i in range(2)]
        t_bdn = [E(nc.sbuf_tensor(f"t_bdn{i}", [128, TC], BF16)) for i in range(2)]
        t_cdn = [E(nc.sbuf_tensor(f"t_cdn{i}", [128, TC], BF16)) for i in range(2)]
        t_u = [E(nc.sbuf_tensor(f"t_u{i}", [128, 2, TC], BF16)) for i in range(2)]
        t_gs = [E(nc.sbuf_tensor(f"t_gs{i}", [128, 2, TC], BF16)) for i in range(2)]
        t_dA = [E(nc.sbuf_tensor(f"t_dA{i}", [128, TC], F32)) for i in range(2)]
        t_dBu = [E(nc.sbuf_tensor(f"t_dBu{i}", [128, TC], BF16)) for i in range(2)]
        t_hs = [E(nc.sbuf_tensor(f"t_hs{i}", [128, TC], BF16)) for i in range(2)]
        t_prod = [E(nc.sbuf_tensor(f"t_prod{i}", [128, TC], BF16)) for i in range(2)]
        t_y2 = [E(nc.sbuf_tensor(f"t_y2{i}", [128, TC], BF16)) for i in range(2)]
        t_wsel = E(nc.sbuf_tensor("t_wsel", [128, 16, 128], BF16))
        t_wdiagf = E(nc.sbuf_tensor("t_wdiagf", [128, 2, 128], F32))
        t_wdiag = E(nc.sbuf_tensor("t_wdiag", [128, 2, 128], BF16))
        t_alog = E(nc.sbuf_tensor("t_alog", [128, NG], F32))
        t_expA = E(nc.sbuf_tensor("t_expA", [128, NG], F32))
        t_negA = E(nc.sbuf_tensor("t_negA", [128, NG], F32))
        t_state = E(nc.sbuf_tensor("t_state", [128, NG], F32))
        psum = [E(nc.psum_tensor(f"ps{i}", [128, SPAN], F32)) for i in range(8)]

        s_drep = [E(nc.semaphore(f"s_drep{i}")) for i in range(2)]
        s_durep = [E(nc.semaphore(f"s_durep{i}")) for i in range(2)]
        s_bdn = [E(nc.semaphore(f"s_bdn{i}")) for i in range(2)]
        s_cdn = [E(nc.semaphore(f"s_cdn{i}")) for i in range(2)]
        s_u = [E(nc.semaphore(f"s_u{i}")) for i in range(2)]
        s_gs = [E(nc.semaphore(f"s_gs{i}")) for i in range(2)]
        s_st = [E(nc.semaphore(f"s_st{i}")) for i in range(2)]
        s_pre = E(nc.semaphore("s_pre"))
        s_dA = E(nc.semaphore("s_dA"))
        s_v = E(nc.semaphore("s_v"))
        s_mm = E(nc.semaphore("s_mm"))
        s_ev = E(nc.semaphore("s_ev"))
        s_sc = E(nc.semaphore("s_sc"))
        s_vA = E(nc.semaphore("s_vA"))
        vch = E(nc.semaphore("vch"))
        block = E(nc.Block())

        @block.sync
        def _(sync):
            sync.dma_start(out=t_wsel[:], in_=wsel).then_inc(s_pre, 16)
            sync.dma_start(out=t_wdiagf[:], in_=wdiag).then_inc(s_pre, 16)
            sync.dma_start(out=t_alog[:], in_=alog_dn).then_inc(s_pre, 16)
            for ch in range(NCHUNK):
                t0 = ch * TC
                sl = ch % 2
                if ch >= 2:
                    sync.wait_ge(s_v, (ch - 1) * 32)
                    sync.wait_ge(s_ev, (ch - 1) * 2)
                    sync.wait_ge(s_mm, (ch - 1) * 32)
                sync.dma_start(out=t_bdn[sl][:], in_=bdn[:, t0:t0 + TC]).then_inc(s_bdn[sl], 16)
                sync.dma_start(out=t_cdn[sl][:], in_=cdn[:, t0:t0 + TC]).then_inc(s_cdn[sl], 16)
                for c_ in range(2):
                    sync.dma_start(out=t_u[sl][:, c_, :], in_=u_cm[c_, :, t0:t0 + TC]).then_inc(s_u[sl], 16)
                    sync.dma_start(out=t_gs[sl][:, c_, :], in_=gs_cm[c_, :, t0:t0 + TC]).then_inc(s_gs[sl], 16)
                for ct in range(2):
                    for g16 in range(16):
                        it = ch * 32 + ct * 16 + g16
                        g = ct * 16 + g16
                        buf = it % 2
                        if it >= 2:
                            sync.wait_ge(s_dA, it - 1)
                            sync.wait_ge(s_v, it - 1)
                        sync.dma_start(out=t_drep[buf][:], in_=drep[g, :, t0:t0 + TC]).then_inc(s_drep[buf], 16)
                        sync.dma_start(out=t_durep[buf][:], in_=durep[g, :, t0:t0 + TC]).then_inc(s_durep[buf], 16)

        @block.scalar
        def _(scalar):
            scalar.wait_ge(s_pre, 48)
            scalar.activation(out=t_expA[:], in_=t_alog[:], func=AF.Exp).then_inc(s_sc, 1)
            scalar.copy(out=t_wdiag[:], in_=t_wdiagf[:]).then_inc(s_sc, 1)
            for it, (ch, ct, g16) in enumerate(iters):
                g = ct * 16 + g16
                buf = it % 2
                if it == 0:
                    scalar.wait_ge(s_vA, 1)
                if g16 == 0 and ch >= 1:
                    ctset_prev = (ch - 1) * 2 + ct
                    scalar.wait_ge(s_ev, ctset_prev + 1)
                    scalar.dma_start(out=y2[ct, :, (ch - 1) * TC:ch * TC],
                                     in_=t_y2[ct][:]).then_inc(s_st[ct], 16)
                scalar.wait_ge(s_drep[buf], 16 * (it // 2 + 1))
                if it >= 2:
                    scalar.wait_ge(s_v, it - 1)
                scalar.activation(out=t_dA[buf][:], in_=t_drep[buf][:], func=AF.Exp,
                                  scale=t_negA[:, g:g + 1]).then_inc(s_dA, 1)
            for ct in range(2):
                ctset = (NCHUNK - 1) * 2 + ct
                scalar.wait_ge(s_ev, ctset + 1)
                scalar.dma_start(out=y2[ct, :, (NCHUNK - 1) * TC:NCHUNK * TC],
                                 in_=t_y2[ct][:]).then_inc(s_st[ct], 16)
            scalar.wait_ge(s_st[0], NCHUNK * 16)
            scalar.wait_ge(s_st[1], NCHUNK * 16)

        @block.vector
        def _(vector):
            vector.wait_ge(s_sc, 1)
            vector.tensor_scalar_mul(out=t_negA[:], in0=t_expA[:], scalar1=-1.0).then_inc(s_vA, 1)
            vchn = [0]

            def vinc(ins):
                vchn[0] += 1
                ins.then_inc(vch, 1)
                return vchn[0]

            for it, (ch, ct, g16) in enumerate(iters):
                g = ct * 16 + g16
                buf = it % 2
                sl = ch % 2
                ctset = ch * 2 + ct
                vector.wait_ge(s_durep[buf], 16 * (it // 2 + 1))
                vector.wait_ge(s_bdn[sl], 16 * (ch // 2 + 1))
                vector.wait_ge(s_dA, it + 1)
                k_dbu = vinc(vector.tensor_tensor(out=t_dBu[buf][:], in0=t_durep[buf][:],
                                                  in1=t_bdn[sl][:], op=ALU.mult))
                vector.wait_ge(vch, k_dbu)
                init = 0.0 if ch % NB_CHUNK == 0 else t_state[:, g:g + 1]
                k_scan = vinc(vector.tensor_tensor_scan(
                    out=t_hs[buf][:], data0=t_dA[buf][:], data1=t_dBu[buf][:],
                    initial=init, op0=ALU.mult, op1=ALU.add))
                vector.wait_ge(vch, k_scan)
                vinc(vector.tensor_copy(out=t_state[:, g:g + 1], in_=t_hs[buf][:, TC - 1:TC]))
                if it >= 2:
                    vector.wait_ge(s_mm, it - 1)
                vector.wait_ge(s_cdn[sl], 16 * (ch // 2 + 1))
                vector.tensor_tensor(out=t_prod[buf][:], in0=t_hs[buf][:],
                                     in1=t_cdn[sl][:], op=ALU.mult).then_inc(s_v, 1)
                if g16 == 15:
                    vector.wait_ge(s_mm, it + 1)
                    if ch >= 1:
                        vector.wait_ge(s_st[ct], 16 * ch)
                    vector.wait_ge(s_gs[sl], 32 * (ch // 2 + 1))
                    pb = (ct % 2) * 4
                    for sp in range(NSPAN):
                        ins = vector.tensor_tensor(
                            out=t_y2[ct][:, sp * SPAN:(sp + 1) * SPAN],
                            in0=psum[pb + sp][:, :],
                            in1=t_gs[sl][:, ct, sp * SPAN:(sp + 1) * SPAN],
                            op=ALU.mult)
                        if sp == NSPAN - 1:
                            ins.then_inc(s_ev, 1)

        @block.tensor
        def _(tensor):
            tensor.wait_ge(s_pre, 48)
            tensor.wait_ge(s_sc, 2)
            for it, (ch, ct, g16) in enumerate(iters):
                buf = it % 2
                sl = ch % 2
                ctset = ch * 2 + ct
                tensor.wait_ge(s_v, it + 1)
                if g16 == 0:
                    if ctset >= 2:
                        tensor.wait_ge(s_ev, ctset - 1)
                    tensor.wait_ge(s_u[sl], 32 * (ch // 2 + 1))
                pb = (ct % 2) * 4
                last = None
                for sp in range(NSPAN):
                    last = tensor.matmul(psum[pb + sp][:, :], t_wsel[:, g16, :],
                                         t_prod[buf][:, sp * SPAN:(sp + 1) * SPAN],
                                         start=(g16 == 0), stop=False)
                if g16 == 15:
                    for sp in range(NSPAN):
                        last = tensor.matmul(psum[pb + sp][:, :], t_wdiag[:, ct, :],
                                             t_u[sl][:, ct, sp * SPAN:(sp + 1) * SPAN],
                                             start=False, stop=(sp == NSPAN - 1))
                last.then_inc(s_mm, 1)
    return nc




T1 = 2048
T3 = 2048


def _engs(g=None, s=None, te=None, sc=None, v=None):
    return {"gpsimd": g, "sync": s, "tensor": te, "scalar": sc, "vector": v}


def build_l1():
    nc = bass.Bass("TRN2", target_bir_lowering=False, debug=False)
    d = {}
    d["xT"] = nc.dram_tensor("xT", [1024, T1 + 3], F32, kind="ExternalInput").ap()
    d["w_inT"] = nc.dram_tensor("w_inT", [1024, 4096], F32, kind="ExternalInput").ap()
    d["conv_w_cm"] = nc.dram_tensor("conv_w_cm", [128, 16, 4], F32, kind="ExternalInput").ap()
    d["conv_b_cm"] = nc.dram_tensor("conv_b_cm", [128, 16], F32, kind="ExternalInput").ap()
    d["w_xT"] = nc.dram_tensor("w_xT", [2048, 96], F32, kind="ExternalInput").ap()
    d["w_dtT"] = nc.dram_tensor("w_dtT", [64, 16, 128], F32, kind="ExternalInput").ap()
    d["dt_bias_cm"] = nc.dram_tensor("dt_bias_cm", [128, 16], F32, kind="ExternalInput").ap()
    d["o_u"] = nc.dram_tensor("o_u", [128, 16, T1], BF16, kind="ExternalOutput").ap()
    d["o_du"] = nc.dram_tensor("o_du", [128, 16, T1], BF16, kind="ExternalOutput").ap()
    d["o_dl"] = nc.dram_tensor("o_dl", [128, 16, T1], BF16, kind="ExternalOutput").ap()
    d["o_gs"] = nc.dram_tensor("o_gs", [128, 16, T1], BF16, kind="ExternalOutput").ap()
    d["o_bc"] = nc.dram_tensor("o_bc", [32, T1], BF16, kind="ExternalOutput").ap()
    es = contextlib.ExitStack()
    with es:
        E = es.enter_context
        t = l1_alloc(nc, E, T1)
        sems = {n: E(nc.semaphore("s_" + n)) for n in L1_SEMS}
        events = {}
        l1_streams(Pass(True, None, events), t, d, None, T1)
        block = E(nc.Block())

        @block.gpsimd
        def _(g):
            l1_streams(Pass(False, sems, events, only="gpsimd"), t, d, _engs(g=g), T1)

        @block.sync
        def _(s):
            l1_streams(Pass(False, sems, events, only="sync"), t, d, _engs(s=s), T1)

        @block.tensor
        def _(te):
            l1_streams(Pass(False, sems, events, only="tensor"), t, d, _engs(te=te), T1)

        @block.scalar
        def _(sc):
            l1_streams(Pass(False, sems, events, only="scalar"), t, d, _engs(sc=sc), T1)

        @block.vector
        def _(v):
            l1_streams(Pass(False, sems, events, only="vector"), t, d, _engs(v=v), T1)
    return nc


def build_l3():
    nc = bass.Bass("TRN2", target_bir_lowering=False, debug=False)
    d = {}
    d["xT_s"] = nc.dram_tensor("xT_s", [128, 8, T3], F32, kind="ExternalInput").ap()
    d["y2T"] = nc.dram_tensor("y2T", [128, 16, T3], BF16, kind="ExternalInput").ap()
    d["w_outT_s"] = nc.dram_tensor("w_outT_s", [128, 16, 1024], F32, kind="ExternalInput").ap()
    d["w1T_s"] = nc.dram_tensor("w1T_s", [128, 8, 4096], F32, kind="ExternalInput").ap()
    d["w2T_s"] = nc.dram_tensor("w2T_s", [128, 8, 4096], F32, kind="ExternalInput").ap()
    for nm, c in (("lng", 8), ("lnb", 8), ("ln2g", 32), ("ln2b", 32), ("b1", 32), ("b2", 8)):
        d[nm] = nc.dram_tensor(nm, [128, c], F32, kind="ExternalInput").ap()
    d["ident"] = nc.dram_tensor("ident", [128, 128], BF16, kind="ExternalInput").ap()
    d["identf"] = nc.dram_tensor("identf", [128, 128], F32, kind="ExternalInput").ap()
    d["onesb"] = nc.dram_tensor("onesb", [128, 1], BF16, kind="ExternalInput").ap()
    d["onesf"] = nc.dram_tensor("onesf", [128, 1], F32, kind="ExternalInput").ap()
    d["onesrow"] = nc.dram_tensor("onesrow", [1, 128], F32, kind="ExternalInput").ap()
    d["cst"] = nc.dram_tensor("cst", [1, 8], F32, kind="ExternalInput").ap()
    d["w1q_d"] = nc.dram_tensor("w1q_d", [128, 8, 4096], BF16, kind="Internal").ap()
    d["w2q_d"] = nc.dram_tensor("w2q_d", [128, 8, 4096], BF16, kind="Internal").ap()
    d["outT_s"] = nc.dram_tensor("outT_s", [128, 8, T3], F32, kind="ExternalOutput").ap()
    es = contextlib.ExitStack()
    with es:
        E = es.enter_context
        t = l3_alloc2(nc, E, T3)
        sems = {n: E(nc.semaphore("s_" + n)) for n in L3_SEMS2}
        events = {}
        l3_streams(Pass(True, None, events), t, d, None, T3)
        block = E(nc.Block())

        @block.gpsimd
        def _(g):
            l3_streams(Pass(False, sems, events, only="gpsimd"), t, d, _engs(g=g), T3)

        @block.sync
        def _(s):
            l3_streams(Pass(False, sems, events, only="sync"), t, d, _engs(s=s), T3)

        @block.tensor
        def _(te):
            l3_streams(Pass(False, sems, events, only="tensor"), t, d, _engs(te=te), T3)

        @block.scalar
        def _(sc):
            l3_streams(Pass(False, sems, events, only="scalar"), t, d, _engs(sc=sc), T3)

        @block.vector
        def _(v):
            l3_streams(Pass(False, sems, events, only="vector"), t, d, _engs(v=v), T3)
    return nc


_P = np.arange(128)


def _pack_l1(x, W_in, conv_w, conv_b, W_x, W_dt, dt_bias):
    cw = np.ascontiguousarray(conv_w.reshape(16, 128, 4).transpose(1, 0, 2))
    cb = np.ascontiguousarray(conv_b.reshape(16, 128).T)
    db = np.ascontiguousarray(dt_bias.reshape(16, 128).T)
    w_inT = np.ascontiguousarray(W_in.T)
    w_xT = np.ascontiguousarray(W_x.T)
    wdt = np.ascontiguousarray(W_dt.reshape(16, 128, 64).transpose(2, 0, 1))
    maps = []
    for c in range(8):
        b, q = c // 4, c % 4
        t0 = q * 2048
        xs = np.zeros((2048 + 3, 1024), np.float32)
        lo = max(0, t0 - 3)
        xs[3 - (t0 - lo):] = x[b, lo:t0 + 2048]
        maps.append({"xT": np.ascontiguousarray(xs.T), "w_inT": w_inT,
                     "conv_w_cm": cw, "conv_b_cm": cb, "dt_bias_cm": db,
                     "w_xT": w_xT, "w_dtT": wdt})
    return maps


def _pack_l2(r1, A_log, D_skip):
    def full(key):
        return np.concatenate(
            [np.ascontiguousarray(r[key]).transpose(1, 0, 2).reshape(2048, 2048)
             for r in r1], axis=1)

    dl = full("o_dl")
    du = full("o_du")
    uu = full("o_u")
    gs = full("o_gs")
    bc = np.concatenate([np.ascontiguousarray(r["o_bc"]) for r in r1], axis=1)
    B, C = bc[0:16], bc[16:32]
    n_of_p = _P // 8
    ci_of_p = _P % 8
    wsel = np.zeros((128, 16, 128), bfd)
    for g16 in range(16):
        wsel[_P, g16, g16 * 8 + ci_of_p] = 1
    maps = []
    for k in range(8):
        ch0 = 256 * k
        gmap = ch0 + 8 * np.arange(32)[:, None] + ci_of_p[None, :]
        drep = dl[gmap]
        durep = du[gmap]
        u_cm = uu[ch0:ch0 + 256].reshape(2, 128, TT)
        gs_cm = gs[ch0:ch0 + 256].reshape(2, 128, TT)
        bdn = B[n_of_p]
        cdn = C[n_of_p]
        alog_dn = np.ascontiguousarray(
            A_log[ch0 + 8 * np.arange(32)[None, :] + ci_of_p[:, None], n_of_p[:, None]]
        ).astype(np.float32)
        wdiag = np.zeros((128, 2, 128), np.float32)
        for ct in range(2):
            wdiag[_P, ct, _P] = D_skip[ch0 + ct * 128 + _P]
        maps.append({"drep": np.ascontiguousarray(drep),
                     "durep": np.ascontiguousarray(durep),
                     "u_cm": np.ascontiguousarray(u_cm),
                     "gs_cm": np.ascontiguousarray(gs_cm),
                     "bdn": np.ascontiguousarray(bdn),
                     "cdn": np.ascontiguousarray(cdn),
                     "alog_dn": alog_dn, "wsel": wsel, "wdiag": wdiag})
    return maps


def _pack_l3(x, r2, W_out, w1, b1, w2, b2, lng, lnb, ln2g, ln2b):
    y2_full = np.concatenate(
        [np.ascontiguousarray(r["y2"]).reshape(256, TT) for r in r2], axis=0)
    woT = np.ascontiguousarray(W_out.T.reshape(16, 128, 1024).transpose(1, 0, 2))
    w1T = np.ascontiguousarray(w1.T.reshape(8, 128, 4096).transpose(1, 0, 2))
    w2l = np.ascontiguousarray(w2.T.reshape(32, 128, 1024).transpose(1, 0, 2))
    w2T = np.ascontiguousarray(w2l.reshape(128, 8, 4096))
    small = {
        "lng": np.ascontiguousarray(lng.reshape(8, 128).T),
        "lnb": np.ascontiguousarray(lnb.reshape(8, 128).T),
        "ln2g": np.ascontiguousarray(ln2g.reshape(32, 128).T),
        "ln2b": np.ascontiguousarray(ln2b.reshape(32, 128).T),
        "b1": np.ascontiguousarray(b1.reshape(32, 128).T),
        "b2": np.ascontiguousarray(b2.reshape(8, 128).T),
        "ident": np.eye(128, dtype=bfd),
        "identf": np.eye(128, dtype=np.float32),
        "onesb": np.ones((128, 1), bfd),
        "onesf": np.ones((128, 1), np.float32),
        "onesrow": np.ones((1, 128), np.float32),
        "cst": np.array([[1e-5, 0, 0, 0, 0, 0, 0, 0]], np.float32),
    }
    maps = []
    for c in range(8):
        b, q = c // 4, c % 4
        t0 = q * 2048
        xT = np.ascontiguousarray(x[b, t0:t0 + 2048].T)
        xT_s = np.ascontiguousarray(xT.reshape(8, 128, 2048).transpose(1, 0, 2))
        y2T = np.ascontiguousarray(
            y2_full[:, c * 2048:(c + 1) * 2048].reshape(16, 128, 2048).transpose(1, 0, 2))
        m = {"xT_s": xT_s, "y2T": y2T, "w_outT_s": woT, "w1T_s": w1T, "w2T_s": w2T}
        m.update(small)
        maps.append(m)
    return maps


_NCS = {}


def _get(name, builder):
    if name not in _NCS:
        _NCS[name] = builder()
    return _NCS[name]


def kernel(L0_attn_hidden_states, L0_attn_W_in, L0_attn_conv_w, L0_attn_conv_b,
           L0_attn_W_x, L0_attn_W_dt, L0_attn_dt_bias, L0_attn_A_log, L0_attn_D_skip,
           L0_attn_W_out, L0_ffn_ln1_g, L0_ffn_ln1_b, L0_ffn_w1, L0_ffn_b1,
           L0_ffn_ln2_g, L0_ffn_ln2_b, L0_ffn_w2, L0_ffn_b2):
    x = np.ascontiguousarray(np.asarray(L0_attn_hidden_states, np.float32))
    m1 = _pack_l1(x, np.asarray(L0_attn_W_in, np.float32),
                  np.asarray(L0_attn_conv_w, np.float32),
                  np.asarray(L0_attn_conv_b, np.float32),
                  np.asarray(L0_attn_W_x, np.float32),
                  np.asarray(L0_attn_W_dt, np.float32),
                  np.asarray(L0_attn_dt_bias, np.float32))
    nc1 = _get("l1", build_l1)
    r1 = run_bass_kernel_spmd(nc1, m1, core_ids=list(range(8))).results

    m2 = _pack_l2(r1, np.asarray(L0_attn_A_log, np.float32),
                  np.asarray(L0_attn_D_skip, np.float32))
    nc2 = _get("l2", build_l2)
    r2 = run_bass_kernel_spmd(nc2, m2, core_ids=list(range(8))).results

    m3 = _pack_l3(x, r2, np.asarray(L0_attn_W_out, np.float32),
                  np.asarray(L0_ffn_w1, np.float32), np.asarray(L0_ffn_b1, np.float32),
                  np.asarray(L0_ffn_w2, np.float32), np.asarray(L0_ffn_b2, np.float32),
                  np.asarray(L0_ffn_ln1_g, np.float32), np.asarray(L0_ffn_ln1_b, np.float32),
                  np.asarray(L0_ffn_ln2_g, np.float32), np.asarray(L0_ffn_ln2_b, np.float32))
    nc3 = _get("l3", build_l3)
    r3 = run_bass_kernel_spmd(nc3, m3, core_ids=list(range(8))).results

    out = np.zeros((2, 8192, 1024), np.float32)
    for c in range(8):
        b, q = c // 4, c % 4
        o = np.asarray(r3[c]["outT_s"], np.float32)
        out[b, q * 2048:(q + 1) * 2048] = o.transpose(2, 1, 0).reshape(2048, 1024)
    return out


# version 02
# speedup vs baseline: 1.1815x; 1.1815x over previous
import numpy as np
import contextlib
import ml_dtypes
import concourse.bass as bass
import concourse.bass_utils
from concourse.bass_utils import run_bass_kernel_spmd

bfd = ml_dtypes.bfloat16
import numpy as np
import concourse.mybir as mybir

F32 = mybir.dt.float32
BF16 = mybir.dt.bfloat16
AF = mybir.ActivationFunctionType
ALU = mybir.AluOpType


class Pass:
    def __init__(self, dry, sems, events, only=None):
        self.dry = dry
        self.sems = sems
        self.events = events
        self.count = {}
        self.only = only
        self.current = None

    def begin(self, name):
        self.current = name

    def _active(self):
        return not self.dry and (self.only is None or self.only == self.current)

    def emit(self, fn, inc=None, step=1, event=None):
        if inc is not None:
            self.count[inc] = self.count.get(inc, 0) + step
            if event is not None:
                self.events[(inc, event)] = self.count[inc]
        if self._active():
            ins = fn()
            if inc is not None:
                ins.then_inc(self.sems[inc], step)

    def wait(self, eng, sem, key=None, value=None):
        if not self._active():
            return
        v = value if value is not None else self.events[(sem, key)]
        eng.wait_ge(self.sems[sem], v)

    def cur(self, sem):
        return self.count.get(sem, 0)


L1_SEMS = ["x", "xc", "w", "wg", "wsm", "small", "mm", "sc", "v", "stu", "stg", "stdu", "stdl", "stbc"]


def l1_alloc(nc, E, T1=2048):
    TCH = 512
    t = {}
    t["x"] = [E(nc.sbuf_tensor(f"l1_x{i}", [128, 8, TCH + 3], BF16)) for i in range(2)]
    t["win"] = E(nc.sbuf_tensor("l1_win", [128, 8, 4096], BF16))
    t["wx"] = E(nc.sbuf_tensor("l1_wx", [128, 16, 96], BF16))
    t["wdt"] = E(nc.sbuf_tensor("l1_wdt", [64, 16, 128], BF16))
    t["cw"] = E(nc.sbuf_tensor("l1_cw", [128, 16, 4], F32))
    t["cb"] = E(nc.sbuf_tensor("l1_cb", [128, 16], F32))
    t["db"] = E(nc.sbuf_tensor("l1_db", [128, 16], F32))
    t["tail"] = E(nc.sbuf_tensor("l1_tail", [128, 16, 3], F32))
    t["cv"] = [E(nc.sbuf_tensor(f"l1_cv{i}", [128, TCH], F32)) for i in range(6)]
    t["uc"] = [E(nc.sbuf_tensor(f"l1_uc{i}", [128, 16, TCH], BF16)) for i in range(2)]
    t["duc"] = E(nc.sbuf_tensor("l1_duc", [128, 16, TCH], BF16))
    t["dlc"] = E(nc.sbuf_tensor("l1_dlc", [128, 16, TCH], BF16))
    t["gsc"] = E(nc.sbuf_tensor("l1_gsc", [128, 16, TCH], BF16))
    t["dtlr"] = E(nc.sbuf_tensor("l1_dtlr", [64, TCH], BF16))
    t["bcc"] = E(nc.sbuf_tensor("l1_bcc", [32, TCH], BF16))
    t["ab"] = [E(nc.sbuf_tensor(f"l1_ab{i}", [128, TCH], F32)) for i in range(4)]
    t["ps"] = [E(nc.psum_tensor(f"l1_ps{i}", [128, TCH], F32)) for i in range(4)]
    t["ps_pt"] = E(nc.psum_tensor("l1_ps_pt", [128, TCH], F32))
    t["ps_d"] = [E(nc.psum_tensor(f"l1_ps_d{i}", [128, TCH], F32)) for i in range(2)]
    t["ps_g3"] = E(nc.psum_tensor("l1_ps_g3", [128, TCH], F32))
    return t


def _l1_psd(t, ctile):
    return t["ps_d"][ctile % 2]


def _l1_psg(t, ctile):
    return [t["ps"][2], t["ps"][3], t["ps_g3"]][ctile % 3]


def l1_streams(p, t, d, eng, T1=2048):
    TCH = 512
    NCH = T1 // TCH
    NCT = 16
    NKT = 8
    E = eng or {k: None for k in ("sync", "gpsimd", "tensor", "scalar", "vector")}

    def stream_gpsimd(g):
        def xload(tc):
            if tc >= 2:
                p.wait(g, "mm", ("gate", tc - 2, NCT - 1))
            for kt in range(NKT):
                p.emit(lambda tc=tc, kt=kt: g.dma_start(
                    out=t["x"][tc % 2][:, kt, :], in_=d["xT"][tc, kt]),
                    inc="x" if tc % 2 == 0 else "xc", step=16)
        for kt in range(NKT):
            p.emit(lambda kt=kt: g.dma_start(out=t["win"][:, kt, 0:2048], in_=d["w_inT"][kt * 128:(kt + 1) * 128, 0:2048]),
                   inc="w", step=16)
        xload(0)
        for kt in range(NKT):
            p.emit(lambda kt=kt: g.dma_start(out=t["win"][:, kt, 2048:4096], in_=d["w_inT"][kt * 128:(kt + 1) * 128, 2048:4096]),
                   inc="wg", step=16)
        xload(1)
        for kt in range(NCT):
            p.emit(lambda kt=kt: g.dma_start(out=t["wx"][:, kt, :], in_=d["w_xT"][kt * 128:(kt + 1) * 128, :]),
                   inc="wsm", step=16)
        p.emit(lambda: g.dma_start(out=t["wdt"][:, :, :], in_=d["w_dtT"]), inc="wsm", step=16)
        for tc in range(2, NCH):
            xload(tc)

    def stream_sync(s):
        p.emit(lambda: s.dma_start(out=t["cw"][:], in_=d["conv_w_cm"]), inc="small", step=16)
        p.emit(lambda: s.dma_start(out=t["cb"][:], in_=d["conv_b_cm"]), inc="small", step=16)
        p.emit(lambda: s.dma_start(out=t["db"][:], in_=d["dt_bias_cm"]), inc="small", step=16)

        def qst(name, o, tc, q, sem):
            qs = slice(4 * q, 4 * q + 4)
            p.emit(lambda name=name, o=o, tc=tc, qs=qs: s.dma_start(
                out=d[o][tc, :, qs, :], in_=t[name][:, qs, :]), inc=sem, step=16)

        def dd_quarter(tc, q):
            p.wait(s, "v", ("du", tc, 4 * q + 3))
            p.wait(s, "sc", ("dl", tc, 4 * q + 3))
            qst("duc", "o_du", tc, q, "stdu")
            qst("dlc", "o_dl", tc, q, "stdl")
            if q == 0:
                p.wait(s, "sc", ("bcc", tc))
                p.emit(lambda tc=tc: s.dma_start(out=d["o_bc"][tc], in_=t["bcc"][:, :]),
                       inc="stbc", step=16)

        for tc in range(NCH):
            for q in range(4):
                if tc >= 1:
                    dd_quarter(tc - 1, q)
                p.wait(s, "sc", ("u", tc, 4 * q + 3))
                qs = slice(4 * q, 4 * q + 4)
                p.emit(lambda tc=tc, qs=qs: s.dma_start(
                    out=d["o_u"][tc, :, qs, :], in_=t["uc"][tc % 2][:, qs, :]), inc="stu", step=16)
                p.wait(s, "sc", ("gevac", tc, 4 * q + 3))
                qst("gsc", "o_gs", tc, q, "stg")
        for q in range(4):
            dd_quarter(NCH - 1, q)
        for sem in ("stu", "stg", "stdu", "stdl", "stbc"):
            p.wait(s, sem, value=p.cur(sem))

    def stream_tensor(te):
        p.wait(te, "x", value=NKT * 16)
        p.wait(te, "w", value=NKT * 16)
        xwaits = {0: NKT * 16}
        for ctile in range(NCT):
            if ctile >= 1:
                p.wait(te, "sc", ("tail", ctile - 1))
            for kt in range(NKT):
                p.emit(lambda ctile=ctile, kt=kt: te.matmul(
                    t["ps_pt"][:, 0:3], t["win"][:, kt, ctile * 128:(ctile + 1) * 128],
                    t["x"][0][:, kt, 0:3], start=(kt == 0), stop=(kt == NKT - 1)),
                    inc=None if kt < NKT - 1 else "mm",
                    event=None if kt < NKT - 1 else ("halo", ctile))
        def emit_delta(tcd, ctile):
            if tcd == 0 and ctile == 0:
                p.wait(te, "wsm", value=(NCT + 1) * 16)
            if not (tcd == 0 and ctile < 2):
                prev = (tcd, ctile - 2) if ctile >= 2 else (tcd - 1, NCT - 2 + ctile)
                p.wait(te, "sc", ("devac",) + prev)
            p.emit(lambda tcd=tcd, ctile=ctile: te.matmul(
                _l1_psd(t, ctile)[:, :], t["wdt"][:, ctile, :], t["dtlr"][:, :],
                start=True, stop=True),
                inc="mm", event=("delta", tcd, ctile))

        def emit_projt_kt(tc, kt):
            if kt == 0:
                if tc == 0:
                    p.wait(te, "wsm", value=(NCT + 1) * 16)
                    p.wait(te, "sc", ("tail", NCT - 1))
                else:
                    p.wait(te, "sc", ("ptevac", tc - 1))
            p.wait(te, "sc", ("u", tc, kt))
            p.emit(lambda tc=tc, kt=kt: te.matmul(
                t["ps_pt"][0:96, :], t["wx"][:, kt, :], t["uc"][tc % 2][:, kt, :],
                start=(kt == 0), stop=(kt == NCT - 1)),
                inc=None if kt < NCT - 1 else "mm",
                event=None if kt < NCT - 1 else ("projt", tc))

        for tc in range(NCH):
            xb = t["x"][tc % 2]
            p.wait(te, "x" if tc % 2 == 0 else "xc", value=(tc // 2 + 1) * NKT * 16)
            for ctile in range(NCT):
                if not (tc == 0 and ctile < 2):
                    prev = (tc, ctile - 2) if ctile >= 2 else (tc - 1, NCT - 2 + ctile)
                    p.wait(te, "v", ("cpdone",) + prev)
                for kt in range(NKT):
                    p.emit(lambda xb=xb, tc=tc, ctile=ctile, kt=kt: te.matmul(
                        t["ps"][ctile % 2][:, :], t["win"][:, kt, ctile * 128:(ctile + 1) * 128],
                        xb[:, kt, 3:3 + TCH], start=(kt == 0), stop=(kt == NKT - 1)),
                        inc=None if kt < NKT - 1 else "mm",
                        event=None if kt < NKT - 1 else ("proj", tc, ctile))
                if tc >= 1:
                    emit_delta(tc - 1, ctile)
                j = 2048 + ctile * 128
                if tc == 0 and ctile == 0:
                    p.wait(te, "wg", value=NKT * 16)
                if not (tc == 0 and ctile < 3):
                    prev = (tc, ctile - 3) if ctile >= 3 else (tc - 1, NCT - 3 + ctile)
                    p.wait(te, "sc", ("gevac",) + prev)
                for kt in range(NKT):
                    p.emit(lambda xb=xb, tc=tc, ctile=ctile, kt=kt, j=j: te.matmul(
                        _l1_psg(t, ctile)[:, :], t["win"][:, kt, j:j + 128],
                        xb[:, kt, 3:3 + TCH], start=(kt == 0), stop=(kt == NKT - 1)),
                        inc=None if kt < NKT - 1 else "mm",
                        event=None if kt < NKT - 1 else ("gate", tc, ctile))
                if ctile >= 4:
                    emit_projt_kt(tc, ctile - 4)
            for kt in range(NCT - 4, NCT):
                emit_projt_kt(tc, kt)
            p.wait(te, "sc", ("dtlr", tc))
        for ctile in range(NCT):
            emit_delta(NCH - 1, ctile)

    def stream_scalar(sc):
        p.wait(sc, "small", value=48)
        for ctile in range(NCT):
            p.wait(sc, "mm", ("halo", ctile))
            p.emit(lambda ctile=ctile: sc.copy(out=t["tail"][:, ctile, :], in_=t["ps_pt"][:, 0:3]),
                   inc="sc", event=("tail", ctile))
        def emit_sp_exp(sc, tcd, ctile):
            ab = t["ab"][ctile % 4]
            p.wait(sc, "mm", ("delta", tcd, ctile))
            if ctile == 0 and tcd >= 1:
                p.wait(sc, "stdl", value=64 * tcd)
            p.emit(lambda ab=ab, ctile=ctile: sc.activation(
                out=ab[:], in_=_l1_psd(t, ctile)[:, :], func=AF.Exp,
                bias=t["db"][:, ctile:ctile + 1]),
                inc="sc", event=("devac", tcd, ctile))

        def emit_sp_ln(sc, tcd, ctile):
            ab = t["ab"][ctile % 4]
            p.wait(sc, "sc", value=p.cur("sc"))
            p.emit(lambda ab=ab, ctile=ctile: sc.activation(
                out=t["dlc"][:, ctile, :], in_=ab[:], func=AF.Ln, bias=1.0),
                inc="sc", event=("dl", tcd, ctile))

        def emit_softplus(sc, tcd, ctile):
            emit_sp_exp(sc, tcd, ctile)
            emit_sp_ln(sc, tcd, ctile)

        for tc in range(NCH):
            def emit_u(tc, ctile):
                p.wait(sc, "v", ("cv", tc, ctile))
                if tc >= 2 and ctile == 0:
                    p.wait(sc, "v", ("duall", tc - 2))
                    p.wait(sc, "stu", value=64 * (tc - 1))
                p.emit(lambda tc=tc, ctile=ctile: sc.activation(
                    out=t["uc"][tc % 2][:, ctile, :], in_=t["cv"][ctile % 6][:], func=AF.Silu,
                    bias=t["cb"][:, ctile:ctile + 1]),
                    inc="sc", event=("u", tc, ctile))

            def emit_gevac(tc, ctile):
                p.wait(sc, "mm", ("gate", tc, ctile))
                if tc >= 1 and ctile == 0:
                    p.wait(sc, "stg", value=64 * tc)
                p.emit(lambda tc=tc, ctile=ctile: sc.activation(
                    out=t["gsc"][:, ctile, :], in_=_l1_psg(t, ctile)[:, :], func=AF.Silu),
                    inc="sc", event=("gevac", tc, ctile))

            for g in range(NCT // 2):
                c0, c1 = 2 * g, 2 * g + 1
                if g >= 1:
                    emit_u(tc, c0 - 2)
                    emit_u(tc, c1 - 2)
                    emit_gevac(tc, c0 - 2)
                    emit_gevac(tc, c1 - 2)
                if tc >= 1:
                    emit_sp_exp(sc, tc - 1, c0)
                    emit_sp_exp(sc, tc - 1, c1)
                    emit_sp_ln(sc, tc - 1, c0)
                    emit_sp_ln(sc, tc - 1, c1)
            emit_u(tc, NCT - 2)
            emit_u(tc, NCT - 1)
            p.events[("sc", ("uc", tc))] = p.cur("sc")
            emit_gevac(tc, NCT - 2)
            emit_gevac(tc, NCT - 1)
            p.events[("sc", ("gs", tc, NCT - 1))] = p.cur("sc")
            p.wait(sc, "mm", ("projt", tc))
            if tc >= 1:
                p.wait(sc, "mm", ("delta", tc - 1, NCT - 1))
            p.emit(lambda: sc.copy(out=t["dtlr"][:, :], in_=t["ps_pt"][0:64, :]),
                   inc="sc", event=("dtlr", tc))
            if tc >= 1:
                p.wait(sc, "stbc", value=16 * tc)
            p.emit(lambda: sc.copy(out=t["bcc"][:, :], in_=t["ps_pt"][64:96, :]),
                   inc="sc", event=("ptevac", tc))
            p.events[("sc", ("bcc", tc))] = p.cur("sc")
        for ctile in range(NCT):
            emit_softplus(sc, NCH - 1, ctile)

    def stream_vector(v):
        def emit_dldu(v, tcd, ctile):
            p.wait(v, "sc", ("dl", tcd, ctile))
            p.wait(v, "sc", ("u", tcd, ctile))
            if ctile == 0 and tcd >= 1:
                p.wait(v, "stdu", value=64 * tcd)
            p.emit(lambda tcd=tcd, ctile=ctile: v.tensor_tensor(
                out=t["duc"][:, ctile, :], in0=t["dlc"][:, ctile, :],
                in1=t["uc"][tcd % 2][:, ctile, :], op=ALU.mult),
                inc="v", event=("du", tcd, ctile))
            if ctile == NCT - 1:
                p.events[("v", ("duall", tcd))] = p.cur("v")

        for tc in range(NCH):
            for ctile in range(NCT):
                cv = t["cv"][ctile % 6]
                ps = t["ps"][ctile % 2]
                p.wait(v, "mm", ("proj", tc, ctile))
                if tc == 0:
                    p.wait(v, "sc", ("tail", ctile))
                if not (tc == 0 and ctile < 6):
                    prev = (tc, ctile - 6) if ctile >= 6 else (tc - 1, NCT - 6 + ctile)
                    p.wait(v, "sc", ("u",) + prev)
                p.emit(lambda cv=cv, ps=ps, ctile=ctile: v.tensor_scalar(
                    out=cv[:], in0=ps[:, 0:TCH], scalar1=t["cw"][:, ctile, 3:4],
                    scalar2=None, op0=ALU.mult), inc="v")
                for k in (2, 1, 0):
                    p.wait(v, "v", value=p.cur("v"))
                    p.emit(lambda cv=cv, ps=ps, ctile=ctile, k=k: v.scalar_tensor_tensor(
                        out=cv[:, 3 - k:TCH], in0=ps[:, 0:TCH - 3 + k],
                        scalar=t["cw"][:, ctile, k:k + 1],
                        in1=cv[:, 3 - k:TCH], op0=ALU.mult, op1=ALU.add), inc="v")
                    p.wait(v, "v", value=p.cur("v"))
                    p.emit(lambda cv=cv, ctile=ctile, k=k: v.scalar_tensor_tensor(
                        out=cv[:, 0:3 - k], in0=t["tail"][:, ctile, k:3],
                        scalar=t["cw"][:, ctile, k:k + 1],
                        in1=cv[:, 0:3 - k], op0=ALU.mult, op1=ALU.add),
                        inc="v", event=None if k > 0 else ("cv", tc, ctile))
                if tc < NCH - 1:
                    p.wait(v, "v", value=p.cur("v"))
                    p.emit(lambda ps=ps, ctile=ctile: v.tensor_copy(
                        out=t["tail"][:, ctile, :], in_=ps[:, TCH - 3:TCH]),
                        inc="v")
                p.events[("v", ("cpdone", tc, ctile))] = p.cur("v")
                if tc >= 1:
                    emit_dldu(v, tc - 1, ctile)
        for ctile in range(NCT):
            emit_dldu(v, NCH - 1, ctile)

    p.begin("gpsimd"); stream_gpsimd(E["gpsimd"])
    p.begin("sync"); stream_sync(E["sync"])
    p.begin("tensor"); stream_tensor(E["tensor"])
    p.begin("scalar"); stream_scalar(E["scalar"])
    p.begin("vector"); stream_vector(E["vector"])


L3_SEMS = ["ld", "w3", "mm3", "sc3", "v3", "st3", "q3"]


def l3_alloc(nc, E, T3=2048):
    TCH = 512
    t = {}
    t["wbuf"] = E(nc.sbuf_tensor("l3_wbuf", [128, 32, 1024], BF16))
    t["wq"] = E(nc.sbuf_tensor("l3_wq", [128, 4096], BF16))
    t["wf"] = [E(nc.sbuf_tensor(f"l3_wf{i}", [128, 4096], F32)) for i in range(2)]
    t["x"] = E(nc.sbuf_tensor("l3_x", [128, 8, TCH], BF16))
    t["y2"] = E(nc.sbuf_tensor("l3_y2", [128, 16, TCH], BF16))
    t["x1"] = E(nc.sbuf_tensor("l3_x1", [128, 8, TCH], BF16))
    t["rbf"] = E(nc.sbuf_tensor("l3_rbf", [128, 8, TCH], BF16))
    t["q1"] = E(nc.sbuf_tensor("l3_q1", [128, 8, TCH], BF16))
    t["g1"] = E(nc.sbuf_tensor("l3_g1", [128, 32, TCH], BF16))
    t["sq"] = E(nc.sbuf_tensor("l3_sq", [128, TCH], BF16))
    t["v"] = [E(nc.sbuf_tensor(f"l3_v{i}", [128, TCH], F32)) for i in range(2)]
    t["sg"] = [E(nc.sbuf_tensor(f"l3_sg{i}", [128, TCH], F32)) for i in range(2)]
    t["msk"] = E(nc.sbuf_tensor("l3_msk", [128, TCH], F32))
    t["qiB"] = E(nc.sbuf_tensor("l3_qiB", [128, 4, TCH], mybir.dt.int32))
    t["mskB"] = E(nc.sbuf_tensor("l3_mskB", [128, 4, TCH], BF16))
    t["qi"] = [E(nc.sbuf_tensor(f"l3_qi{i}", [128, TCH], mybir.dt.int32)) for i in range(2)]
    t["r3"] = E(nc.sbuf_tensor("l3_r3", [128, 8, TCH], F32))
    t["out"] = E(nc.sbuf_tensor("l3_out", [128, 8, TCH], F32))
    t["stat"] = E(nc.sbuf_tensor("l3_stat", [1, TCH * 8], F32))
    t["smax"] = E(nc.sbuf_tensor("l3_smax", [128, 8], F32))
    t["sflat"] = E(nc.sbuf_tensor("l3_sflat", [4, 128 * 4], F32))
    t["sbc"] = E(nc.sbuf_tensor("l3_sbc", [128, TCH], F32))
    t["rbc"] = E(nc.sbuf_tensor("l3_rbc", [128, TCH], F32))
    t["ones"] = E(nc.sbuf_tensor("l3_ones", [128, 2], BF16))
    t["onesf"] = E(nc.sbuf_tensor("l3_onesf", [128, 2], F32))
    t["lng"] = E(nc.sbuf_tensor("l3_lng", [128, 8], F32))
    t["lnb"] = E(nc.sbuf_tensor("l3_lnb", [128, 8], F32))
    t["ln2g"] = E(nc.sbuf_tensor("l3_ln2g", [128, 32], F32))
    t["ln2b"] = E(nc.sbuf_tensor("l3_ln2b", [128, 32], F32))
    t["b1"] = E(nc.sbuf_tensor("l3_b1", [128, 32], F32))
    t["b2"] = E(nc.sbuf_tensor("l3_b2", [128, 8], F32))
    t["swc"] = E(nc.sbuf_tensor("l3_swc", [128, 4], F32))
    t["red"] = E(nc.sbuf_tensor("l3_red", [128, 8], F32))
    t["scl"] = E(nc.sbuf_tensor("l3_scl", [1, 64], F32))
    t["cst"] = E(nc.sbuf_tensor("l3_cst", [1, 8], F32))
    t["ps"] = [E(nc.psum_tensor(f"l3_ps{i}", [128, TCH], F32)) for i in range(8)]
    t["pss"] = E(nc.psum_tensor("l3_pss", [1, 2 * TCH], F32))
    t["psb"] = [E(nc.psum_tensor(f"l3_psb{i}", [128, TCH], F32)) for i in range(0)]
    return t



def rep0(ap, n):
    return type(ap)(tensor=ap.tensor, offset=ap.offset,
                    ap=[[ap.ap[0][0], ap.ap[0][1]], [0, n], [list(ap.ap[1])[0], list(ap.ap[1])[1]]])


def l3_streams(p, t, d, eng, T3=2048):
    TCH = 512
    NCH = T3 // TCH
    E = eng or {}
    ps = t["ps"]
    W = t["wbuf"]
    NJT, NHT = 8, 32
    g1f = t["g1"].ap().rearrange("p a b -> p (a b)")
    g1w = t["g1"].ap().rearrange("p a b -> p (a b)").bitcast(F32)
    wqa = [lambda: g1w[:, 0:4096], lambda: g1w[:, 4096:8192]]
    _q1f = lambda: t["q1"].ap().rearrange("p a b -> p (a b)")
    _rbff = lambda: t["rbf"].ap().rearrange("p a b -> p (a b)")
    qc1s = [_q1f, _rbff]
    qc2 = lambda: t["x1"].ap().rearrange("p a b -> p (a b)")
    LN_EPS = 1e-5
    w1slot = W.ap()[:, 16:32, :].rearrange("p a b -> p (a b)").rearrange("p (c e) -> p c e", c=8)
    w2slot = W.ap()[:, 16:32, :]

    def xs(tc):
        return t["x"][:, :, :]

    def y2s(tc):
        return t["y2"][:, :, :]

    def stream_gpsimd(g):
        for i in range(32):
            rnd, wi, kt = i // 16, (i % 16) // 8, i % 8
            wsrc = "w1T_s" if wi == 0 else "w2T_s"
            if i >= 2:
                p.wait(g, "v3", ("wfree", i - 2))
            p.emit(lambda wsrc=wsrc, kt=kt, i=i: g.dma_start(
                out=wqa[i % 2](), in_=d[wsrc][:, kt, :]),
                inc="q%d" % (i % 2), step=16, event=("wcast", i))
        p.emit(lambda: g.dma_start(out=W[:, 0:16, :], in_=d["w_outT_s"]),
               inc="qo", step=16, event=("wout",))
        for tc in range(NCH):
            if tc >= 1:
                p.wait(g, "v3", ("Aev", tc - 1, NJT - 1))
            p.emit(lambda tc=tc: g.dma_start(out=xs(tc), in_=d["xT_s"][:, :, tc * TCH:(tc + 1) * TCH]),
                   inc="qx", step=16, event=("xc", tc))

    def stream_sync(s):
        for nm in ("lng", "lnb", "ln2g", "ln2b", "b1", "b2", "ident", "identf", "onesb", "onesf", "onesrow", "cst"):
            p.emit(lambda nm=nm: s.dma_start(out=t[nm][:], in_=d[nm]), inc="ls", step=16)
        for i in range(16):
            wi, kt = i // 8, i % 8
            dst = "w1q_d" if wi == 0 else "w2q_d"
            qc1 = qc1s[i % 2]
            p.wait(s, "v3", ("wq", i + 16))
            p.emit(lambda dst=dst, kt=kt, qc1=qc1: s.dma_start(out=d[dst][:, kt, :], in_=qc1()),
                   inc="sw%d" % (i % 2), step=16, event=("wqst", i))
        p.emit(lambda: s.dma_start(out=y2s(0), in_=d["y2T"][:, :, 0:TCH]),
               inc="ly", step=16, event=("y2c", 0))
        for tc in range(NCH):
            for h in range(4):
                if True:
                    pass
                if h == 1:
                    p.wait(s, "v3", ("sfl", tc, 1))
                    p.emit(lambda: s.dma_start(out=t["stat"][0:1, 0:2 * TCH].rearrange("p (a b) -> p a b", a=8),
                                               in_=t["sflat"][0:8, 0:128]),
                           inc="sx", step=16, event=("sxd", tc, 1))
                if h == 3:
                    p.wait(s, "v3", ("sfl", tc, 2))
                    p.emit(lambda: s.dma_start(out=t["stat"][0:1, 0:2 * TCH].rearrange("p (a b) -> p a b", a=8),
                                               in_=t["sflat"][0:8, 0:128]),
                           inc="sx", step=16, event=("sxd", tc, 2))
                p.wait(s, "sw0", value=8 * 16)
                p.wait(s, "sw1", value=8 * 16)
                if not (tc == 0 and h == 0):
                    p.wait(s, "mm3", ("slotfree", tc, h))
                if h < 2:
                    p.emit(lambda h=h: s.dma_start(out=w1slot, in_=d["w1q_d"][:, :, h * 2048:(h + 1) * 2048]),
                           inc="lw", step=16, event=("wslot", tc, h))
                else:
                    p.emit(lambda h=h: s.dma_start(out=w2slot, in_=d["w2q_d"].rearrange(
                        "p a b -> p (a b)").rearrange("p (c e) -> p c e", c=32)[:, (h - 2) * 16:(h - 1) * 16, :]),
                        inc="lw", step=16, event=("wslot", tc, h))
            if tc + 1 < NCH:
                p.wait(s, "mm3", ("Adone", tc))
                p.emit(lambda tc=tc: s.dma_start(out=y2s(tc + 1),
                                                 in_=d["y2T"][:, :, (tc + 1) * TCH:(tc + 2) * TCH]),
                       inc="ly", step=16, event=("y2c", tc + 1))
            for jt in range(NJT):
                p.wait(s, "v3", ("outj", tc, jt))
                p.emit(lambda tc=tc, jt=jt: s.dma_start(
                    out=d["outT_s"][:, jt, tc * TCH:(tc + 1) * TCH], in_=t["r3"][:, jt, :]),
                    inc="st3", step=16, event=("outst", tc, jt))
        p.wait(s, "st3", value=p.cur("st3"))

    def bc(te_, row_fn, out_ps, ev=None):
        p.emit(lambda: te_.matmul(out_ps, t["onesrow"][0:1, :], row_fn(), start=True, stop=True),
               inc="mm3", event=ev)

    def stream_tensor(te_):
        p.wait(te_, "ls", value=12 * 16)
        for wi in range(2):
            p.wait(te_, "v3", ("wred", wi * 8 + 7))
            p.emit(lambda wi=wi: te_.matmul(ps[6][0:1, wi * 8:wi * 8 + 8], t["onesf"][:, 0:1],
                                            t["red"][:, wi * 8:wi * 8 + 8], start=True, stop=True),
                   inc="mm3", event=("wtot", wi))
        p.wait(te_, "v3", ("thr",))
        bc(te_, lambda: t["scl"][0:1, 0:6], ps[7][:, 0:6], ev=("swcb",))
        def emit_A(tc):
            p.wait(te_, "qo", ("wout",))
            p.wait(te_, "ly", ("y2c", tc))
            for jt in range(NJT):
                if tc == 0 and jt < 2:
                    pass
                elif jt < 2 and tc >= 1:
                    p.wait(te_, "v3", ("Gap", tc - 1, jt))
                    p.wait(te_, "v3", ("Aev", tc - 1, NJT - 2 + jt))
                else:
                    p.wait(te_, "v3", ("Aev", tc, jt - 2))
                for kt in range(16):
                    p.emit(lambda tc=tc, jt=jt, kt=kt: te_.matmul(
                        ps[jt % 2][:, :], W[:, kt, jt * 128:(jt + 1) * 128],
                        y2s(tc)[:, kt, :], start=(kt == 0), stop=(kt == 15)),
                        inc=None if kt < 15 else "mm3",
                        event=None if kt < 15 else ("A", tc, jt))
            p.events[("mm3", ("Adone", tc))] = p.cur("mm3")

        for tc in range(NCH):
            emit_A(tc)
            for jt in range(NJT):
                p.wait(te_, "v3", ("Aev", tc, jt))
                p.wait(te_, "sc3", ("sq0", tc, jt))
                p.emit(lambda tc=tc, jt=jt: te_.matmul(ps[2][0:1, :], t["onesb"][:, 0:1],
                                                       t["rbf"][:, jt, :], start=(jt == 0), stop=(jt == 7)),
                       inc="mm3")
                p.emit(lambda: te_.matmul(ps[3][0:1, :], t["onesb"][:, 0:1], t["sq"][:, :],
                                          start=(jt == 0), stop=(jt == 7)),
                       inc="mm3", event=("Bsum", tc, jt))
            p.wait(te_, "v3", ("stat0", tc))
            bc(te_, lambda: t["stat"][0:1, 0:TCH], ps[4][:, :], ev=("rs0b", tc))
            bc(te_, lambda: t["stat"][0:1, TCH:2 * TCH], ps[5][:, :], ev=("m20b", tc))
            p.wait(te_, "v3", ("mx1", tc))
            for tb in range(4):
                p.emit(lambda tb=tb: te_.transpose(ps[6][:, tb * 128:(tb + 1) * 128],
                                                   t["mx"][:, tb * 128:(tb + 1) * 128], t["identf"][:]),
                       inc=None if tb < 3 else "mm3", event=None if tb < 3 else ("mxT1", tc))
            p.wait(te_, "v3", ("s1c", tc))
            p.emit(lambda: te_.transpose(ps[7][0:8, 0:128], t["smax"][:, 0:8], t["identf"][:]),
                   inc="mm3", event=("sT1", tc))
            p.wait(te_, "sx", ("sxd", tc, 1))
            bc(te_, lambda: t["stat"][0:1, 0:TCH], ps[4][:, :], ev=("sb1", tc))
            bc(te_, lambda: t["stat"][0:1, TCH:2 * TCH], ps[5][:, :], ev=("rb1", tc))
            p.wait(te_, "lw", ("wslot", tc, 0))
            p.wait(te_, "v3", ("q1", tc))
            for jt2 in range(NHT):
                if jt2 == 16:
                    p.wait(te_, "lw", ("wslot", tc, 1))
                if (tc, jt2) not in ((0, 0), (0, 1)):
                    prev = (tc, jt2 - 2) if jt2 >= 2 else (tc - 1, NHT - 2 + jt2)
                    p.wait(te_, "v3", ("Cz",) + prev)
                for kt in range(NJT):
                    p.emit(lambda jt2=jt2, kt=kt: te_.matmul(
                        ps[jt2 % 2][:, :], w1slot[:, kt, (jt2 % 16) * 128:(jt2 % 16) * 128 + 128],
                        t["q1"][:, kt, :], start=(kt == 0), stop=(kt == 7)),
                        inc=None if kt < 7 else "mm3",
                        event=None if kt < 7 else ("C", tc, jt2))
                if jt2 == 15:
                    p.events[("mm3", ("slotfree", tc, 1))] = p.cur("mm3")
                if jt2 == 31:
                    p.events[("mm3", ("slotfree", tc, 2))] = p.cur("mm3")

                def dsum(j):
                    p.wait(te_, "sc3", ("gelu", tc, j))
                    p.wait(te_, "sc3", ("sq2", tc, j))
                    p.emit(lambda j=j: te_.matmul(ps[2][0:1, :], t["onesb"][:, 0:1],
                                                  t["g1"][:, j, :], start=(j == 0), stop=(j == 31)),
                           inc="mm3")
                    p.emit(lambda j=j: te_.matmul(ps[3][0:1, :], t["onesb"][:, 0:1], t["sq"][:, :],
                                                  start=(j == 0), stop=(j == 31)),
                           inc="mm3", event=("Dsum", tc, j))

                if jt2 >= 2:
                    dsum(jt2 - 2)
            dsum(NHT - 2)
            dsum(NHT - 1)
            p.wait(te_, "v3", ("stat2", tc))
            bc(te_, lambda: t["stat"][0:1, 0:TCH], ps[4][:, :], ev=("rs2b", tc))
            bc(te_, lambda: t["stat"][0:1, TCH:2 * TCH], ps[5][:, :], ev=("m22b", tc))
            p.wait(te_, "v3", ("mx2", tc))
            for tb in range(4):
                p.emit(lambda tb=tb: te_.transpose(ps[6][:, tb * 128:(tb + 1) * 128],
                                                   t["mx"][:, tb * 128:(tb + 1) * 128], t["identf"][:]),
                       inc=None if tb < 3 else "mm3", event=None if tb < 3 else ("mxT2", tc))
            p.wait(te_, "v3", ("s2c", tc))
            p.emit(lambda: te_.transpose(ps[7][0:8, 0:128], t["smax"][:, 0:8], t["identf"][:]),
                   inc="mm3", event=("sT2", tc))
            p.wait(te_, "sx", ("sxd", tc, 2))
            bc(te_, lambda: t["stat"][0:1, 0:TCH], ps[4][:, :], ev=("sb2", tc))
            bc(te_, lambda: t["stat"][0:1, TCH:2 * TCH], ps[5][:, :], ev=("rb2", tc))
            p.wait(te_, "lw", ("wslot", tc, 2))
            p.wait(te_, "v3", ("sbc2ev", tc))
            if tc >= 1:
                p.wait(te_, "v3", ("Gap", tc - 1, NJT - 1))
            for kt2 in range(NHT):
                if kt2 == 16:
                    p.wait(te_, "lw", ("wslot", tc, 3))
                if kt2 % 4 == 0:
                    p.wait(te_, "v3", ("q2b", tc, kt2 // 4))
                half = ((kt2 // 4) % 2) * 4
                for jt in range(NJT):
                    p.emit(lambda kt2=kt2, jt=jt, half=half: te_.matmul(
                        ps[jt][:, :], w2slot[:, kt2 % 16, jt * 128:(jt + 1) * 128],
                        t["q1"][:, half + kt2 % 4, :], start=(kt2 == 0), stop=(kt2 == 31)),
                        inc=None if jt < 7 else "mm3",
                        event=None if jt < 7 else ("F", tc, kt2))
                if kt2 == 15:
                    p.events[("mm3", ("slotfree", tc, 3))] = p.cur("mm3")
                if kt2 == 31:
                    p.events[("mm3", ("slotfree", tc + 1, 0))] = p.cur("mm3")
            p.wait(te_, "v3", ("Gap", tc, 5))
            for jt in range(NJT):
                p.wait(te_, "v3", ("r3bf", tc, jt))
                p.wait(te_, "sc3", ("sq3", tc, jt))
                p.emit(lambda jt=jt: te_.matmul(ps[2][0:1, :], t["onesb"][:, 0:1],
                                                t["rbf"][:, jt, :], start=(jt == 0), stop=(jt == 7)),
                       inc="mm3")
                p.emit(lambda: te_.matmul(ps[3][0:1, :], t["onesb"][:, 0:1], t["sq"][:, :],
                                          start=(jt == 0), stop=(jt == 7)),
                       inc="mm3", event=("Gsum", tc, jt))
            p.wait(te_, "v3", ("stat3", tc))
            bc(te_, lambda: t["stat"][0:1, 0:TCH], ps[4][:, :], ev=("rs3b", tc))
            bc(te_, lambda: t["stat"][0:1, TCH:2 * TCH], ps[5][:, :], ev=("m23b", tc))

    def stream_scalar(sc_):
        p.wait(sc_, "ls", value=12 * 16)
        for tc in range(NCH):
            for jt in range(NJT):
                p.wait(sc_, "v3", ("Aev", tc, jt))
                if not (tc == 0 and jt == 0):
                    prevj = ("Bsum", tc, jt - 1) if jt >= 1 else ("Bsum", tc - 1, NJT - 1)
                    p.wait(sc_, "mm3", prevj)
                p.emit(lambda jt=jt: sc_.activation(out=t["sq"][:, :], in_=t["rbf"][:, jt, :],
                                                    func=AF.Square),
                       inc="sc3", event=("sq0", tc, jt))
            p.wait(sc_, "v3", ("var0", tc))
            p.emit(lambda: sc_.activation(out=t["stat"][0:1, 2 * TCH:3 * TCH],
                                          in_=t["stat"][0:1, 3 * TCH:4 * TCH],
                                          func=AF.Sqrt, bias=t["cst"][0:1, 0:1]),
                   inc="sc3", event=("sd0", tc))
            for jt2 in range(NHT):
                p.wait(sc_, "v3", ("Cz", tc, jt2))
                p.emit(lambda jt2=jt2: sc_.activation(
                    out=t["g1"][:, jt2, :], in_=t["v"][jt2 % 2][:], func=AF.Gelu,
                    bias=t["b1"][:, jt2:jt2 + 1]),
                    inc="sc3", event=("gelu", tc, jt2))
                if not (tc == 0 and jt2 == 0):
                    prevj = ("Dsum", tc, jt2 - 1) if jt2 >= 1 else ("Dsum", tc - 1, NHT - 1)
                    p.wait(sc_, "mm3", prevj)
                p.wait(sc_, "sc3", value=p.cur("sc3"))
                p.emit(lambda jt2=jt2: sc_.activation(out=t["sq"][:, :], in_=t["g1"][:, jt2, :],
                                                      func=AF.Square),
                       inc="sc3", event=("sq2", tc, jt2))
            p.wait(sc_, "v3", ("var2", tc))
            p.emit(lambda: sc_.activation(out=t["stat"][0:1, 2 * TCH:3 * TCH],
                                          in_=t["stat"][0:1, 3 * TCH:4 * TCH],
                                          func=AF.Sqrt, bias=t["cst"][0:1, 0:1]),
                   inc="sc3", event=("sd2", tc))
            for jt in range(NJT):
                p.wait(sc_, "v3", ("r3bf", tc, jt))
                if not (tc == 0 and jt == 0):
                    prevj = ("Gsum", tc, jt - 1) if jt >= 1 else ("Gsum", tc - 1, NJT - 1)
                    p.wait(sc_, "mm3", prevj)
                p.emit(lambda jt=jt: sc_.activation(out=t["sq"][:, :], in_=t["rbf"][:, jt, :],
                                                    func=AF.Square),
                       inc="sc3", event=("sq3", tc, jt))
            p.wait(sc_, "v3", ("var3", tc))
            p.emit(lambda: sc_.activation(out=t["stat"][0:1, 2 * TCH:3 * TCH],
                                          in_=t["stat"][0:1, 3 * TCH:4 * TCH],
                                          func=AF.Sqrt, bias=t["cst"][0:1, 0:1]),
                   inc="sc3", event=("sd3", tc))

    def stream_vector(v_):
        ch = [0]

        def vw(val=None):
            p.wait(v_, "v3", value=val if val is not None else p.cur("v3"))

        for i in range(16):
            wi, kt = i // 8, i % 8
            p.wait(v_, "q%d" % (i % 2), ("wcast", i))
            p.emit(lambda i=i, wi=wi, kt=kt: v_.reduce_sum(
                out=t["red"][:, wi * 8 + kt:wi * 8 + kt + 1], in_=wqa[i % 2](),
                axis=mybir.AxisListType.X, apply_absolute_value=True),
                inc="v3", event=("wred", i) if True else None)
            p.events[("v3", ("wfree", i))] = p.cur("v3")
        for wi in range(2):
            p.wait(v_, "mm3", ("wtot", wi))
            p.emit(lambda wi=wi: v_.reduce_sum(out=t["scl"][0:1, 16 + wi:17 + wi],
                                               in_=ps[6][0:1, wi * 8:wi * 8 + 8],
                                               axis=mybir.AxisListType.X), inc="v3")
            vw()
            N = 1024 * 4096.0
            p.emit(lambda wi=wi, N=N: v_.tensor_scalar(
                out=t["scl"][0:1, 4 + wi:5 + wi], in0=t["scl"][0:1, 16 + wi:17 + wi],
                scalar1=1.0 / N, scalar2=1e-5, op0=ALU.mult, op1=ALU.max), inc="v3")
            vw()
            p.emit(lambda wi=wi: v_.tensor_scalar_mul(
                out=t["scl"][0:1, 0 + wi:1 + wi], in0=t["scl"][0:1, 4 + wi:5 + wi],
                scalar1=0.5), inc="v3")
            vw()
            p.emit(lambda wi=wi: v_.tensor_scalar_mul(
                out=t["scl"][0:1, 2 + wi:3 + wi], in0=t["scl"][0:1, 0 + wi:1 + wi],
                scalar1=-1.0), inc="v3", event=("thr",) if wi == 1 else None)
        p.wait(v_, "mm3", ("swcb",))
        p.emit(lambda: v_.tensor_copy(out=t["swc"][:, 0:6], in_=ps[7][:, 0:6]),
               inc="v3", event=("swcev",))
        for i in range(16, 32):
            wi, kt = (i - 16) // 8, (i - 16) % 8
            qc1 = qc1s[i % 2]
            p.wait(v_, "q%d" % (i % 2), ("wcast", i))
            if i >= 18:
                p.wait(v_, "sw%d" % (i % 2), ("wqst", i - 18))
            vw()
            p.emit(lambda i=i, wi=wi, qc1=qc1: v_.tensor_scalar(
                out=qc1(), in0=wqa[i % 2](), scalar1=t["swc"][:, 0 + wi:1 + wi],
                scalar2=None, op0=ALU.is_ge), inc="v3")
            vw()
            p.emit(lambda i=i, wi=wi: v_.tensor_scalar(
                out=qc2(), in0=wqa[i % 2](), scalar1=t["swc"][:, 2 + wi:3 + wi],
                scalar2=None, op0=ALU.is_le), inc="v3")
            vw()
            p.emit(lambda qc1=qc1: v_.tensor_sub(out=qc1(), in0=qc1(), in1=qc2()),
                   inc="v3", event=("wq", i))
            p.events[("v3", ("wfree", i))] = p.cur("v3")

        def statmath(tcev, Nf, sumev, varev, sdev, statev):
            S = TCH
            p.wait(v_, "mm3", sumev)
            p.emit(lambda: v_.tensor_scalar_mul(out=t["stat"][0:1, 4 * S:5 * S],
                                                in0=ps[2][0:1, :], scalar1=1.0 / Nf), inc="v3")
            vw()
            p.emit(lambda: v_.tensor_scalar_mul(out=t["stat"][0:1, 5 * S:6 * S],
                                                in0=ps[3][0:1, :], scalar1=1.0 / Nf), inc="v3")
            vw()
            p.emit(lambda: v_.tensor_mul(out=t["stat"][0:1, 2 * S:3 * S],
                                         in0=t["stat"][0:1, 4 * S:5 * S],
                                         in1=t["stat"][0:1, 4 * S:5 * S]), inc="v3")
            vw()
            p.emit(lambda: v_.tensor_sub(out=t["stat"][0:1, 3 * S:4 * S],
                                         in0=t["stat"][0:1, 5 * S:6 * S],
                                         in1=t["stat"][0:1, 2 * S:3 * S]),
                   inc="v3", event=varev)
            p.wait(v_, "sc3", sdev)
            p.emit(lambda: v_.reciprocal(out=t["stat"][0:1, 0:S],
                                         in_=t["stat"][0:1, 2 * S:3 * S]), inc="v3")
            vw()
            p.emit(lambda: v_.tensor_mul(out=t["stat"][0:1, S:2 * S],
                                         in0=t["stat"][0:1, 4 * S:5 * S],
                                         in1=t["stat"][0:1, 0:S]),
                   inc="v3", event=statev)

        def scalerow(tc, mxev, mxTev, sTev, scev, rowev, wi):
            p.wait(v_, "mm3", mxTev)
            p.emit(lambda: v_.reduce_max(out=t["smax"][:, 0:4],
                                         in_=ps[6].ap().rearrange("p (a b) -> p a b", a=4),
                                         axis=mybir.AxisListType.X), inc="v3")
            vw()
            p.emit(lambda: v_.tensor_scalar_max(out=t["smax"][:, 0:4], in0=t["smax"][:, 0:4],
                                                scalar1=1e-5), inc="v3")
            vw()
            p.emit(lambda wi=wi: v_.tensor_scalar(
                out=t["smax"][:, 4:8], in0=t["smax"][:, 0:4],
                scalar1=t["swc"][:, 4 + wi:5 + wi], scalar2=1.0 / 127.0,
                op0=ALU.mult, op1=ALU.mult), inc="v3")
            vw()
            p.emit(lambda: v_.reciprocal(out=t["smax"][:, 0:4], in_=t["smax"][:, 0:4]), inc="v3")
            vw()
            p.emit(lambda: v_.tensor_scalar_mul(out=t["smax"][:, 0:4], in0=t["smax"][:, 0:4],
                                                scalar1=127.0), inc="v3", event=scev)
            p.wait(v_, "mm3", sTev)
            p.emit(lambda: v_.tensor_copy(out=t["sflat"][0:8, 0:128], in_=ps[7][0:8, 0:128]),
                   inc="v3", event=rowev)

        for tc in range(NCH):
            if tc == 0:
                p.wait(v_, "sw0", value=8 * 16)
                p.wait(v_, "sw1", value=8 * 16)
            for jt in range(NJT):
                p.wait(v_, "mm3", ("A", tc, jt))
                p.wait(v_, "qx", ("xc", tc))
                p.emit(lambda tc=tc, jt=jt: v_.tensor_tensor(
                    out=t["rbf"][:, jt, :], in0=ps[jt % 2][:, :], in1=xs(tc)[:, jt, :],
                    op=ALU.add), inc="v3", event=("Aev", tc, jt))
            statmath(tc, 1024.0, ("Bsum", tc, NJT - 1), ("var0", tc), ("sd0", tc), ("stat0", tc))
            p.wait(v_, "mm3", ("rs0b", tc))
            p.wait(v_, "mm3", ("m20b", tc))
            if tc >= 1:
                p.wait(v_, "st3", ("outst", tc - 1, NJT - 1))
            vw()
            p.emit(lambda: v_.tensor_copy(out=t["v"][0][:], in_=ps[4][:, :]), inc="v3")
            vw()
            p.emit(lambda: v_.tensor_copy(out=t["v"][1][:], in_=ps[5][:, :]), inc="v3")
            vw()
            p.emit(lambda: v_.tensor_tensor(out=t["r3"][:, :, :], in0=t["rbf"][:, :, :],
                                            in1=rep0(t["v"][0][:], NJT), op=ALU.mult), inc="v3")
            vw()
            p.emit(lambda: v_.tensor_tensor(out=t["x1"][:, :, :], in0=t["r3"][:, :, :],
                                            in1=rep0(t["v"][1][:], NJT), op=ALU.subtract),
                   inc="v3", event=("x1", tc, NJT - 1))
            for jt in range(NJT):
                vw()
                p.emit(lambda jt=jt: v_.tensor_scalar(
                    out=t["rbf"][:, jt, :], in0=t["x1"][:, jt, :],
                    scalar1=t["lng"][:, jt:jt + 1], scalar2=t["lnb"][:, jt:jt + 1],
                    op0=ALU.mult, op1=ALU.add), inc="v3", event=("l1o", tc, jt))
            p.emit(lambda: v_.scalar_tensor_tensor(out=t["mx"][:, :], in0=t["rbf"][:, 0, :],
                                                   scalar=-1.0, in1=t["rbf"][:, 0, :],
                                                   op0=ALU.mult, op1=ALU.max), inc="v3")
            for jt in range(1, NJT):
                vw()
                p.emit(lambda jt=jt: v_.tensor_tensor(out=t["mx"][:, :], in0=t["mx"][:, :],
                                                      in1=t["rbf"][:, jt, :], op=ALU.max), inc="v3")
                vw()
                p.emit(lambda jt=jt: v_.scalar_tensor_tensor(out=t["mx"][:, :], in0=t["rbf"][:, jt, :],
                                                             scalar=-1.0, in1=t["mx"][:, :],
                                                             op0=ALU.mult, op1=ALU.max),
                       inc="v3", event=("mx1", tc) if jt == NJT - 1 else None)
            scalerow(tc, None, ("mxT1", tc), ("sT1", tc), ("s1c", tc), ("sfl", tc, 1), 0)
            p.wait(v_, "mm3", ("sb1", tc))
            p.emit(lambda: v_.tensor_copy(out=t["sbc"][:, :], in_=ps[4][:, :]), inc="v3")
            p.wait(v_, "mm3", ("rb1", tc))
            vw()
            p.emit(lambda: v_.tensor_copy(out=t["rbc"][:, :], in_=ps[5][:, :]),
                   inc="v3", event=("sbc1ev", tc))
            for jt in range(NJT):
                vw()
                p.emit(lambda jt=jt: v_.tensor_mul(out=t["v"][jt % 2][:], in0=t["rbf"][:, jt, :],
                                                   in1=t["sbc"][:, :]),
                       inc="v3", event=("v1", tc, jt))
                vw()
                p.emit(lambda jt=jt: v_.tensor_scalar_add(out=t["v"][jt % 2][:], in0=t["v"][jt % 2][:],
                                                          scalar1=0.5), inc="v3")
                vw()
                p.emit(lambda jt=jt: v_.tensor_copy(out=t["qi"][jt % 2][:], in_=t["v"][jt % 2][:]),
                       inc="v3")
                vw()
                p.emit(lambda jt=jt: v_.tensor_copy(out=t["sg"][jt % 2][:], in_=t["qi"][jt % 2][:]),
                       inc="v3")
                vw()
                p.emit(lambda jt=jt: v_.tensor_tensor(out=t["msk"][:], in0=t["sg"][jt % 2][:],
                                                      in1=t["v"][jt % 2][:], op=ALU.is_gt), inc="v3")
                vw()
                p.emit(lambda jt=jt: v_.tensor_tensor(out=t["q1"][:, jt, :], in0=t["sg"][jt % 2][:],
                                                      in1=t["msk"][:], op=ALU.subtract),
                       inc="v3", event=("q1", tc) if jt == NJT - 1 else ("q1t", tc, jt))
            for jt2 in range(NHT):
                p.wait(v_, "mm3", ("C", tc, jt2))
                if jt2 >= 2:
                    p.wait(v_, "sc3", ("gelu", tc, jt2 - 2))
                p.emit(lambda jt2=jt2: v_.tensor_mul(out=t["v"][jt2 % 2][:], in0=ps[jt2 % 2][:, :],
                                                     in1=t["rbc"][:, :]),
                       inc="v3", event=("Cz", tc, jt2))
            statmath(tc, 4096.0, ("Dsum", tc, NHT - 1), ("var2", tc), ("sd2", tc), ("stat2", tc))
            p.wait(v_, "mm3", ("rs2b", tc))
            p.wait(v_, "mm3", ("m22b", tc))
            vw()
            p.emit(lambda: v_.tensor_copy(out=t["v"][0][:], in_=ps[4][:, :]), inc="v3")
            vw()
            p.emit(lambda: v_.tensor_copy(out=t["v"][1][:], in_=ps[5][:, :]), inc="v3")
            for b4 in range(NHT // 4):
                vw()
                p.emit(lambda b4=b4: v_.tensor_tensor(
                    out=t["r3"][:, 0:4, :], in0=t["g1"][:, b4 * 4:b4 * 4 + 4, :],
                    in1=rep0(t["v"][0][:], 4), op=ALU.mult), inc="v3")
                vw()
                p.emit(lambda b4=b4: v_.tensor_tensor(
                    out=t["r3"][:, 0:4, :], in0=t["r3"][:, 0:4, :],
                    in1=rep0(t["v"][1][:], 4), op=ALU.subtract), inc="v3")
                for k4 in range(4):
                    kt2 = b4 * 4 + k4
                    vw()
                    p.emit(lambda kt2=kt2, k4=k4: v_.tensor_scalar(
                        out=t["g1"][:, kt2, :], in0=t["r3"][:, k4, :],
                        scalar1=t["ln2g"][:, kt2:kt2 + 1], scalar2=t["ln2b"][:, kt2:kt2 + 1],
                        op0=ALU.mult, op1=ALU.add), inc="v3", event=("l2o", tc, kt2))
            p.emit(lambda: v_.scalar_tensor_tensor(out=t["mx"][:, :], in0=t["g1"][:, 0, :],
                                                   scalar=-1.0, in1=t["g1"][:, 0, :],
                                                   op0=ALU.mult, op1=ALU.max), inc="v3")
            for kt2 in range(1, NHT):
                vw()
                p.emit(lambda kt2=kt2: v_.tensor_tensor(out=t["mx"][:, :], in0=t["mx"][:, :],
                                                        in1=t["g1"][:, kt2, :], op=ALU.max), inc="v3")
                vw()
                p.emit(lambda kt2=kt2: v_.scalar_tensor_tensor(out=t["mx"][:, :], in0=t["g1"][:, kt2, :],
                                                               scalar=-1.0, in1=t["mx"][:, :],
                                                               op0=ALU.mult, op1=ALU.max),
                       inc="v3", event=("mx2", tc) if kt2 == NHT - 1 else None)
            scalerow(tc, None, ("mxT2", tc), ("sT2", tc), ("s2c", tc), ("sfl", tc, 2), 1)
            p.wait(v_, "mm3", ("sb2", tc))
            p.emit(lambda: v_.tensor_copy(out=t["sbc"][:, :], in_=ps[4][:, :]), inc="v3")
            p.wait(v_, "mm3", ("rb2", tc))
            vw()
            p.emit(lambda: v_.tensor_copy(out=t["rbc"][:, :], in_=ps[5][:, :]),
                   inc="v3", event=("sbc2ev", tc))
            for b4 in range(NHT // 4):
                half = (b4 % 2) * 4
                vw()
                if b4 == 0:
                    p.wait(v_, "mm3", ("C", tc, NHT - 1))
                if b4 >= 2:
                    p.wait(v_, "mm3", ("F", tc, (b4 - 2) * 4 + 3))
                p.emit(lambda b4=b4: v_.tensor_tensor(
                    out=t["r3"][:, 4:8, :], in0=t["g1"][:, b4 * 4:b4 * 4 + 4, :],
                    in1=rep0(t["sbc"][:, :], 4), op=ALU.mult), inc="v3")
                vw()
                p.emit(lambda: v_.tensor_scalar_add(out=t["qiB"][:, :, :], in0=t["r3"][:, 4:8, :],
                                                    scalar1=0.5), inc="v3")
                vw()
                p.emit(lambda: v_.tensor_copy(out=t["mskB"][:, :, :], in_=t["qiB"][:, :, :]), inc="v3")
                vw()
                p.emit(lambda: v_.scalar_tensor_tensor(out=t["r3"][:, 4:8, :], in0=t["mskB"][:, :, :],
                                                       scalar=-0.5, in1=t["r3"][:, 4:8, :],
                                                       op0=ALU.add, op1=ALU.is_gt), inc="v3")
                vw()
                p.emit(lambda half=half: v_.tensor_tensor(
                    out=t["q1"][:, half:half + 4, :], in0=t["mskB"][:, :, :],
                    in1=t["r3"][:, 4:8, :], op=ALU.subtract),
                    inc="v3", event=("q2b", tc, b4))
            for jt in range(NJT):
                p.wait(v_, "mm3", ("F", tc, NHT - 1))
                p.emit(lambda jt=jt: v_.tensor_mul(out=t["v"][jt % 2][:], in0=ps[jt][:, :],
                                                   in1=t["rbc"][:, :]), inc="v3")
                vw()
                p.emit(lambda jt=jt: v_.tensor_scalar(
                    out=t["v"][jt % 2][:], in0=t["v"][jt % 2][:],
                    scalar1=t["b2"][:, jt:jt + 1], scalar2=None, op0=ALU.add), inc="v3")
                vw()
                if tc >= 1:
                    p.wait(v_, "st3", ("outst", tc - 1, 7))
                p.emit(lambda jt=jt: v_.tensor_tensor(out=t["r3"][:, jt, :], in0=t["v"][jt % 2][:],
                                                      in1=t["x1"][:, jt, :], op=ALU.add),
                       inc="v3", event=("Gap", tc, jt))
                vw()
                p.emit(lambda jt=jt: v_.tensor_copy(out=t["rbf"][:, jt, :], in_=t["r3"][:, jt, :]),
                       inc="v3", event=("r3bf", tc, jt))
            statmath(tc, 1024.0, ("Gsum", tc, NJT - 1), ("var3", tc), ("sd3", tc), ("stat3", tc))
            for jt in range(NJT):
                p.wait(v_, "mm3", ("rs3b", tc))
                p.wait(v_, "mm3", ("m23b", tc))
                vw()
                p.emit(lambda jt=jt: v_.tensor_mul(out=t["v"][jt % 2][:], in0=t["r3"][:, jt, :],
                                                   in1=ps[4][:, :]), inc="v3")
                vw()
                p.emit(lambda jt=jt: v_.tensor_sub(out=t["r3"][:, jt, :], in0=t["v"][jt % 2][:],
                                                   in1=ps[5][:, :]), inc="v3", event=("outj", tc, jt))

    p.begin("gpsimd"); stream_gpsimd(E.get("gpsimd"))
    p.begin("sync"); stream_sync(E.get("sync"))
    p.begin("tensor"); stream_tensor(E.get("tensor"))
    p.begin("scalar"); stream_scalar(E.get("scalar"))
    p.begin("vector"); stream_vector(E.get("vector"))


L3_SEMS2 = ["ls", "ly", "lw", "q0", "q1", "qo", "qx", "sx", "sw0", "sw1", "mm3", "sc3", "v3", "st3"]


def l3_alloc2(nc, E, T3=2048):
    TCH = 512
    t = {}
    t["wbuf"] = E(nc.sbuf_tensor("l3_wbuf", [128, 32, 1024], BF16))
    t["x"] = E(nc.sbuf_tensor("l3_x", [128, 8, TCH], BF16))
    t["y2"] = E(nc.sbuf_tensor("l3_y2", [128, 16, TCH], BF16))
    t["x1"] = E(nc.sbuf_tensor("l3_x1", [128, 8, TCH], BF16))
    t["rbf"] = E(nc.sbuf_tensor("l3_rbf", [128, 8, TCH], BF16))
    t["q1"] = E(nc.sbuf_tensor("l3_q1", [128, 8, TCH], BF16))
    t["g1"] = E(nc.sbuf_tensor("l3_g1", [128, 32, TCH], BF16))
    t["sq"] = E(nc.sbuf_tensor("l3_sq", [128, TCH], BF16))
    t["mx"] = E(nc.sbuf_tensor("l3_mx", [128, TCH], F32))
    t["v"] = [E(nc.sbuf_tensor(f"l3_v{i}", [128, TCH], F32)) for i in range(2)]
    t["sg"] = [E(nc.sbuf_tensor(f"l3_sg{i}", [128, TCH], F32)) for i in range(2)]
    t["msk"] = E(nc.sbuf_tensor("l3_msk", [128, TCH], F32))
    t["qiB"] = E(nc.sbuf_tensor("l3_qiB", [128, 4, TCH], mybir.dt.int32))
    t["mskB"] = E(nc.sbuf_tensor("l3_mskB", [128, 4, TCH], BF16))
    t["qi"] = [E(nc.sbuf_tensor(f"l3_qi{i}", [128, TCH], mybir.dt.int32)) for i in range(2)]
    t["r3"] = E(nc.sbuf_tensor("l3_r3", [128, 8, TCH], F32))
    t["stat"] = E(nc.sbuf_tensor("l3_stat", [1, TCH * 6], F32))
    t["smax"] = E(nc.sbuf_tensor("l3_smax", [128, 8], F32))
    t["sflat"] = E(nc.sbuf_tensor("l3_sflat", [8, 128], F32))
    t["sbc"] = E(nc.sbuf_tensor("l3_sbc", [128, TCH], F32))
    t["rbc"] = E(nc.sbuf_tensor("l3_rbc", [128, TCH], F32))
    t["onesb"] = E(nc.sbuf_tensor("l3_onesb", [128, 1], BF16))
    t["onesf"] = E(nc.sbuf_tensor("l3_onesf", [128, 1], F32))
    t["onesrow"] = E(nc.sbuf_tensor("l3_onesrow", [1, 128], F32))
    t["ident"] = E(nc.sbuf_tensor("l3_ident", [128, 128], BF16))
    t["identf"] = E(nc.sbuf_tensor("l3_identf", [128, 128], F32))
    t["lng"] = E(nc.sbuf_tensor("l3_lng", [128, 8], F32))
    t["lnb"] = E(nc.sbuf_tensor("l3_lnb", [128, 8], F32))
    t["ln2g"] = E(nc.sbuf_tensor("l3_ln2g", [128, 32], F32))
    t["ln2b"] = E(nc.sbuf_tensor("l3_ln2b", [128, 32], F32))
    t["b1"] = E(nc.sbuf_tensor("l3_b1", [128, 32], F32))
    t["b2"] = E(nc.sbuf_tensor("l3_b2", [128, 8], F32))
    t["swc"] = E(nc.sbuf_tensor("l3_swc", [128, 8], F32))
    t["red"] = E(nc.sbuf_tensor("l3_red", [128, 16], F32))
    t["scl"] = E(nc.sbuf_tensor("l3_scl", [1, 64], F32))
    t["cst"] = E(nc.sbuf_tensor("l3_cst", [1, 8], F32))
    t["ps"] = [E(nc.psum_tensor(f"l3_ps{i}", [128, TCH], F32)) for i in range(8)]
    return t


TT = 16384
TC = 2048
NCHUNK = TT // TC
NB_CHUNK = NCHUNK // 2
NG = 32
SPAN = min(512, TC)
NSPAN = TC // SPAN

def build_l2():
    nc = bass.Bass("TRN2", target_bir_lowering=False, debug=False)
    drep = nc.dram_tensor("drep", [NG, 128, TT], BF16, kind="ExternalInput").ap()
    durep = nc.dram_tensor("durep", [NG, 128, TT], BF16, kind="ExternalInput").ap()
    u_cm = nc.dram_tensor("u_cm", [2, 128, TT], BF16, kind="ExternalInput").ap()
    gs_cm = nc.dram_tensor("gs_cm", [2, 128, TT], BF16, kind="ExternalInput").ap()
    bdn = nc.dram_tensor("bdn", [128, TT], BF16, kind="ExternalInput").ap()
    cdn = nc.dram_tensor("cdn", [128, TT], BF16, kind="ExternalInput").ap()
    alog_dn = nc.dram_tensor("alog_dn", [128, NG], F32, kind="ExternalInput").ap()
    wsel = nc.dram_tensor("wsel", [128, 16, 128], BF16, kind="ExternalInput").ap()
    wdiag = nc.dram_tensor("wdiag", [128, 2, 128], F32, kind="ExternalInput").ap()
    y2 = nc.dram_tensor("y2", [2, 128, TT], BF16, kind="ExternalOutput").ap()

    iters = [(ch, ct, g16) for ch in range(NCHUNK) for ct in range(2) for g16 in range(16)]

    es = contextlib.ExitStack()
    with es:
        E = es.enter_context
        t_drep = [E(nc.sbuf_tensor(f"t_drep{i}", [128, TC], BF16)) for i in range(2)]
        t_durep = [E(nc.sbuf_tensor(f"t_durep{i}", [128, TC], BF16)) for i in range(2)]
        t_bdn = [E(nc.sbuf_tensor(f"t_bdn{i}", [128, TC], BF16)) for i in range(2)]
        t_cdn = [E(nc.sbuf_tensor(f"t_cdn{i}", [128, TC], BF16)) for i in range(2)]
        t_u = [E(nc.sbuf_tensor(f"t_u{i}", [128, 2, TC], BF16)) for i in range(2)]
        t_gs = [E(nc.sbuf_tensor(f"t_gs{i}", [128, 2, TC], BF16)) for i in range(2)]
        t_dA = [E(nc.sbuf_tensor(f"t_dA{i}", [128, TC], F32)) for i in range(2)]
        t_dBu = [E(nc.sbuf_tensor(f"t_dBu{i}", [128, TC], BF16)) for i in range(2)]
        t_hs = [E(nc.sbuf_tensor(f"t_hs{i}", [128, TC], BF16)) for i in range(2)]
        t_prod = [E(nc.sbuf_tensor(f"t_prod{i}", [128, TC], BF16)) for i in range(2)]
        t_y2 = [E(nc.sbuf_tensor(f"t_y2{i}", [128, TC], BF16)) for i in range(2)]
        t_wsel = E(nc.sbuf_tensor("t_wsel", [128, 16, 128], BF16))
        t_wdiagf = E(nc.sbuf_tensor("t_wdiagf", [128, 2, 128], F32))
        t_wdiag = E(nc.sbuf_tensor("t_wdiag", [128, 2, 128], BF16))
        t_alog = E(nc.sbuf_tensor("t_alog", [128, NG], F32))
        t_expA = E(nc.sbuf_tensor("t_expA", [128, NG], F32))
        t_negA = E(nc.sbuf_tensor("t_negA", [128, NG], F32))
        t_state = E(nc.sbuf_tensor("t_state", [128, NG], F32))
        psum = [E(nc.psum_tensor(f"ps{i}", [128, SPAN], F32)) for i in range(8)]

        s_drep = [E(nc.semaphore(f"s_drep{i}")) for i in range(2)]
        s_durep = [E(nc.semaphore(f"s_durep{i}")) for i in range(2)]
        s_bdn = [E(nc.semaphore(f"s_bdn{i}")) for i in range(2)]
        s_cdn = [E(nc.semaphore(f"s_cdn{i}")) for i in range(2)]
        s_u = [E(nc.semaphore(f"s_u{i}")) for i in range(2)]
        s_gs = [E(nc.semaphore(f"s_gs{i}")) for i in range(2)]
        s_st = [E(nc.semaphore(f"s_st{i}")) for i in range(2)]
        s_pre = E(nc.semaphore("s_pre"))
        s_dA = E(nc.semaphore("s_dA"))
        s_v = E(nc.semaphore("s_v"))
        s_mm = E(nc.semaphore("s_mm"))
        s_ev = E(nc.semaphore("s_ev"))
        s_sc = E(nc.semaphore("s_sc"))
        s_vA = E(nc.semaphore("s_vA"))
        vch = E(nc.semaphore("vch"))
        block = E(nc.Block())

        @block.sync
        def _(sync):
            sync.dma_start(out=t_wsel[:], in_=wsel).then_inc(s_pre, 16)
            sync.dma_start(out=t_wdiagf[:], in_=wdiag).then_inc(s_pre, 16)
            sync.dma_start(out=t_alog[:], in_=alog_dn).then_inc(s_pre, 16)
            for ch in range(NCHUNK):
                t0 = ch * TC
                sl = ch % 2
                if ch >= 2:
                    sync.wait_ge(s_v, (ch - 1) * 32)
                    sync.wait_ge(s_ev, (ch - 1) * 2)
                    sync.wait_ge(s_mm, (ch - 1) * 32)
                sync.dma_start(out=t_bdn[sl][:], in_=bdn[:, t0:t0 + TC]).then_inc(s_bdn[sl], 16)
                sync.dma_start(out=t_cdn[sl][:], in_=cdn[:, t0:t0 + TC]).then_inc(s_cdn[sl], 16)
                for c_ in range(2):
                    sync.dma_start(out=t_u[sl][:, c_, :], in_=u_cm[c_, :, t0:t0 + TC]).then_inc(s_u[sl], 16)
                    sync.dma_start(out=t_gs[sl][:, c_, :], in_=gs_cm[c_, :, t0:t0 + TC]).then_inc(s_gs[sl], 16)
                for ct in range(2):
                    for g16 in range(16):
                        it = ch * 32 + ct * 16 + g16
                        g = ct * 16 + g16
                        buf = it % 2
                        if it >= 2:
                            sync.wait_ge(s_dA, it - 1)
                            sync.wait_ge(s_v, it - 1)
                        sync.dma_start(out=t_drep[buf][:], in_=drep[g, :, t0:t0 + TC]).then_inc(s_drep[buf], 16)
                        sync.dma_start(out=t_durep[buf][:], in_=durep[g, :, t0:t0 + TC]).then_inc(s_durep[buf], 16)

        @block.scalar
        def _(scalar):
            scalar.wait_ge(s_pre, 48)
            scalar.activation(out=t_expA[:], in_=t_alog[:], func=AF.Exp).then_inc(s_sc, 1)
            scalar.copy(out=t_wdiag[:], in_=t_wdiagf[:]).then_inc(s_sc, 1)
            for it, (ch, ct, g16) in enumerate(iters):
                g = ct * 16 + g16
                buf = it % 2
                if it == 0:
                    scalar.wait_ge(s_vA, 1)
                if g16 == 0 and ch >= 1:
                    ctset_prev = (ch - 1) * 2 + ct
                    scalar.wait_ge(s_ev, ctset_prev + 1)
                    scalar.dma_start(out=y2[ct, :, (ch - 1) * TC:ch * TC],
                                     in_=t_y2[ct][:]).then_inc(s_st[ct], 16)
                scalar.wait_ge(s_drep[buf], 16 * (it // 2 + 1))
                if it >= 2:
                    scalar.wait_ge(s_v, it - 1)
                scalar.activation(out=t_dA[buf][:], in_=t_drep[buf][:], func=AF.Exp,
                                  scale=t_negA[:, g:g + 1]).then_inc(s_dA, 1)
            for ct in range(2):
                ctset = (NCHUNK - 1) * 2 + ct
                scalar.wait_ge(s_ev, ctset + 1)
                scalar.dma_start(out=y2[ct, :, (NCHUNK - 1) * TC:NCHUNK * TC],
                                 in_=t_y2[ct][:]).then_inc(s_st[ct], 16)
            scalar.wait_ge(s_st[0], NCHUNK * 16)
            scalar.wait_ge(s_st[1], NCHUNK * 16)

        @block.vector
        def _(vector):
            vector.wait_ge(s_sc, 1)
            vector.tensor_scalar_mul(out=t_negA[:], in0=t_expA[:], scalar1=-1.0).then_inc(s_vA, 1)
            vchn = [0]

            def vinc(ins):
                vchn[0] += 1
                ins.then_inc(vch, 1)
                return vchn[0]

            for it, (ch, ct, g16) in enumerate(iters):
                g = ct * 16 + g16
                buf = it % 2
                sl = ch % 2
                ctset = ch * 2 + ct
                vector.wait_ge(s_durep[buf], 16 * (it // 2 + 1))
                vector.wait_ge(s_bdn[sl], 16 * (ch // 2 + 1))
                vector.wait_ge(s_dA, it + 1)
                k_dbu = vinc(vector.tensor_tensor(out=t_dBu[buf][:], in0=t_durep[buf][:],
                                                  in1=t_bdn[sl][:], op=ALU.mult))
                vector.wait_ge(vch, k_dbu)
                init = 0.0 if ch % NB_CHUNK == 0 else t_state[:, g:g + 1]
                k_scan = vinc(vector.tensor_tensor_scan(
                    out=t_hs[buf][:], data0=t_dA[buf][:], data1=t_dBu[buf][:],
                    initial=init, op0=ALU.mult, op1=ALU.add))
                vector.wait_ge(vch, k_scan)
                vinc(vector.tensor_copy(out=t_state[:, g:g + 1], in_=t_hs[buf][:, TC - 1:TC]))
                if it >= 2:
                    vector.wait_ge(s_mm, it - 1)
                vector.wait_ge(s_cdn[sl], 16 * (ch // 2 + 1))
                vector.tensor_tensor(out=t_prod[buf][:], in0=t_hs[buf][:],
                                     in1=t_cdn[sl][:], op=ALU.mult).then_inc(s_v, 1)
                if g16 == 15:
                    vector.wait_ge(s_mm, it + 1)
                    if ch >= 1:
                        vector.wait_ge(s_st[ct], 16 * ch)
                    vector.wait_ge(s_gs[sl], 32 * (ch // 2 + 1))
                    pb = (ct % 2) * 4
                    for sp in range(NSPAN):
                        ins = vector.tensor_tensor(
                            out=t_y2[ct][:, sp * SPAN:(sp + 1) * SPAN],
                            in0=psum[pb + sp][:, :],
                            in1=t_gs[sl][:, ct, sp * SPAN:(sp + 1) * SPAN],
                            op=ALU.mult)
                        if sp == NSPAN - 1:
                            ins.then_inc(s_ev, 1)

        @block.tensor
        def _(tensor):
            tensor.wait_ge(s_pre, 48)
            tensor.wait_ge(s_sc, 2)
            for it, (ch, ct, g16) in enumerate(iters):
                buf = it % 2
                sl = ch % 2
                ctset = ch * 2 + ct
                tensor.wait_ge(s_v, it + 1)
                if g16 == 0:
                    if ctset >= 2:
                        tensor.wait_ge(s_ev, ctset - 1)
                    tensor.wait_ge(s_u[sl], 32 * (ch // 2 + 1))
                pb = (ct % 2) * 4
                last = None
                for sp in range(NSPAN):
                    last = tensor.matmul(psum[pb + sp][:, :], t_wsel[:, g16, :],
                                         t_prod[buf][:, sp * SPAN:(sp + 1) * SPAN],
                                         start=(g16 == 0), stop=False)
                if g16 == 15:
                    for sp in range(NSPAN):
                        last = tensor.matmul(psum[pb + sp][:, :], t_wdiag[:, ct, :],
                                             t_u[sl][:, ct, sp * SPAN:(sp + 1) * SPAN],
                                             start=False, stop=(sp == NSPAN - 1))
                last.then_inc(s_mm, 1)
    return nc




T1 = 2048
T3 = 2048


def _engs(g=None, s=None, te=None, sc=None, v=None):
    return {"gpsimd": g, "sync": s, "tensor": te, "scalar": sc, "vector": v}


def build_l1():
    nc = bass.Bass("TRN2", target_bir_lowering=False, debug=False)
    d = {}
    d["xT"] = nc.dram_tensor("xT", [T1 // 512, 8, 128, 515], F32, kind="ExternalInput").ap()
    d["w_inT"] = nc.dram_tensor("w_inT", [1024, 4096], F32, kind="ExternalInput").ap()
    d["conv_w_cm"] = nc.dram_tensor("conv_w_cm", [128, 16, 4], F32, kind="ExternalInput").ap()
    d["conv_b_cm"] = nc.dram_tensor("conv_b_cm", [128, 16], F32, kind="ExternalInput").ap()
    d["w_xT"] = nc.dram_tensor("w_xT", [2048, 96], F32, kind="ExternalInput").ap()
    d["w_dtT"] = nc.dram_tensor("w_dtT", [64, 16, 128], F32, kind="ExternalInput").ap()
    d["dt_bias_cm"] = nc.dram_tensor("dt_bias_cm", [128, 16], F32, kind="ExternalInput").ap()
    NCH1 = T1 // 512
    d["o_u"] = nc.dram_tensor("o_u", [NCH1, 128, 16, 512], BF16, kind="ExternalOutput").ap()
    d["o_du"] = nc.dram_tensor("o_du", [NCH1, 128, 16, 512], BF16, kind="ExternalOutput").ap()
    d["o_dl"] = nc.dram_tensor("o_dl", [NCH1, 128, 16, 512], BF16, kind="ExternalOutput").ap()
    d["o_gs"] = nc.dram_tensor("o_gs", [NCH1, 128, 16, 512], BF16, kind="ExternalOutput").ap()
    d["o_bc"] = nc.dram_tensor("o_bc", [NCH1, 32, 512], BF16, kind="ExternalOutput").ap()
    es = contextlib.ExitStack()
    with es:
        E = es.enter_context
        t = l1_alloc(nc, E, T1)
        sems = {n: E(nc.semaphore("s_" + n)) for n in L1_SEMS}
        events = {}
        l1_streams(Pass(True, None, events), t, d, None, T1)
        block = E(nc.Block())

        @block.gpsimd
        def _(g):
            l1_streams(Pass(False, sems, events, only="gpsimd"), t, d, _engs(g=g), T1)

        @block.sync
        def _(s):
            l1_streams(Pass(False, sems, events, only="sync"), t, d, _engs(s=s), T1)

        @block.tensor
        def _(te):
            l1_streams(Pass(False, sems, events, only="tensor"), t, d, _engs(te=te), T1)

        @block.scalar
        def _(sc):
            l1_streams(Pass(False, sems, events, only="scalar"), t, d, _engs(sc=sc), T1)

        @block.vector
        def _(v):
            l1_streams(Pass(False, sems, events, only="vector"), t, d, _engs(v=v), T1)
    return nc


def build_l3():
    nc = bass.Bass("TRN2", target_bir_lowering=False, debug=False)
    d = {}
    d["xT_s"] = nc.dram_tensor("xT_s", [128, 8, T3], F32, kind="ExternalInput").ap()
    d["y2T"] = nc.dram_tensor("y2T", [128, 16, T3], BF16, kind="ExternalInput").ap()
    d["w_outT_s"] = nc.dram_tensor("w_outT_s", [128, 16, 1024], F32, kind="ExternalInput").ap()
    d["w1T_s"] = nc.dram_tensor("w1T_s", [128, 8, 4096], F32, kind="ExternalInput").ap()
    d["w2T_s"] = nc.dram_tensor("w2T_s", [128, 8, 4096], F32, kind="ExternalInput").ap()
    for nm, c in (("lng", 8), ("lnb", 8), ("ln2g", 32), ("ln2b", 32), ("b1", 32), ("b2", 8)):
        d[nm] = nc.dram_tensor(nm, [128, c], F32, kind="ExternalInput").ap()
    d["ident"] = nc.dram_tensor("ident", [128, 128], BF16, kind="ExternalInput").ap()
    d["identf"] = nc.dram_tensor("identf", [128, 128], F32, kind="ExternalInput").ap()
    d["onesb"] = nc.dram_tensor("onesb", [128, 1], BF16, kind="ExternalInput").ap()
    d["onesf"] = nc.dram_tensor("onesf", [128, 1], F32, kind="ExternalInput").ap()
    d["onesrow"] = nc.dram_tensor("onesrow", [1, 128], F32, kind="ExternalInput").ap()
    d["cst"] = nc.dram_tensor("cst", [1, 8], F32, kind="ExternalInput").ap()
    d["w1q_d"] = nc.dram_tensor("w1q_d", [128, 8, 4096], BF16, kind="Internal").ap()
    d["w2q_d"] = nc.dram_tensor("w2q_d", [128, 8, 4096], BF16, kind="Internal").ap()
    d["outT_s"] = nc.dram_tensor("outT_s", [128, 8, T3], F32, kind="ExternalOutput").ap()
    es = contextlib.ExitStack()
    with es:
        E = es.enter_context
        t = l3_alloc2(nc, E, T3)
        sems = {n: E(nc.semaphore("s_" + n)) for n in L3_SEMS2}
        events = {}
        l3_streams(Pass(True, None, events), t, d, None, T3)
        block = E(nc.Block())

        @block.gpsimd
        def _(g):
            l3_streams(Pass(False, sems, events, only="gpsimd"), t, d, _engs(g=g), T3)

        @block.sync
        def _(s):
            l3_streams(Pass(False, sems, events, only="sync"), t, d, _engs(s=s), T3)

        @block.tensor
        def _(te):
            l3_streams(Pass(False, sems, events, only="tensor"), t, d, _engs(te=te), T3)

        @block.scalar
        def _(sc):
            l3_streams(Pass(False, sems, events, only="scalar"), t, d, _engs(sc=sc), T3)

        @block.vector
        def _(v):
            l3_streams(Pass(False, sems, events, only="vector"), t, d, _engs(v=v), T3)
    return nc


_P = np.arange(128)


def _pack_l1(x, W_in, conv_w, conv_b, W_x, W_dt, dt_bias):
    cw = np.ascontiguousarray(conv_w.reshape(16, 128, 4).transpose(1, 0, 2))
    cb = np.ascontiguousarray(conv_b.reshape(16, 128).T)
    db = np.ascontiguousarray(dt_bias.reshape(16, 128).T)
    w_inT = np.ascontiguousarray(W_in.T)
    w_xT = np.ascontiguousarray(W_x.T)
    wdt = np.ascontiguousarray(W_dt.reshape(16, 128, 64).transpose(2, 0, 1))
    maps = []
    for c in range(8):
        b, q = c // 4, c % 4
        t0 = q * 2048
        xs = np.zeros((2048 + 3, 1024), np.float32)
        lo = max(0, t0 - 3)
        xs[3 - (t0 - lo):] = x[b, lo:t0 + 2048]
        xTf = np.ascontiguousarray(xs.T)
        xc = np.empty((4, 8, 128, 515), np.float32)
        for tc in range(4):
            for kt in range(8):
                xc[tc, kt] = xTf[kt * 128:(kt + 1) * 128, tc * 512:tc * 512 + 515]
        maps.append({"xT": xc, "w_inT": w_inT,
                     "conv_w_cm": cw, "conv_b_cm": cb, "dt_bias_cm": db,
                     "w_xT": w_xT, "w_dtT": wdt})
    return maps


def _pack_l2(r1, A_log, D_skip):
    def full(key):
        return np.concatenate(
            [np.ascontiguousarray(r[key]).transpose(2, 1, 0, 3).reshape(2048, 2048)
             for r in r1], axis=1)

    dl = full("o_dl")
    du = full("o_du")
    uu = full("o_u")
    gs = full("o_gs")
    bc = np.concatenate(
        [np.ascontiguousarray(r["o_bc"]).transpose(1, 0, 2).reshape(32, 2048) for r in r1], axis=1)
    B, C = bc[0:16], bc[16:32]
    n_of_p = _P // 8
    ci_of_p = _P % 8
    wsel = np.zeros((128, 16, 128), bfd)
    for g16 in range(16):
        wsel[_P, g16, g16 * 8 + ci_of_p] = 1
    maps = []
    for k in range(8):
        ch0 = 256 * k
        gmap = ch0 + 8 * np.arange(32)[:, None] + ci_of_p[None, :]
        drep = dl[gmap]
        durep = du[gmap]
        u_cm = uu[ch0:ch0 + 256].reshape(2, 128, TT)
        gs_cm = gs[ch0:ch0 + 256].reshape(2, 128, TT)
        bdn = B[n_of_p]
        cdn = C[n_of_p]
        alog_dn = np.ascontiguousarray(
            A_log[ch0 + 8 * np.arange(32)[None, :] + ci_of_p[:, None], n_of_p[:, None]]
        ).astype(np.float32)
        wdiag = np.zeros((128, 2, 128), np.float32)
        for ct in range(2):
            wdiag[_P, ct, _P] = D_skip[ch0 + ct * 128 + _P]
        maps.append({"drep": np.ascontiguousarray(drep),
                     "durep": np.ascontiguousarray(durep),
                     "u_cm": np.ascontiguousarray(u_cm),
                     "gs_cm": np.ascontiguousarray(gs_cm),
                     "bdn": np.ascontiguousarray(bdn),
                     "cdn": np.ascontiguousarray(cdn),
                     "alog_dn": alog_dn, "wsel": wsel, "wdiag": wdiag})
    return maps


def _pack_l3(x, r2, W_out, w1, b1, w2, b2, lng, lnb, ln2g, ln2b):
    y2_full = np.concatenate(
        [np.ascontiguousarray(r["y2"]).reshape(256, TT) for r in r2], axis=0)
    woT = np.ascontiguousarray(W_out.T.reshape(16, 128, 1024).transpose(1, 0, 2))
    w1T = np.ascontiguousarray(w1.T.reshape(8, 128, 4096).transpose(1, 0, 2))
    w2l = np.ascontiguousarray(w2.T.reshape(32, 128, 1024).transpose(1, 0, 2))
    w2T = np.ascontiguousarray(w2l.reshape(128, 8, 4096))
    small = {
        "lng": np.ascontiguousarray(lng.reshape(8, 128).T),
        "lnb": np.ascontiguousarray(lnb.reshape(8, 128).T),
        "ln2g": np.ascontiguousarray(ln2g.reshape(32, 128).T),
        "ln2b": np.ascontiguousarray(ln2b.reshape(32, 128).T),
        "b1": np.ascontiguousarray(b1.reshape(32, 128).T),
        "b2": np.ascontiguousarray(b2.reshape(8, 128).T),
        "ident": np.eye(128, dtype=bfd),
        "identf": np.eye(128, dtype=np.float32),
        "onesb": np.ones((128, 1), bfd),
        "onesf": np.ones((128, 1), np.float32),
        "onesrow": np.ones((1, 128), np.float32),
        "cst": np.array([[1e-5, 0, 0, 0, 0, 0, 0, 0]], np.float32),
    }
    maps = []
    for c in range(8):
        b, q = c // 4, c % 4
        t0 = q * 2048
        xT = np.ascontiguousarray(x[b, t0:t0 + 2048].T)
        xT_s = np.ascontiguousarray(xT.reshape(8, 128, 2048).transpose(1, 0, 2))
        y2T = np.ascontiguousarray(
            y2_full[:, c * 2048:(c + 1) * 2048].reshape(16, 128, 2048).transpose(1, 0, 2))
        m = {"xT_s": xT_s, "y2T": y2T, "w_outT_s": woT, "w1T_s": w1T, "w2T_s": w2T}
        m.update(small)
        maps.append(m)
    return maps


_NCS = {}


def _get(name, builder):
    if name not in _NCS:
        _NCS[name] = builder()
    return _NCS[name]


def kernel(L0_attn_hidden_states, L0_attn_W_in, L0_attn_conv_w, L0_attn_conv_b,
           L0_attn_W_x, L0_attn_W_dt, L0_attn_dt_bias, L0_attn_A_log, L0_attn_D_skip,
           L0_attn_W_out, L0_ffn_ln1_g, L0_ffn_ln1_b, L0_ffn_w1, L0_ffn_b1,
           L0_ffn_ln2_g, L0_ffn_ln2_b, L0_ffn_w2, L0_ffn_b2):
    x = np.ascontiguousarray(np.asarray(L0_attn_hidden_states, np.float32))
    m1 = _pack_l1(x, np.asarray(L0_attn_W_in, np.float32),
                  np.asarray(L0_attn_conv_w, np.float32),
                  np.asarray(L0_attn_conv_b, np.float32),
                  np.asarray(L0_attn_W_x, np.float32),
                  np.asarray(L0_attn_W_dt, np.float32),
                  np.asarray(L0_attn_dt_bias, np.float32))
    nc1 = _get("l1", build_l1)
    r1 = run_bass_kernel_spmd(nc1, m1, core_ids=list(range(8))).results

    m2 = _pack_l2(r1, np.asarray(L0_attn_A_log, np.float32),
                  np.asarray(L0_attn_D_skip, np.float32))
    nc2 = _get("l2", build_l2)
    r2 = run_bass_kernel_spmd(nc2, m2, core_ids=list(range(8))).results

    m3 = _pack_l3(x, r2, np.asarray(L0_attn_W_out, np.float32),
                  np.asarray(L0_ffn_w1, np.float32), np.asarray(L0_ffn_b1, np.float32),
                  np.asarray(L0_ffn_w2, np.float32), np.asarray(L0_ffn_b2, np.float32),
                  np.asarray(L0_ffn_ln1_g, np.float32), np.asarray(L0_ffn_ln1_b, np.float32),
                  np.asarray(L0_ffn_ln2_g, np.float32), np.asarray(L0_ffn_ln2_b, np.float32))
    nc3 = _get("l3", build_l3)
    r3 = run_bass_kernel_spmd(nc3, m3, core_ids=list(range(8))).results

    out = np.zeros((2, 8192, 1024), np.float32)
    for c in range(8):
        b, q = c // 4, c % 4
        o = np.asarray(r3[c]["outT_s"], np.float32)
        out[b, q * 2048:(q + 1) * 2048] = o.transpose(2, 1, 0).reshape(2048, 1024)
    return out


# version 03
# speedup vs baseline: 1.1991x; 1.0149x over previous
import numpy as np
import contextlib
import ml_dtypes
import concourse.bass as bass
import concourse.bass_utils
from concourse.bass_utils import run_bass_kernel_spmd

bfd = ml_dtypes.bfloat16

import numpy as np
import concourse.mybir as mybir

F32 = mybir.dt.float32
BF16 = mybir.dt.bfloat16
AF = mybir.ActivationFunctionType
ALU = mybir.AluOpType


class Pass:
    def __init__(self, dry, sems, events, only=None):
        self.dry = dry
        self.sems = sems
        self.events = events
        self.count = {}
        self.only = only
        self.current = None

    def begin(self, name):
        self.current = name

    def _active(self):
        return not self.dry and (self.only is None or self.only == self.current)

    def emit(self, fn, inc=None, step=1, event=None):
        if inc is not None:
            self.count[inc] = self.count.get(inc, 0) + step
            if event is not None:
                self.events[(inc, event)] = self.count[inc]
        if self._active():
            ins = fn()
            if inc is not None:
                ins.then_inc(self.sems[inc], step)

    def wait(self, eng, sem, key=None, value=None):
        if not self._active():
            return
        v = value if value is not None else self.events[(sem, key)]
        eng.wait_ge(self.sems[sem], v)

    def cur(self, sem):
        return self.count.get(sem, 0)


L1_SEMS = ["x", "xc", "w", "wg", "wsm", "small", "mm", "sc", "v", "stu", "stg", "stdu", "stdl", "stbc"]


def l1_alloc(nc, E, T1=2048):
    TCH = 512
    t = {}
    t["x"] = [E(nc.sbuf_tensor(f"l1_x{i}", [128, 8, TCH + 3], BF16)) for i in range(2)]
    t["win"] = E(nc.sbuf_tensor("l1_win", [128, 8, 4096], BF16))
    t["wx"] = E(nc.sbuf_tensor("l1_wx", [128, 16, 96], BF16))
    t["wdt"] = E(nc.sbuf_tensor("l1_wdt", [64, 16, 128], BF16))
    t["cw"] = E(nc.sbuf_tensor("l1_cw", [128, 16, 4], F32))
    t["cb"] = E(nc.sbuf_tensor("l1_cb", [128, 16], F32))
    t["db"] = E(nc.sbuf_tensor("l1_db", [128, 16], F32))
    t["tail"] = E(nc.sbuf_tensor("l1_tail", [128, 16, 3], F32))
    t["cv"] = [E(nc.sbuf_tensor(f"l1_cv{i}", [128, TCH], F32)) for i in range(6)]
    t["uc"] = [E(nc.sbuf_tensor(f"l1_uc{i}", [128, 16, TCH], BF16)) for i in range(2)]
    t["duc"] = E(nc.sbuf_tensor("l1_duc", [128, 16, TCH], BF16))
    t["dlc"] = E(nc.sbuf_tensor("l1_dlc", [128, 16, TCH], BF16))
    t["gsc"] = E(nc.sbuf_tensor("l1_gsc", [128, 16, TCH], BF16))
    t["dtlr"] = E(nc.sbuf_tensor("l1_dtlr", [64, TCH], BF16))
    t["bcc"] = E(nc.sbuf_tensor("l1_bcc", [32, TCH], BF16))
    t["ab"] = [E(nc.sbuf_tensor(f"l1_ab{i}", [128, TCH], F32)) for i in range(4)]
    t["ps"] = [E(nc.psum_tensor(f"l1_ps{i}", [128, TCH], F32)) for i in range(4)]
    t["ps_pt"] = E(nc.psum_tensor("l1_ps_pt", [128, TCH], F32))
    t["ps_d"] = [E(nc.psum_tensor(f"l1_ps_d{i}", [128, TCH], F32)) for i in range(2)]
    t["ps_g3"] = E(nc.psum_tensor("l1_ps_g3", [128, TCH], F32))
    return t


def _l1_psd(t, ctile):
    return t["ps_d"][ctile % 2]


def _l1_psg(t, ctile):
    return [t["ps"][2], t["ps"][3], t["ps_g3"]][ctile % 3]


def l1_streams(p, t, d, eng, T1=2048):
    TCH = 512
    NCH = T1 // TCH
    NCT = 16
    NKT = 8
    E = eng or {k: None for k in ("sync", "gpsimd", "tensor", "scalar", "vector")}

    def stream_gpsimd(g):
        def xload(tc):
            if tc >= 2:
                p.wait(g, "mm", ("gate", tc - 2, NCT - 1))
            for kt in range(NKT):
                p.emit(lambda tc=tc, kt=kt: g.dma_start(
                    out=t["x"][tc % 2][:, kt, :], in_=d["xT"][tc, kt]),
                    inc="x" if tc % 2 == 0 else "xc", step=16)
        for kt in range(NKT):
            p.emit(lambda kt=kt: g.dma_start(out=t["win"][:, kt, 0:2048], in_=d["w_inT"][kt * 128:(kt + 1) * 128, 0:2048]),
                   inc="w", step=16)
        xload(0)
        for kt in range(NKT):
            p.emit(lambda kt=kt: g.dma_start(out=t["win"][:, kt, 2048:4096], in_=d["w_inT"][kt * 128:(kt + 1) * 128, 2048:4096]),
                   inc="wg", step=16)
        xload(1)
        for kt in range(NCT):
            p.emit(lambda kt=kt: g.dma_start(out=t["wx"][:, kt, :], in_=d["w_xT"][kt * 128:(kt + 1) * 128, :]),
                   inc="wsm", step=16)
        p.emit(lambda: g.dma_start(out=t["wdt"][:, :, :], in_=d["w_dtT"]), inc="wsm", step=16)
        for tc in range(2, NCH):
            xload(tc)

    def stream_sync(s):
        p.emit(lambda: s.dma_start(out=t["cw"][:], in_=d["conv_w_cm"]), inc="small", step=16)
        p.emit(lambda: s.dma_start(out=t["cb"][:], in_=d["conv_b_cm"]), inc="small", step=16)
        p.emit(lambda: s.dma_start(out=t["db"][:], in_=d["dt_bias_cm"]), inc="small", step=16)

        def qst(name, o, tc, q, sem):
            qs = slice(4 * q, 4 * q + 4)
            p.emit(lambda name=name, o=o, tc=tc, qs=qs: s.dma_start(
                out=d[o][tc, :, qs, :], in_=t[name][:, qs, :]), inc=sem, step=16)

        def dd_quarter(tc, q):
            p.wait(s, "v", ("du", tc, 4 * q + 3))
            p.wait(s, "sc", ("dl", tc, 4 * q + 3))
            qst("duc", "o_du", tc, q, "stdu")
            qst("dlc", "o_dl", tc, q, "stdl")
            if q == 0:
                p.wait(s, "sc", ("bcc", tc))
                p.emit(lambda tc=tc: s.dma_start(out=d["o_bc"][tc], in_=t["bcc"][:, :]),
                       inc="stbc", step=16)

        for tc in range(NCH):
            for q in range(4):
                if tc >= 1:
                    dd_quarter(tc - 1, q)
                p.wait(s, "sc", ("u", tc, 4 * q + 3))
                qs = slice(4 * q, 4 * q + 4)
                p.emit(lambda tc=tc, qs=qs: s.dma_start(
                    out=d["o_u"][tc, :, qs, :], in_=t["uc"][tc % 2][:, qs, :]), inc="stu", step=16)
                p.wait(s, "sc", ("gevac", tc, 4 * q + 3))
                qst("gsc", "o_gs", tc, q, "stg")
        for q in range(4):
            dd_quarter(NCH - 1, q)
        for sem in ("stu", "stg", "stdu", "stdl", "stbc"):
            p.wait(s, sem, value=p.cur(sem))

    def stream_tensor(te):
        p.wait(te, "x", value=NKT * 16)
        p.wait(te, "w", value=NKT * 16)
        xwaits = {0: NKT * 16}
        for ctile in range(NCT):
            if ctile >= 1:
                p.wait(te, "sc", ("tail", ctile - 1))
            for kt in range(NKT):
                p.emit(lambda ctile=ctile, kt=kt: te.matmul(
                    t["ps_pt"][:, 0:3], t["win"][:, kt, ctile * 128:(ctile + 1) * 128],
                    t["x"][0][:, kt, 0:3], start=(kt == 0), stop=(kt == NKT - 1)),
                    inc=None if kt < NKT - 1 else "mm",
                    event=None if kt < NKT - 1 else ("halo", ctile))
        def emit_delta(tcd, ctile):
            if tcd == 0 and ctile == 0:
                p.wait(te, "wsm", value=(NCT + 1) * 16)
            if not (tcd == 0 and ctile < 2):
                prev = (tcd, ctile - 2) if ctile >= 2 else (tcd - 1, NCT - 2 + ctile)
                p.wait(te, "sc", ("devac",) + prev)
            p.emit(lambda tcd=tcd, ctile=ctile: te.matmul(
                _l1_psd(t, ctile)[:, :], t["wdt"][:, ctile, :], t["dtlr"][:, :],
                start=True, stop=True),
                inc="mm", event=("delta", tcd, ctile))

        def emit_projt_kt(tc, kt):
            if kt == 0:
                if tc == 0:
                    p.wait(te, "wsm", value=(NCT + 1) * 16)
                    p.wait(te, "sc", ("tail", NCT - 1))
                else:
                    p.wait(te, "sc", ("ptevac", tc - 1))
            p.wait(te, "sc", ("u", tc, kt))
            p.emit(lambda tc=tc, kt=kt: te.matmul(
                t["ps_pt"][0:96, :], t["wx"][:, kt, :], t["uc"][tc % 2][:, kt, :],
                start=(kt == 0), stop=(kt == NCT - 1)),
                inc=None if kt < NCT - 1 else "mm",
                event=None if kt < NCT - 1 else ("projt", tc))

        for tc in range(NCH):
            xb = t["x"][tc % 2]
            p.wait(te, "x" if tc % 2 == 0 else "xc", value=(tc // 2 + 1) * NKT * 16)
            for ctile in range(NCT):
                if not (tc == 0 and ctile < 2):
                    prev = (tc, ctile - 2) if ctile >= 2 else (tc - 1, NCT - 2 + ctile)
                    p.wait(te, "v", ("cpdone",) + prev)
                for kt in range(NKT):
                    p.emit(lambda xb=xb, tc=tc, ctile=ctile, kt=kt: te.matmul(
                        t["ps"][ctile % 2][:, :], t["win"][:, kt, ctile * 128:(ctile + 1) * 128],
                        xb[:, kt, 3:3 + TCH], start=(kt == 0), stop=(kt == NKT - 1)),
                        inc=None if kt < NKT - 1 else "mm",
                        event=None if kt < NKT - 1 else ("proj", tc, ctile))
                if tc >= 1:
                    emit_delta(tc - 1, ctile)
                j = 2048 + ctile * 128
                if tc == 0 and ctile == 0:
                    p.wait(te, "wg", value=NKT * 16)
                if not (tc == 0 and ctile < 3):
                    prev = (tc, ctile - 3) if ctile >= 3 else (tc - 1, NCT - 3 + ctile)
                    p.wait(te, "sc", ("gevac",) + prev)
                for kt in range(NKT):
                    p.emit(lambda xb=xb, tc=tc, ctile=ctile, kt=kt, j=j: te.matmul(
                        _l1_psg(t, ctile)[:, :], t["win"][:, kt, j:j + 128],
                        xb[:, kt, 3:3 + TCH], start=(kt == 0), stop=(kt == NKT - 1)),
                        inc=None if kt < NKT - 1 else "mm",
                        event=None if kt < NKT - 1 else ("gate", tc, ctile))
                if ctile >= 4:
                    emit_projt_kt(tc, ctile - 4)
            for kt in range(NCT - 4, NCT):
                emit_projt_kt(tc, kt)
            p.wait(te, "sc", ("dtlr", tc))
        for ctile in range(NCT):
            emit_delta(NCH - 1, ctile)

    def stream_scalar(sc):
        p.wait(sc, "small", value=48)
        for ctile in range(NCT):
            p.wait(sc, "mm", ("halo", ctile))
            p.emit(lambda ctile=ctile: sc.copy(out=t["tail"][:, ctile, :], in_=t["ps_pt"][:, 0:3]),
                   inc="sc", event=("tail", ctile))
        def emit_sp_exp(sc, tcd, ctile):
            ab = t["ab"][ctile % 4]
            p.wait(sc, "mm", ("delta", tcd, ctile))
            if ctile == 0 and tcd >= 1:
                p.wait(sc, "stdl", value=64 * tcd)
            p.emit(lambda ab=ab, ctile=ctile: sc.activation(
                out=ab[:], in_=_l1_psd(t, ctile)[:, :], func=AF.Exp,
                bias=t["db"][:, ctile:ctile + 1]),
                inc="sc", event=("devac", tcd, ctile))

        def emit_sp_ln(sc, tcd, ctile):
            ab = t["ab"][ctile % 4]
            p.wait(sc, "sc", value=p.cur("sc"))
            p.emit(lambda ab=ab, ctile=ctile: sc.activation(
                out=t["dlc"][:, ctile, :], in_=ab[:], func=AF.Ln, bias=1.0),
                inc="sc", event=("dl", tcd, ctile))

        def emit_softplus(sc, tcd, ctile):
            emit_sp_exp(sc, tcd, ctile)
            emit_sp_ln(sc, tcd, ctile)

        for tc in range(NCH):
            def emit_u(tc, ctile):
                p.wait(sc, "v", ("cv", tc, ctile))
                if tc >= 2 and ctile == 0:
                    p.wait(sc, "v", ("duall", tc - 2))
                    p.wait(sc, "stu", value=64 * (tc - 1))
                p.emit(lambda tc=tc, ctile=ctile: sc.activation(
                    out=t["uc"][tc % 2][:, ctile, :], in_=t["cv"][ctile % 6][:], func=AF.Silu,
                    bias=t["cb"][:, ctile:ctile + 1]),
                    inc="sc", event=("u", tc, ctile))

            def emit_gevac(tc, ctile):
                p.wait(sc, "mm", ("gate", tc, ctile))
                if tc >= 1 and ctile == 0:
                    p.wait(sc, "stg", value=64 * tc)
                p.emit(lambda tc=tc, ctile=ctile: sc.activation(
                    out=t["gsc"][:, ctile, :], in_=_l1_psg(t, ctile)[:, :], func=AF.Silu),
                    inc="sc", event=("gevac", tc, ctile))

            for g in range(NCT // 2):
                c0, c1 = 2 * g, 2 * g + 1
                if g >= 1:
                    emit_u(tc, c0 - 2)
                    emit_u(tc, c1 - 2)
                    emit_gevac(tc, c0 - 2)
                    emit_gevac(tc, c1 - 2)
                if tc >= 1:
                    emit_sp_exp(sc, tc - 1, c0)
                    emit_sp_exp(sc, tc - 1, c1)
                    emit_sp_ln(sc, tc - 1, c0)
                    emit_sp_ln(sc, tc - 1, c1)
            emit_u(tc, NCT - 2)
            emit_u(tc, NCT - 1)
            p.events[("sc", ("uc", tc))] = p.cur("sc")
            emit_gevac(tc, NCT - 2)
            emit_gevac(tc, NCT - 1)
            p.events[("sc", ("gs", tc, NCT - 1))] = p.cur("sc")
            p.wait(sc, "mm", ("projt", tc))
            if tc >= 1:
                p.wait(sc, "mm", ("delta", tc - 1, NCT - 1))
            p.emit(lambda: sc.copy(out=t["dtlr"][:, :], in_=t["ps_pt"][0:64, :]),
                   inc="sc", event=("dtlr", tc))
            if tc >= 1:
                p.wait(sc, "stbc", value=16 * tc)
            p.emit(lambda: sc.copy(out=t["bcc"][:, :], in_=t["ps_pt"][64:96, :]),
                   inc="sc", event=("ptevac", tc))
            p.events[("sc", ("bcc", tc))] = p.cur("sc")
        for ctile in range(NCT):
            emit_softplus(sc, NCH - 1, ctile)

    def stream_vector(v):
        def emit_dldu(v, tcd, ctile):
            p.wait(v, "sc", ("dl", tcd, ctile))
            p.wait(v, "sc", ("u", tcd, ctile))
            if ctile == 0 and tcd >= 1:
                p.wait(v, "stdu", value=64 * tcd)
            p.emit(lambda tcd=tcd, ctile=ctile: v.tensor_tensor(
                out=t["duc"][:, ctile, :], in0=t["dlc"][:, ctile, :],
                in1=t["uc"][tcd % 2][:, ctile, :], op=ALU.mult),
                inc="v", event=("du", tcd, ctile))
            if ctile == NCT - 1:
                p.events[("v", ("duall", tcd))] = p.cur("v")

        for tc in range(NCH):
            for ctile in range(NCT):
                cv = t["cv"][ctile % 6]
                ps = t["ps"][ctile % 2]
                p.wait(v, "mm", ("proj", tc, ctile))
                if tc == 0:
                    p.wait(v, "sc", ("tail", ctile))
                if not (tc == 0 and ctile < 6):
                    prev = (tc, ctile - 6) if ctile >= 6 else (tc - 1, NCT - 6 + ctile)
                    p.wait(v, "sc", ("u",) + prev)
                p.emit(lambda cv=cv, ps=ps, ctile=ctile: v.tensor_scalar(
                    out=cv[:], in0=ps[:, 0:TCH], scalar1=t["cw"][:, ctile, 3:4],
                    scalar2=None, op0=ALU.mult), inc="v")
                for k in (2, 1, 0):
                    p.wait(v, "v", value=p.cur("v"))
                    p.emit(lambda cv=cv, ps=ps, ctile=ctile, k=k: v.scalar_tensor_tensor(
                        out=cv[:, 3 - k:TCH], in0=ps[:, 0:TCH - 3 + k],
                        scalar=t["cw"][:, ctile, k:k + 1],
                        in1=cv[:, 3 - k:TCH], op0=ALU.mult, op1=ALU.add), inc="v")
                    p.wait(v, "v", value=p.cur("v"))
                    p.emit(lambda cv=cv, ctile=ctile, k=k: v.scalar_tensor_tensor(
                        out=cv[:, 0:3 - k], in0=t["tail"][:, ctile, k:3],
                        scalar=t["cw"][:, ctile, k:k + 1],
                        in1=cv[:, 0:3 - k], op0=ALU.mult, op1=ALU.add),
                        inc="v", event=None if k > 0 else ("cv", tc, ctile))
                if tc < NCH - 1:
                    p.wait(v, "v", value=p.cur("v"))
                    p.emit(lambda ps=ps, ctile=ctile: v.tensor_copy(
                        out=t["tail"][:, ctile, :], in_=ps[:, TCH - 3:TCH]),
                        inc="v")
                p.events[("v", ("cpdone", tc, ctile))] = p.cur("v")
                if tc >= 1:
                    emit_dldu(v, tc - 1, ctile)
        for ctile in range(NCT):
            emit_dldu(v, NCH - 1, ctile)

    p.begin("gpsimd"); stream_gpsimd(E["gpsimd"])
    p.begin("sync"); stream_sync(E["sync"])
    p.begin("tensor"); stream_tensor(E["tensor"])
    p.begin("scalar"); stream_scalar(E["scalar"])
    p.begin("vector"); stream_vector(E["vector"])


L3_SEMS = ["ld", "w3", "mm3", "sc3", "v3", "st3", "q3"]


def l3_alloc(nc, E, T3=2048):
    TCH = 512
    t = {}
    t["wbuf"] = E(nc.sbuf_tensor("l3_wbuf", [128, 32, 1024], BF16))
    t["wq"] = E(nc.sbuf_tensor("l3_wq", [128, 4096], BF16))
    t["wf"] = [E(nc.sbuf_tensor(f"l3_wf{i}", [128, 4096], F32)) for i in range(2)]
    t["x"] = E(nc.sbuf_tensor("l3_x", [128, 8, TCH], BF16))
    t["y2"] = E(nc.sbuf_tensor("l3_y2", [128, 16, TCH], BF16))
    t["x1"] = E(nc.sbuf_tensor("l3_x1", [128, 8, TCH], BF16))
    t["rbf"] = E(nc.sbuf_tensor("l3_rbf", [128, 8, TCH], BF16))
    t["q1"] = E(nc.sbuf_tensor("l3_q1", [128, 8, TCH], BF16))
    t["g1"] = E(nc.sbuf_tensor("l3_g1", [128, 32, TCH], BF16))
    t["sq"] = E(nc.sbuf_tensor("l3_sq", [128, TCH], BF16))
    t["v"] = [E(nc.sbuf_tensor(f"l3_v{i}", [128, TCH], F32)) for i in range(2)]
    t["sg"] = [E(nc.sbuf_tensor(f"l3_sg{i}", [128, TCH], BF16)) for i in range(2)]
    t["qiB"] = E(nc.sbuf_tensor("l3_qiB", [128, 4, TCH], mybir.dt.int32))
    t["mskB"] = E(nc.sbuf_tensor("l3_mskB", [128, 4, TCH], BF16))
    t["qi"] = [E(nc.sbuf_tensor(f"l3_qi{i}", [128, TCH], mybir.dt.int32)) for i in range(2)]
    t["r3"] = E(nc.sbuf_tensor("l3_r3", [128, 8, TCH], F32))
    t["out"] = E(nc.sbuf_tensor("l3_out", [128, 8, TCH], F32))
    t["stat"] = E(nc.sbuf_tensor("l3_stat", [1, TCH * 8], F32))
    t["smax"] = E(nc.sbuf_tensor("l3_smax", [128, 8], F32))
    t["sflat"] = E(nc.sbuf_tensor("l3_sflat", [4, 128 * 4], F32))
    t["sbc"] = E(nc.sbuf_tensor("l3_sbc", [128, TCH], BF16))
    t["rbc"] = E(nc.sbuf_tensor("l3_rbc", [128, TCH], BF16))
    t["ones"] = E(nc.sbuf_tensor("l3_ones", [128, 2], BF16))
    t["onesf"] = E(nc.sbuf_tensor("l3_onesf", [128, 2], F32))
    t["lng"] = E(nc.sbuf_tensor("l3_lng", [128, 8], F32))
    t["lnb"] = E(nc.sbuf_tensor("l3_lnb", [128, 8], F32))
    t["ln2g"] = E(nc.sbuf_tensor("l3_ln2g", [128, 32], F32))
    t["ln2b"] = E(nc.sbuf_tensor("l3_ln2b", [128, 32], F32))
    t["b1"] = E(nc.sbuf_tensor("l3_b1", [128, 32], F32))
    t["b2"] = E(nc.sbuf_tensor("l3_b2", [128, 8], F32))
    t["swc"] = E(nc.sbuf_tensor("l3_swc", [128, 4], F32))
    t["red"] = E(nc.sbuf_tensor("l3_red", [128, 8], F32))
    t["scl"] = E(nc.sbuf_tensor("l3_scl", [1, 64], F32))
    t["cst"] = E(nc.sbuf_tensor("l3_cst", [1, 8], F32))
    t["ps"] = [E(nc.psum_tensor(f"l3_ps{i}", [128, TCH], F32)) for i in range(8)]
    t["pss"] = E(nc.psum_tensor("l3_pss", [1, 2 * TCH], F32))
    t["psb"] = [E(nc.psum_tensor(f"l3_psb{i}", [128, TCH], F32)) for i in range(0)]
    return t



def rep0(ap, n):
    return type(ap)(tensor=ap.tensor, offset=ap.offset,
                    ap=[[ap.ap[0][0], ap.ap[0][1]], [0, n], [list(ap.ap[1])[0], list(ap.ap[1])[1]]])


def l3_streams(p, t, d, eng, T3=2048):
    TCH = 512
    NCH = T3 // TCH
    E = eng or {}
    ps = t["ps"]
    W = t["wbuf"]
    NJT, NHT = 8, 32
    g1f = t["g1"].ap().rearrange("p a b -> p (a b)")
    g1w = t["g1"].ap().rearrange("p a b -> p (a b)").bitcast(F32)
    wqa = [lambda: g1w[:, 0:4096], lambda: g1w[:, 4096:8192]]
    _q1f = lambda: t["q1"].ap().rearrange("p a b -> p (a b)")
    _rbff = lambda: t["rbf"].ap().rearrange("p a b -> p (a b)")
    qc1s = [_q1f, _rbff]
    qc2 = lambda: t["x1"][0].ap().rearrange("p a b -> p (a b)")
    LN_EPS = 1e-5
    w1slot = W.ap()[:, 16:32, :].rearrange("p a b -> p (a b)").rearrange("p (c e) -> p c e", c=8)
    w2slot = W.ap()[:, 16:32, :]

    def xs(tc):
        return t["x"][:, :, :]

    def y2s(tc):
        return t["y2"][:, :, :]

    def stream_gpsimd(g):
        for i in range(32):
            rnd, wi, kt = i // 16, (i % 16) // 8, i % 8
            wsrc = "w1T_s" if wi == 0 else "w2T_s"
            if i >= 2:
                p.wait(g, "v3", ("wfree", i - 2))
            p.emit(lambda wsrc=wsrc, kt=kt, i=i: g.dma_start(
                out=wqa[i % 2](), in_=d[wsrc][:, kt, :]),
                inc="q%d" % (i % 2), step=16, event=("wcast", i))
        p.emit(lambda: g.dma_start(out=W[:, 0:16, :], in_=d["w_outT_s"]),
               inc="qo", step=16, event=("wout",))
        def outst(tc):
            p.wait(g, "v3", ("outj", tc, NJT - 1))
            for jt in range(NJT):
                p.emit(lambda tc=tc, jt=jt: g.dma_start(
                    out=d["outT_s"][:, jt, tc * TCH:(tc + 1) * TCH], in_=t["r3"][:, jt, :]),
                    inc="st3", step=16,
                    event=("outst", tc, jt) if jt == NJT - 1 else None)

        for tc in range(NCH):
            if tc >= 1:
                p.wait(g, "v3", ("Aev", tc - 1, NJT - 1))
            p.emit(lambda tc=tc: g.dma_start(out=xs(tc), in_=d["xT_s"][:, :, tc * TCH:(tc + 1) * TCH]),
                   inc="qx", step=16, event=("xc", tc))
            if tc >= 2:
                outst(tc - 2)
        if NCH >= 2:
            outst(NCH - 2)
        outst(NCH - 1)
        p.wait(g, "st3", value=p.cur("st3"))

    def stream_sync(s):
        for nm in ("lng", "lnb", "ln2g", "ln2b", "b1", "b2", "identf", "onesb", "onesf", "onesrow", "cst"):
            p.emit(lambda nm=nm: s.dma_start(out=t[nm][:], in_=d[nm]), inc="ls", step=16)
        for i in range(16):
            wi, kt = i // 8, i % 8
            dst = "w1q_d" if wi == 0 else "w2q_d"
            qc1 = qc1s[i % 2]
            p.wait(s, "v3", ("wq", i + 16))
            p.emit(lambda dst=dst, kt=kt, qc1=qc1: s.dma_start(out=d[dst][:, kt, :], in_=qc1()),
                   inc="sw%d" % (i % 2), step=16, event=("wqst", i))
        p.emit(lambda: s.dma_start(out=y2s(0), in_=d["y2T"][:, :, 0:TCH]),
               inc="ly", step=16, event=("y2c", 0))
        def sxd(tcx, which, flev, rowev):
            p.wait(s, "v3", flev)
            p.wait(s, "mm3", rowev)
            p.emit(lambda: s.dma_start(out=t["stat"][0:1, 0:2 * TCH].rearrange("p (a b) -> p a b", a=8),
                                       in_=t["sflat"][0:8, 0:128]),
                   inc="sx", step=16, event=("sxd", tcx, which))

        sxd(0, 1, ("sfl", 0, 1), ("m20b", 0))
        for tc in range(NCH):
            if tc + 1 < NCH:
                p.wait(s, "mm3", ("Adone", tc))
                p.emit(lambda tc=tc: s.dma_start(out=y2s(tc + 1),
                                                 in_=d["y2T"][:, :, (tc + 1) * TCH:(tc + 2) * TCH]),
                       inc="ly", step=16, event=("y2c", tc + 1))
            for h in range(2):
                p.wait(s, "sw0", value=8 * 16)
                p.wait(s, "sw1", value=8 * 16)
                if not (tc == 0 and h == 0):
                    p.wait(s, "mm3", ("slotfree", tc, h))
                p.emit(lambda h=h: s.dma_start(out=w1slot, in_=d["w1q_d"][:, :, h * 2048:(h + 1) * 2048]),
                       inc="lw", step=16, event=("wslot", tc, h))
            if tc + 1 < NCH:
                sxd(tc + 1, 1, ("sfl", tc + 1, 1), ("m20b", tc + 1))
            sxd(tc, 2, ("sfl", tc, 2), ("m22b", tc))
            for h in range(2, 4):
                p.wait(s, "mm3", ("slotfree", tc, h))
                p.emit(lambda h=h: s.dma_start(out=w2slot, in_=d["w2q_d"].rearrange(
                    "p a b -> p (a b)").rearrange("p (c e) -> p c e", c=32)[:, (h - 2) * 16:(h - 1) * 16, :]),
                    inc="lw", step=16, event=("wslot", tc, h))
        pass

    def bc(te_, row_fn, out_ps, ev=None):
        p.emit(lambda: te_.matmul(out_ps, t["onesrow"][0:1, :], row_fn(), start=True, stop=True),
               inc="mm3", event=ev)

    S = TCH
    x1b = lambda tcn: t["x1"][tcn % 2]
    r3i = lambda: t["r3"].ap().rearrange("p a b -> p (a b)").bitcast(mybir.dt.int32).rearrange(
        "p (c e) -> p c e", c=8)
    r3h = lambda: t["r3"].ap().rearrange("p a b -> p (a b)").bitcast(BF16).rearrange(
        "p (c e) -> p c e", c=16)
    qi1 = lambda jt: r3i()[:, jt % 2, :]
    qiB = lambda: r3i()[:, 0:4, :]
    vbF = lambda b: r3h()[:, 8 + 4 * (b % 2):12 + 4 * (b % 2), :]

    def cbank(jt2):
        return jt2 % 2 if jt2 < 16 else (0, 1, 4, 5)[jt2 % 4]

    def stream_tensor(te_):
        p.wait(te_, "ls", value=11 * 16)
        for wi in range(2):
            p.wait(te_, "v3", ("wred", wi * 8 + 7))
            p.emit(lambda wi=wi: te_.matmul(ps[6][0:1, wi * 8:wi * 8 + 8], t["onesf"][:, 0:1],
                                            t["red"][:, wi * 8:wi * 8 + 8], start=True, stop=True),
                   inc="mm3", event=("wtot", wi))
        p.wait(te_, "v3", ("thr",))
        bc(te_, lambda: t["scl"][0:1, 0:6], ps[7][:, 0:6], ev=("swcb",))

        def emit_A_jt(tcn, jt):
            if jt == 0:
                p.wait(te_, "qo", ("wout",))
                p.wait(te_, "ly", ("y2c", tcn))
                if tcn >= 2:
                    p.wait(te_, "v3", ("dv3", tcn - 2))
            if jt >= 2:
                p.wait(te_, "v3", ("Aev", tcn, jt - 2))
            for kt in range(16):
                p.emit(lambda tcn=tcn, jt=jt, kt=kt: te_.matmul(
                    ps[4 + jt % 2][:, :], W[:, kt, jt * 128:(jt + 1) * 128],
                    y2s(tcn)[:, kt, :], start=(kt == 0), stop=(kt == 15)),
                    inc=None if kt < 15 else "mm3",
                    event=None if kt < 15 else ("A", tcn, jt))
            if jt == NJT - 1:
                p.events[("mm3", ("Adone", tcn))] = p.cur("mm3")

        def emit_Bsum(tcn, jt):
            p.wait(te_, "v3", ("Aev", tcn, jt))
            p.wait(te_, "sc3", ("sq0", tcn, jt))
            if jt == 0:
                p.wait(te_, "v3", ("swcev",))
                if tcn >= 2:
                    p.wait(te_, "v3", ("var3", tcn - 2))
            p.emit(lambda tcn=tcn, jt=jt: te_.matmul(ps[6][0:1, :], t["onesb"][:, 0:1],
                                                     t["rbf"][:, jt, :], start=(jt == 0), stop=(jt == 7)),
                   inc="mm3")
            p.emit(lambda tcn=tcn, jt=jt: te_.matmul(ps[7][0:1, :], t["onesb"][:, 0:1],
                                                     t["sqA"][:, 0, :], start=(jt == 0), stop=(jt == 7)),
                   inc="mm3", event=("Bsum", tcn, jt))

        def emit_front_mid(tcn):
            p.wait(te_, "v3", ("stat0", tcn))
            bc(te_, lambda: t["stat"][0:1, 0:S], ps[6][:, :], ev=("rs0b", tcn))
            bc(te_, lambda: t["stat"][0:1, S:2 * S], ps[7][:, :], ev=("m20b", tcn))

        def emit_front_mxT(tcn):
            p.wait(te_, "v3", ("mx1", tcn))
            p.wait(te_, "v3", ("x1", tcn, NJT - 1))
            for tb in range(4):
                p.emit(lambda tb=tb: te_.transpose(ps[6][:, tb * 128:(tb + 1) * 128],
                                                   t["v"][0][:, tb * 128:(tb + 1) * 128], t["identf"][:]),
                       inc=None if tb < 3 else "mm3", event=None if tb < 3 else ("mxT1", tcn))

        def emit_front_sT(tcn):
            p.wait(te_, "v3", ("s1c", tcn))
            p.emit(lambda: te_.transpose(ps[7][0:8, 0:128], t["smax"][:, 0:8], t["identf"][:]),
                   inc="mm3", event=("sT1", tcn))

        def emit_front_sb(tcn):
            p.wait(te_, "sx", ("sxd", tcn, 1))
            bc(te_, lambda: t["stat"][0:1, 0:S], ps[6][:, :], ev=("sb1", tcn))
            bc(te_, lambda: t["stat"][0:1, S:2 * S], ps[7][:, :], ev=("rb1", tcn))

        for jt in range(2):
            emit_A_jt(0, jt)
        for jt in range(2, NJT):
            emit_A_jt(0, jt)
            emit_Bsum(0, jt - 2)
        emit_Bsum(0, 6)
        emit_Bsum(0, 7)
        emit_front_mid(0)
        emit_front_mxT(0)
        emit_front_sT(0)
        emit_front_sb(0)

        for tc in range(NCH):
            nxt = tc + 1 < NCH

            def dsum(j):
                p.wait(te_, "sc3", ("gelu", tc, j))
                p.wait(te_, "sc3", ("sq2", tc, j))
                if j == 0 and tc >= 1:
                    p.wait(te_, "v3", ("gev", tc - 1, 3))
                p.emit(lambda j=j: te_.matmul(ps[2][0:1, :], t["onesb"][:, 0:1],
                                              t["g1"][:, j, :], start=(j == 0), stop=(j == 31)),
                       inc="mm3")
                p.emit(lambda j=j: te_.matmul(ps[3][0:1, :], t["onesb"][:, 0:1],
                                              t["sq"][:, j % 2, :], start=(j == 0), stop=(j == 31)),
                       inc="mm3", event=("Dsum", tc, j))

            p.wait(te_, "lw", ("wslot", tc, 0))
            p.wait(te_, "v3", ("q1", tc))
            for jt2 in range(NHT):
                if jt2 == 16:
                    p.wait(te_, "lw", ("wslot", tc, 1))
                if (tc, jt2) not in ((0, 0), (0, 1)):
                    if jt2 < 2:
                        p.wait(te_, "v3", ("gev", tc - 1, jt2))
                    elif jt2 < 18:
                        p.wait(te_, "sc3", ("gelu", tc, jt2 - 2))
                    else:
                        p.wait(te_, "sc3", ("gelu", tc, jt2 - 4))
                if jt2 == 16:
                    if nxt:
                        p.wait(te_, "v3", ("Aev", tc + 1, NJT - 1))
                    elif tc >= 1:
                        p.wait(te_, "v3", ("dv3", tc - 1))
                for kt in range(NJT):
                    p.emit(lambda jt2=jt2, kt=kt: te_.matmul(
                        ps[cbank(jt2)][:, :], w1slot[:, kt, (jt2 % 16) * 128:(jt2 % 16) * 128 + 128],
                        t["q1"][:, kt, :], start=(kt == 0), stop=(kt == 7)),
                        inc=None if kt < 7 else "mm3",
                        event=None if kt < 7 else ("C", tc, jt2))
                if jt2 == 15:
                    p.events[("mm3", ("slotfree", tc, 1))] = p.cur("mm3")
                if jt2 == 31:
                    p.events[("mm3", ("slotfree", tc, 2))] = p.cur("mm3")
                if jt2 >= 4:
                    dsum(jt2 - 4)
                if nxt:
                    if jt2 % 2 == 1 and jt2 // 2 < NJT:
                        emit_A_jt(tc + 1, jt2 // 2)
                    if jt2 % 2 == 0 and 4 <= jt2 <= 18:
                        emit_Bsum(tc + 1, (jt2 - 4) // 2)
                    if jt2 == 21:
                        emit_front_mid(tc + 1)
                    if jt2 == 26:
                        emit_front_mxT(tc + 1)
                    if jt2 == 28:
                        emit_front_sT(tc + 1)
                    if jt2 == 30:
                        emit_front_sb(tc + 1)
            for j in range(NHT - 4, NHT):
                dsum(j)
            p.wait(te_, "v3", ("stat2", tc))
            if nxt:
                p.wait(te_, "v3", ("Aev", tc + 1, NJT - 1))
            bc(te_, lambda: t["stat"][0:1, 0:S], ps[4][:, :], ev=("rs2b", tc))
            bc(te_, lambda: t["stat"][0:1, S:2 * S], ps[5][:, :], ev=("m22b", tc))
            p.wait(te_, "v3", ("mx2", tc))
            p.wait(te_, "v3", ("dv0", tc))
            for tb in range(4):
                p.emit(lambda tb=tb: te_.transpose(ps[4][:, tb * 128:(tb + 1) * 128],
                                                   t["v"][0][:, tb * 128:(tb + 1) * 128], t["identf"][:]),
                       inc=None if tb < 3 else "mm3", event=None if tb < 3 else ("mxT2", tc))
            p.wait(te_, "v3", ("s2c", tc))
            p.wait(te_, "v3", ("dv1", tc))
            p.emit(lambda: te_.transpose(ps[5][0:8, 0:128], t["smax"][:, 0:8], t["identf"][:]),
                   inc="mm3", event=("sT2", tc))
            p.wait(te_, "sx", ("sxd", tc, 2))
            bc(te_, lambda: t["stat"][0:1, 0:S], ps[4][:, :], ev=("sb2", tc))
            bc(te_, lambda: t["stat"][0:1, S:2 * S], ps[5][:, :], ev=("rb2", tc))
            p.wait(te_, "lw", ("wslot", tc, 2))
            p.wait(te_, "v3", ("sbc2ev", tc))
            p.wait(te_, "sc3", ("gelu", tc, NHT - 1))
            if nxt:
                p.wait(te_, "v3", ("sbc1ev", tc + 1))
            for kt2 in range(NHT):
                if kt2 == 16:
                    p.wait(te_, "lw", ("wslot", tc, 3))
                if kt2 % 4 == 0:
                    p.wait(te_, "v3", ("q2b", tc, kt2 // 4))
                half = ((kt2 // 4) % 2) * 4
                for jt in range(NJT):
                    p.emit(lambda kt2=kt2, jt=jt, half=half: te_.matmul(
                        ps[jt][:, :], w2slot[:, kt2 % 16, jt * 128:(jt + 1) * 128],
                        t["q2"][:, half + kt2 % 4, :], start=(kt2 == 0), stop=(kt2 == 31)),
                        inc=None if jt < 7 else "mm3",
                        event=None if jt < 7 else ("F", tc, kt2))
                if kt2 == 15:
                    p.events[("mm3", ("slotfree", tc, 3))] = p.cur("mm3")
                if kt2 == 31:
                    p.events[("mm3", ("slotfree", tc + 1, 0))] = p.cur("mm3")
            p.wait(te_, "v3", ("gev", tc, NJT - 1))
            for jt in range(NJT):
                p.wait(te_, "sc3", ("r3bf", tc, jt))
                p.wait(te_, "sc3", ("sq3", tc, jt))
                p.emit(lambda jt=jt: te_.matmul(ps[6][0:1, :], t["onesb"][:, 0:1],
                                                t["rbf"][:, jt, :], start=(jt == 0), stop=(jt == 7)),
                       inc="mm3")
                p.emit(lambda jt=jt: te_.matmul(ps[7][0:1, :], t["onesb"][:, 0:1],
                                                t["sq"][:, jt % 2, :], start=(jt == 0), stop=(jt == 7)),
                       inc="mm3", event=("Gsum", tc, jt))
            p.wait(te_, "v3", ("stat3", tc))
            bc(te_, lambda: t["stat"][0:1, 0:S], ps[4][:, :], ev=("rs3b", tc))
            bc(te_, lambda: t["stat"][0:1, S:2 * S], ps[5][:, :], ev=("m23b", tc))

    def stream_scalar(sc_):
        p.wait(sc_, "ls", value=11 * 16)

        def sq0(tcn, jt):
            p.wait(sc_, "v3", ("Aev", tcn, jt))
            if jt >= 1:
                p.wait(sc_, "mm3", ("Bsum", tcn, jt - 1))
            p.emit(lambda tcn=tcn, jt=jt: sc_.activation(out=t["sqA"][:, 0, :],
                                                         in_=t["rbf"][:, jt, :], func=AF.Square),
                   inc="sc3", event=("sq0", tcn, jt))

        def sd(tcn, varev, ev):
            p.wait(sc_, "v3", varev)
            p.emit(lambda: sc_.activation(out=t["stat"][0:1, 2 * S:3 * S],
                                          in_=t["stat"][0:1, 4 * S:5 * S],
                                          func=AF.Sqrt, bias=t["cst"][0:1, 0:1]),
                   inc="sc3", event=ev)

        for jt in range(NJT):
            sq0(0, jt)
        sd(0, ("var0", 0), ("sd0", 0))

        for tc in range(NCH):
            nxt = tc + 1 < NCH
            for jt2 in range(NHT):
                p.wait(sc_, "mm3", ("C", tc, jt2))
                p.emit(lambda jt2=jt2: sc_.activation(
                    out=t["g1"][:, jt2, :], in_=ps[cbank(jt2)][:, :], func=AF.Gelu,
                    bias=t["b1"][:, jt2:jt2 + 1]),
                    inc="sc3", event=("gelu", tc, jt2))
                if jt2 >= 2:
                    p.wait(sc_, "mm3", ("Dsum", tc, jt2 - 2))
                p.wait(sc_, "sc3", value=p.cur("sc3"))
                p.emit(lambda jt2=jt2: sc_.activation(out=t["sq"][:, jt2 % 2, :],
                                                      in_=t["g1"][:, jt2, :], func=AF.Square),
                       inc="sc3", event=("sq2", tc, jt2))
                if nxt:
                    if jt2 % 2 == 1 and jt2 // 2 < NJT:
                        sq0(tc + 1, jt2 // 2)
                    if jt2 == 19:
                        sd(tc + 1, ("var0", tc + 1), ("sd0", tc + 1))
            sd(tc, ("var2", tc), ("sd2", tc))
            for jt in range(NJT):
                p.wait(sc_, "v3", ("Gap", tc, jt))
                if jt == 0 and nxt:
                    p.wait(sc_, "v3", ("q1", tc + 1))
                if not (tc == 0 and jt == 0):
                    prevj = ("Gsum", tc, jt - 1) if jt >= 1 else ("Gsum", tc - 1, NJT - 1)
                    p.wait(sc_, "mm3", prevj)
                p.emit(lambda jt=jt: sc_.copy(out=t["rbf"][:, jt, :], in_=t["r3"][:, jt, :]),
                       inc="sc3", event=("r3bf", tc, jt))
                p.wait(sc_, "sc3", value=p.cur("sc3"))
                p.emit(lambda jt=jt: sc_.activation(out=t["sq"][:, jt % 2, :],
                                                    in_=t["rbf"][:, jt, :], func=AF.Square),
                       inc="sc3", event=("sq3", tc, jt))
            sd(tc, ("var3", tc), ("sd3", tc))

    def stream_vector(v_):
        def vw(val=None):
            p.wait(v_, "v3", value=val if val is not None else p.cur("v3"))

        for i in range(16):
            wi, kt = i // 8, i % 8
            p.wait(v_, "q%d" % (i % 2), ("wcast", i))
            p.emit(lambda i=i, wi=wi, kt=kt: v_.reduce_sum(
                out=t["red"][:, wi * 8 + kt:wi * 8 + kt + 1], in_=wqa[i % 2](),
                axis=mybir.AxisListType.X, apply_absolute_value=True),
                inc="v3", event=("wred", i))
            p.events[("v3", ("wfree", i))] = p.cur("v3")
        for wi in range(2):
            p.wait(v_, "mm3", ("wtot", wi))
            p.emit(lambda wi=wi: v_.reduce_sum(out=t["scl"][0:1, 16 + wi:17 + wi],
                                               in_=ps[6][0:1, wi * 8:wi * 8 + 8],
                                               axis=mybir.AxisListType.X), inc="v3")
            vw()
            N = 1024 * 4096.0
            p.emit(lambda wi=wi, N=N: v_.tensor_scalar(
                out=t["scl"][0:1, 4 + wi:5 + wi], in0=t["scl"][0:1, 16 + wi:17 + wi],
                scalar1=1.0 / N, scalar2=1e-5, op0=ALU.mult, op1=ALU.max), inc="v3")
            vw()
            p.emit(lambda wi=wi: v_.tensor_scalar_mul(
                out=t["scl"][0:1, 0 + wi:1 + wi], in0=t["scl"][0:1, 4 + wi:5 + wi],
                scalar1=0.5), inc="v3")
            vw()
            p.emit(lambda wi=wi: v_.tensor_scalar_mul(
                out=t["scl"][0:1, 2 + wi:3 + wi], in0=t["scl"][0:1, 0 + wi:1 + wi],
                scalar1=-1.0), inc="v3", event=("thr",) if wi == 1 else None)
        p.wait(v_, "mm3", ("swcb",))
        p.emit(lambda: v_.tensor_copy(out=t["swc"][:, 0:6], in_=ps[7][:, 0:6]),
               inc="v3", event=("swcev",))
        for i in range(16, 32):
            wi, kt = (i - 16) // 8, (i - 16) % 8
            qc1 = qc1s[i % 2]
            p.wait(v_, "q%d" % (i % 2), ("wcast", i))
            if i >= 18:
                p.wait(v_, "sw%d" % (i % 2), ("wqst", i - 18))
            vw()
            p.emit(lambda i=i, wi=wi, qc1=qc1: v_.tensor_scalar(
                out=qc1(), in0=wqa[i % 2](), scalar1=t["swc"][:, 0 + wi:1 + wi],
                scalar2=None, op0=ALU.is_ge), inc="v3")
            vw()
            p.emit(lambda i=i, wi=wi: v_.tensor_scalar(
                out=qc2(), in0=wqa[i % 2](), scalar1=t["swc"][:, 2 + wi:3 + wi],
                scalar2=None, op0=ALU.is_le), inc="v3")
            vw()
            p.emit(lambda qc1=qc1: v_.tensor_sub(out=qc1(), in0=qc1(), in1=qc2()),
                   inc="v3", event=("wq", i))
            p.events[("v3", ("wfree", i))] = p.cur("v3")

        def statmath(Nf, banks, sumev, varev, sdev, statev, rowfree, sdfree):
            bs, bq = banks
            p.wait(v_, "mm3", sumev)
            p.emit(lambda bs=bs: v_.tensor_scalar_mul(out=t["stat"][0:1, 3 * S:4 * S],
                                                      in0=ps[bs][0:1, :], scalar1=1.0 / Nf), inc="v3")
            vw()
            p.emit(lambda bq=bq: v_.tensor_scalar_mul(out=t["stat"][0:1, 4 * S:5 * S],
                                                      in0=ps[bq][0:1, :], scalar1=1.0 / Nf), inc="v3")
            vw()
            p.emit(lambda: v_.tensor_mul(out=t["stat"][0:1, 2 * S:3 * S],
                                         in0=t["stat"][0:1, 3 * S:4 * S],
                                         in1=t["stat"][0:1, 3 * S:4 * S]), inc="v3")
            vw()
            if sdfree is not None:
                p.wait(v_, "sc3", sdfree)
            p.emit(lambda: v_.tensor_sub(out=t["stat"][0:1, 4 * S:5 * S],
                                         in0=t["stat"][0:1, 4 * S:5 * S],
                                         in1=t["stat"][0:1, 2 * S:3 * S]),
                   inc="v3", event=varev)
            p.wait(v_, "sc3", sdev)
            if rowfree is not None:
                p.wait(v_, "mm3", rowfree)
            p.emit(lambda: v_.reciprocal(out=t["stat"][0:1, 0:S],
                                         in_=t["stat"][0:1, 2 * S:3 * S]), inc="v3")
            vw()
            p.emit(lambda: v_.tensor_mul(out=t["stat"][0:1, S:2 * S],
                                         in0=t["stat"][0:1, 3 * S:4 * S],
                                         in1=t["stat"][0:1, 0:S]),
                   inc="v3", event=statev)

        def scalerow(mxb, sTb, scev, rowev, wi, sxfree, mxTev):
            p.wait(v_, "mm3", mxTev)
            p.emit(lambda mxb=mxb: v_.reduce_max(out=t["smax"][:, 0:4],
                                                 in_=ps[mxb].ap().rearrange("p (a b) -> p a b", a=4),
                                                 axis=mybir.AxisListType.X), inc="v3")
            vw()
            p.emit(lambda: v_.tensor_scalar_max(out=t["smax"][:, 0:4], in0=t["smax"][:, 0:4],
                                                scalar1=1e-5), inc="v3")
            vw()
            p.emit(lambda wi=wi: v_.tensor_scalar(
                out=t["smax"][:, 4:8], in0=t["smax"][:, 0:4],
                scalar1=t["swc"][:, 4 + wi:5 + wi], scalar2=1.0 / 127.0,
                op0=ALU.mult, op1=ALU.mult), inc="v3")
            vw()
            p.emit(lambda: v_.reciprocal(out=t["smax"][:, 0:4], in_=t["smax"][:, 0:4]), inc="v3")
            vw()
            p.emit(lambda: v_.tensor_scalar_mul(out=t["smax"][:, 0:4], in0=t["smax"][:, 0:4],
                                                scalar1=127.0), inc="v3", event=scev)
            p.wait(v_, "mm3", sTev_holder[0])
            if sxfree is not None:
                p.wait(v_, "sx", sxfree)
            p.emit(lambda sTb=sTb: v_.tensor_copy(out=t["sflat"][0:8, 0:128], in_=ps[sTb][0:8, 0:128]),
                   inc="v3", event=rowev)

        sTev_holder = [None]

        def front(tcn):
            for jt in range(NJT):
                p.wait(v_, "mm3", ("A", tcn, jt))
                if jt == 0:
                    p.wait(v_, "qx", ("xc", tcn))
                    if tcn == 0:
                        p.wait(v_, "sw0", value=8 * 16)
                        p.wait(v_, "sw1", value=8 * 16)
                    if tcn >= 2:
                        p.wait(v_, "mm3", ("Gsum", tcn - 2, NJT - 1))
                p.emit(lambda tcn=tcn, jt=jt: v_.tensor_tensor(
                    out=t["rbf"][:, jt, :], in0=ps[4 + jt % 2][:, :], in1=xs(tcn)[:, jt, :],
                    op=ALU.add), inc="v3", event=("Aev", tcn, jt))
            if tcn == 0:
                rowfree, sdfree = None, None
            elif tcn == 1:
                rowfree, sdfree = ("rb1", 0), ("sd0", 0)
            else:
                rowfree, sdfree = ("m23b", tcn - 2), ("sd3", tcn - 2)
            statmath(1024.0, (6, 7), ("Bsum", tcn, NJT - 1), ("var0", tcn), ("sd0", tcn),
                     ("stat0", tcn), rowfree, sdfree)
            p.wait(v_, "mm3", ("rs0b", tcn))
            p.wait(v_, "mm3", ("m20b", tcn))
            if tcn >= 2:
                p.wait(v_, "st3", ("outst", tcn - 2, NJT - 1))
            vw()
            p.emit(lambda: v_.tensor_copy(out=t["v"][0][:], in_=ps[6][:, :]), inc="v3")
            vw()
            p.emit(lambda: v_.tensor_copy(out=t["v"][1][:], in_=ps[7][:, :]), inc="v3")
            vw()
            p.emit(lambda: v_.tensor_tensor(out=t["r3"][:, :, :], in0=t["rbf"][:, :, :],
                                            in1=rep0(t["v"][0][:], NJT), op=ALU.mult), inc="v3")
            vw()
            p.emit(lambda tcn=tcn: v_.tensor_tensor(out=x1b(tcn)[:, :, :], in0=t["r3"][:, :, :],
                                                    in1=rep0(t["v"][1][:], NJT), op=ALU.subtract),
                   inc="v3", event=("x1", tcn, NJT - 1))
            for jt in range(NJT):
                vw()
                p.emit(lambda tcn=tcn, jt=jt: v_.tensor_scalar(
                    out=t["rbf"][:, jt, :], in0=x1b(tcn)[:, jt, :],
                    scalar1=t["lng"][:, jt:jt + 1], scalar2=t["lnb"][:, jt:jt + 1],
                    op0=ALU.mult, op1=ALU.add), inc="v3", event=("l1o", tcn, jt))
            p.emit(lambda: v_.scalar_tensor_tensor(out=t["mx"][:, :], in0=t["rbf"][:, 0, :],
                                                   scalar=-1.0, in1=t["rbf"][:, 0, :],
                                                   op0=ALU.mult, op1=ALU.max), inc="v3")
            for jt in range(1, NJT):
                vw()
                p.emit(lambda jt=jt: v_.tensor_tensor(out=t["mx"][:, :], in0=t["mx"][:, :],
                                                      in1=t["rbf"][:, jt, :], op=ALU.max), inc="v3")
                vw()
                p.emit(lambda jt=jt: v_.scalar_tensor_tensor(out=t["mx"][:, :], in0=t["rbf"][:, jt, :],
                                                             scalar=-1.0, in1=t["mx"][:, :],
                                                             op0=ALU.mult, op1=ALU.max), inc="v3")
            vw()
            p.emit(lambda: v_.tensor_copy(out=t["v"][0][:], in_=t["mx"][:, :]),
                   inc="v3", event=("mx1", tcn))
            sTev_holder[0] = ("sT1", tcn)
            scalerow(6, 7, ("s1c", tcn), ("sfl", tcn, 1), 0,
                     ("sxd", tcn - 2, 2) if tcn >= 2 else None, ("mxT1", tcn))
            p.wait(v_, "mm3", ("sb1", tcn))
            p.emit(lambda: v_.tensor_copy(out=t["sbc"][:, :], in_=ps[6][:, :]), inc="v3")
            p.wait(v_, "mm3", ("rb1", tcn))
            vw()
            p.emit(lambda: v_.tensor_copy(out=t["rbc"][:, :], in_=ps[7][:, :]),
                   inc="v3", event=("sbc1ev", tcn))
            def msksub1(j):
                vw()
                p.emit(lambda j=j: v_.scalar_tensor_tensor(
                    out=t["sg"][j % 2][:], in0=qi1(j)[:, :], scalar=-0.5,
                    in1=t["sg"][j % 2][:], op0=ALU.add, op1=ALU.is_gt), inc="v3")
                vw()
                p.emit(lambda j=j: v_.tensor_tensor(out=t["q1"][:, j, :],
                                                    in0=qi1(j)[:, :],
                                                    in1=t["sg"][j % 2][:], op=ALU.subtract),
                       inc="v3", event=("q1t", tcn, j))

            for jt in range(NJT):
                vw()
                if jt == 0 and tcn >= 1:
                    p.wait(v_, "mm3", ("slotfree", tcn - 1, 2))
                p.emit(lambda jt=jt: v_.tensor_mul(out=t["sg"][jt % 2][:], in0=t["rbf"][:, jt, :],
                                                   in1=t["sbc"][:, :]), inc="v3")
                if jt >= 1:
                    msksub1(jt - 1)
                vw()
                p.emit(lambda jt=jt: v_.tensor_scalar_add(out=qi1(jt)[:, :], in0=t["sg"][jt % 2][:],
                                                          scalar1=0.5),
                       inc="v3", event=("qiev1", tcn, jt))
            msksub1(NJT - 1)
            vw()
            p.emit(lambda: v_.tensor_tensor(out=t["q1"][:, :, :], in0=t["q1"][:, :, :],
                                            in1=rep0(t["rbc"][:, :], NJT), op=ALU.mult),
                   inc="v3", event=("q1", tcn))

        def back(tc):
            nxt = tc + 1 < NCH
            statmath(4096.0, (2, 3), ("Dsum", tc, NHT - 1), ("var2", tc), ("sd2", tc),
                     ("stat2", tc), ("rb1", tc + 1) if nxt else ("rb1", tc),
                     ("sd0", tc + 1) if nxt else ("sd0", tc))
            p.wait(v_, "mm3", ("rs2b", tc))
            p.wait(v_, "mm3", ("m22b", tc))
            if tc >= 1:
                p.wait(v_, "st3", ("outst", tc - 1, NJT - 1))
            if nxt:
                p.wait(v_, "mm3", ("mxT1", tc + 1))
            vw()
            p.emit(lambda: v_.tensor_copy(out=t["v"][0][:], in_=ps[4][:, :]),
                   inc="v3", event=("dv0", tc))
            vw()
            p.emit(lambda: v_.tensor_copy(out=t["v"][1][:], in_=ps[5][:, :]),
                   inc="v3", event=("dv1", tc))
            vw()
            p.emit(lambda: v_.tensor_copy(out=t["sg"][0][:], in_=t["v"][0][:]), inc="v3")
            vw()
            p.emit(lambda: v_.tensor_copy(out=t["sg"][1][:], in_=t["v"][1][:]), inc="v3")
            for b4 in range(NHT // 4):
                vw()
                p.emit(lambda b4=b4: v_.tensor_tensor(
                    out=r3h()[:, 0:4, :], in0=t["g1"][:, b4 * 4:b4 * 4 + 4, :],
                    in1=rep0(t["sg"][0][:], 4), op=ALU.mult), inc="v3")
                vw()
                p.emit(lambda b4=b4: v_.tensor_tensor(
                    out=r3h()[:, 0:4, :], in0=r3h()[:, 0:4, :],
                    in1=rep0(t["sg"][1][:], 4), op=ALU.subtract), inc="v3")
                for k4 in range(4):
                    kt2 = b4 * 4 + k4
                    vw()
                    p.emit(lambda kt2=kt2, k4=k4: v_.tensor_scalar(
                        out=t["g1"][:, kt2, :], in0=r3h()[:, k4, :],
                        scalar1=t["ln2g"][:, kt2:kt2 + 1], scalar2=t["ln2b"][:, kt2:kt2 + 1],
                        op0=ALU.mult, op1=ALU.add), inc="v3", event=("l2o", tc, kt2))
            p.emit(lambda: v_.scalar_tensor_tensor(out=t["mx"][:, :], in0=t["g1"][:, 0, :],
                                                   scalar=-1.0, in1=t["g1"][:, 0, :],
                                                   op0=ALU.mult, op1=ALU.max), inc="v3")
            for kt2 in range(1, NHT):
                vw()
                p.emit(lambda kt2=kt2: v_.tensor_tensor(out=t["mx"][:, :], in0=t["mx"][:, :],
                                                        in1=t["g1"][:, kt2, :], op=ALU.max), inc="v3")
                vw()
                p.emit(lambda kt2=kt2: v_.scalar_tensor_tensor(out=t["mx"][:, :], in0=t["g1"][:, kt2, :],
                                                               scalar=-1.0, in1=t["mx"][:, :],
                                                               op0=ALU.mult, op1=ALU.max), inc="v3")
            vw()
            if nxt:
                p.wait(v_, "mm3", ("mxT1", tc + 1))
            p.emit(lambda: v_.tensor_copy(out=t["v"][0][:], in_=t["mx"][:, :]),
                   inc="v3", event=("mx2", tc))
            sTev_holder[0] = ("sT2", tc)
            if nxt:
                p.wait(v_, "mm3", ("sT1", tc + 1))
            scalerow(4, 5, ("s2c", tc), ("sfl", tc, 2), 1,
                     ("sxd", tc + 1, 1) if nxt else None, ("mxT2", tc))
            p.wait(v_, "mm3", ("sb2", tc))
            p.emit(lambda: v_.tensor_copy(out=t["sbc"][:, :], in_=ps[4][:, :]), inc="v3")
            p.wait(v_, "mm3", ("rb2", tc))
            vw()
            p.emit(lambda: v_.tensor_copy(out=t["rbc"][:, :], in_=ps[5][:, :]),
                   inc="v3", event=("sbc2ev", tc))
            def msksubF(b):
                vw()
                p.emit(lambda b=b: v_.scalar_tensor_tensor(
                    out=vbF(b), in0=qiB()[:, :, :], scalar=-0.5,
                    in1=vbF(b), op0=ALU.add, op1=ALU.is_gt), inc="v3")
                vw()
                half = (b % 2) * 4
                p.emit(lambda b=b, half=half: v_.tensor_tensor(
                    out=t["q2"][:, half:half + 4, :], in0=qiB()[:, :, :],
                    in1=vbF(b), op=ALU.subtract),
                    inc="v3", event=("q2b", tc, b))

            for b4 in range(NHT // 4):
                vw()
                if b4 == 0:
                    p.wait(v_, "mm3", ("C", tc, NHT - 1))
                if b4 >= 2:
                    p.wait(v_, "mm3", ("F", tc, (b4 - 2) * 4 + 3))
                p.emit(lambda b4=b4: v_.tensor_tensor(
                    out=vbF(b4), in0=t["g1"][:, b4 * 4:b4 * 4 + 4, :],
                    in1=rep0(t["sbc"][:, :], 4), op=ALU.mult), inc="v3")
                if b4 >= 1:
                    msksubF(b4 - 1)
                vw()
                p.emit(lambda b4=b4: v_.tensor_scalar_add(out=qiB()[:, :, :], in0=vbF(b4),
                                                          scalar1=0.5),
                       inc="v3", event=("qiev2", tc, b4))
            msksubF(NHT // 4 - 1)
            for jt in range(NJT):
                if jt == 0:
                    p.wait(v_, "mm3", ("F", tc, NHT - 1))
                vw()
                p.emit(lambda jt=jt: v_.tensor_mul(out=t["v"][jt % 2][:], in0=ps[jt][:, :],
                                                   in1=t["rbc"][:, :]),
                       inc="v3", event=("gev", tc, jt))
                vw()
                p.emit(lambda tc=tc, jt=jt: v_.scalar_tensor_tensor(
                    out=t["r3"][:, jt, :], in0=t["v"][jt % 2][:],
                    scalar=t["b2"][:, jt:jt + 1], in1=x1b(tc)[:, jt, :],
                    op0=ALU.add, op1=ALU.add),
                    inc="v3", event=("Gap", tc, jt))
            statmath(1024.0, (6, 7), ("Gsum", tc, NJT - 1), ("var3", tc), ("sd3", tc),
                     ("stat3", tc), ("rb2", tc), ("sd2", tc))
            p.wait(v_, "mm3", ("rs3b", tc))
            p.wait(v_, "mm3", ("m23b", tc))
            vw()
            p.emit(lambda: v_.tensor_copy(out=t["v"][0][:], in_=ps[4][:, :]), inc="v3")
            vw()
            p.emit(lambda: v_.tensor_copy(out=t["v"][1][:], in_=ps[5][:, :]),
                   inc="v3", event=("dv3", tc))
            vw()
            p.emit(lambda: v_.tensor_tensor(out=t["r3"][:, :, :], in0=t["r3"][:, :, :],
                                            in1=rep0(t["v"][0][:], NJT), op=ALU.mult), inc="v3")
            vw()
            p.emit(lambda: v_.tensor_tensor(out=t["r3"][:, :, :], in0=t["r3"][:, :, :],
                                            in1=rep0(t["v"][1][:], NJT), op=ALU.subtract),
                   inc="v3", event=("outj", tc, NJT - 1))

        front(0)
        for tc in range(NCH):
            if tc + 1 < NCH:
                front(tc + 1)
            back(tc)

    p.begin("gpsimd"); stream_gpsimd(E.get("gpsimd"))
    p.begin("sync"); stream_sync(E.get("sync"))
    p.begin("tensor"); stream_tensor(E.get("tensor"))
    p.begin("scalar"); stream_scalar(E.get("scalar"))
    p.begin("vector"); stream_vector(E.get("vector"))


L3_SEMS2 = ["ls", "ly", "lw", "q0", "q1", "qo", "qx", "sx", "sw0", "sw1", "mm3", "sc3", "v3", "st3"]


def l3_alloc2(nc, E, T3=2048):
    TCH = 512
    t = {}
    t["wbuf"] = E(nc.sbuf_tensor("l3_wbuf", [128, 32, 1024], BF16))
    t["x"] = E(nc.sbuf_tensor("l3_x", [128, 8, TCH], BF16))
    t["y2"] = E(nc.sbuf_tensor("l3_y2", [128, 16, TCH], BF16))
    t["x1"] = [E(nc.sbuf_tensor(f"l3_x1{i}", [128, 8, TCH], BF16)) for i in range(2)]
    t["rbf"] = E(nc.sbuf_tensor("l3_rbf", [128, 8, TCH], BF16))
    t["q1"] = E(nc.sbuf_tensor("l3_q1", [128, 8, TCH], BF16))
    t["q2"] = E(nc.sbuf_tensor("l3_q2", [128, 8, TCH], BF16))
    t["g1"] = E(nc.sbuf_tensor("l3_g1", [128, 32, TCH], BF16))
    t["sq"] = E(nc.sbuf_tensor("l3_sq", [128, 2, TCH], BF16))
    t["sqA"] = E(nc.sbuf_tensor("l3_sqA", [128, 1, TCH], BF16))
    t["v"] = [E(nc.sbuf_tensor(f"l3_v{i}", [128, TCH], F32)) for i in range(2)]
    t["sg"] = [E(nc.sbuf_tensor(f"l3_sg{i}", [128, TCH], BF16)) for i in range(2)]
    t["mx"] = t["sg"][0]
    t["mskB"] = E(nc.sbuf_tensor("l3_mskB", [128, 4, TCH], BF16))
    t["r3"] = E(nc.sbuf_tensor("l3_r3", [128, 8, TCH], F32))
    t["stat"] = E(nc.sbuf_tensor("l3_stat", [1, TCH * 5], F32))
    t["smax"] = E(nc.sbuf_tensor("l3_smax", [128, 8], F32))
    t["sflat"] = E(nc.sbuf_tensor("l3_sflat", [8, 128], F32))
    t["sbc"] = E(nc.sbuf_tensor("l3_sbc", [128, TCH], BF16))
    t["rbc"] = E(nc.sbuf_tensor("l3_rbc", [128, TCH], BF16))
    t["onesb"] = E(nc.sbuf_tensor("l3_onesb", [128, 1], BF16))
    t["onesf"] = E(nc.sbuf_tensor("l3_onesf", [128, 1], F32))
    t["onesrow"] = E(nc.sbuf_tensor("l3_onesrow", [1, 128], F32))
    t["identf"] = E(nc.sbuf_tensor("l3_identf", [128, 128], F32))
    t["lng"] = E(nc.sbuf_tensor("l3_lng", [128, 8], F32))
    t["lnb"] = E(nc.sbuf_tensor("l3_lnb", [128, 8], F32))
    t["ln2g"] = E(nc.sbuf_tensor("l3_ln2g", [128, 32], F32))
    t["ln2b"] = E(nc.sbuf_tensor("l3_ln2b", [128, 32], F32))
    t["b1"] = E(nc.sbuf_tensor("l3_b1", [128, 32], F32))
    t["b2"] = E(nc.sbuf_tensor("l3_b2", [128, 8], F32))
    t["swc"] = E(nc.sbuf_tensor("l3_swc", [128, 8], F32))
    t["red"] = E(nc.sbuf_tensor("l3_red", [128, 16], F32))
    t["scl"] = E(nc.sbuf_tensor("l3_scl", [1, 64], F32))
    t["cst"] = E(nc.sbuf_tensor("l3_cst", [1, 8], F32))
    t["ps"] = [E(nc.psum_tensor(f"l3_ps{i}", [128, TCH], F32)) for i in range(8)]
    return t


TT = 16384
TC = 2048
NCHUNK = TT // TC
NB_CHUNK = NCHUNK // 2
NG = 32
SPAN = min(512, TC)
NSPAN = TC // SPAN

def build_l2():
    nc = bass.Bass("TRN2", target_bir_lowering=False, debug=False)
    drep = nc.dram_tensor("drep", [NG, 128, TT], BF16, kind="ExternalInput").ap()
    durep = nc.dram_tensor("durep", [NG, 128, TT], BF16, kind="ExternalInput").ap()
    u_cm = nc.dram_tensor("u_cm", [2, 128, TT], BF16, kind="ExternalInput").ap()
    gs_cm = nc.dram_tensor("gs_cm", [2, 128, TT], BF16, kind="ExternalInput").ap()
    bdn = nc.dram_tensor("bdn", [128, TT], BF16, kind="ExternalInput").ap()
    cdn = nc.dram_tensor("cdn", [128, TT], BF16, kind="ExternalInput").ap()
    alog_dn = nc.dram_tensor("alog_dn", [128, NG], F32, kind="ExternalInput").ap()
    wsel = nc.dram_tensor("wsel", [128, 16, 128], BF16, kind="ExternalInput").ap()
    wdiag = nc.dram_tensor("wdiag", [128, 2, 128], F32, kind="ExternalInput").ap()
    y2 = nc.dram_tensor("y2", [2, 128, TT], BF16, kind="ExternalOutput").ap()

    iters = [(ch, ct, g16) for ch in range(NCHUNK) for ct in range(2) for g16 in range(16)]

    es = contextlib.ExitStack()
    with es:
        E = es.enter_context
        t_drep = [E(nc.sbuf_tensor(f"t_drep{i}", [128, TC], BF16)) for i in range(2)]
        t_durep = [E(nc.sbuf_tensor(f"t_durep{i}", [128, TC], BF16)) for i in range(2)]
        t_bdn = [E(nc.sbuf_tensor(f"t_bdn{i}", [128, TC], BF16)) for i in range(2)]
        t_cdn = [E(nc.sbuf_tensor(f"t_cdn{i}", [128, TC], BF16)) for i in range(2)]
        t_u = [E(nc.sbuf_tensor(f"t_u{i}", [128, 2, TC], BF16)) for i in range(2)]
        t_gs = [E(nc.sbuf_tensor(f"t_gs{i}", [128, 2, TC], BF16)) for i in range(2)]
        t_dA = [E(nc.sbuf_tensor(f"t_dA{i}", [128, TC], F32)) for i in range(2)]
        t_dBu = [E(nc.sbuf_tensor(f"t_dBu{i}", [128, TC], BF16)) for i in range(2)]
        t_hs = [E(nc.sbuf_tensor(f"t_hs{i}", [128, TC], BF16)) for i in range(2)]
        t_prod = [E(nc.sbuf_tensor(f"t_prod{i}", [128, TC], BF16)) for i in range(2)]
        t_y2 = [E(nc.sbuf_tensor(f"t_y2{i}", [128, TC], BF16)) for i in range(2)]
        t_wsel = E(nc.sbuf_tensor("t_wsel", [128, 16, 128], BF16))
        t_wdiagf = E(nc.sbuf_tensor("t_wdiagf", [128, 2, 128], F32))
        t_wdiag = E(nc.sbuf_tensor("t_wdiag", [128, 2, 128], BF16))
        t_alog = E(nc.sbuf_tensor("t_alog", [128, NG], F32))
        t_expA = E(nc.sbuf_tensor("t_expA", [128, NG], F32))
        t_negA = E(nc.sbuf_tensor("t_negA", [128, NG], F32))
        t_state = E(nc.sbuf_tensor("t_state", [128, NG], F32))
        psum = [E(nc.psum_tensor(f"ps{i}", [128, SPAN], F32)) for i in range(8)]

        s_drep = [E(nc.semaphore(f"s_drep{i}")) for i in range(2)]
        s_durep = [E(nc.semaphore(f"s_durep{i}")) for i in range(2)]
        s_bdn = [E(nc.semaphore(f"s_bdn{i}")) for i in range(2)]
        s_cdn = [E(nc.semaphore(f"s_cdn{i}")) for i in range(2)]
        s_u = [E(nc.semaphore(f"s_u{i}")) for i in range(2)]
        s_gs = [E(nc.semaphore(f"s_gs{i}")) for i in range(2)]
        s_st = [E(nc.semaphore(f"s_st{i}")) for i in range(2)]
        s_pre = E(nc.semaphore("s_pre"))
        s_dA = E(nc.semaphore("s_dA"))
        s_v = E(nc.semaphore("s_v"))
        s_mm = E(nc.semaphore("s_mm"))
        s_ev = E(nc.semaphore("s_ev"))
        s_sc = E(nc.semaphore("s_sc"))
        s_vA = E(nc.semaphore("s_vA"))
        vch = E(nc.semaphore("vch"))
        block = E(nc.Block())

        @block.sync
        def _(sync):
            sync.dma_start(out=t_wsel[:], in_=wsel).then_inc(s_pre, 16)
            sync.dma_start(out=t_wdiagf[:], in_=wdiag).then_inc(s_pre, 16)
            sync.dma_start(out=t_alog[:], in_=alog_dn).then_inc(s_pre, 16)
            for ch in range(NCHUNK):
                t0 = ch * TC
                sl = ch % 2
                if ch >= 2:
                    sync.wait_ge(s_v, (ch - 1) * 32)
                    sync.wait_ge(s_ev, (ch - 1) * 2)
                    sync.wait_ge(s_mm, (ch - 1) * 32)
                sync.dma_start(out=t_bdn[sl][:], in_=bdn[:, t0:t0 + TC]).then_inc(s_bdn[sl], 16)
                sync.dma_start(out=t_cdn[sl][:], in_=cdn[:, t0:t0 + TC]).then_inc(s_cdn[sl], 16)
                for c_ in range(2):
                    sync.dma_start(out=t_u[sl][:, c_, :], in_=u_cm[c_, :, t0:t0 + TC]).then_inc(s_u[sl], 16)
                    sync.dma_start(out=t_gs[sl][:, c_, :], in_=gs_cm[c_, :, t0:t0 + TC]).then_inc(s_gs[sl], 16)
                for ct in range(2):
                    for g16 in range(16):
                        it = ch * 32 + ct * 16 + g16
                        g = ct * 16 + g16
                        buf = it % 2
                        if it >= 2:
                            sync.wait_ge(s_dA, it - 1)
                            sync.wait_ge(s_v, it - 1)
                        sync.dma_start(out=t_drep[buf][:], in_=drep[g, :, t0:t0 + TC]).then_inc(s_drep[buf], 16)
                        sync.dma_start(out=t_durep[buf][:], in_=durep[g, :, t0:t0 + TC]).then_inc(s_durep[buf], 16)

        @block.scalar
        def _(scalar):
            scalar.wait_ge(s_pre, 48)
            scalar.activation(out=t_expA[:], in_=t_alog[:], func=AF.Exp).then_inc(s_sc, 1)
            scalar.copy(out=t_wdiag[:], in_=t_wdiagf[:]).then_inc(s_sc, 1)
            for it, (ch, ct, g16) in enumerate(iters):
                g = ct * 16 + g16
                buf = it % 2
                if it == 0:
                    scalar.wait_ge(s_vA, 1)
                if g16 == 0 and ch >= 1:
                    ctset_prev = (ch - 1) * 2 + ct
                    scalar.wait_ge(s_ev, ctset_prev + 1)
                    scalar.dma_start(out=y2[ct, :, (ch - 1) * TC:ch * TC],
                                     in_=t_y2[ct][:]).then_inc(s_st[ct], 16)
                scalar.wait_ge(s_drep[buf], 16 * (it // 2 + 1))
                if it >= 2:
                    scalar.wait_ge(s_v, it - 1)
                scalar.activation(out=t_dA[buf][:], in_=t_drep[buf][:], func=AF.Exp,
                                  scale=t_negA[:, g:g + 1]).then_inc(s_dA, 1)
            for ct in range(2):
                ctset = (NCHUNK - 1) * 2 + ct
                scalar.wait_ge(s_ev, ctset + 1)
                scalar.dma_start(out=y2[ct, :, (NCHUNK - 1) * TC:NCHUNK * TC],
                                 in_=t_y2[ct][:]).then_inc(s_st[ct], 16)
            scalar.wait_ge(s_st[0], NCHUNK * 16)
            scalar.wait_ge(s_st[1], NCHUNK * 16)

        @block.vector
        def _(vector):
            vector.wait_ge(s_sc, 1)
            vector.tensor_scalar_mul(out=t_negA[:], in0=t_expA[:], scalar1=-1.0).then_inc(s_vA, 1)
            vchn = [0]

            def vinc(ins):
                vchn[0] += 1
                ins.then_inc(vch, 1)
                return vchn[0]

            for it, (ch, ct, g16) in enumerate(iters):
                g = ct * 16 + g16
                buf = it % 2
                sl = ch % 2
                ctset = ch * 2 + ct
                vector.wait_ge(s_durep[buf], 16 * (it // 2 + 1))
                vector.wait_ge(s_bdn[sl], 16 * (ch // 2 + 1))
                vector.wait_ge(s_dA, it + 1)
                k_dbu = vinc(vector.tensor_tensor(out=t_dBu[buf][:], in0=t_durep[buf][:],
                                                  in1=t_bdn[sl][:], op=ALU.mult))
                vector.wait_ge(vch, k_dbu)
                init = 0.0 if ch % NB_CHUNK == 0 else t_state[:, g:g + 1]
                k_scan = vinc(vector.tensor_tensor_scan(
                    out=t_hs[buf][:], data0=t_dA[buf][:], data1=t_dBu[buf][:],
                    initial=init, op0=ALU.mult, op1=ALU.add))
                vector.wait_ge(vch, k_scan)
                vinc(vector.tensor_copy(out=t_state[:, g:g + 1], in_=t_hs[buf][:, TC - 1:TC]))
                if it >= 2:
                    vector.wait_ge(s_mm, it - 1)
                vector.wait_ge(s_cdn[sl], 16 * (ch // 2 + 1))
                vector.tensor_tensor(out=t_prod[buf][:], in0=t_hs[buf][:],
                                     in1=t_cdn[sl][:], op=ALU.mult).then_inc(s_v, 1)
                if g16 == 15:
                    vector.wait_ge(s_mm, it + 1)
                    if ch >= 1:
                        vector.wait_ge(s_st[ct], 16 * ch)
                    vector.wait_ge(s_gs[sl], 32 * (ch // 2 + 1))
                    pb = (ct % 2) * 4
                    for sp in range(NSPAN):
                        ins = vector.tensor_tensor(
                            out=t_y2[ct][:, sp * SPAN:(sp + 1) * SPAN],
                            in0=psum[pb + sp][:, :],
                            in1=t_gs[sl][:, ct, sp * SPAN:(sp + 1) * SPAN],
                            op=ALU.mult)
                        if sp == NSPAN - 1:
                            ins.then_inc(s_ev, 1)

        @block.tensor
        def _(tensor):
            tensor.wait_ge(s_pre, 48)
            tensor.wait_ge(s_sc, 2)
            for it, (ch, ct, g16) in enumerate(iters):
                buf = it % 2
                sl = ch % 2
                ctset = ch * 2 + ct
                tensor.wait_ge(s_v, it + 1)
                if g16 == 0:
                    if ctset >= 2:
                        tensor.wait_ge(s_ev, ctset - 1)
                    tensor.wait_ge(s_u[sl], 32 * (ch // 2 + 1))
                pb = (ct % 2) * 4
                last = None
                for sp in range(NSPAN):
                    last = tensor.matmul(psum[pb + sp][:, :], t_wsel[:, g16, :],
                                         t_prod[buf][:, sp * SPAN:(sp + 1) * SPAN],
                                         start=(g16 == 0), stop=False)
                if g16 == 15:
                    for sp in range(NSPAN):
                        last = tensor.matmul(psum[pb + sp][:, :], t_wdiag[:, ct, :],
                                             t_u[sl][:, ct, sp * SPAN:(sp + 1) * SPAN],
                                             start=False, stop=(sp == NSPAN - 1))
                last.then_inc(s_mm, 1)
    return nc






T1 = 2048
T3 = 2048


def _engs(g=None, s=None, te=None, sc=None, v=None):
    return {"gpsimd": g, "sync": s, "tensor": te, "scalar": sc, "vector": v}


def build_l1():
    nc = bass.Bass("TRN2", target_bir_lowering=False, debug=False)
    d = {}
    d["xT"] = nc.dram_tensor("xT", [T1 // 512, 8, 128, 515], F32, kind="ExternalInput").ap()
    d["w_inT"] = nc.dram_tensor("w_inT", [1024, 4096], F32, kind="ExternalInput").ap()
    d["conv_w_cm"] = nc.dram_tensor("conv_w_cm", [128, 16, 4], F32, kind="ExternalInput").ap()
    d["conv_b_cm"] = nc.dram_tensor("conv_b_cm", [128, 16], F32, kind="ExternalInput").ap()
    d["w_xT"] = nc.dram_tensor("w_xT", [2048, 96], F32, kind="ExternalInput").ap()
    d["w_dtT"] = nc.dram_tensor("w_dtT", [64, 16, 128], F32, kind="ExternalInput").ap()
    d["dt_bias_cm"] = nc.dram_tensor("dt_bias_cm", [128, 16], F32, kind="ExternalInput").ap()
    NCH1 = T1 // 512
    d["o_u"] = nc.dram_tensor("o_u", [NCH1, 128, 16, 512], BF16, kind="ExternalOutput").ap()
    d["o_du"] = nc.dram_tensor("o_du", [NCH1, 128, 16, 512], BF16, kind="ExternalOutput").ap()
    d["o_dl"] = nc.dram_tensor("o_dl", [NCH1, 128, 16, 512], BF16, kind="ExternalOutput").ap()
    d["o_gs"] = nc.dram_tensor("o_gs", [NCH1, 128, 16, 512], BF16, kind="ExternalOutput").ap()
    d["o_bc"] = nc.dram_tensor("o_bc", [NCH1, 32, 512], BF16, kind="ExternalOutput").ap()
    es = contextlib.ExitStack()
    with es:
        E = es.enter_context
        t = l1_alloc(nc, E, T1)
        sems = {n: E(nc.semaphore("s_" + n)) for n in L1_SEMS}
        events = {}
        l1_streams(Pass(True, None, events), t, d, None, T1)
        block = E(nc.Block())

        @block.gpsimd
        def _(g):
            l1_streams(Pass(False, sems, events, only="gpsimd"), t, d, _engs(g=g), T1)

        @block.sync
        def _(s):
            l1_streams(Pass(False, sems, events, only="sync"), t, d, _engs(s=s), T1)

        @block.tensor
        def _(te):
            l1_streams(Pass(False, sems, events, only="tensor"), t, d, _engs(te=te), T1)

        @block.scalar
        def _(sc):
            l1_streams(Pass(False, sems, events, only="scalar"), t, d, _engs(sc=sc), T1)

        @block.vector
        def _(v):
            l1_streams(Pass(False, sems, events, only="vector"), t, d, _engs(v=v), T1)
    return nc


def build_l3():
    nc = bass.Bass("TRN2", target_bir_lowering=False, debug=False)
    d = {}
    d["xT_s"] = nc.dram_tensor("xT_s", [128, 8, T3], F32, kind="ExternalInput").ap()
    d["y2T"] = nc.dram_tensor("y2T", [128, 16, T3], BF16, kind="ExternalInput").ap()
    d["w_outT_s"] = nc.dram_tensor("w_outT_s", [128, 16, 1024], F32, kind="ExternalInput").ap()
    d["w1T_s"] = nc.dram_tensor("w1T_s", [128, 8, 4096], F32, kind="ExternalInput").ap()
    d["w2T_s"] = nc.dram_tensor("w2T_s", [128, 8, 4096], F32, kind="ExternalInput").ap()
    for nm, c in (("lng", 8), ("lnb", 8), ("ln2g", 32), ("ln2b", 32), ("b1", 32), ("b2", 8)):
        d[nm] = nc.dram_tensor(nm, [128, c], F32, kind="ExternalInput").ap()
    d["ident"] = nc.dram_tensor("ident", [128, 128], BF16, kind="ExternalInput").ap()
    d["identf"] = nc.dram_tensor("identf", [128, 128], F32, kind="ExternalInput").ap()
    d["onesb"] = nc.dram_tensor("onesb", [128, 1], BF16, kind="ExternalInput").ap()
    d["onesf"] = nc.dram_tensor("onesf", [128, 1], F32, kind="ExternalInput").ap()
    d["onesrow"] = nc.dram_tensor("onesrow", [1, 128], F32, kind="ExternalInput").ap()
    d["cst"] = nc.dram_tensor("cst", [1, 8], F32, kind="ExternalInput").ap()
    d["w1q_d"] = nc.dram_tensor("w1q_d", [128, 8, 4096], BF16, kind="Internal").ap()
    d["w2q_d"] = nc.dram_tensor("w2q_d", [128, 8, 4096], BF16, kind="Internal").ap()
    d["outT_s"] = nc.dram_tensor("outT_s", [128, 8, T3], F32, kind="ExternalOutput").ap()
    es = contextlib.ExitStack()
    with es:
        E = es.enter_context
        t = l3_alloc2(nc, E, T3)
        sems = {n: E(nc.semaphore("s_" + n)) for n in L3_SEMS2}
        events = {}
        l3_streams(Pass(True, None, events), t, d, None, T3)
        block = E(nc.Block())

        @block.gpsimd
        def _(g):
            l3_streams(Pass(False, sems, events, only="gpsimd"), t, d, _engs(g=g), T3)

        @block.sync
        def _(s):
            l3_streams(Pass(False, sems, events, only="sync"), t, d, _engs(s=s), T3)

        @block.tensor
        def _(te):
            l3_streams(Pass(False, sems, events, only="tensor"), t, d, _engs(te=te), T3)

        @block.scalar
        def _(sc):
            l3_streams(Pass(False, sems, events, only="scalar"), t, d, _engs(sc=sc), T3)

        @block.vector
        def _(v):
            l3_streams(Pass(False, sems, events, only="vector"), t, d, _engs(v=v), T3)
    return nc


_P = np.arange(128)


def _pack_l1(x, W_in, conv_w, conv_b, W_x, W_dt, dt_bias):
    cw = np.ascontiguousarray(conv_w.reshape(16, 128, 4).transpose(1, 0, 2))
    cb = np.ascontiguousarray(conv_b.reshape(16, 128).T)
    db = np.ascontiguousarray(dt_bias.reshape(16, 128).T)
    w_inT = np.ascontiguousarray(W_in.T)
    w_xT = np.ascontiguousarray(W_x.T)
    wdt = np.ascontiguousarray(W_dt.reshape(16, 128, 64).transpose(2, 0, 1))
    maps = []
    for c in range(8):
        b, q = c // 4, c % 4
        t0 = q * 2048
        xs = np.zeros((2048 + 3, 1024), np.float32)
        lo = max(0, t0 - 3)
        xs[3 - (t0 - lo):] = x[b, lo:t0 + 2048]
        xTf = np.ascontiguousarray(xs.T)
        xc = np.empty((4, 8, 128, 515), np.float32)
        for tc in range(4):
            for kt in range(8):
                xc[tc, kt] = xTf[kt * 128:(kt + 1) * 128, tc * 512:tc * 512 + 515]
        maps.append({"xT": xc, "w_inT": w_inT,
                     "conv_w_cm": cw, "conv_b_cm": cb, "dt_bias_cm": db,
                     "w_xT": w_xT, "w_dtT": wdt})
    return maps


def _pack_l2(r1, A_log, D_skip):
    def full(key):
        return np.concatenate(
            [np.ascontiguousarray(r[key]).transpose(2, 1, 0, 3).reshape(2048, 2048)
             for r in r1], axis=1)

    dl = full("o_dl")
    du = full("o_du")
    uu = full("o_u")
    gs = full("o_gs")
    bc = np.concatenate(
        [np.ascontiguousarray(r["o_bc"]).transpose(1, 0, 2).reshape(32, 2048) for r in r1], axis=1)
    B, C = bc[0:16], bc[16:32]
    n_of_p = _P // 8
    ci_of_p = _P % 8
    wsel = np.zeros((128, 16, 128), bfd)
    for g16 in range(16):
        wsel[_P, g16, g16 * 8 + ci_of_p] = 1
    maps = []
    for k in range(8):
        ch0 = 256 * k
        gmap = ch0 + 8 * np.arange(32)[:, None] + ci_of_p[None, :]
        drep = dl[gmap]
        durep = du[gmap]
        u_cm = uu[ch0:ch0 + 256].reshape(2, 128, TT)
        gs_cm = gs[ch0:ch0 + 256].reshape(2, 128, TT)
        bdn = B[n_of_p]
        cdn = C[n_of_p]
        alog_dn = np.ascontiguousarray(
            A_log[ch0 + 8 * np.arange(32)[None, :] + ci_of_p[:, None], n_of_p[:, None]]
        ).astype(np.float32)
        wdiag = np.zeros((128, 2, 128), np.float32)
        for ct in range(2):
            wdiag[_P, ct, _P] = D_skip[ch0 + ct * 128 + _P]
        maps.append({"drep": np.ascontiguousarray(drep),
                     "durep": np.ascontiguousarray(durep),
                     "u_cm": np.ascontiguousarray(u_cm),
                     "gs_cm": np.ascontiguousarray(gs_cm),
                     "bdn": np.ascontiguousarray(bdn),
                     "cdn": np.ascontiguousarray(cdn),
                     "alog_dn": alog_dn, "wsel": wsel, "wdiag": wdiag})
    return maps


def _pack_l3(x, r2, W_out, w1, b1, w2, b2, lng, lnb, ln2g, ln2b):
    y2_full = np.concatenate(
        [np.ascontiguousarray(r["y2"]).reshape(256, TT) for r in r2], axis=0)
    woT = np.ascontiguousarray(W_out.T.reshape(16, 128, 1024).transpose(1, 0, 2))
    w1T = np.ascontiguousarray(w1.T.reshape(8, 128, 4096).transpose(1, 0, 2))
    w2l = np.ascontiguousarray(w2.T.reshape(32, 128, 1024).transpose(1, 0, 2))
    w2T = np.ascontiguousarray(w2l.reshape(128, 8, 4096))
    small = {
        "lng": np.ascontiguousarray(lng.reshape(8, 128).T),
        "lnb": np.ascontiguousarray(lnb.reshape(8, 128).T),
        "ln2g": np.ascontiguousarray(ln2g.reshape(32, 128).T),
        "ln2b": np.ascontiguousarray(ln2b.reshape(32, 128).T),
        "b1": np.ascontiguousarray(b1.reshape(32, 128).T),
        "b2": np.ascontiguousarray(b2.reshape(8, 128).T),
        "ident": np.eye(128, dtype=bfd),
        "identf": np.eye(128, dtype=np.float32),
        "onesb": np.ones((128, 1), bfd),
        "onesf": np.ones((128, 1), np.float32),
        "onesrow": np.ones((1, 128), np.float32),
        "cst": np.array([[1e-5, 0, 0, 0, 0, 0, 0, 0]], np.float32),
    }
    maps = []
    for c in range(8):
        b, q = c // 4, c % 4
        t0 = q * 2048
        xT = np.ascontiguousarray(x[b, t0:t0 + 2048].T)
        xT_s = np.ascontiguousarray(xT.reshape(8, 128, 2048).transpose(1, 0, 2))
        y2T = np.ascontiguousarray(
            y2_full[:, c * 2048:(c + 1) * 2048].reshape(16, 128, 2048).transpose(1, 0, 2))
        m = {"xT_s": xT_s, "y2T": y2T, "w_outT_s": woT, "w1T_s": w1T, "w2T_s": w2T}
        m.update(small)
        maps.append(m)
    return maps


_NCS = {}


def _get(name, builder):
    if name not in _NCS:
        _NCS[name] = builder()
    return _NCS[name]


def kernel(L0_attn_hidden_states, L0_attn_W_in, L0_attn_conv_w, L0_attn_conv_b,
           L0_attn_W_x, L0_attn_W_dt, L0_attn_dt_bias, L0_attn_A_log, L0_attn_D_skip,
           L0_attn_W_out, L0_ffn_ln1_g, L0_ffn_ln1_b, L0_ffn_w1, L0_ffn_b1,
           L0_ffn_ln2_g, L0_ffn_ln2_b, L0_ffn_w2, L0_ffn_b2):
    x = np.ascontiguousarray(np.asarray(L0_attn_hidden_states, np.float32))
    m1 = _pack_l1(x, np.asarray(L0_attn_W_in, np.float32),
                  np.asarray(L0_attn_conv_w, np.float32),
                  np.asarray(L0_attn_conv_b, np.float32),
                  np.asarray(L0_attn_W_x, np.float32),
                  np.asarray(L0_attn_W_dt, np.float32),
                  np.asarray(L0_attn_dt_bias, np.float32))
    nc1 = _get("l1", build_l1)
    r1 = run_bass_kernel_spmd(nc1, m1, core_ids=list(range(8))).results

    m2 = _pack_l2(r1, np.asarray(L0_attn_A_log, np.float32),
                  np.asarray(L0_attn_D_skip, np.float32))
    nc2 = _get("l2", build_l2)
    r2 = run_bass_kernel_spmd(nc2, m2, core_ids=list(range(8))).results

    m3 = _pack_l3(x, r2, np.asarray(L0_attn_W_out, np.float32),
                  np.asarray(L0_ffn_w1, np.float32), np.asarray(L0_ffn_b1, np.float32),
                  np.asarray(L0_ffn_w2, np.float32), np.asarray(L0_ffn_b2, np.float32),
                  np.asarray(L0_ffn_ln1_g, np.float32), np.asarray(L0_ffn_ln1_b, np.float32),
                  np.asarray(L0_ffn_ln2_g, np.float32), np.asarray(L0_ffn_ln2_b, np.float32))
    nc3 = _get("l3", build_l3)
    r3 = run_bass_kernel_spmd(nc3, m3, core_ids=list(range(8))).results

    out = np.zeros((2, 8192, 1024), np.float32)
    for c in range(8):
        b, q = c // 4, c % 4
        o = np.asarray(r3[c]["outT_s"], np.float32)
        out[b, q * 2048:(q + 1) * 2048] = o.transpose(2, 1, 0).reshape(2048, 1024)
    return out


# version 04
# speedup vs baseline: 1.2285x; 1.0246x over previous
import numpy as np
import contextlib
import ml_dtypes
import concourse.bass as bass
import concourse.bass_utils
from concourse.bass_utils import run_bass_kernel_spmd

bfd = ml_dtypes.bfloat16

import numpy as np
import concourse.mybir as mybir

F32 = mybir.dt.float32
BF16 = mybir.dt.bfloat16
AF = mybir.ActivationFunctionType
ALU = mybir.AluOpType


class Pass:
    def __init__(self, dry, sems, events, only=None):
        self.dry = dry
        self.sems = sems
        self.events = events
        self.count = {}
        self.only = only
        self.current = None

    def begin(self, name):
        self.current = name

    def _active(self):
        return not self.dry and (self.only is None or self.only == self.current)

    def emit(self, fn, inc=None, step=1, event=None):
        if inc is not None:
            self.count[inc] = self.count.get(inc, 0) + step
            if event is not None:
                self.events[(inc, event)] = self.count[inc]
        if self._active():
            ins = fn()
            if inc is not None:
                ins.then_inc(self.sems[inc], step)

    def wait(self, eng, sem, key=None, value=None):
        if not self._active():
            return
        v = value if value is not None else self.events[(sem, key)]
        eng.wait_ge(self.sems[sem], v)

    def cur(self, sem):
        return self.count.get(sem, 0)


L1_SEMS = ["x", "xc", "w", "wg", "wsm", "small", "mm", "sc", "v", "stu", "stg", "stdu", "stdl", "stbc"]


def l1_alloc(nc, E, T1=2048):
    TCH = 512
    t = {}
    t["x"] = [E(nc.sbuf_tensor(f"l1_x{i}", [128, 8, TCH + 3], BF16)) for i in range(2)]
    t["win"] = E(nc.sbuf_tensor("l1_win", [128, 8, 4096], BF16))
    t["wx"] = E(nc.sbuf_tensor("l1_wx", [128, 16, 96], BF16))
    t["wdt"] = E(nc.sbuf_tensor("l1_wdt", [64, 16, 128], BF16))
    t["cw"] = E(nc.sbuf_tensor("l1_cw", [128, 16, 4], F32))
    t["cb"] = E(nc.sbuf_tensor("l1_cb", [128, 16], F32))
    t["db"] = E(nc.sbuf_tensor("l1_db", [128, 16], F32))
    t["tail"] = E(nc.sbuf_tensor("l1_tail", [128, 16, 3], F32))
    t["cv"] = [E(nc.sbuf_tensor(f"l1_cv{i}", [128, TCH], F32)) for i in range(6)]
    t["uc"] = [E(nc.sbuf_tensor(f"l1_uc{i}", [128, 16, TCH], BF16)) for i in range(2)]
    t["duc"] = E(nc.sbuf_tensor("l1_duc", [128, 16, TCH], BF16))
    t["dlc"] = E(nc.sbuf_tensor("l1_dlc", [128, 16, TCH], BF16))
    t["gsc"] = E(nc.sbuf_tensor("l1_gsc", [128, 16, TCH], BF16))
    t["dtlr"] = E(nc.sbuf_tensor("l1_dtlr", [64, TCH], BF16))
    t["bcc"] = E(nc.sbuf_tensor("l1_bcc", [32, TCH], BF16))
    t["ab"] = [E(nc.sbuf_tensor(f"l1_ab{i}", [128, TCH], F32)) for i in range(4)]
    t["ps"] = [E(nc.psum_tensor(f"l1_ps{i}", [128, TCH], F32)) for i in range(4)]
    t["ps_pt"] = E(nc.psum_tensor("l1_ps_pt", [128, TCH], F32))
    t["ps_d"] = [E(nc.psum_tensor(f"l1_ps_d{i}", [128, TCH], F32)) for i in range(2)]
    t["ps_g3"] = E(nc.psum_tensor("l1_ps_g3", [128, TCH], F32))
    return t


def _l1_psd(t, ctile):
    return t["ps_d"][ctile % 2]


def _l1_psg(t, ctile):
    return [t["ps"][2], t["ps"][3], t["ps_g3"]][ctile % 3]


def l1_streams(p, t, d, eng, T1=2048):
    TCH = 512
    NCH = T1 // TCH
    NCT = 16
    NKT = 8
    E = eng or {k: None for k in ("sync", "gpsimd", "tensor", "scalar", "vector")}

    def stream_gpsimd(g):
        def xload(tc):
            if tc >= 2:
                p.wait(g, "mm", ("gate", tc - 2, NCT - 1))
            for kt in range(NKT):
                p.emit(lambda tc=tc, kt=kt: g.dma_start(
                    out=t["x"][tc % 2][:, kt, :], in_=d["xT"][tc, kt]),
                    inc="x" if tc % 2 == 0 else "xc", step=16)
        for kt in range(NKT):
            p.emit(lambda kt=kt: g.dma_start(out=t["win"][:, kt, 0:2048], in_=d["w_inT"][kt * 128:(kt + 1) * 128, 0:2048]),
                   inc="w", step=16)
        xload(0)
        for kt in range(NKT):
            p.emit(lambda kt=kt: g.dma_start(out=t["win"][:, kt, 2048:4096], in_=d["w_inT"][kt * 128:(kt + 1) * 128, 2048:4096]),
                   inc="wg", step=16)
        xload(1)
        for kt in range(NCT):
            p.emit(lambda kt=kt: g.dma_start(out=t["wx"][:, kt, :], in_=d["w_xT"][kt * 128:(kt + 1) * 128, :]),
                   inc="wsm", step=16)
        p.emit(lambda: g.dma_start(out=t["wdt"][:, :, :], in_=d["w_dtT"]), inc="wsm", step=16)
        for tc in range(2, NCH):
            xload(tc)

    def stream_sync(s):
        p.emit(lambda: s.dma_start(out=t["cw"][:], in_=d["conv_w_cm"]), inc="small", step=16)
        p.emit(lambda: s.dma_start(out=t["cb"][:], in_=d["conv_b_cm"]), inc="small", step=16)
        p.emit(lambda: s.dma_start(out=t["db"][:], in_=d["dt_bias_cm"]), inc="small", step=16)

        def qst(name, o, tc, q, sem):
            qs = slice(4 * q, 4 * q + 4)
            p.emit(lambda name=name, o=o, tc=tc, qs=qs: s.dma_start(
                out=d[o][tc, :, qs, :], in_=t[name][:, qs, :]), inc=sem, step=16)

        def dd_quarter(tc, q):
            p.wait(s, "v", ("du", tc, 4 * q + 3))
            p.wait(s, "sc", ("dl", tc, 4 * q + 3))
            qst("duc", "o_du", tc, q, "stdu")
            qst("dlc", "o_dl", tc, q, "stdl")
            if q == 0:
                p.wait(s, "sc", ("bcc", tc))
                p.emit(lambda tc=tc: s.dma_start(out=d["o_bc"][tc], in_=t["bcc"][:, :]),
                       inc="stbc", step=16)

        for tc in range(NCH):
            for q in range(4):
                if tc >= 1:
                    dd_quarter(tc - 1, q)
                p.wait(s, "sc", ("u", tc, 4 * q + 3))
                qs = slice(4 * q, 4 * q + 4)
                p.emit(lambda tc=tc, qs=qs: s.dma_start(
                    out=d["o_u"][tc, :, qs, :], in_=t["uc"][tc % 2][:, qs, :]), inc="stu", step=16)
                p.wait(s, "sc", ("gevac", tc, 4 * q + 3))
                qst("gsc", "o_gs", tc, q, "stg")
        for q in range(4):
            dd_quarter(NCH - 1, q)
        for sem in ("stu", "stg", "stdu", "stdl", "stbc"):
            p.wait(s, sem, value=p.cur(sem))

    def stream_tensor(te):
        p.wait(te, "x", value=NKT * 16)
        p.wait(te, "w", value=NKT * 16)
        xwaits = {0: NKT * 16}
        for ctile in range(NCT):
            if ctile >= 1:
                p.wait(te, "sc", ("tail", ctile - 1))
            for kt in range(NKT):
                p.emit(lambda ctile=ctile, kt=kt: te.matmul(
                    t["ps_pt"][:, 0:3], t["win"][:, kt, ctile * 128:(ctile + 1) * 128],
                    t["x"][0][:, kt, 0:3], start=(kt == 0), stop=(kt == NKT - 1)),
                    inc=None if kt < NKT - 1 else "mm",
                    event=None if kt < NKT - 1 else ("halo", ctile))
        def emit_delta(tcd, ctile):
            if tcd == 0 and ctile == 0:
                p.wait(te, "wsm", value=(NCT + 1) * 16)
            if not (tcd == 0 and ctile < 2):
                prev = (tcd, ctile - 2) if ctile >= 2 else (tcd - 1, NCT - 2 + ctile)
                p.wait(te, "sc", ("devac",) + prev)
            p.emit(lambda tcd=tcd, ctile=ctile: te.matmul(
                _l1_psd(t, ctile)[:, :], t["wdt"][:, ctile, :], t["dtlr"][:, :],
                start=True, stop=True),
                inc="mm", event=("delta", tcd, ctile))

        def emit_projt_kt(tc, kt):
            if kt == 0:
                if tc == 0:
                    p.wait(te, "wsm", value=(NCT + 1) * 16)
                    p.wait(te, "sc", ("tail", NCT - 1))
                else:
                    p.wait(te, "sc", ("ptevac", tc - 1))
            p.wait(te, "sc", ("u", tc, kt))
            p.emit(lambda tc=tc, kt=kt: te.matmul(
                t["ps_pt"][0:96, :], t["wx"][:, kt, :], t["uc"][tc % 2][:, kt, :],
                start=(kt == 0), stop=(kt == NCT - 1)),
                inc=None if kt < NCT - 1 else "mm",
                event=None if kt < NCT - 1 else ("projt", tc))

        for tc in range(NCH):
            xb = t["x"][tc % 2]
            p.wait(te, "x" if tc % 2 == 0 else "xc", value=(tc // 2 + 1) * NKT * 16)
            for ctile in range(NCT):
                if not (tc == 0 and ctile < 2):
                    prev = (tc, ctile - 2) if ctile >= 2 else (tc - 1, NCT - 2 + ctile)
                    p.wait(te, "v", ("cpdone",) + prev)
                for kt in range(NKT):
                    p.emit(lambda xb=xb, tc=tc, ctile=ctile, kt=kt: te.matmul(
                        t["ps"][ctile % 2][:, :], t["win"][:, kt, ctile * 128:(ctile + 1) * 128],
                        xb[:, kt, 3:3 + TCH], start=(kt == 0), stop=(kt == NKT - 1)),
                        inc=None if kt < NKT - 1 else "mm",
                        event=None if kt < NKT - 1 else ("proj", tc, ctile))
                if tc >= 1:
                    emit_delta(tc - 1, ctile)
                j = 2048 + ctile * 128
                if tc == 0 and ctile == 0:
                    p.wait(te, "wg", value=NKT * 16)
                if not (tc == 0 and ctile < 3):
                    prev = (tc, ctile - 3) if ctile >= 3 else (tc - 1, NCT - 3 + ctile)
                    p.wait(te, "sc", ("gevac",) + prev)
                for kt in range(NKT):
                    p.emit(lambda xb=xb, tc=tc, ctile=ctile, kt=kt, j=j: te.matmul(
                        _l1_psg(t, ctile)[:, :], t["win"][:, kt, j:j + 128],
                        xb[:, kt, 3:3 + TCH], start=(kt == 0), stop=(kt == NKT - 1)),
                        inc=None if kt < NKT - 1 else "mm",
                        event=None if kt < NKT - 1 else ("gate", tc, ctile))
                if ctile >= 4:
                    emit_projt_kt(tc, ctile - 4)
            for kt in range(NCT - 4, NCT):
                emit_projt_kt(tc, kt)
            p.wait(te, "sc", ("dtlr", tc))
        for ctile in range(NCT):
            emit_delta(NCH - 1, ctile)

    def stream_scalar(sc):
        p.wait(sc, "small", value=48)
        for ctile in range(NCT):
            p.wait(sc, "mm", ("halo", ctile))
            p.emit(lambda ctile=ctile: sc.copy(out=t["tail"][:, ctile, :], in_=t["ps_pt"][:, 0:3]),
                   inc="sc", event=("tail", ctile))
        def emit_sp_exp(sc, tcd, ctile):
            ab = t["ab"][ctile % 4]
            p.wait(sc, "mm", ("delta", tcd, ctile))
            if ctile == 0 and tcd >= 1:
                p.wait(sc, "stdl", value=64 * tcd)
            p.emit(lambda ab=ab, ctile=ctile: sc.activation(
                out=ab[:], in_=_l1_psd(t, ctile)[:, :], func=AF.Exp,
                bias=t["db"][:, ctile:ctile + 1]),
                inc="sc", event=("devac", tcd, ctile))

        def emit_sp_ln(sc, tcd, ctile):
            ab = t["ab"][ctile % 4]
            p.wait(sc, "sc", value=p.cur("sc"))
            p.emit(lambda ab=ab, ctile=ctile: sc.activation(
                out=t["dlc"][:, ctile, :], in_=ab[:], func=AF.Ln, bias=1.0),
                inc="sc", event=("dl", tcd, ctile))

        def emit_softplus(sc, tcd, ctile):
            emit_sp_exp(sc, tcd, ctile)
            emit_sp_ln(sc, tcd, ctile)

        for tc in range(NCH):
            def emit_u(tc, ctile):
                p.wait(sc, "v", ("cv", tc, ctile))
                if tc >= 2 and ctile == 0:
                    p.wait(sc, "v", ("duall", tc - 2))
                    p.wait(sc, "stu", value=64 * (tc - 1))
                p.emit(lambda tc=tc, ctile=ctile: sc.activation(
                    out=t["uc"][tc % 2][:, ctile, :], in_=t["cv"][ctile % 6][:], func=AF.Silu,
                    bias=t["cb"][:, ctile:ctile + 1]),
                    inc="sc", event=("u", tc, ctile))

            def emit_gevac(tc, ctile):
                p.wait(sc, "mm", ("gate", tc, ctile))
                if tc >= 1 and ctile == 0:
                    p.wait(sc, "stg", value=64 * tc)
                p.emit(lambda tc=tc, ctile=ctile: sc.activation(
                    out=t["gsc"][:, ctile, :], in_=_l1_psg(t, ctile)[:, :], func=AF.Silu),
                    inc="sc", event=("gevac", tc, ctile))

            for g in range(NCT // 2):
                c0, c1 = 2 * g, 2 * g + 1
                if g >= 1:
                    emit_u(tc, c0 - 2)
                    emit_u(tc, c1 - 2)
                    emit_gevac(tc, c0 - 2)
                    emit_gevac(tc, c1 - 2)
                if tc >= 1:
                    emit_sp_exp(sc, tc - 1, c0)
                    emit_sp_exp(sc, tc - 1, c1)
                    emit_sp_ln(sc, tc - 1, c0)
                    emit_sp_ln(sc, tc - 1, c1)
            emit_u(tc, NCT - 2)
            emit_u(tc, NCT - 1)
            p.events[("sc", ("uc", tc))] = p.cur("sc")
            emit_gevac(tc, NCT - 2)
            emit_gevac(tc, NCT - 1)
            p.events[("sc", ("gs", tc, NCT - 1))] = p.cur("sc")
            p.wait(sc, "mm", ("projt", tc))
            if tc >= 1:
                p.wait(sc, "mm", ("delta", tc - 1, NCT - 1))
            p.emit(lambda: sc.copy(out=t["dtlr"][:, :], in_=t["ps_pt"][0:64, :]),
                   inc="sc", event=("dtlr", tc))
            if tc >= 1:
                p.wait(sc, "stbc", value=16 * tc)
            p.emit(lambda: sc.copy(out=t["bcc"][:, :], in_=t["ps_pt"][64:96, :]),
                   inc="sc", event=("ptevac", tc))
            p.events[("sc", ("bcc", tc))] = p.cur("sc")
        for ctile in range(NCT):
            emit_softplus(sc, NCH - 1, ctile)

    def stream_vector(v):
        def emit_dldu(v, tcd, ctile):
            p.wait(v, "sc", ("dl", tcd, ctile))
            p.wait(v, "sc", ("u", tcd, ctile))
            if ctile == 0 and tcd >= 1:
                p.wait(v, "stdu", value=64 * tcd)
            p.emit(lambda tcd=tcd, ctile=ctile: v.tensor_tensor(
                out=t["duc"][:, ctile, :], in0=t["dlc"][:, ctile, :],
                in1=t["uc"][tcd % 2][:, ctile, :], op=ALU.mult),
                inc="v", event=("du", tcd, ctile))
            if ctile == NCT - 1:
                p.events[("v", ("duall", tcd))] = p.cur("v")

        for tc in range(NCH):
            for ctile in range(NCT):
                cv = t["cv"][ctile % 6]
                ps = t["ps"][ctile % 2]
                p.wait(v, "mm", ("proj", tc, ctile))
                if tc == 0:
                    p.wait(v, "sc", ("tail", ctile))
                if not (tc == 0 and ctile < 6):
                    prev = (tc, ctile - 6) if ctile >= 6 else (tc - 1, NCT - 6 + ctile)
                    p.wait(v, "sc", ("u",) + prev)
                p.emit(lambda cv=cv, ps=ps, ctile=ctile: v.tensor_scalar(
                    out=cv[:], in0=ps[:, 0:TCH], scalar1=t["cw"][:, ctile, 3:4],
                    scalar2=None, op0=ALU.mult), inc="v")
                for k in (2, 1, 0):
                    p.wait(v, "v", value=p.cur("v"))
                    p.emit(lambda cv=cv, ps=ps, ctile=ctile, k=k: v.scalar_tensor_tensor(
                        out=cv[:, 3 - k:TCH], in0=ps[:, 0:TCH - 3 + k],
                        scalar=t["cw"][:, ctile, k:k + 1],
                        in1=cv[:, 3 - k:TCH], op0=ALU.mult, op1=ALU.add), inc="v")
                    p.wait(v, "v", value=p.cur("v"))
                    p.emit(lambda cv=cv, ctile=ctile, k=k: v.scalar_tensor_tensor(
                        out=cv[:, 0:3 - k], in0=t["tail"][:, ctile, k:3],
                        scalar=t["cw"][:, ctile, k:k + 1],
                        in1=cv[:, 0:3 - k], op0=ALU.mult, op1=ALU.add),
                        inc="v", event=None if k > 0 else ("cv", tc, ctile))
                if tc < NCH - 1:
                    p.wait(v, "v", value=p.cur("v"))
                    p.emit(lambda ps=ps, ctile=ctile: v.tensor_copy(
                        out=t["tail"][:, ctile, :], in_=ps[:, TCH - 3:TCH]),
                        inc="v")
                p.events[("v", ("cpdone", tc, ctile))] = p.cur("v")
                if tc >= 1:
                    emit_dldu(v, tc - 1, ctile)
        for ctile in range(NCT):
            emit_dldu(v, NCH - 1, ctile)

    p.begin("gpsimd"); stream_gpsimd(E["gpsimd"])
    p.begin("sync"); stream_sync(E["sync"])
    p.begin("tensor"); stream_tensor(E["tensor"])
    p.begin("scalar"); stream_scalar(E["scalar"])
    p.begin("vector"); stream_vector(E["vector"])


L3_SEMS = ["ld", "w3", "mm3", "sc3", "v3", "st3", "q3"]


def l3_alloc(nc, E, T3=2048):
    TCH = 512
    t = {}
    t["wbuf"] = E(nc.sbuf_tensor("l3_wbuf", [128, 32, 1024], BF16))
    t["wq"] = E(nc.sbuf_tensor("l3_wq", [128, 4096], BF16))
    t["wf"] = [E(nc.sbuf_tensor(f"l3_wf{i}", [128, 4096], F32)) for i in range(2)]
    t["x"] = E(nc.sbuf_tensor("l3_x", [128, 8, TCH], BF16))
    t["y2"] = E(nc.sbuf_tensor("l3_y2", [128, 16, TCH], BF16))
    t["x1"] = E(nc.sbuf_tensor("l3_x1", [128, 8, TCH], BF16))
    t["rbf"] = E(nc.sbuf_tensor("l3_rbf", [128, 8, TCH], BF16))
    t["q1"] = E(nc.sbuf_tensor("l3_q1", [128, 8, TCH], BF16))
    t["g1"] = E(nc.sbuf_tensor("l3_g1", [128, 32, TCH], BF16))
    t["sq"] = E(nc.sbuf_tensor("l3_sq", [128, TCH], BF16))
    t["v"] = [E(nc.sbuf_tensor(f"l3_v{i}", [128, TCH], F32)) for i in range(2)]
    t["sg"] = [E(nc.sbuf_tensor(f"l3_sg{i}", [128, TCH], BF16)) for i in range(2)]
    t["qiB"] = E(nc.sbuf_tensor("l3_qiB", [128, 4, TCH], mybir.dt.int32))
    t["mskB"] = E(nc.sbuf_tensor("l3_mskB", [128, 4, TCH], BF16))
    t["qi"] = [E(nc.sbuf_tensor(f"l3_qi{i}", [128, TCH], mybir.dt.int32)) for i in range(2)]
    t["r3"] = E(nc.sbuf_tensor("l3_r3", [128, 8, TCH], F32))
    t["out"] = E(nc.sbuf_tensor("l3_out", [128, 8, TCH], F32))
    t["stat"] = E(nc.sbuf_tensor("l3_stat", [1, TCH * 8], F32))
    t["smax"] = E(nc.sbuf_tensor("l3_smax", [128, 8], F32))
    t["sflat"] = E(nc.sbuf_tensor("l3_sflat", [4, 128 * 4], F32))
    t["sbc"] = E(nc.sbuf_tensor("l3_sbc", [128, TCH], BF16))
    t["rbc"] = E(nc.sbuf_tensor("l3_rbc", [128, TCH], BF16))
    t["ones"] = E(nc.sbuf_tensor("l3_ones", [128, 2], BF16))
    t["onesf"] = E(nc.sbuf_tensor("l3_onesf", [128, 2], F32))
    t["lng"] = E(nc.sbuf_tensor("l3_lng", [128, 8], F32))
    t["lnb"] = E(nc.sbuf_tensor("l3_lnb", [128, 8], F32))
    t["ln2g"] = E(nc.sbuf_tensor("l3_ln2g", [128, 32], F32))
    t["ln2b"] = E(nc.sbuf_tensor("l3_ln2b", [128, 32], F32))
    t["b1"] = E(nc.sbuf_tensor("l3_b1", [128, 32], F32))
    t["b2"] = E(nc.sbuf_tensor("l3_b2", [128, 8], F32))
    t["swc"] = E(nc.sbuf_tensor("l3_swc", [128, 4], F32))
    t["red"] = E(nc.sbuf_tensor("l3_red", [128, 8], F32))
    t["scl"] = E(nc.sbuf_tensor("l3_scl", [1, 64], F32))
    t["cst"] = E(nc.sbuf_tensor("l3_cst", [1, 8], F32))
    t["ps"] = [E(nc.psum_tensor(f"l3_ps{i}", [128, TCH], F32)) for i in range(8)]
    t["pss"] = E(nc.psum_tensor("l3_pss", [1, 2 * TCH], F32))
    t["psb"] = [E(nc.psum_tensor(f"l3_psb{i}", [128, TCH], F32)) for i in range(0)]
    return t



def rep0(ap, n):
    return type(ap)(tensor=ap.tensor, offset=ap.offset,
                    ap=[[ap.ap[0][0], ap.ap[0][1]], [0, n], [list(ap.ap[1])[0], list(ap.ap[1])[1]]])


def l3_streams(p, t, d, eng, T3=2048):
    TCH = 512
    NCH = T3 // TCH
    E = eng or {}
    ps = t["ps"]
    W = t["wbuf"]
    NJT, NHT = 8, 32
    g1f = t["g1"].ap().rearrange("p a b -> p (a b)")
    g1w = t["g1"].ap().rearrange("p a b -> p (a b)").bitcast(F32)
    wqa = [lambda: g1w[:, 0:4096], lambda: g1w[:, 4096:8192]]
    _q1f = lambda: t["q1"].ap().rearrange("p a b -> p (a b)")
    _rbff = lambda: t["rbf"].ap().rearrange("p a b -> p (a b)")
    qc1s = [_q1f, _rbff]
    qc2 = lambda: t["x1"][0].ap().rearrange("p a b -> p (a b)")
    LN_EPS = 1e-5
    w1slot = W.ap()[:, 16:32, :].rearrange("p a b -> p (a b)").rearrange("p (c e) -> p c e", c=8)
    w2slot = W.ap()[:, 16:32, :]

    def xs(tc):
        return t["x"][:, :, :]

    def y2s(tc):
        return t["y2"][:, :, :]

    def stream_gpsimd(g):
        for i in range(32):
            rnd, wi, kt = i // 16, (i % 16) // 8, i % 8
            wsrc = "w1T_s" if wi == 0 else "w2T_s"
            if i >= 2:
                p.wait(g, "v3", ("wfree", i - 2))
            p.emit(lambda wsrc=wsrc, kt=kt, i=i: g.dma_start(
                out=wqa[i % 2](), in_=d[wsrc][:, kt, :]),
                inc="q%d" % (i % 2), step=16, event=("wcast", i))
        p.emit(lambda: g.dma_start(out=W[:, 0:16, :], in_=d["w_outT_s"]),
               inc="qo", step=16, event=("wout",))
        def outst(tc):
            p.wait(g, "v3", ("outj", tc, NJT - 1))
            for jt in range(NJT):
                p.emit(lambda tc=tc, jt=jt: g.dma_start(
                    out=d["outT_s"][:, jt, tc * TCH:(tc + 1) * TCH], in_=t["r3"][:, jt, :]),
                    inc="st3", step=16,
                    event=("outst", tc, jt) if jt == NJT - 1 else None)

        for tc in range(NCH):
            if tc >= 1:
                p.wait(g, "v3", ("Aev", tc - 1, NJT - 1))
            p.emit(lambda tc=tc: g.dma_start(out=xs(tc), in_=d["xT_s"][:, :, tc * TCH:(tc + 1) * TCH]),
                   inc="qx", step=16, event=("xc", tc))
            if tc >= 2:
                outst(tc - 2)
        if NCH >= 2:
            outst(NCH - 2)
        outst(NCH - 1)
        p.wait(g, "st3", value=p.cur("st3"))

    def stream_sync(s):
        for nm in ("lng", "lnb", "ln2g", "ln2b", "b1", "b2", "identf", "onesb", "onesf", "onesrow", "cst"):
            p.emit(lambda nm=nm: s.dma_start(out=t[nm][:], in_=d[nm]), inc="ls", step=16)
        for i in range(16):
            wi, kt = i // 8, i % 8
            dst = "w1q_d" if wi == 0 else "w2q_d"
            qc1 = qc1s[i % 2]
            p.wait(s, "v3", ("wq", i + 16))
            p.emit(lambda dst=dst, kt=kt, qc1=qc1: s.dma_start(out=d[dst][:, kt, :], in_=qc1()),
                   inc="sw%d" % (i % 2), step=16, event=("wqst", i))
        p.emit(lambda: s.dma_start(out=y2s(0), in_=d["y2T"][:, :, 0:TCH]),
               inc="ly", step=16, event=("y2c", 0))
        def sxd(tcx, which, flev, rowev):
            p.wait(s, "v3", flev)
            p.wait(s, "mm3", rowev)
            p.emit(lambda: s.dma_start(out=t["stat"][0:1, 0:2 * TCH].rearrange("p (a b) -> p a b", a=8),
                                       in_=t["sflat"][0:8, 0:128]),
                   inc="sx", step=16, event=("sxd", tcx, which))

        sxd(0, 1, ("sfl", 0, 1), ("m20b", 0))
        for tc in range(NCH):
            if tc + 1 < NCH:
                p.wait(s, "mm3", ("Adone", tc))
                p.emit(lambda tc=tc: s.dma_start(out=y2s(tc + 1),
                                                 in_=d["y2T"][:, :, (tc + 1) * TCH:(tc + 2) * TCH]),
                       inc="ly", step=16, event=("y2c", tc + 1))
            for h in range(2):
                p.wait(s, "sw0", value=8 * 16)
                p.wait(s, "sw1", value=8 * 16)
                if not (tc == 0 and h == 0):
                    p.wait(s, "mm3", ("slotfree", tc, h))
                p.emit(lambda h=h: s.dma_start(out=w1slot, in_=d["w1q_d"][:, :, h * 2048:(h + 1) * 2048]),
                       inc="lw", step=16, event=("wslot", tc, h))
            if tc + 1 < NCH:
                sxd(tc + 1, 1, ("sfl", tc + 1, 1), ("m20b", tc + 1))
            sxd(tc, 2, ("sfl", tc, 2), ("m22b", tc))
            for h in range(2, 4):
                p.wait(s, "mm3", ("slotfree", tc, h))
                p.emit(lambda h=h: s.dma_start(out=w2slot, in_=d["w2q_d"].rearrange(
                    "p a b -> p (a b)").rearrange("p (c e) -> p c e", c=32)[:, (h - 2) * 16:(h - 1) * 16, :]),
                    inc="lw", step=16, event=("wslot", tc, h))
        pass

    def bc(te_, row_fn, out_ps, ev=None):
        p.emit(lambda: te_.matmul(out_ps, t["onesrow"][0:1, :], row_fn(), start=True, stop=True),
               inc="mm3", event=ev)

    S = TCH
    x1b = lambda tcn: t["x1"][tcn % 2]
    r3i = lambda: t["r3"].ap().rearrange("p a b -> p (a b)").bitcast(mybir.dt.int32).rearrange(
        "p (c e) -> p c e", c=8)
    r3h = lambda: t["r3"].ap().rearrange("p a b -> p (a b)").bitcast(BF16).rearrange(
        "p (c e) -> p c e", c=16)
    qi1 = lambda jt: r3i()[:, jt % 2, :]
    qiB = lambda: r3i()[:, 0:4, :]
    vbF = lambda b: r3h()[:, 8 + 4 * (b % 2):12 + 4 * (b % 2), :]

    def cbank(jt2):
        return jt2 % 2 if jt2 < 16 else (0, 1, 4, 5)[jt2 % 4]

    def stream_tensor(te_):
        p.wait(te_, "ls", value=11 * 16)
        for wi in range(2):
            p.wait(te_, "v3", ("wred", wi * 8 + 7))
            p.emit(lambda wi=wi: te_.matmul(ps[6][0:1, wi * 8:wi * 8 + 8], t["onesf"][:, 0:1],
                                            t["red"][:, wi * 8:wi * 8 + 8], start=True, stop=True),
                   inc="mm3", event=("wtot", wi))
        p.wait(te_, "v3", ("thr",))
        bc(te_, lambda: t["scl"][0:1, 0:6], ps[7][:, 0:6], ev=("swcb",))

        def emit_A_jt(tcn, jt):
            if jt == 0:
                p.wait(te_, "qo", ("wout",))
                p.wait(te_, "ly", ("y2c", tcn))
                if tcn >= 2:
                    p.wait(te_, "v3", ("dv3", tcn - 2))
            if jt >= 2:
                p.wait(te_, "v3", ("Aev", tcn, jt - 2))
            for kt in range(16):
                p.emit(lambda tcn=tcn, jt=jt, kt=kt: te_.matmul(
                    ps[4 + jt % 2][:, :], W[:, kt, jt * 128:(jt + 1) * 128],
                    y2s(tcn)[:, kt, :], start=(kt == 0), stop=(kt == 15)),
                    inc=None if kt < 15 else "mm3",
                    event=None if kt < 15 else ("A", tcn, jt))
            if jt == NJT - 1:
                p.events[("mm3", ("Adone", tcn))] = p.cur("mm3")

        def emit_Bsum(tcn, jt):
            p.wait(te_, "v3", ("Aev", tcn, jt))
            p.wait(te_, "sc3", ("sq0", tcn, jt))
            if jt == 0:
                p.wait(te_, "v3", ("swcev",))
                if tcn >= 2:
                    p.wait(te_, "v3", ("var3", tcn - 2))
            p.emit(lambda tcn=tcn, jt=jt: te_.matmul(ps[6][0:1, :], t["onesb"][:, 0:1],
                                                     t["rbf"][:, jt, :], start=(jt == 0), stop=(jt == 7)),
                   inc="mm3")
            p.emit(lambda tcn=tcn, jt=jt: te_.matmul(ps[7][0:1, :], t["onesb"][:, 0:1],
                                                     t["sqA"][:, 0, :], start=(jt == 0), stop=(jt == 7)),
                   inc="mm3", event=("Bsum", tcn, jt))

        def emit_front_mid(tcn):
            p.wait(te_, "v3", ("stat0", tcn))
            bc(te_, lambda: t["stat"][0:1, 0:S], ps[6][:, :], ev=("rs0b", tcn))
            bc(te_, lambda: t["stat"][0:1, S:2 * S], ps[7][:, :], ev=("m20b", tcn))

        def emit_front_mxT(tcn):
            p.wait(te_, "v3", ("mx1", tcn))
            p.wait(te_, "v3", ("x1", tcn, NJT - 1))
            for tb in range(4):
                p.emit(lambda tb=tb: te_.transpose(ps[6][:, tb * 128:(tb + 1) * 128],
                                                   t["v"][0][:, tb * 128:(tb + 1) * 128], t["identf"][:]),
                       inc=None if tb < 3 else "mm3", event=None if tb < 3 else ("mxT1", tcn))

        def emit_front_sT(tcn):
            p.wait(te_, "v3", ("s1c", tcn))
            p.emit(lambda: te_.transpose(ps[7][0:8, 0:128], t["smax"][:, 0:8], t["identf"][:]),
                   inc="mm3", event=("sT1", tcn))

        def emit_front_sb(tcn):
            p.wait(te_, "sx", ("sxd", tcn, 1))
            bc(te_, lambda: t["stat"][0:1, 0:S], ps[6][:, :], ev=("sb1", tcn))
            bc(te_, lambda: t["stat"][0:1, S:2 * S], ps[7][:, :], ev=("rb1", tcn))

        for jt in range(2):
            emit_A_jt(0, jt)
        for jt in range(2, NJT):
            emit_A_jt(0, jt)
            emit_Bsum(0, jt - 2)
        emit_Bsum(0, 6)
        emit_Bsum(0, 7)
        emit_front_mid(0)
        emit_front_mxT(0)
        emit_front_sT(0)
        emit_front_sb(0)

        for tc in range(NCH):
            nxt = tc + 1 < NCH

            def dsum(j):
                p.wait(te_, "sc3", ("gelu", tc, j))
                p.wait(te_, "sc3", ("sq2", tc, j))
                if j == 0 and tc >= 1:
                    p.wait(te_, "v3", ("gev", tc - 1, 3))
                p.emit(lambda j=j: te_.matmul(ps[2][0:1, :], t["onesb"][:, 0:1],
                                              t["g1"][:, j, :], start=(j == 0), stop=(j == 31)),
                       inc="mm3")
                p.emit(lambda j=j: te_.matmul(ps[3][0:1, :], t["onesb"][:, 0:1],
                                              t["sq"][:, j % 2, :], start=(j == 0), stop=(j == 31)),
                       inc="mm3", event=("Dsum", tc, j))

            p.wait(te_, "lw", ("wslot", tc, 0))
            p.wait(te_, "v3", ("q1", tc))
            for jt2 in range(NHT):
                if jt2 == 16:
                    p.wait(te_, "lw", ("wslot", tc, 1))
                if (tc, jt2) not in ((0, 0), (0, 1)):
                    if jt2 < 2:
                        p.wait(te_, "v3", ("gev", tc - 1, jt2))
                    elif jt2 < 18:
                        p.wait(te_, "sc3", ("gelu", tc, jt2 - 2))
                    else:
                        p.wait(te_, "sc3", ("gelu", tc, jt2 - 4))
                if jt2 == 16:
                    if nxt:
                        p.wait(te_, "v3", ("Aev", tc + 1, NJT - 1))
                    elif tc >= 1:
                        p.wait(te_, "v3", ("dv3", tc - 1))
                for kt in range(NJT):
                    p.emit(lambda jt2=jt2, kt=kt: te_.matmul(
                        ps[cbank(jt2)][:, :], w1slot[:, kt, (jt2 % 16) * 128:(jt2 % 16) * 128 + 128],
                        t["q1"][:, kt, :], start=(kt == 0), stop=(kt == 7)),
                        inc=None if kt < 7 else "mm3",
                        event=None if kt < 7 else ("C", tc, jt2))
                if jt2 == 15:
                    p.events[("mm3", ("slotfree", tc, 1))] = p.cur("mm3")
                if jt2 == 31:
                    p.events[("mm3", ("slotfree", tc, 2))] = p.cur("mm3")
                if jt2 >= 4:
                    dsum(jt2 - 4)
                if nxt:
                    if jt2 % 2 == 1 and jt2 // 2 < NJT:
                        emit_A_jt(tc + 1, jt2 // 2)
                    if jt2 % 2 == 0 and 4 <= jt2 <= 18:
                        emit_Bsum(tc + 1, (jt2 - 4) // 2)
                    if jt2 == 21:
                        emit_front_mid(tc + 1)
                    if jt2 == 26:
                        emit_front_mxT(tc + 1)
                    if jt2 == 28:
                        emit_front_sT(tc + 1)
                    if jt2 == 30:
                        emit_front_sb(tc + 1)
            for j in range(NHT - 4, NHT):
                dsum(j)
            p.wait(te_, "v3", ("stat2", tc))
            if nxt:
                p.wait(te_, "v3", ("Aev", tc + 1, NJT - 1))
            bc(te_, lambda: t["stat"][0:1, 0:S], ps[4][:, :], ev=("rs2b", tc))
            bc(te_, lambda: t["stat"][0:1, S:2 * S], ps[5][:, :], ev=("m22b", tc))
            p.wait(te_, "v3", ("mx2", tc))
            p.wait(te_, "v3", ("dv0", tc))
            for tb in range(4):
                p.emit(lambda tb=tb: te_.transpose(ps[4][:, tb * 128:(tb + 1) * 128],
                                                   t["v"][0][:, tb * 128:(tb + 1) * 128], t["identf"][:]),
                       inc=None if tb < 3 else "mm3", event=None if tb < 3 else ("mxT2", tc))
            p.wait(te_, "v3", ("s2c", tc))
            p.wait(te_, "v3", ("dv1", tc))
            p.emit(lambda: te_.transpose(ps[5][0:8, 0:128], t["smax"][:, 0:8], t["identf"][:]),
                   inc="mm3", event=("sT2", tc))
            p.wait(te_, "sx", ("sxd", tc, 2))
            bc(te_, lambda: t["stat"][0:1, 0:S], ps[4][:, :], ev=("sb2", tc))
            bc(te_, lambda: t["stat"][0:1, S:2 * S], ps[5][:, :], ev=("rb2", tc))
            p.wait(te_, "lw", ("wslot", tc, 2))
            p.wait(te_, "v3", ("sbc2ev", tc))
            p.wait(te_, "sc3", ("gelu", tc, NHT - 1))
            if nxt:
                p.wait(te_, "v3", ("sbc1ev", tc + 1))
            for kt2 in range(NHT):
                if kt2 == 16:
                    p.wait(te_, "lw", ("wslot", tc, 3))
                if kt2 % 4 == 0:
                    p.wait(te_, "v3", ("q2b", tc, kt2 // 4))
                half = ((kt2 // 4) % 2) * 4
                for jt in range(NJT):
                    p.emit(lambda kt2=kt2, jt=jt, half=half: te_.matmul(
                        ps[jt][:, :], w2slot[:, kt2 % 16, jt * 128:(jt + 1) * 128],
                        t["q2"][:, half + kt2 % 4, :], start=(kt2 == 0), stop=(kt2 == 31)),
                        inc=None if jt < 7 else "mm3",
                        event=None if jt < 7 else ("F", tc, kt2))
                if kt2 == 15:
                    p.events[("mm3", ("slotfree", tc, 3))] = p.cur("mm3")
                if kt2 == 31:
                    p.events[("mm3", ("slotfree", tc + 1, 0))] = p.cur("mm3")
            p.wait(te_, "v3", ("gev", tc, NJT - 1))
            for jt in range(NJT):
                p.wait(te_, "sc3", ("r3bf", tc, jt))
                p.wait(te_, "sc3", ("sq3", tc, jt))
                p.emit(lambda jt=jt: te_.matmul(ps[6][0:1, :], t["onesb"][:, 0:1],
                                                t["rbf"][:, jt, :], start=(jt == 0), stop=(jt == 7)),
                       inc="mm3")
                p.emit(lambda jt=jt: te_.matmul(ps[7][0:1, :], t["onesb"][:, 0:1],
                                                t["sq"][:, jt % 2, :], start=(jt == 0), stop=(jt == 7)),
                       inc="mm3", event=("Gsum", tc, jt))
            p.wait(te_, "v3", ("stat3", tc))
            bc(te_, lambda: t["stat"][0:1, 0:S], ps[4][:, :], ev=("rs3b", tc))
            bc(te_, lambda: t["stat"][0:1, S:2 * S], ps[5][:, :], ev=("m23b", tc))

    def stream_scalar(sc_):
        p.wait(sc_, "ls", value=11 * 16)

        def sq0(tcn, jt):
            p.wait(sc_, "v3", ("Aev", tcn, jt))
            if jt >= 1:
                p.wait(sc_, "mm3", ("Bsum", tcn, jt - 1))
            p.emit(lambda tcn=tcn, jt=jt: sc_.activation(out=t["sqA"][:, 0, :],
                                                         in_=t["rbf"][:, jt, :], func=AF.Square),
                   inc="sc3", event=("sq0", tcn, jt))

        def sd(tcn, varev, ev, rowfree):
            p.wait(sc_, "v3", varev)
            p.emit(lambda: sc_.activation(out=t["stat"][0:1, 2 * S:3 * S],
                                          in_=t["stat"][0:1, 4 * S:5 * S],
                                          func=AF.Sqrt, bias=t["cst"][0:1, 0:1]),
                   inc="sc3", event=ev)

        for jt in range(NJT):
            sq0(0, jt)
        sd(0, ("var0", 0), ("sd0", 0), None)

        for tc in range(NCH):
            nxt = tc + 1 < NCH
            for jt2 in range(NHT):
                p.wait(sc_, "mm3", ("C", tc, jt2))
                p.emit(lambda jt2=jt2: sc_.activation(
                    out=t["g1"][:, jt2, :], in_=ps[cbank(jt2)][:, :], func=AF.Gelu,
                    bias=t["b1"][:, jt2:jt2 + 1]),
                    inc="sc3", event=("gelu", tc, jt2))
                if jt2 >= 2:
                    p.wait(sc_, "mm3", ("Dsum", tc, jt2 - 2))
                p.wait(sc_, "sc3", value=p.cur("sc3"))
                p.emit(lambda jt2=jt2: sc_.activation(out=t["sq"][:, jt2 % 2, :],
                                                      in_=t["g1"][:, jt2, :], func=AF.Square),
                       inc="sc3", event=("sq2", tc, jt2))
                if nxt:
                    if jt2 % 2 == 1 and jt2 // 2 < NJT:
                        sq0(tc + 1, jt2 // 2)
                    if jt2 == 19:
                        sd(tc + 1, ("var0", tc + 1), ("sd0", tc + 1),
                           ("rb1", 0) if tc == 0 else ("m23b", tc - 1))
            sd(tc, ("var2", tc), ("sd2", tc), ("rb1", tc + 1) if nxt else ("rb1", tc))
            for jt in range(NJT):
                p.wait(sc_, "v3", ("Gap", tc, jt))
                if jt == 0 and nxt:
                    p.wait(sc_, "v3", ("q1", tc + 1))
                if not (tc == 0 and jt == 0):
                    prevj = ("Gsum", tc, jt - 1) if jt >= 1 else ("Gsum", tc - 1, NJT - 1)
                    p.wait(sc_, "mm3", prevj)
                p.emit(lambda jt=jt: sc_.copy(out=t["rbf"][:, jt, :], in_=t["r3"][:, jt, :]),
                       inc="sc3", event=("r3bf", tc, jt))
                p.wait(sc_, "sc3", value=p.cur("sc3"))
                p.emit(lambda jt=jt: sc_.activation(out=t["sq"][:, jt % 2, :],
                                                    in_=t["rbf"][:, jt, :], func=AF.Square),
                       inc="sc3", event=("sq3", tc, jt))
            sd(tc, ("var3", tc), ("sd3", tc), ("rb2", tc))

    def stream_vector(v_):
        def vw(val=None):
            p.wait(v_, "v3", value=val if val is not None else p.cur("v3"))

        for i in range(16):
            wi, kt = i // 8, i % 8
            p.wait(v_, "q%d" % (i % 2), ("wcast", i))
            p.emit(lambda i=i, wi=wi, kt=kt: v_.reduce_sum(
                out=t["red"][:, wi * 8 + kt:wi * 8 + kt + 1], in_=wqa[i % 2](),
                axis=mybir.AxisListType.X, apply_absolute_value=True),
                inc="v3", event=("wred", i))
            p.events[("v3", ("wfree", i))] = p.cur("v3")
        for wi in range(2):
            p.wait(v_, "mm3", ("wtot", wi))
            p.emit(lambda wi=wi: v_.reduce_sum(out=t["scl"][0:1, 16 + wi:17 + wi],
                                               in_=ps[6][0:1, wi * 8:wi * 8 + 8],
                                               axis=mybir.AxisListType.X), inc="v3")
            vw()
            N = 1024 * 4096.0
            p.emit(lambda wi=wi, N=N: v_.tensor_scalar(
                out=t["scl"][0:1, 4 + wi:5 + wi], in0=t["scl"][0:1, 16 + wi:17 + wi],
                scalar1=1.0 / N, scalar2=1e-5, op0=ALU.mult, op1=ALU.max), inc="v3")
            vw()
            p.emit(lambda wi=wi: v_.tensor_scalar_mul(
                out=t["scl"][0:1, 0 + wi:1 + wi], in0=t["scl"][0:1, 4 + wi:5 + wi],
                scalar1=0.5), inc="v3")
            vw()
            p.emit(lambda wi=wi: v_.tensor_scalar_mul(
                out=t["scl"][0:1, 2 + wi:3 + wi], in0=t["scl"][0:1, 0 + wi:1 + wi],
                scalar1=-1.0), inc="v3", event=("thr",) if wi == 1 else None)
        p.wait(v_, "mm3", ("swcb",))
        p.emit(lambda: v_.tensor_copy(out=t["swc"][:, 0:6], in_=ps[7][:, 0:6]),
               inc="v3", event=("swcev",))
        for i in range(16, 32):
            wi, kt = (i - 16) // 8, (i - 16) % 8
            qc1 = qc1s[i % 2]
            p.wait(v_, "q%d" % (i % 2), ("wcast", i))
            if i >= 18:
                p.wait(v_, "sw%d" % (i % 2), ("wqst", i - 18))
            vw()
            p.emit(lambda i=i, wi=wi, qc1=qc1: v_.tensor_scalar(
                out=qc1(), in0=wqa[i % 2](), scalar1=t["swc"][:, 0 + wi:1 + wi],
                scalar2=None, op0=ALU.is_ge), inc="v3")
            vw()
            p.emit(lambda i=i, wi=wi: v_.tensor_scalar(
                out=qc2(), in0=wqa[i % 2](), scalar1=t["swc"][:, 2 + wi:3 + wi],
                scalar2=None, op0=ALU.is_le), inc="v3")
            vw()
            p.emit(lambda qc1=qc1: v_.tensor_sub(out=qc1(), in0=qc1(), in1=qc2()),
                   inc="v3", event=("wq", i))
            p.events[("v3", ("wfree", i))] = p.cur("v3")

        def statmath(Nf, banks, sumev, varev, sdev, statev, rowfree, sdfree):
            bs, bq = banks
            p.wait(v_, "mm3", sumev)
            p.emit(lambda bs=bs: v_.tensor_scalar_mul(out=t["stat"][0:1, 3 * S:4 * S],
                                                      in0=ps[bs][0:1, :], scalar1=1.0 / Nf), inc="v3")
            vw()
            p.emit(lambda bq=bq: v_.tensor_scalar_mul(out=t["stat"][0:1, 4 * S:5 * S],
                                                      in0=ps[bq][0:1, :], scalar1=1.0 / Nf), inc="v3")
            vw()
            p.emit(lambda: v_.tensor_mul(out=t["stat"][0:1, 2 * S:3 * S],
                                         in0=t["stat"][0:1, 3 * S:4 * S],
                                         in1=t["stat"][0:1, 3 * S:4 * S]), inc="v3")
            vw()
            if sdfree is not None:
                p.wait(v_, "sc3", sdfree)
            p.emit(lambda: v_.tensor_sub(out=t["stat"][0:1, 4 * S:5 * S],
                                         in0=t["stat"][0:1, 4 * S:5 * S],
                                         in1=t["stat"][0:1, 2 * S:3 * S]),
                   inc="v3", event=varev)
            p.wait(v_, "sc3", sdev)
            if rowfree is not None:
                p.wait(v_, "mm3", rowfree)
            p.emit(lambda: v_.reciprocal(out=t["stat"][0:1, 0:S],
                                         in_=t["stat"][0:1, 2 * S:3 * S]), inc="v3")
            vw()
            p.emit(lambda: v_.tensor_mul(out=t["stat"][0:1, S:2 * S],
                                         in0=t["stat"][0:1, 3 * S:4 * S],
                                         in1=t["stat"][0:1, 0:S]),
                   inc="v3", event=statev)

        def scalerow(mxb, sTb, scev, rowev, wi, sxfree, mxTev):
            p.wait(v_, "mm3", mxTev)
            p.emit(lambda mxb=mxb: v_.reduce_max(out=t["smax"][:, 0:4],
                                                 in_=ps[mxb].ap().rearrange("p (a b) -> p a b", a=4),
                                                 axis=mybir.AxisListType.X), inc="v3")
            vw()
            p.emit(lambda: v_.tensor_scalar_max(out=t["smax"][:, 0:4], in0=t["smax"][:, 0:4],
                                                scalar1=1e-5), inc="v3")
            vw()
            p.emit(lambda wi=wi: v_.tensor_scalar(
                out=t["smax"][:, 4:8], in0=t["smax"][:, 0:4],
                scalar1=t["swc"][:, 4 + wi:5 + wi], scalar2=1.0 / 127.0,
                op0=ALU.mult, op1=ALU.mult), inc="v3")
            vw()
            p.emit(lambda: v_.reciprocal(out=t["smax"][:, 0:4], in_=t["smax"][:, 0:4]), inc="v3")
            vw()
            p.emit(lambda: v_.tensor_scalar_mul(out=t["smax"][:, 0:4], in0=t["smax"][:, 0:4],
                                                scalar1=127.0), inc="v3", event=scev)
            p.wait(v_, "mm3", sTev_holder[0])
            if sxfree is not None:
                p.wait(v_, "sx", sxfree)
            p.emit(lambda sTb=sTb: v_.tensor_copy(out=t["sflat"][0:8, 0:128], in_=ps[sTb][0:8, 0:128]),
                   inc="v3", event=rowev)

        sTev_holder = [None]

        def front(tcn):
            for jt in range(NJT):
                p.wait(v_, "mm3", ("A", tcn, jt))
                if jt == 0:
                    p.wait(v_, "qx", ("xc", tcn))
                    if tcn == 0:
                        p.wait(v_, "sw0", value=8 * 16)
                        p.wait(v_, "sw1", value=8 * 16)
                    if tcn >= 2:
                        p.wait(v_, "mm3", ("Gsum", tcn - 2, NJT - 1))
                p.emit(lambda tcn=tcn, jt=jt: v_.tensor_tensor(
                    out=t["rbf"][:, jt, :], in0=ps[4 + jt % 2][:, :], in1=xs(tcn)[:, jt, :],
                    op=ALU.add), inc="v3", event=("Aev", tcn, jt))
            if tcn == 0:
                rowfree, sdfree = None, None
            elif tcn == 1:
                rowfree, sdfree = ("rb1", 0), ("sd0", 0)
            else:
                rowfree, sdfree = ("m23b", tcn - 2), ("sd3", tcn - 2)
            statmath(1024.0, (6, 7), ("Bsum", tcn, NJT - 1), ("var0", tcn), ("sd0", tcn),
                     ("stat0", tcn), rowfree, sdfree)
            p.wait(v_, "mm3", ("rs0b", tcn))
            p.wait(v_, "mm3", ("m20b", tcn))
            if tcn >= 2:
                p.wait(v_, "st3", ("outst", tcn - 2, NJT - 1))
            vw()
            p.emit(lambda: v_.tensor_copy(out=t["v"][0][:], in_=ps[6][:, :]), inc="v3")
            vw()
            p.emit(lambda: v_.tensor_copy(out=t["v"][1][:], in_=ps[7][:, :]), inc="v3")
            vw()
            p.emit(lambda: v_.tensor_tensor(out=t["r3"][:, :, :], in0=t["rbf"][:, :, :],
                                            in1=rep0(t["v"][0][:], NJT), op=ALU.mult), inc="v3")
            vw()
            p.emit(lambda tcn=tcn: v_.tensor_tensor(out=x1b(tcn)[:, :, :], in0=t["r3"][:, :, :],
                                                    in1=rep0(t["v"][1][:], NJT), op=ALU.subtract),
                   inc="v3", event=("x1", tcn, NJT - 1))
            for jt in range(NJT):
                vw()
                p.emit(lambda tcn=tcn, jt=jt: v_.tensor_scalar(
                    out=t["rbf"][:, jt, :], in0=x1b(tcn)[:, jt, :],
                    scalar1=t["lng"][:, jt:jt + 1], scalar2=t["lnb"][:, jt:jt + 1],
                    op0=ALU.mult, op1=ALU.add), inc="v3", event=("l1o", tcn, jt))
            vw()
            p.emit(lambda: v_.tensor_scalar_mul(out=r3h()[:, 0:8, :], in0=t["rbf"][:, :, :],
                                                scalar1=-1.0), inc="v3")
            vw()
            p.emit(lambda: v_.tensor_tensor(out=r3h()[:, 0:8, :], in0=t["rbf"][:, :, :],
                                            in1=r3h()[:, 0:8, :], op=ALU.max), inc="v3")
            for w in (4, 2, 1):
                vw()
                p.emit(lambda w=w: v_.tensor_tensor(out=r3h()[:, 0:w, :], in0=r3h()[:, 0:w, :],
                                                    in1=r3h()[:, w:2 * w, :], op=ALU.max), inc="v3")
            vw()
            p.emit(lambda: v_.tensor_copy(out=t["v"][0][:], in_=r3h()[:, 0, :]),
                   inc="v3", event=("mx1", tcn))
            sTev_holder[0] = ("sT1", tcn)
            scalerow(6, 7, ("s1c", tcn), ("sfl", tcn, 1), 0,
                     ("sxd", tcn - 2, 2) if tcn >= 2 else None, ("mxT1", tcn))
            p.wait(v_, "mm3", ("sb1", tcn))
            p.emit(lambda: v_.tensor_copy(out=t["sbc"][:, :], in_=ps[6][:, :]), inc="v3")
            p.wait(v_, "mm3", ("rb1", tcn))
            vw()
            p.emit(lambda: v_.tensor_copy(out=t["rbc"][:, :], in_=ps[7][:, :]),
                   inc="v3", event=("sbc1ev", tcn))
            def msksub1(j):
                vw()
                p.emit(lambda j=j: v_.scalar_tensor_tensor(
                    out=t["sg"][j % 2][:], in0=qi1(j)[:, :], scalar=-0.5,
                    in1=t["sg"][j % 2][:], op0=ALU.add, op1=ALU.is_gt), inc="v3")
                vw()
                p.emit(lambda j=j: v_.tensor_tensor(out=t["q1"][:, j, :],
                                                    in0=qi1(j)[:, :],
                                                    in1=t["sg"][j % 2][:], op=ALU.subtract),
                       inc="v3", event=("q1t", tcn, j))

            for jt in range(NJT):
                vw()
                if jt == 0 and tcn >= 1:
                    p.wait(v_, "mm3", ("slotfree", tcn - 1, 2))
                p.emit(lambda jt=jt: v_.tensor_mul(out=t["sg"][jt % 2][:], in0=t["rbf"][:, jt, :],
                                                   in1=t["sbc"][:, :]), inc="v3")
                if jt >= 1:
                    msksub1(jt - 1)
                vw()
                p.emit(lambda jt=jt: v_.tensor_scalar_add(out=qi1(jt)[:, :], in0=t["sg"][jt % 2][:],
                                                          scalar1=0.5),
                       inc="v3", event=("qiev1", tcn, jt))
            msksub1(NJT - 1)
            vw()
            p.emit(lambda: v_.tensor_tensor(out=t["q1"][:, :, :], in0=t["q1"][:, :, :],
                                            in1=rep0(t["rbc"][:, :], NJT), op=ALU.mult),
                   inc="v3", event=("q1", tcn))

        def back(tc):
            nxt = tc + 1 < NCH
            statmath(4096.0, (2, 3), ("Dsum", tc, NHT - 1), ("var2", tc), ("sd2", tc),
                     ("stat2", tc), ("rb1", tc + 1) if nxt else ("rb1", tc),
                     ("sd0", tc + 1) if nxt else ("sd0", tc))
            p.wait(v_, "mm3", ("rs2b", tc))
            p.wait(v_, "mm3", ("m22b", tc))
            if tc >= 1:
                p.wait(v_, "st3", ("outst", tc - 1, NJT - 1))
            if nxt:
                p.wait(v_, "mm3", ("mxT1", tc + 1))
            vw()
            p.emit(lambda: v_.tensor_copy(out=t["v"][0][:], in_=ps[4][:, :]),
                   inc="v3", event=("dv0", tc))
            vw()
            p.emit(lambda: v_.tensor_copy(out=t["v"][1][:], in_=ps[5][:, :]),
                   inc="v3", event=("dv1", tc))
            vw()
            p.emit(lambda: v_.tensor_copy(out=t["sg"][0][:], in_=t["v"][0][:]), inc="v3")
            vw()
            p.emit(lambda: v_.tensor_copy(out=t["sg"][1][:], in_=t["v"][1][:]), inc="v3")
            for b4 in range(NHT // 4):
                vw()
                p.emit(lambda b4=b4: v_.tensor_tensor(
                    out=r3h()[:, 0:4, :], in0=t["g1"][:, b4 * 4:b4 * 4 + 4, :],
                    in1=rep0(t["sg"][0][:], 4), op=ALU.mult), inc="v3")
                vw()
                p.emit(lambda b4=b4: v_.tensor_tensor(
                    out=r3h()[:, 0:4, :], in0=r3h()[:, 0:4, :],
                    in1=rep0(t["sg"][1][:], 4), op=ALU.subtract), inc="v3")
                for k4 in range(4):
                    kt2 = b4 * 4 + k4
                    vw()
                    p.emit(lambda kt2=kt2, k4=k4: v_.tensor_scalar(
                        out=t["g1"][:, kt2, :], in0=r3h()[:, k4, :],
                        scalar1=t["ln2g"][:, kt2:kt2 + 1], scalar2=t["ln2b"][:, kt2:kt2 + 1],
                        op0=ALU.mult, op1=ALU.add), inc="v3", event=("l2o", tc, kt2))
            for h2 in range(2):
                vw()
                p.emit(lambda h2=h2: v_.tensor_scalar_mul(
                    out=r3h()[:, 8 * h2:8 * h2 + 8, :],
                    in0=t["g1"][:, 16 * h2:16 * h2 + 8, :], scalar1=-1.0), inc="v3")
                vw()
                p.emit(lambda h2=h2: v_.tensor_tensor(
                    out=r3h()[:, 8 * h2:8 * h2 + 8, :], in0=t["g1"][:, 16 * h2:16 * h2 + 8, :],
                    in1=r3h()[:, 8 * h2:8 * h2 + 8, :], op=ALU.max), inc="v3")
                vw()
                p.emit(lambda h2=h2: v_.tensor_scalar_mul(
                    out=t["rbf"][:, :, :],
                    in0=t["g1"][:, 16 * h2 + 8:16 * h2 + 16, :], scalar1=-1.0), inc="v3")
                vw()
                p.emit(lambda h2=h2: v_.tensor_tensor(
                    out=t["rbf"][:, :, :], in0=t["g1"][:, 16 * h2 + 8:16 * h2 + 16, :],
                    in1=t["rbf"][:, :, :], op=ALU.max), inc="v3")
                vw()
                p.emit(lambda h2=h2: v_.tensor_tensor(
                    out=r3h()[:, 8 * h2:8 * h2 + 8, :], in0=r3h()[:, 8 * h2:8 * h2 + 8, :],
                    in1=t["rbf"][:, :, :], op=ALU.max), inc="v3")
            vw()
            p.emit(lambda: v_.tensor_tensor(out=r3h()[:, 0:8, :], in0=r3h()[:, 0:8, :],
                                            in1=r3h()[:, 8:16, :], op=ALU.max), inc="v3")
            for w in (4, 2, 1):
                vw()
                p.emit(lambda w=w: v_.tensor_tensor(out=r3h()[:, 0:w, :], in0=r3h()[:, 0:w, :],
                                                    in1=r3h()[:, w:2 * w, :], op=ALU.max), inc="v3")
            vw()
            if nxt:
                p.wait(v_, "mm3", ("mxT1", tc + 1))
            p.emit(lambda: v_.tensor_copy(out=t["v"][0][:], in_=r3h()[:, 0, :]),
                   inc="v3", event=("mx2", tc))
            sTev_holder[0] = ("sT2", tc)
            if nxt:
                p.wait(v_, "mm3", ("sT1", tc + 1))
            scalerow(4, 5, ("s2c", tc), ("sfl", tc, 2), 1,
                     ("sxd", tc + 1, 1) if nxt else None, ("mxT2", tc))
            p.wait(v_, "mm3", ("sb2", tc))
            p.emit(lambda: v_.tensor_copy(out=t["sbc"][:, :], in_=ps[4][:, :]), inc="v3")
            p.wait(v_, "mm3", ("rb2", tc))
            vw()
            p.emit(lambda: v_.tensor_copy(out=t["rbc"][:, :], in_=ps[5][:, :]),
                   inc="v3", event=("sbc2ev", tc))
            def msksubF(b):
                vw()
                p.emit(lambda b=b: v_.scalar_tensor_tensor(
                    out=vbF(b), in0=qiB()[:, :, :], scalar=-0.5,
                    in1=vbF(b), op0=ALU.add, op1=ALU.is_gt), inc="v3")
                vw()
                half = (b % 2) * 4
                p.emit(lambda b=b, half=half: v_.tensor_tensor(
                    out=t["q2"][:, half:half + 4, :], in0=qiB()[:, :, :],
                    in1=vbF(b), op=ALU.subtract),
                    inc="v3", event=("q2b", tc, b))

            for b4 in range(NHT // 4):
                vw()
                if b4 == 0:
                    p.wait(v_, "mm3", ("C", tc, NHT - 1))
                if b4 >= 2:
                    p.wait(v_, "mm3", ("F", tc, (b4 - 2) * 4 + 3))
                p.emit(lambda b4=b4: v_.tensor_tensor(
                    out=vbF(b4), in0=t["g1"][:, b4 * 4:b4 * 4 + 4, :],
                    in1=rep0(t["sbc"][:, :], 4), op=ALU.mult), inc="v3")
                if b4 >= 1:
                    msksubF(b4 - 1)
                vw()
                p.emit(lambda b4=b4: v_.tensor_scalar_add(out=qiB()[:, :, :], in0=vbF(b4),
                                                          scalar1=0.5),
                       inc="v3", event=("qiev2", tc, b4))
            msksubF(NHT // 4 - 1)
            for jt in range(NJT):
                if jt == 0:
                    p.wait(v_, "mm3", ("F", tc, NHT - 1))
                vw()
                p.emit(lambda jt=jt: v_.tensor_mul(out=t["v"][jt % 2][:], in0=ps[jt][:, :],
                                                   in1=t["rbc"][:, :]),
                       inc="v3", event=("gev", tc, jt))
                vw()
                p.emit(lambda tc=tc, jt=jt: v_.scalar_tensor_tensor(
                    out=t["r3"][:, jt, :], in0=t["v"][jt % 2][:],
                    scalar=t["b2"][:, jt:jt + 1], in1=x1b(tc)[:, jt, :],
                    op0=ALU.add, op1=ALU.add),
                    inc="v3", event=("Gap", tc, jt))
            statmath(1024.0, (6, 7), ("Gsum", tc, NJT - 1), ("var3", tc), ("sd3", tc),
                     ("stat3", tc), ("rb2", tc), ("sd2", tc))
            p.wait(v_, "mm3", ("rs3b", tc))
            p.wait(v_, "mm3", ("m23b", tc))
            vw()
            p.emit(lambda: v_.tensor_copy(out=t["v"][0][:], in_=ps[4][:, :]), inc="v3")
            vw()
            p.emit(lambda: v_.tensor_copy(out=t["v"][1][:], in_=ps[5][:, :]),
                   inc="v3", event=("dv3", tc))
            vw()
            p.emit(lambda: v_.tensor_tensor(out=t["r3"][:, :, :], in0=t["r3"][:, :, :],
                                            in1=rep0(t["v"][0][:], NJT), op=ALU.mult), inc="v3")
            vw()
            p.emit(lambda: v_.tensor_tensor(out=t["r3"][:, :, :], in0=t["r3"][:, :, :],
                                            in1=rep0(t["v"][1][:], NJT), op=ALU.subtract),
                   inc="v3", event=("outj", tc, NJT - 1))

        front(0)
        for tc in range(NCH):
            if tc + 1 < NCH:
                front(tc + 1)
            back(tc)

    p.begin("gpsimd"); stream_gpsimd(E.get("gpsimd"))
    p.begin("sync"); stream_sync(E.get("sync"))
    p.begin("tensor"); stream_tensor(E.get("tensor"))
    p.begin("scalar"); stream_scalar(E.get("scalar"))
    p.begin("vector"); stream_vector(E.get("vector"))


L3_SEMS2 = ["ls", "ly", "lw", "q0", "q1", "qo", "qx", "sx", "sw0", "sw1", "mm3", "sc3", "v3", "st3"]


def l3_alloc2(nc, E, T3=2048):
    TCH = 512
    t = {}
    t["wbuf"] = E(nc.sbuf_tensor("l3_wbuf", [128, 32, 1024], BF16))
    t["x"] = E(nc.sbuf_tensor("l3_x", [128, 8, TCH], BF16))
    t["y2"] = E(nc.sbuf_tensor("l3_y2", [128, 16, TCH], BF16))
    t["x1"] = [E(nc.sbuf_tensor(f"l3_x1{i}", [128, 8, TCH], BF16)) for i in range(2)]
    t["rbf"] = E(nc.sbuf_tensor("l3_rbf", [128, 8, TCH], BF16))
    t["q1"] = E(nc.sbuf_tensor("l3_q1", [128, 8, TCH], BF16))
    t["q2"] = E(nc.sbuf_tensor("l3_q2", [128, 8, TCH], BF16))
    t["g1"] = E(nc.sbuf_tensor("l3_g1", [128, 32, TCH], BF16))
    t["sq"] = E(nc.sbuf_tensor("l3_sq", [128, 2, TCH], BF16))
    t["sqA"] = E(nc.sbuf_tensor("l3_sqA", [128, 1, TCH], BF16))
    t["v"] = [E(nc.sbuf_tensor(f"l3_v{i}", [128, TCH], F32)) for i in range(2)]
    t["sg"] = [E(nc.sbuf_tensor(f"l3_sg{i}", [128, TCH], BF16)) for i in range(2)]
    t["mx"] = t["sg"][0]
    t["mskB"] = E(nc.sbuf_tensor("l3_mskB", [128, 4, TCH], BF16))
    t["r3"] = E(nc.sbuf_tensor("l3_r3", [128, 8, TCH], F32))
    t["stat"] = E(nc.sbuf_tensor("l3_stat", [1, TCH * 5], F32))
    t["smax"] = E(nc.sbuf_tensor("l3_smax", [128, 8], F32))
    t["sflat"] = E(nc.sbuf_tensor("l3_sflat", [8, 128], F32))
    t["sbc"] = E(nc.sbuf_tensor("l3_sbc", [128, TCH], BF16))
    t["rbc"] = E(nc.sbuf_tensor("l3_rbc", [128, TCH], BF16))
    t["onesb"] = E(nc.sbuf_tensor("l3_onesb", [128, 1], BF16))
    t["onesf"] = E(nc.sbuf_tensor("l3_onesf", [128, 1], F32))
    t["onesrow"] = E(nc.sbuf_tensor("l3_onesrow", [1, 128], F32))
    t["identf"] = E(nc.sbuf_tensor("l3_identf", [128, 128], F32))
    t["lng"] = E(nc.sbuf_tensor("l3_lng", [128, 8], F32))
    t["lnb"] = E(nc.sbuf_tensor("l3_lnb", [128, 8], F32))
    t["ln2g"] = E(nc.sbuf_tensor("l3_ln2g", [128, 32], F32))
    t["ln2b"] = E(nc.sbuf_tensor("l3_ln2b", [128, 32], F32))
    t["b1"] = E(nc.sbuf_tensor("l3_b1", [128, 32], F32))
    t["b2"] = E(nc.sbuf_tensor("l3_b2", [128, 8], F32))
    t["swc"] = E(nc.sbuf_tensor("l3_swc", [128, 8], F32))
    t["red"] = E(nc.sbuf_tensor("l3_red", [128, 16], F32))
    t["scl"] = E(nc.sbuf_tensor("l3_scl", [1, 64], F32))
    t["cst"] = E(nc.sbuf_tensor("l3_cst", [1, 8], F32))
    t["ps"] = [E(nc.psum_tensor(f"l3_ps{i}", [128, TCH], F32)) for i in range(8)]
    return t


TT = 16384
TC = 2048
NCHUNK = TT // TC
NB_CHUNK = NCHUNK // 2
NG = 32
SPAN = min(512, TC)
NSPAN = TC // SPAN

def build_l2():
    nc = bass.Bass("TRN2", target_bir_lowering=False, debug=False)
    drep = nc.dram_tensor("drep", [NG, 128, TT], BF16, kind="ExternalInput").ap()
    durep = nc.dram_tensor("durep", [NG, 128, TT], BF16, kind="ExternalInput").ap()
    u_cm = nc.dram_tensor("u_cm", [2, 128, TT], BF16, kind="ExternalInput").ap()
    gs_cm = nc.dram_tensor("gs_cm", [2, 128, TT], BF16, kind="ExternalInput").ap()
    bdn = nc.dram_tensor("bdn", [128, TT], BF16, kind="ExternalInput").ap()
    cdn = nc.dram_tensor("cdn", [128, TT], BF16, kind="ExternalInput").ap()
    alog_dn = nc.dram_tensor("alog_dn", [128, NG], F32, kind="ExternalInput").ap()
    wsel = nc.dram_tensor("wsel", [128, 16, 128], BF16, kind="ExternalInput").ap()
    wdiag = nc.dram_tensor("wdiag", [128, 2, 128], F32, kind="ExternalInput").ap()
    y2 = nc.dram_tensor("y2", [2, 128, TT], BF16, kind="ExternalOutput").ap()

    iters = [(ch, ct, g16) for ch in range(NCHUNK) for ct in range(2) for g16 in range(16)]

    es = contextlib.ExitStack()
    with es:
        E = es.enter_context
        t_drep = [E(nc.sbuf_tensor(f"t_drep{i}", [128, TC], BF16)) for i in range(2)]
        t_durep = [E(nc.sbuf_tensor(f"t_durep{i}", [128, TC], BF16)) for i in range(2)]
        t_bdn = [E(nc.sbuf_tensor(f"t_bdn{i}", [128, TC], BF16)) for i in range(2)]
        t_cdn = [E(nc.sbuf_tensor(f"t_cdn{i}", [128, TC], BF16)) for i in range(2)]
        t_u = [E(nc.sbuf_tensor(f"t_u{i}", [128, 2, TC], BF16)) for i in range(2)]
        t_gs = [E(nc.sbuf_tensor(f"t_gs{i}", [128, 2, TC], BF16)) for i in range(2)]
        t_dA = [E(nc.sbuf_tensor(f"t_dA{i}", [128, TC], F32)) for i in range(2)]
        t_dBu = [E(nc.sbuf_tensor(f"t_dBu{i}", [128, TC], BF16)) for i in range(2)]
        t_hs = [E(nc.sbuf_tensor(f"t_hs{i}", [128, TC], BF16)) for i in range(2)]
        t_prod = [E(nc.sbuf_tensor(f"t_prod{i}", [128, TC], BF16)) for i in range(2)]
        t_y2 = [E(nc.sbuf_tensor(f"t_y2{i}", [128, TC], BF16)) for i in range(2)]
        t_wsel = E(nc.sbuf_tensor("t_wsel", [128, 16, 128], BF16))
        t_wdiagf = E(nc.sbuf_tensor("t_wdiagf", [128, 2, 128], F32))
        t_wdiag = E(nc.sbuf_tensor("t_wdiag", [128, 2, 128], BF16))
        t_alog = E(nc.sbuf_tensor("t_alog", [128, NG], F32))
        t_expA = E(nc.sbuf_tensor("t_expA", [128, NG], F32))
        t_negA = E(nc.sbuf_tensor("t_negA", [128, NG], F32))
        t_state = E(nc.sbuf_tensor("t_state", [128, NG], F32))
        psum = [E(nc.psum_tensor(f"ps{i}", [128, SPAN], F32)) for i in range(8)]

        s_drep = [E(nc.semaphore(f"s_drep{i}")) for i in range(2)]
        s_durep = [E(nc.semaphore(f"s_durep{i}")) for i in range(2)]
        s_bdn = [E(nc.semaphore(f"s_bdn{i}")) for i in range(2)]
        s_cdn = [E(nc.semaphore(f"s_cdn{i}")) for i in range(2)]
        s_u = [E(nc.semaphore(f"s_u{i}")) for i in range(2)]
        s_gs = [E(nc.semaphore(f"s_gs{i}")) for i in range(2)]
        s_st = [E(nc.semaphore(f"s_st{i}")) for i in range(2)]
        s_pre = E(nc.semaphore("s_pre"))
        s_dA = E(nc.semaphore("s_dA"))
        s_v = E(nc.semaphore("s_v"))
        s_mm = E(nc.semaphore("s_mm"))
        s_ev = E(nc.semaphore("s_ev"))
        s_sc = E(nc.semaphore("s_sc"))
        s_vA = E(nc.semaphore("s_vA"))
        vch = E(nc.semaphore("vch"))
        block = E(nc.Block())

        @block.sync
        def _(sync):
            sync.dma_start(out=t_wsel[:], in_=wsel).then_inc(s_pre, 16)
            sync.dma_start(out=t_wdiagf[:], in_=wdiag).then_inc(s_pre, 16)
            sync.dma_start(out=t_alog[:], in_=alog_dn).then_inc(s_pre, 16)
            for ch in range(NCHUNK):
                t0 = ch * TC
                sl = ch % 2
                if ch >= 2:
                    sync.wait_ge(s_v, (ch - 1) * 32)
                    sync.wait_ge(s_ev, (ch - 1) * 2)
                    sync.wait_ge(s_mm, (ch - 1) * 32)
                sync.dma_start(out=t_bdn[sl][:], in_=bdn[:, t0:t0 + TC]).then_inc(s_bdn[sl], 16)
                sync.dma_start(out=t_cdn[sl][:], in_=cdn[:, t0:t0 + TC]).then_inc(s_cdn[sl], 16)
                for c_ in range(2):
                    sync.dma_start(out=t_u[sl][:, c_, :], in_=u_cm[c_, :, t0:t0 + TC]).then_inc(s_u[sl], 16)
                    sync.dma_start(out=t_gs[sl][:, c_, :], in_=gs_cm[c_, :, t0:t0 + TC]).then_inc(s_gs[sl], 16)
                for ct in range(2):
                    for g16 in range(16):
                        it = ch * 32 + ct * 16 + g16
                        g = ct * 16 + g16
                        buf = it % 2
                        if it >= 2:
                            sync.wait_ge(s_dA, it - 1)
                            sync.wait_ge(s_v, it - 1)
                        sync.dma_start(out=t_drep[buf][:], in_=drep[g, :, t0:t0 + TC]).then_inc(s_drep[buf], 16)
                        sync.dma_start(out=t_durep[buf][:], in_=durep[g, :, t0:t0 + TC]).then_inc(s_durep[buf], 16)

        @block.scalar
        def _(scalar):
            scalar.wait_ge(s_pre, 48)
            scalar.activation(out=t_expA[:], in_=t_alog[:], func=AF.Exp).then_inc(s_sc, 1)
            scalar.copy(out=t_wdiag[:], in_=t_wdiagf[:]).then_inc(s_sc, 1)
            for it, (ch, ct, g16) in enumerate(iters):
                g = ct * 16 + g16
                buf = it % 2
                if it == 0:
                    scalar.wait_ge(s_vA, 1)
                if g16 == 0 and ch >= 1:
                    ctset_prev = (ch - 1) * 2 + ct
                    scalar.wait_ge(s_ev, ctset_prev + 1)
                    scalar.dma_start(out=y2[ct, :, (ch - 1) * TC:ch * TC],
                                     in_=t_y2[ct][:]).then_inc(s_st[ct], 16)
                scalar.wait_ge(s_drep[buf], 16 * (it // 2 + 1))
                if it >= 2:
                    scalar.wait_ge(s_v, it - 1)
                scalar.activation(out=t_dA[buf][:], in_=t_drep[buf][:], func=AF.Exp,
                                  scale=t_negA[:, g:g + 1]).then_inc(s_dA, 1)
            for ct in range(2):
                ctset = (NCHUNK - 1) * 2 + ct
                scalar.wait_ge(s_ev, ctset + 1)
                scalar.dma_start(out=y2[ct, :, (NCHUNK - 1) * TC:NCHUNK * TC],
                                 in_=t_y2[ct][:]).then_inc(s_st[ct], 16)
            scalar.wait_ge(s_st[0], NCHUNK * 16)
            scalar.wait_ge(s_st[1], NCHUNK * 16)

        @block.vector
        def _(vector):
            vector.wait_ge(s_sc, 1)
            vector.tensor_scalar_mul(out=t_negA[:], in0=t_expA[:], scalar1=-1.0).then_inc(s_vA, 1)
            vchn = [0]

            def vinc(ins):
                vchn[0] += 1
                ins.then_inc(vch, 1)
                return vchn[0]

            for it, (ch, ct, g16) in enumerate(iters):
                g = ct * 16 + g16
                buf = it % 2
                sl = ch % 2
                ctset = ch * 2 + ct
                vector.wait_ge(s_durep[buf], 16 * (it // 2 + 1))
                vector.wait_ge(s_bdn[sl], 16 * (ch // 2 + 1))
                vector.wait_ge(s_dA, it + 1)
                k_dbu = vinc(vector.tensor_tensor(out=t_dBu[buf][:], in0=t_durep[buf][:],
                                                  in1=t_bdn[sl][:], op=ALU.mult))
                vector.wait_ge(vch, k_dbu)
                init = 0.0 if ch % NB_CHUNK == 0 else t_state[:, g:g + 1]
                k_scan = vinc(vector.tensor_tensor_scan(
                    out=t_hs[buf][:], data0=t_dA[buf][:], data1=t_dBu[buf][:],
                    initial=init, op0=ALU.mult, op1=ALU.add))
                vector.wait_ge(vch, k_scan)
                vinc(vector.tensor_copy(out=t_state[:, g:g + 1], in_=t_hs[buf][:, TC - 1:TC]))
                if it >= 2:
                    vector.wait_ge(s_mm, it - 1)
                vector.wait_ge(s_cdn[sl], 16 * (ch // 2 + 1))
                vector.tensor_tensor(out=t_prod[buf][:], in0=t_hs[buf][:],
                                     in1=t_cdn[sl][:], op=ALU.mult).then_inc(s_v, 1)
                if g16 == 15:
                    vector.wait_ge(s_mm, it + 1)
                    if ch >= 1:
                        vector.wait_ge(s_st[ct], 16 * ch)
                    vector.wait_ge(s_gs[sl], 32 * (ch // 2 + 1))
                    pb = (ct % 2) * 4
                    for sp in range(NSPAN):
                        ins = vector.tensor_tensor(
                            out=t_y2[ct][:, sp * SPAN:(sp + 1) * SPAN],
                            in0=psum[pb + sp][:, :],
                            in1=t_gs[sl][:, ct, sp * SPAN:(sp + 1) * SPAN],
                            op=ALU.mult)
                        if sp == NSPAN - 1:
                            ins.then_inc(s_ev, 1)

        @block.tensor
        def _(tensor):
            tensor.wait_ge(s_pre, 48)
            tensor.wait_ge(s_sc, 2)
            for it, (ch, ct, g16) in enumerate(iters):
                buf = it % 2
                sl = ch % 2
                ctset = ch * 2 + ct
                tensor.wait_ge(s_v, it + 1)
                if g16 == 0:
                    if ctset >= 2:
                        tensor.wait_ge(s_ev, ctset - 1)
                    tensor.wait_ge(s_u[sl], 32 * (ch // 2 + 1))
                pb = (ct % 2) * 4
                last = None
                for sp in range(NSPAN):
                    last = tensor.matmul(psum[pb + sp][:, :], t_wsel[:, g16, :],
                                         t_prod[buf][:, sp * SPAN:(sp + 1) * SPAN],
                                         start=(g16 == 0), stop=False)
                if g16 == 15:
                    for sp in range(NSPAN):
                        last = tensor.matmul(psum[pb + sp][:, :], t_wdiag[:, ct, :],
                                             t_u[sl][:, ct, sp * SPAN:(sp + 1) * SPAN],
                                             start=False, stop=(sp == NSPAN - 1))
                last.then_inc(s_mm, 1)
    return nc








T1 = 2048
T3 = 2048


def _engs(g=None, s=None, te=None, sc=None, v=None):
    return {"gpsimd": g, "sync": s, "tensor": te, "scalar": sc, "vector": v}


def build_l1():
    nc = bass.Bass("TRN2", target_bir_lowering=False, debug=False)
    d = {}
    d["xT"] = nc.dram_tensor("xT", [T1 // 512, 8, 128, 515], F32, kind="ExternalInput").ap()
    d["w_inT"] = nc.dram_tensor("w_inT", [1024, 4096], F32, kind="ExternalInput").ap()
    d["conv_w_cm"] = nc.dram_tensor("conv_w_cm", [128, 16, 4], F32, kind="ExternalInput").ap()
    d["conv_b_cm"] = nc.dram_tensor("conv_b_cm", [128, 16], F32, kind="ExternalInput").ap()
    d["w_xT"] = nc.dram_tensor("w_xT", [2048, 96], F32, kind="ExternalInput").ap()
    d["w_dtT"] = nc.dram_tensor("w_dtT", [64, 16, 128], F32, kind="ExternalInput").ap()
    d["dt_bias_cm"] = nc.dram_tensor("dt_bias_cm", [128, 16], F32, kind="ExternalInput").ap()
    NCH1 = T1 // 512
    d["o_u"] = nc.dram_tensor("o_u", [NCH1, 128, 16, 512], BF16, kind="ExternalOutput").ap()
    d["o_du"] = nc.dram_tensor("o_du", [NCH1, 128, 16, 512], BF16, kind="ExternalOutput").ap()
    d["o_dl"] = nc.dram_tensor("o_dl", [NCH1, 128, 16, 512], BF16, kind="ExternalOutput").ap()
    d["o_gs"] = nc.dram_tensor("o_gs", [NCH1, 128, 16, 512], BF16, kind="ExternalOutput").ap()
    d["o_bc"] = nc.dram_tensor("o_bc", [NCH1, 32, 512], BF16, kind="ExternalOutput").ap()
    es = contextlib.ExitStack()
    with es:
        E = es.enter_context
        t = l1_alloc(nc, E, T1)
        sems = {n: E(nc.semaphore("s_" + n)) for n in L1_SEMS}
        events = {}
        l1_streams(Pass(True, None, events), t, d, None, T1)
        block = E(nc.Block())

        @block.gpsimd
        def _(g):
            l1_streams(Pass(False, sems, events, only="gpsimd"), t, d, _engs(g=g), T1)

        @block.sync
        def _(s):
            l1_streams(Pass(False, sems, events, only="sync"), t, d, _engs(s=s), T1)

        @block.tensor
        def _(te):
            l1_streams(Pass(False, sems, events, only="tensor"), t, d, _engs(te=te), T1)

        @block.scalar
        def _(sc):
            l1_streams(Pass(False, sems, events, only="scalar"), t, d, _engs(sc=sc), T1)

        @block.vector
        def _(v):
            l1_streams(Pass(False, sems, events, only="vector"), t, d, _engs(v=v), T1)
    return nc


def build_l3():
    nc = bass.Bass("TRN2", target_bir_lowering=False, debug=False)
    d = {}
    d["xT_s"] = nc.dram_tensor("xT_s", [128, 8, T3], F32, kind="ExternalInput").ap()
    d["y2T"] = nc.dram_tensor("y2T", [128, 16, T3], BF16, kind="ExternalInput").ap()
    d["w_outT_s"] = nc.dram_tensor("w_outT_s", [128, 16, 1024], F32, kind="ExternalInput").ap()
    d["w1T_s"] = nc.dram_tensor("w1T_s", [128, 8, 4096], F32, kind="ExternalInput").ap()
    d["w2T_s"] = nc.dram_tensor("w2T_s", [128, 8, 4096], F32, kind="ExternalInput").ap()
    for nm, c in (("lng", 8), ("lnb", 8), ("ln2g", 32), ("ln2b", 32), ("b1", 32), ("b2", 8)):
        d[nm] = nc.dram_tensor(nm, [128, c], F32, kind="ExternalInput").ap()
    d["ident"] = nc.dram_tensor("ident", [128, 128], BF16, kind="ExternalInput").ap()
    d["identf"] = nc.dram_tensor("identf", [128, 128], F32, kind="ExternalInput").ap()
    d["onesb"] = nc.dram_tensor("onesb", [128, 1], BF16, kind="ExternalInput").ap()
    d["onesf"] = nc.dram_tensor("onesf", [128, 1], F32, kind="ExternalInput").ap()
    d["onesrow"] = nc.dram_tensor("onesrow", [1, 128], F32, kind="ExternalInput").ap()
    d["cst"] = nc.dram_tensor("cst", [1, 8], F32, kind="ExternalInput").ap()
    d["w1q_d"] = nc.dram_tensor("w1q_d", [128, 8, 4096], BF16, kind="Internal").ap()
    d["w2q_d"] = nc.dram_tensor("w2q_d", [128, 8, 4096], BF16, kind="Internal").ap()
    d["outT_s"] = nc.dram_tensor("outT_s", [128, 8, T3], F32, kind="ExternalOutput").ap()
    es = contextlib.ExitStack()
    with es:
        E = es.enter_context
        t = l3_alloc2(nc, E, T3)
        sems = {n: E(nc.semaphore("s_" + n)) for n in L3_SEMS2}
        events = {}
        l3_streams(Pass(True, None, events), t, d, None, T3)
        block = E(nc.Block())

        @block.gpsimd
        def _(g):
            l3_streams(Pass(False, sems, events, only="gpsimd"), t, d, _engs(g=g), T3)

        @block.sync
        def _(s):
            l3_streams(Pass(False, sems, events, only="sync"), t, d, _engs(s=s), T3)

        @block.tensor
        def _(te):
            l3_streams(Pass(False, sems, events, only="tensor"), t, d, _engs(te=te), T3)

        @block.scalar
        def _(sc):
            l3_streams(Pass(False, sems, events, only="scalar"), t, d, _engs(sc=sc), T3)

        @block.vector
        def _(v):
            l3_streams(Pass(False, sems, events, only="vector"), t, d, _engs(v=v), T3)
    return nc


_P = np.arange(128)


def _pack_l1(x, W_in, conv_w, conv_b, W_x, W_dt, dt_bias):
    cw = np.ascontiguousarray(conv_w.reshape(16, 128, 4).transpose(1, 0, 2))
    cb = np.ascontiguousarray(conv_b.reshape(16, 128).T)
    db = np.ascontiguousarray(dt_bias.reshape(16, 128).T)
    w_inT = np.ascontiguousarray(W_in.T)
    w_xT = np.ascontiguousarray(W_x.T)
    wdt = np.ascontiguousarray(W_dt.reshape(16, 128, 64).transpose(2, 0, 1))
    maps = []
    for c in range(8):
        b, q = c // 4, c % 4
        t0 = q * 2048
        xs = np.zeros((2048 + 3, 1024), np.float32)
        lo = max(0, t0 - 3)
        xs[3 - (t0 - lo):] = x[b, lo:t0 + 2048]
        xTf = np.ascontiguousarray(xs.T)
        xc = np.empty((4, 8, 128, 515), np.float32)
        for tc in range(4):
            for kt in range(8):
                xc[tc, kt] = xTf[kt * 128:(kt + 1) * 128, tc * 512:tc * 512 + 515]
        maps.append({"xT": xc, "w_inT": w_inT,
                     "conv_w_cm": cw, "conv_b_cm": cb, "dt_bias_cm": db,
                     "w_xT": w_xT, "w_dtT": wdt})
    return maps


def _pack_l2(r1, A_log, D_skip):
    def full(key):
        return np.concatenate(
            [np.ascontiguousarray(r[key]).transpose(2, 1, 0, 3).reshape(2048, 2048)
             for r in r1], axis=1)

    dl = full("o_dl")
    du = full("o_du")
    uu = full("o_u")
    gs = full("o_gs")
    bc = np.concatenate(
        [np.ascontiguousarray(r["o_bc"]).transpose(1, 0, 2).reshape(32, 2048) for r in r1], axis=1)
    B, C = bc[0:16], bc[16:32]
    n_of_p = _P // 8
    ci_of_p = _P % 8
    wsel = np.zeros((128, 16, 128), bfd)
    for g16 in range(16):
        wsel[_P, g16, g16 * 8 + ci_of_p] = 1
    maps = []
    for k in range(8):
        ch0 = 256 * k
        gmap = ch0 + 8 * np.arange(32)[:, None] + ci_of_p[None, :]
        drep = dl[gmap]
        durep = du[gmap]
        u_cm = uu[ch0:ch0 + 256].reshape(2, 128, TT)
        gs_cm = gs[ch0:ch0 + 256].reshape(2, 128, TT)
        bdn = B[n_of_p]
        cdn = C[n_of_p]
        alog_dn = np.ascontiguousarray(
            A_log[ch0 + 8 * np.arange(32)[None, :] + ci_of_p[:, None], n_of_p[:, None]]
        ).astype(np.float32)
        wdiag = np.zeros((128, 2, 128), np.float32)
        for ct in range(2):
            wdiag[_P, ct, _P] = D_skip[ch0 + ct * 128 + _P]
        maps.append({"drep": np.ascontiguousarray(drep),
                     "durep": np.ascontiguousarray(durep),
                     "u_cm": np.ascontiguousarray(u_cm),
                     "gs_cm": np.ascontiguousarray(gs_cm),
                     "bdn": np.ascontiguousarray(bdn),
                     "cdn": np.ascontiguousarray(cdn),
                     "alog_dn": alog_dn, "wsel": wsel, "wdiag": wdiag})
    return maps


def _pack_l3(x, r2, W_out, w1, b1, w2, b2, lng, lnb, ln2g, ln2b):
    y2_full = np.concatenate(
        [np.ascontiguousarray(r["y2"]).reshape(256, TT) for r in r2], axis=0)
    woT = np.ascontiguousarray(W_out.T.reshape(16, 128, 1024).transpose(1, 0, 2))
    w1T = np.ascontiguousarray(w1.T.reshape(8, 128, 4096).transpose(1, 0, 2))
    w2l = np.ascontiguousarray(w2.T.reshape(32, 128, 1024).transpose(1, 0, 2))
    w2T = np.ascontiguousarray(w2l.reshape(128, 8, 4096))
    small = {
        "lng": np.ascontiguousarray(lng.reshape(8, 128).T),
        "lnb": np.ascontiguousarray(lnb.reshape(8, 128).T),
        "ln2g": np.ascontiguousarray(ln2g.reshape(32, 128).T),
        "ln2b": np.ascontiguousarray(ln2b.reshape(32, 128).T),
        "b1": np.ascontiguousarray(b1.reshape(32, 128).T),
        "b2": np.ascontiguousarray(b2.reshape(8, 128).T),
        "ident": np.eye(128, dtype=bfd),
        "identf": np.eye(128, dtype=np.float32),
        "onesb": np.ones((128, 1), bfd),
        "onesf": np.ones((128, 1), np.float32),
        "onesrow": np.ones((1, 128), np.float32),
        "cst": np.array([[1e-5, 0, 0, 0, 0, 0, 0, 0]], np.float32),
    }
    maps = []
    for c in range(8):
        b, q = c // 4, c % 4
        t0 = q * 2048
        xT = np.ascontiguousarray(x[b, t0:t0 + 2048].T)
        xT_s = np.ascontiguousarray(xT.reshape(8, 128, 2048).transpose(1, 0, 2))
        y2T = np.ascontiguousarray(
            y2_full[:, c * 2048:(c + 1) * 2048].reshape(16, 128, 2048).transpose(1, 0, 2))
        m = {"xT_s": xT_s, "y2T": y2T, "w_outT_s": woT, "w1T_s": w1T, "w2T_s": w2T}
        m.update(small)
        maps.append(m)
    return maps


_NCS = {}


def _get(name, builder):
    if name not in _NCS:
        _NCS[name] = builder()
    return _NCS[name]


def kernel(L0_attn_hidden_states, L0_attn_W_in, L0_attn_conv_w, L0_attn_conv_b,
           L0_attn_W_x, L0_attn_W_dt, L0_attn_dt_bias, L0_attn_A_log, L0_attn_D_skip,
           L0_attn_W_out, L0_ffn_ln1_g, L0_ffn_ln1_b, L0_ffn_w1, L0_ffn_b1,
           L0_ffn_ln2_g, L0_ffn_ln2_b, L0_ffn_w2, L0_ffn_b2):
    x = np.ascontiguousarray(np.asarray(L0_attn_hidden_states, np.float32))
    m1 = _pack_l1(x, np.asarray(L0_attn_W_in, np.float32),
                  np.asarray(L0_attn_conv_w, np.float32),
                  np.asarray(L0_attn_conv_b, np.float32),
                  np.asarray(L0_attn_W_x, np.float32),
                  np.asarray(L0_attn_W_dt, np.float32),
                  np.asarray(L0_attn_dt_bias, np.float32))
    nc1 = _get("l1", build_l1)
    r1 = run_bass_kernel_spmd(nc1, m1, core_ids=list(range(8))).results

    m2 = _pack_l2(r1, np.asarray(L0_attn_A_log, np.float32),
                  np.asarray(L0_attn_D_skip, np.float32))
    nc2 = _get("l2", build_l2)
    r2 = run_bass_kernel_spmd(nc2, m2, core_ids=list(range(8))).results

    m3 = _pack_l3(x, r2, np.asarray(L0_attn_W_out, np.float32),
                  np.asarray(L0_ffn_w1, np.float32), np.asarray(L0_ffn_b1, np.float32),
                  np.asarray(L0_ffn_w2, np.float32), np.asarray(L0_ffn_b2, np.float32),
                  np.asarray(L0_ffn_ln1_g, np.float32), np.asarray(L0_ffn_ln1_b, np.float32),
                  np.asarray(L0_ffn_ln2_g, np.float32), np.asarray(L0_ffn_ln2_b, np.float32))
    nc3 = _get("l3", build_l3)
    r3 = run_bass_kernel_spmd(nc3, m3, core_ids=list(range(8))).results

    out = np.zeros((2, 8192, 1024), np.float32)
    for c in range(8):
        b, q = c // 4, c % 4
        o = np.asarray(r3[c]["outT_s"], np.float32)
        out[b, q * 2048:(q + 1) * 2048] = o.transpose(2, 1, 0).reshape(2048, 1024)
    return out


# version 05 (final)
# speedup vs baseline: 1.2317x; 1.0026x over previous
import numpy as np
import contextlib
import ml_dtypes
import concourse.bass as bass
import concourse.bass_utils
from concourse.bass_utils import run_bass_kernel_spmd

bfd = ml_dtypes.bfloat16

import numpy as np
import concourse.mybir as mybir

F32 = mybir.dt.float32
BF16 = mybir.dt.bfloat16
AF = mybir.ActivationFunctionType
ALU = mybir.AluOpType


class Pass:
    def __init__(self, dry, sems, events, only=None):
        self.dry = dry
        self.sems = sems
        self.events = events
        self.count = {}
        self.only = only
        self.current = None

    def begin(self, name):
        self.current = name

    def _active(self):
        return not self.dry and (self.only is None or self.only == self.current)

    def emit(self, fn, inc=None, step=1, event=None):
        if inc is not None:
            self.count[inc] = self.count.get(inc, 0) + step
            if event is not None:
                self.events[(inc, event)] = self.count[inc]
        if self._active():
            ins = fn()
            if inc is not None:
                ins.then_inc(self.sems[inc], step)

    def wait(self, eng, sem, key=None, value=None):
        if not self._active():
            return
        v = value if value is not None else self.events[(sem, key)]
        eng.wait_ge(self.sems[sem], v)

    def cur(self, sem):
        return self.count.get(sem, 0)


L1_SEMS = ["x", "xc", "w", "wg", "wsm", "small", "mm", "sc", "v", "stu", "stg", "stdu", "stdl", "stbc"]


def l1_alloc(nc, E, T1=2048):
    TCH = 512
    t = {}
    t["x"] = [E(nc.sbuf_tensor(f"l1_x{i}", [128, 8, TCH + 3], BF16)) for i in range(2)]
    t["win"] = E(nc.sbuf_tensor("l1_win", [128, 8, 4096], BF16))
    t["wx"] = E(nc.sbuf_tensor("l1_wx", [128, 16, 96], BF16))
    t["wdt"] = E(nc.sbuf_tensor("l1_wdt", [64, 16, 128], BF16))
    t["cw"] = E(nc.sbuf_tensor("l1_cw", [128, 16, 4], F32))
    t["cb"] = E(nc.sbuf_tensor("l1_cb", [128, 16], F32))
    t["db"] = E(nc.sbuf_tensor("l1_db", [128, 16], F32))
    t["tail"] = E(nc.sbuf_tensor("l1_tail", [128, 16, 3], F32))
    t["cv"] = [E(nc.sbuf_tensor(f"l1_cv{i}", [128, TCH], F32)) for i in range(6)]
    t["uc"] = [E(nc.sbuf_tensor(f"l1_uc{i}", [128, 16, TCH], BF16)) for i in range(2)]
    t["duc"] = E(nc.sbuf_tensor("l1_duc", [128, 16, TCH], BF16))
    t["dlc"] = E(nc.sbuf_tensor("l1_dlc", [128, 16, TCH], BF16))
    t["gsc"] = E(nc.sbuf_tensor("l1_gsc", [128, 16, TCH], BF16))
    t["dtlr"] = E(nc.sbuf_tensor("l1_dtlr", [64, TCH], BF16))
    t["bcc"] = E(nc.sbuf_tensor("l1_bcc", [32, TCH], BF16))
    t["ab"] = [E(nc.sbuf_tensor(f"l1_ab{i}", [128, TCH], F32)) for i in range(4)]
    t["ps"] = [E(nc.psum_tensor(f"l1_ps{i}", [128, TCH], F32)) for i in range(4)]
    t["ps_pt"] = E(nc.psum_tensor("l1_ps_pt", [128, TCH], F32))
    t["ps_d"] = [E(nc.psum_tensor(f"l1_ps_d{i}", [128, TCH], F32)) for i in range(2)]
    t["ps_g3"] = E(nc.psum_tensor("l1_ps_g3", [128, TCH], F32))
    return t


def _l1_psd(t, ctile):
    return t["ps_d"][ctile % 2]


def _l1_psg(t, ctile):
    return [t["ps"][2], t["ps"][3], t["ps_g3"]][ctile % 3]


def l1_streams(p, t, d, eng, T1=2048):
    TCH = 512
    NCH = T1 // TCH
    NCT = 16
    NKT = 8
    E = eng or {k: None for k in ("sync", "gpsimd", "tensor", "scalar", "vector")}

    def stream_gpsimd(g):
        def xload(tc):
            if tc >= 2:
                p.wait(g, "mm", ("gate", tc - 2, NCT - 1))
            for kt in range(NKT):
                p.emit(lambda tc=tc, kt=kt: g.dma_start(
                    out=t["x"][tc % 2][:, kt, :], in_=d["xT"][tc, kt]),
                    inc="x" if tc % 2 == 0 else "xc", step=16)
        for kt in range(NKT):
            p.emit(lambda kt=kt: g.dma_start(out=t["win"][:, kt, 0:2048], in_=d["w_inT"][kt * 128:(kt + 1) * 128, 0:2048]),
                   inc="w", step=16)
        xload(0)
        for kt in range(NKT):
            p.emit(lambda kt=kt: g.dma_start(out=t["win"][:, kt, 2048:4096], in_=d["w_inT"][kt * 128:(kt + 1) * 128, 2048:4096]),
                   inc="wg", step=16)
        xload(1)
        for kt in range(NCT):
            p.emit(lambda kt=kt: g.dma_start(out=t["wx"][:, kt, :], in_=d["w_xT"][kt * 128:(kt + 1) * 128, :]),
                   inc="wsm", step=16)
        p.emit(lambda: g.dma_start(out=t["wdt"][:, :, :], in_=d["w_dtT"]), inc="wsm", step=16)
        for tc in range(2, NCH):
            xload(tc)

    def stream_sync(s):
        p.emit(lambda: s.dma_start(out=t["cw"][:], in_=d["conv_w_cm"]), inc="small", step=16)
        p.emit(lambda: s.dma_start(out=t["cb"][:], in_=d["conv_b_cm"]), inc="small", step=16)
        p.emit(lambda: s.dma_start(out=t["db"][:], in_=d["dt_bias_cm"]), inc="small", step=16)

        def qst(name, o, tc, q, sem):
            qs = slice(4 * q, 4 * q + 4)
            p.emit(lambda name=name, o=o, tc=tc, qs=qs: s.dma_start(
                out=d[o][tc, :, qs, :], in_=t[name][:, qs, :]), inc=sem, step=16)

        def dd_quarter(tc, q):
            p.wait(s, "v", ("du", tc, 4 * q + 3))
            p.wait(s, "sc", ("dl", tc, 4 * q + 3))
            qst("duc", "o_du", tc, q, "stdu")
            qst("dlc", "o_dl", tc, q, "stdl")
            if q == 0:
                p.wait(s, "sc", ("bcc", tc))
                p.emit(lambda tc=tc: s.dma_start(out=d["o_bc"][tc], in_=t["bcc"][:, :]),
                       inc="stbc", step=16)

        for tc in range(NCH):
            for q in range(4):
                if tc >= 1:
                    dd_quarter(tc - 1, q)
                p.wait(s, "sc", ("u", tc, 4 * q + 3))
                qs = slice(4 * q, 4 * q + 4)
                p.emit(lambda tc=tc, qs=qs: s.dma_start(
                    out=d["o_u"][tc, :, qs, :], in_=t["uc"][tc % 2][:, qs, :]), inc="stu", step=16)
                p.wait(s, "sc", ("gevac", tc, 4 * q + 3))
                qst("gsc", "o_gs", tc, q, "stg")
        for q in range(4):
            dd_quarter(NCH - 1, q)
        for sem in ("stu", "stg", "stdu", "stdl", "stbc"):
            p.wait(s, sem, value=p.cur(sem))

    def stream_tensor(te):
        p.wait(te, "x", value=NKT * 16)
        p.wait(te, "w", value=NKT * 16)
        xwaits = {0: NKT * 16}
        for ctile in range(NCT):
            if ctile >= 1:
                p.wait(te, "sc", ("tail", ctile - 1))
            for kt in range(NKT):
                p.emit(lambda ctile=ctile, kt=kt: te.matmul(
                    t["ps_pt"][:, 0:3], t["win"][:, kt, ctile * 128:(ctile + 1) * 128],
                    t["x"][0][:, kt, 0:3], start=(kt == 0), stop=(kt == NKT - 1)),
                    inc=None if kt < NKT - 1 else "mm",
                    event=None if kt < NKT - 1 else ("halo", ctile))
        def emit_delta(tcd, ctile):
            if tcd == 0 and ctile == 0:
                p.wait(te, "wsm", value=(NCT + 1) * 16)
            if not (tcd == 0 and ctile < 2):
                prev = (tcd, ctile - 2) if ctile >= 2 else (tcd - 1, NCT - 2 + ctile)
                p.wait(te, "sc", ("devac",) + prev)
            p.emit(lambda tcd=tcd, ctile=ctile: te.matmul(
                _l1_psd(t, ctile)[:, :], t["wdt"][:, ctile, :], t["dtlr"][:, :],
                start=True, stop=True),
                inc="mm", event=("delta", tcd, ctile))

        def emit_projt_kt(tc, kt):
            if kt == 0:
                if tc == 0:
                    p.wait(te, "wsm", value=(NCT + 1) * 16)
                    p.wait(te, "sc", ("tail", NCT - 1))
                else:
                    p.wait(te, "sc", ("ptevac", tc - 1))
            p.wait(te, "sc", ("u", tc, kt))
            p.emit(lambda tc=tc, kt=kt: te.matmul(
                t["ps_pt"][0:96, :], t["wx"][:, kt, :], t["uc"][tc % 2][:, kt, :],
                start=(kt == 0), stop=(kt == NCT - 1)),
                inc=None if kt < NCT - 1 else "mm",
                event=None if kt < NCT - 1 else ("projt", tc))

        for tc in range(NCH):
            xb = t["x"][tc % 2]
            p.wait(te, "x" if tc % 2 == 0 else "xc", value=(tc // 2 + 1) * NKT * 16)
            for ctile in range(NCT):
                if not (tc == 0 and ctile < 2):
                    prev = (tc, ctile - 2) if ctile >= 2 else (tc - 1, NCT - 2 + ctile)
                    p.wait(te, "v", ("cpdone",) + prev)
                for kt in range(NKT):
                    p.emit(lambda xb=xb, tc=tc, ctile=ctile, kt=kt: te.matmul(
                        t["ps"][ctile % 2][:, :], t["win"][:, kt, ctile * 128:(ctile + 1) * 128],
                        xb[:, kt, 3:3 + TCH], start=(kt == 0), stop=(kt == NKT - 1)),
                        inc=None if kt < NKT - 1 else "mm",
                        event=None if kt < NKT - 1 else ("proj", tc, ctile))
                if tc >= 1:
                    emit_delta(tc - 1, ctile)
                j = 2048 + ctile * 128
                if tc == 0 and ctile == 0:
                    p.wait(te, "wg", value=NKT * 16)
                if not (tc == 0 and ctile < 3):
                    prev = (tc, ctile - 3) if ctile >= 3 else (tc - 1, NCT - 3 + ctile)
                    p.wait(te, "sc", ("gevac",) + prev)
                for kt in range(NKT):
                    p.emit(lambda xb=xb, tc=tc, ctile=ctile, kt=kt, j=j: te.matmul(
                        _l1_psg(t, ctile)[:, :], t["win"][:, kt, j:j + 128],
                        xb[:, kt, 3:3 + TCH], start=(kt == 0), stop=(kt == NKT - 1)),
                        inc=None if kt < NKT - 1 else "mm",
                        event=None if kt < NKT - 1 else ("gate", tc, ctile))
                if ctile >= 4:
                    emit_projt_kt(tc, ctile - 4)
            for kt in range(NCT - 4, NCT):
                emit_projt_kt(tc, kt)
            p.wait(te, "sc", ("dtlr", tc))
        for ctile in range(NCT):
            emit_delta(NCH - 1, ctile)

    def stream_scalar(sc):
        p.wait(sc, "small", value=48)
        for ctile in range(NCT):
            p.wait(sc, "mm", ("halo", ctile))
            p.emit(lambda ctile=ctile: sc.copy(out=t["tail"][:, ctile, :], in_=t["ps_pt"][:, 0:3]),
                   inc="sc", event=("tail", ctile))
        def emit_sp_exp(sc, tcd, ctile):
            ab = t["ab"][ctile % 4]
            p.wait(sc, "mm", ("delta", tcd, ctile))
            if ctile == 0 and tcd >= 1:
                p.wait(sc, "stdl", value=64 * tcd)
            p.emit(lambda ab=ab, ctile=ctile: sc.activation(
                out=ab[:], in_=_l1_psd(t, ctile)[:, :], func=AF.Exp,
                bias=t["db"][:, ctile:ctile + 1]),
                inc="sc", event=("devac", tcd, ctile))

        def emit_sp_ln(sc, tcd, ctile):
            ab = t["ab"][ctile % 4]
            p.wait(sc, "sc", value=p.cur("sc"))
            p.emit(lambda ab=ab, ctile=ctile: sc.activation(
                out=t["dlc"][:, ctile, :], in_=ab[:], func=AF.Ln, bias=1.0),
                inc="sc", event=("dl", tcd, ctile))

        def emit_softplus(sc, tcd, ctile):
            emit_sp_exp(sc, tcd, ctile)
            emit_sp_ln(sc, tcd, ctile)

        for tc in range(NCH):
            def emit_u(tc, ctile):
                p.wait(sc, "v", ("cv", tc, ctile))
                if tc >= 2 and ctile == 0:
                    p.wait(sc, "v", ("duall", tc - 2))
                    p.wait(sc, "stu", value=64 * (tc - 1))
                p.emit(lambda tc=tc, ctile=ctile: sc.activation(
                    out=t["uc"][tc % 2][:, ctile, :], in_=t["cv"][ctile % 6][:], func=AF.Silu,
                    bias=t["cb"][:, ctile:ctile + 1]),
                    inc="sc", event=("u", tc, ctile))

            def emit_gevac(tc, ctile):
                p.wait(sc, "mm", ("gate", tc, ctile))
                if tc >= 1 and ctile == 0:
                    p.wait(sc, "stg", value=64 * tc)
                p.emit(lambda tc=tc, ctile=ctile: sc.activation(
                    out=t["gsc"][:, ctile, :], in_=_l1_psg(t, ctile)[:, :], func=AF.Silu),
                    inc="sc", event=("gevac", tc, ctile))

            for g in range(NCT // 2):
                c0, c1 = 2 * g, 2 * g + 1
                if g >= 1:
                    emit_u(tc, c0 - 2)
                    emit_u(tc, c1 - 2)
                    emit_gevac(tc, c0 - 2)
                    emit_gevac(tc, c1 - 2)
                if tc >= 1:
                    emit_sp_exp(sc, tc - 1, c0)
                    emit_sp_exp(sc, tc - 1, c1)
                    emit_sp_ln(sc, tc - 1, c0)
                    emit_sp_ln(sc, tc - 1, c1)
            emit_u(tc, NCT - 2)
            emit_u(tc, NCT - 1)
            p.events[("sc", ("uc", tc))] = p.cur("sc")
            emit_gevac(tc, NCT - 2)
            emit_gevac(tc, NCT - 1)
            p.events[("sc", ("gs", tc, NCT - 1))] = p.cur("sc")
            p.wait(sc, "mm", ("projt", tc))
            if tc >= 1:
                p.wait(sc, "mm", ("delta", tc - 1, NCT - 1))
            p.emit(lambda: sc.copy(out=t["dtlr"][:, :], in_=t["ps_pt"][0:64, :]),
                   inc="sc", event=("dtlr", tc))
            if tc >= 1:
                p.wait(sc, "stbc", value=16 * tc)
            p.emit(lambda: sc.copy(out=t["bcc"][:, :], in_=t["ps_pt"][64:96, :]),
                   inc="sc", event=("ptevac", tc))
            p.events[("sc", ("bcc", tc))] = p.cur("sc")
        for ctile in range(NCT):
            emit_softplus(sc, NCH - 1, ctile)

    def stream_vector(v):
        def emit_dldu(v, tcd, ctile):
            p.wait(v, "sc", ("dl", tcd, ctile))
            p.wait(v, "sc", ("u", tcd, ctile))
            if ctile == 0 and tcd >= 1:
                p.wait(v, "stdu", value=64 * tcd)
            p.emit(lambda tcd=tcd, ctile=ctile: v.tensor_tensor(
                out=t["duc"][:, ctile, :], in0=t["dlc"][:, ctile, :],
                in1=t["uc"][tcd % 2][:, ctile, :], op=ALU.mult),
                inc="v", event=("du", tcd, ctile))
            if ctile == NCT - 1:
                p.events[("v", ("duall", tcd))] = p.cur("v")

        for tc in range(NCH):
            for ctile in range(NCT):
                cv = t["cv"][ctile % 6]
                ps = t["ps"][ctile % 2]
                p.wait(v, "mm", ("proj", tc, ctile))
                if tc == 0:
                    p.wait(v, "sc", ("tail", ctile))
                if not (tc == 0 and ctile < 6):
                    prev = (tc, ctile - 6) if ctile >= 6 else (tc - 1, NCT - 6 + ctile)
                    p.wait(v, "sc", ("u",) + prev)
                p.emit(lambda cv=cv, ps=ps, ctile=ctile: v.tensor_scalar(
                    out=cv[:], in0=ps[:, 0:TCH], scalar1=t["cw"][:, ctile, 3:4],
                    scalar2=None, op0=ALU.mult), inc="v")
                for k in (2, 1, 0):
                    p.wait(v, "v", value=p.cur("v"))
                    p.emit(lambda cv=cv, ps=ps, ctile=ctile, k=k: v.scalar_tensor_tensor(
                        out=cv[:, 3 - k:TCH], in0=ps[:, 0:TCH - 3 + k],
                        scalar=t["cw"][:, ctile, k:k + 1],
                        in1=cv[:, 3 - k:TCH], op0=ALU.mult, op1=ALU.add), inc="v")
                    p.wait(v, "v", value=p.cur("v"))
                    p.emit(lambda cv=cv, ctile=ctile, k=k: v.scalar_tensor_tensor(
                        out=cv[:, 0:3 - k], in0=t["tail"][:, ctile, k:3],
                        scalar=t["cw"][:, ctile, k:k + 1],
                        in1=cv[:, 0:3 - k], op0=ALU.mult, op1=ALU.add),
                        inc="v", event=None if k > 0 else ("cv", tc, ctile))
                if tc < NCH - 1:
                    p.wait(v, "v", value=p.cur("v"))
                    p.emit(lambda ps=ps, ctile=ctile: v.tensor_copy(
                        out=t["tail"][:, ctile, :], in_=ps[:, TCH - 3:TCH]),
                        inc="v")
                p.events[("v", ("cpdone", tc, ctile))] = p.cur("v")
                if tc >= 1:
                    emit_dldu(v, tc - 1, ctile)
        for ctile in range(NCT):
            emit_dldu(v, NCH - 1, ctile)

    p.begin("gpsimd"); stream_gpsimd(E["gpsimd"])
    p.begin("sync"); stream_sync(E["sync"])
    p.begin("tensor"); stream_tensor(E["tensor"])
    p.begin("scalar"); stream_scalar(E["scalar"])
    p.begin("vector"); stream_vector(E["vector"])


L3_SEMS = ["ld", "w3", "mm3", "sc3", "v3", "st3", "q3"]


def l3_alloc(nc, E, T3=2048):
    TCH = 512
    t = {}
    t["wbuf"] = E(nc.sbuf_tensor("l3_wbuf", [128, 32, 1024], BF16))
    t["wq"] = E(nc.sbuf_tensor("l3_wq", [128, 4096], BF16))
    t["wf"] = [E(nc.sbuf_tensor(f"l3_wf{i}", [128, 4096], F32)) for i in range(2)]
    t["x"] = E(nc.sbuf_tensor("l3_x", [128, 8, TCH], BF16))
    t["y2"] = E(nc.sbuf_tensor("l3_y2", [128, 16, TCH], BF16))
    t["x1"] = E(nc.sbuf_tensor("l3_x1", [128, 8, TCH], BF16))
    t["rbf"] = E(nc.sbuf_tensor("l3_rbf", [128, 8, TCH], BF16))
    t["q1"] = E(nc.sbuf_tensor("l3_q1", [128, 8, TCH], BF16))
    t["g1"] = E(nc.sbuf_tensor("l3_g1", [128, 32, TCH], BF16))
    t["sq"] = E(nc.sbuf_tensor("l3_sq", [128, TCH], BF16))
    t["v"] = [E(nc.sbuf_tensor(f"l3_v{i}", [128, TCH], F32)) for i in range(2)]
    t["sg"] = [E(nc.sbuf_tensor(f"l3_sg{i}", [128, TCH], BF16)) for i in range(2)]
    t["qiB"] = E(nc.sbuf_tensor("l3_qiB", [128, 4, TCH], mybir.dt.int32))
    t["mskB"] = E(nc.sbuf_tensor("l3_mskB", [128, 4, TCH], BF16))
    t["qi"] = [E(nc.sbuf_tensor(f"l3_qi{i}", [128, TCH], mybir.dt.int32)) for i in range(2)]
    t["r3"] = E(nc.sbuf_tensor("l3_r3", [128, 8, TCH], F32))
    t["out"] = E(nc.sbuf_tensor("l3_out", [128, 8, TCH], F32))
    t["stat"] = E(nc.sbuf_tensor("l3_stat", [1, TCH * 8], F32))
    t["smax"] = E(nc.sbuf_tensor("l3_smax", [128, 8], F32))
    t["sflat"] = E(nc.sbuf_tensor("l3_sflat", [4, 128 * 4], F32))
    t["sbc"] = E(nc.sbuf_tensor("l3_sbc", [128, TCH], BF16))
    t["rbc"] = E(nc.sbuf_tensor("l3_rbc", [128, TCH], BF16))
    t["ones"] = E(nc.sbuf_tensor("l3_ones", [128, 2], BF16))
    t["onesf"] = E(nc.sbuf_tensor("l3_onesf", [128, 2], F32))
    t["lng"] = E(nc.sbuf_tensor("l3_lng", [128, 8], F32))
    t["lnb"] = E(nc.sbuf_tensor("l3_lnb", [128, 8], F32))
    t["ln2g"] = E(nc.sbuf_tensor("l3_ln2g", [128, 32], F32))
    t["ln2b"] = E(nc.sbuf_tensor("l3_ln2b", [128, 32], F32))
    t["b1"] = E(nc.sbuf_tensor("l3_b1", [128, 32], F32))
    t["b2"] = E(nc.sbuf_tensor("l3_b2", [128, 8], F32))
    t["swc"] = E(nc.sbuf_tensor("l3_swc", [128, 4], F32))
    t["red"] = E(nc.sbuf_tensor("l3_red", [128, 8], F32))
    t["scl"] = E(nc.sbuf_tensor("l3_scl", [1, 64], F32))
    t["cst"] = E(nc.sbuf_tensor("l3_cst", [1, 8], F32))
    t["ps"] = [E(nc.psum_tensor(f"l3_ps{i}", [128, TCH], F32)) for i in range(8)]
    t["pss"] = E(nc.psum_tensor("l3_pss", [1, 2 * TCH], F32))
    t["psb"] = [E(nc.psum_tensor(f"l3_psb{i}", [128, TCH], F32)) for i in range(0)]
    return t



def rep0(ap, n):
    return type(ap)(tensor=ap.tensor, offset=ap.offset,
                    ap=[[ap.ap[0][0], ap.ap[0][1]], [0, n], [list(ap.ap[1])[0], list(ap.ap[1])[1]]])


def l3_streams(p, t, d, eng, T3=2048):
    TCH = 512
    NCH = T3 // TCH
    E = eng or {}
    ps = t["ps"]
    W = t["wbuf"]
    NJT, NHT = 8, 32
    g1f = t["g1"].ap().rearrange("p a b -> p (a b)")
    g1w = t["g1"].ap().rearrange("p a b -> p (a b)").bitcast(F32)
    wqa = [lambda: g1w[:, 0:4096], lambda: g1w[:, 4096:8192]]
    _q1f = lambda: t["q1"].ap().rearrange("p a b -> p (a b)")
    _rbff = lambda: t["rbf"].ap().rearrange("p a b -> p (a b)")
    qc1s = [_q1f, _rbff]
    qc2 = lambda: t["x1"][0].ap().rearrange("p a b -> p (a b)")
    LN_EPS = 1e-5
    w1slot = W.ap()[:, 16:32, :].rearrange("p a b -> p (a b)").rearrange("p (c e) -> p c e", c=8)
    w2slot = W.ap()[:, 16:32, :]

    def xs(tc):
        return t["x"][:, :, :]

    def y2s(tc):
        return t["y2"][:, :, :]

    def stream_gpsimd(g):
        for i in range(32):
            rnd, wi, kt = i // 16, (i % 16) // 8, i % 8
            wsrc = "w1T_s" if wi == 0 else "w2T_s"
            if i >= 2:
                p.wait(g, "v3", ("wfree", i - 2))
            p.emit(lambda wsrc=wsrc, kt=kt, i=i: g.dma_start(
                out=wqa[i % 2](), in_=d[wsrc][:, kt, :]),
                inc="q%d" % (i % 2), step=16, event=("wcast", i))
        p.emit(lambda: g.dma_start(out=W[:, 0:16, :], in_=d["w_outT_s"]),
               inc="qo", step=16, event=("wout",))
        def outst(tc):
            p.wait(g, "v3", ("outj", tc, NJT - 1))
            for jt in range(NJT):
                p.emit(lambda tc=tc, jt=jt: g.dma_start(
                    out=d["outT_s"][:, jt, tc * TCH:(tc + 1) * TCH], in_=t["r3"][:, jt, :]),
                    inc="st3", step=16,
                    event=("outst", tc, jt) if jt == NJT - 1 else None)

        for tc in range(NCH):
            if tc >= 1:
                p.wait(g, "v3", ("Aev", tc - 1, NJT - 1))
            p.emit(lambda tc=tc: g.dma_start(out=xs(tc), in_=d["xT_s"][:, :, tc * TCH:(tc + 1) * TCH]),
                   inc="qx", step=16, event=("xc", tc))
            if tc >= 2:
                outst(tc - 2)
        if NCH >= 2:
            outst(NCH - 2)
        outst(NCH - 1)
        p.wait(g, "st3", value=p.cur("st3"))

    def stream_sync(s):
        for nm in ("lng", "lnb", "ln2g", "ln2b", "b1", "b2", "identf", "onesb", "onesf", "onesrow", "cst"):
            p.emit(lambda nm=nm: s.dma_start(out=t[nm][:], in_=d[nm]), inc="ls", step=16)
        for i in range(16):
            wi, kt = i // 8, i % 8
            dst = "w1q_d" if wi == 0 else "w2q_d"
            qc1 = qc1s[i % 2]
            p.wait(s, "v3", ("wq", i + 16))
            p.emit(lambda dst=dst, kt=kt, qc1=qc1: s.dma_start(out=d[dst][:, kt, :], in_=qc1()),
                   inc="sw%d" % (i % 2), step=16, event=("wqst", i))
        p.emit(lambda: s.dma_start(out=y2s(0), in_=d["y2T"][:, :, 0:TCH]),
               inc="ly", step=16, event=("y2c", 0))
        def sxd(tcx, which, flev, rowev):
            p.wait(s, "v3", flev)
            p.wait(s, "mm3", rowev)
            p.emit(lambda: s.dma_start(out=t["stat"][0:1, 0:2 * TCH].rearrange("p (a b) -> p a b", a=8),
                                       in_=t["sflat"][0:8, 0:128]),
                   inc="sx", step=16, event=("sxd", tcx, which))

        sxd(0, 1, ("sfl", 0, 1), ("m20b", 0))
        for tc in range(NCH):
            if tc + 1 < NCH:
                p.wait(s, "mm3", ("Adone", tc))
                p.emit(lambda tc=tc: s.dma_start(out=y2s(tc + 1),
                                                 in_=d["y2T"][:, :, (tc + 1) * TCH:(tc + 2) * TCH]),
                       inc="ly", step=16, event=("y2c", tc + 1))
            for h in range(2):
                p.wait(s, "sw0", value=8 * 16)
                p.wait(s, "sw1", value=8 * 16)
                if not (tc == 0 and h == 0):
                    p.wait(s, "mm3", ("slotfree", tc, h))
                p.emit(lambda h=h: s.dma_start(out=w1slot, in_=d["w1q_d"][:, :, h * 2048:(h + 1) * 2048]),
                       inc="lw", step=16, event=("wslot", tc, h))
            if tc + 1 < NCH:
                sxd(tc + 1, 1, ("sfl", tc + 1, 1), ("m20b", tc + 1))
            sxd(tc, 2, ("sfl", tc, 2), ("m22b", tc))
            for h in range(2, 4):
                p.wait(s, "mm3", ("slotfree", tc, h))
                p.emit(lambda h=h: s.dma_start(out=w2slot, in_=d["w2q_d"].rearrange(
                    "p a b -> p (a b)").rearrange("p (c e) -> p c e", c=32)[:, (h - 2) * 16:(h - 1) * 16, :]),
                    inc="lw", step=16, event=("wslot", tc, h))
        pass

    def bc(te_, row_fn, out_ps, ev=None):
        p.emit(lambda: te_.matmul(out_ps, t["onesrow"][0:1, :], row_fn(), start=True, stop=True),
               inc="mm3", event=ev)

    S = TCH
    x1b = lambda tcn: t["x1"][tcn % 2]
    r3i = lambda: t["r3"].ap().rearrange("p a b -> p (a b)").bitcast(mybir.dt.int32).rearrange(
        "p (c e) -> p c e", c=8)
    r3h = lambda: t["r3"].ap().rearrange("p a b -> p (a b)").bitcast(BF16).rearrange(
        "p (c e) -> p c e", c=16)
    qi1 = lambda jt: r3i()[:, jt % 2, :]
    qiB = lambda: r3i()[:, 0:4, :]
    vbF = lambda b: r3h()[:, 8 + 4 * (b % 2):12 + 4 * (b % 2), :]

    def cbank(jt2):
        return jt2 % 2 if jt2 < 16 else (0, 1, 4, 5)[jt2 % 4]

    def stream_tensor(te_):
        p.wait(te_, "ls", value=11 * 16)
        for wi in range(2):
            p.wait(te_, "v3", ("wred", wi * 8 + 7))
            p.emit(lambda wi=wi: te_.matmul(ps[6][0:1, wi * 8:wi * 8 + 8], t["onesf"][:, 0:1],
                                            t["red"][:, wi * 8:wi * 8 + 8], start=True, stop=True),
                   inc="mm3", event=("wtot", wi))
        p.wait(te_, "v3", ("thr",))
        bc(te_, lambda: t["scl"][0:1, 0:6], ps[7][:, 0:6], ev=("swcb",))

        def emit_A_jt(tcn, jt):
            if jt == 0:
                p.wait(te_, "qo", ("wout",))
                p.wait(te_, "ly", ("y2c", tcn))
                if tcn >= 2:
                    p.wait(te_, "v3", ("dv3", tcn - 2))
            if jt >= 2:
                p.wait(te_, "v3", ("Aev", tcn, jt - 2))
            for kt in range(16):
                p.emit(lambda tcn=tcn, jt=jt, kt=kt: te_.matmul(
                    ps[4 + jt % 2][:, :], W[:, kt, jt * 128:(jt + 1) * 128],
                    y2s(tcn)[:, kt, :], start=(kt == 0), stop=(kt == 15)),
                    inc=None if kt < 15 else "mm3",
                    event=None if kt < 15 else ("A", tcn, jt))
            if jt == NJT - 1:
                p.events[("mm3", ("Adone", tcn))] = p.cur("mm3")

        def emit_Bsum(tcn, jt):
            p.wait(te_, "v3", ("Aev", tcn, jt))
            p.wait(te_, "sc3", ("sq0", tcn, jt))
            if jt == 0:
                p.wait(te_, "v3", ("swcev",))
                if tcn >= 2:
                    p.wait(te_, "v3", ("stat3", tcn - 2))
            p.emit(lambda tcn=tcn, jt=jt: te_.matmul(ps[6][0:1, :], t["onesb"][:, 0:1],
                                                     t["rbf"][:, jt, :], start=(jt == 0), stop=(jt == 7)),
                   inc="mm3")
            p.emit(lambda tcn=tcn, jt=jt: te_.matmul(ps[7][0:1, :], t["onesb"][:, 0:1],
                                                     t["sqA"][:, 0, :], start=(jt == 0), stop=(jt == 7)),
                   inc="mm3", event=("Bsum", tcn, jt))

        def emit_front_mid(tcn):
            p.wait(te_, "v3", ("stat0", tcn))
            bc(te_, lambda: t["stat"][0:1, 0:S], ps[6][:, :], ev=("rs0b", tcn))
            bc(te_, lambda: t["stat"][0:1, S:2 * S], ps[7][:, :], ev=("m20b", tcn))

        def emit_front_mxT(tcn):
            p.wait(te_, "v3", ("mx1", tcn))
            p.wait(te_, "v3", ("x1", tcn, NJT - 1))
            for tb in range(4):
                p.emit(lambda tb=tb: te_.transpose(ps[6][:, tb * 128:(tb + 1) * 128],
                                                   t["v"][0][:, tb * 128:(tb + 1) * 128], t["identf"][:]),
                       inc=None if tb < 3 else "mm3", event=None if tb < 3 else ("mxT1", tcn))

        def emit_front_sT(tcn):
            p.wait(te_, "v3", ("s1c", tcn))
            p.emit(lambda: te_.transpose(ps[7][0:8, 0:128], t["smax"][:, 0:8], t["identf"][:]),
                   inc="mm3", event=("sT1", tcn))

        def emit_front_sb(tcn):
            p.wait(te_, "sx", ("sxd", tcn, 1))
            bc(te_, lambda: t["stat"][0:1, 0:S], ps[6][:, :], ev=("sb1", tcn))
            bc(te_, lambda: t["stat"][0:1, S:2 * S], ps[7][:, :], ev=("rb1", tcn))

        for jt in range(2):
            emit_A_jt(0, jt)
        for jt in range(2, NJT):
            emit_A_jt(0, jt)
            emit_Bsum(0, jt - 2)
        emit_Bsum(0, 6)
        emit_Bsum(0, 7)
        emit_front_mid(0)
        emit_front_mxT(0)
        emit_front_sT(0)
        emit_front_sb(0)

        for tc in range(NCH):
            nxt = tc + 1 < NCH

            def dsum(j):
                p.wait(te_, "sc3", ("gelu", tc, j))
                p.wait(te_, "sc3", ("sq2", tc, j))
                if j == 0 and tc >= 1:
                    p.wait(te_, "v3", ("gev", tc - 1, 3))
                p.emit(lambda j=j: te_.matmul(ps[2][0:1, :], t["onesb"][:, 1:2],
                                              t["g1"][:, j, :], start=(j == 0), stop=(j == 31)),
                       inc="mm3")
                p.emit(lambda j=j: te_.matmul(ps[3][0:1, :], t["onesb"][:, 1:2],
                                              t["sq"][:, j % 2, :], start=(j == 0), stop=(j == 31)),
                       inc="mm3", event=("Dsum", tc, j))

            p.wait(te_, "lw", ("wslot", tc, 0))
            p.wait(te_, "v3", ("q1", tc))
            for jt2 in range(NHT):
                if jt2 == 16:
                    p.wait(te_, "lw", ("wslot", tc, 1))
                if (tc, jt2) not in ((0, 0), (0, 1)):
                    if jt2 < 2:
                        p.wait(te_, "v3", ("gev", tc - 1, jt2))
                    elif jt2 < 18:
                        p.wait(te_, "sc3", ("gelu", tc, jt2 - 2))
                    else:
                        p.wait(te_, "sc3", ("gelu", tc, jt2 - 4))
                if jt2 == 16:
                    if nxt:
                        p.wait(te_, "v3", ("Aev", tc + 1, NJT - 1))
                    elif tc >= 1:
                        p.wait(te_, "v3", ("dv3", tc - 1))
                for kt in range(NJT):
                    p.emit(lambda jt2=jt2, kt=kt: te_.matmul(
                        ps[cbank(jt2)][:, :], w1slot[:, kt, (jt2 % 16) * 128:(jt2 % 16) * 128 + 128],
                        t["q1"][:, kt, :], start=(kt == 0), stop=(kt == 7)),
                        inc=None if kt < 7 else "mm3",
                        event=None if kt < 7 else ("C", tc, jt2))
                if jt2 == 15:
                    p.events[("mm3", ("slotfree", tc, 1))] = p.cur("mm3")
                if jt2 == 31:
                    p.events[("mm3", ("slotfree", tc, 2))] = p.cur("mm3")
                if jt2 >= 4:
                    dsum(jt2 - 4)
                if nxt:
                    if jt2 % 2 == 1 and jt2 // 2 < NJT:
                        emit_A_jt(tc + 1, jt2 // 2)
                    if jt2 % 2 == 0 and 4 <= jt2 <= 18:
                        emit_Bsum(tc + 1, (jt2 - 4) // 2)
                    if jt2 == 21:
                        emit_front_mid(tc + 1)
                    if jt2 == 26:
                        emit_front_mxT(tc + 1)
                    if jt2 == 28:
                        emit_front_sT(tc + 1)
                    if jt2 == 30:
                        emit_front_sb(tc + 1)
            for j in range(NHT - 4, NHT):
                dsum(j)
            p.wait(te_, "v3", ("stat2", tc))
            if nxt:
                p.wait(te_, "v3", ("Aev", tc + 1, NJT - 1))
            bc(te_, lambda: t["stat"][0:1, 0:S], ps[4][:, :], ev=("rs2b", tc))
            bc(te_, lambda: t["stat"][0:1, S:2 * S], ps[5][:, :], ev=("m22b", tc))
            p.wait(te_, "v3", ("mx2", tc))
            p.wait(te_, "v3", ("dv0", tc))
            for tb in range(4):
                p.emit(lambda tb=tb: te_.transpose(ps[4][:, tb * 128:(tb + 1) * 128],
                                                   t["v"][0][:, tb * 128:(tb + 1) * 128], t["identf"][:]),
                       inc=None if tb < 3 else "mm3", event=None if tb < 3 else ("mxT2", tc))
            p.wait(te_, "v3", ("s2c", tc))
            p.wait(te_, "v3", ("dv1", tc))
            p.emit(lambda: te_.transpose(ps[5][0:8, 0:128], t["smax"][:, 0:8], t["identf"][:]),
                   inc="mm3", event=("sT2", tc))
            p.wait(te_, "sx", ("sxd", tc, 2))
            bc(te_, lambda: t["stat"][0:1, 0:S], ps[4][:, :], ev=("sb2", tc))
            bc(te_, lambda: t["stat"][0:1, S:2 * S], ps[5][:, :], ev=("rb2", tc))
            p.wait(te_, "lw", ("wslot", tc, 2))
            p.wait(te_, "v3", ("sbc2ev", tc))
            p.wait(te_, "sc3", ("gelu", tc, NHT - 1))
            if nxt:
                p.wait(te_, "v3", ("sbc1ev", tc + 1))
            for kt2 in range(NHT):
                if kt2 == 16:
                    p.wait(te_, "lw", ("wslot", tc, 3))
                if kt2 % 4 == 0:
                    p.wait(te_, "v3", ("q2b", tc, kt2 // 4))
                half = ((kt2 // 4) % 2) * 4
                for jt in range(NJT):
                    p.emit(lambda kt2=kt2, jt=jt, half=half: te_.matmul(
                        ps[jt][:, :], w2slot[:, kt2 % 16, jt * 128:(jt + 1) * 128],
                        t["q2"][:, half + kt2 % 4, :], start=(kt2 == 0), stop=(kt2 == 31)),
                        inc=None if jt < 7 else "mm3",
                        event=None if jt < 7 else ("F", tc, kt2))
                if kt2 == 15:
                    p.events[("mm3", ("slotfree", tc, 3))] = p.cur("mm3")
                if kt2 == 31:
                    p.events[("mm3", ("slotfree", tc + 1, 0))] = p.cur("mm3")
            p.wait(te_, "v3", ("gev", tc, NJT - 1))
            p.wait(te_, "sc3", ("sq3", tc, NJT - 1))
            for jt in range(NJT):
                p.emit(lambda jt=jt: te_.matmul(ps[6][0:1, :], t["onesb"][:, 0:1],
                                                t["rbf"][:, jt, :], start=(jt == 0), stop=(jt == 7)),
                       inc="mm3")
                p.emit(lambda jt=jt: te_.matmul(ps[7][0:1, :], t["onesb"][:, 0:1],
                                                t["q2"][:, jt, :], start=(jt == 0), stop=(jt == 7)),
                       inc="mm3", event=("Gsum", tc, jt))
            p.wait(te_, "v3", ("stat3", tc))
            bc(te_, lambda: t["stat"][0:1, 0:S], ps[4][:, :], ev=("rs3b", tc))
            bc(te_, lambda: t["stat"][0:1, S:2 * S], ps[5][:, :], ev=("m23b", tc))

    def stream_scalar(sc_):
        p.wait(sc_, "ls", value=11 * 16)

        def sq0(tcn, jt):
            p.wait(sc_, "v3", ("Aev", tcn, jt))
            if jt >= 1:
                p.wait(sc_, "mm3", ("Bsum", tcn, jt - 1))
            p.emit(lambda tcn=tcn, jt=jt: sc_.activation(out=t["sqA"][:, 0, :],
                                                         in_=t["rbf"][:, jt, :], func=AF.Square),
                   inc="sc3", event=("sq0", tcn, jt))

        def sd(tcn, varev, ev, rowfree):
            p.wait(sc_, "v3", varev)
            p.emit(lambda: sc_.activation(out=t["stat"][0:1, 2 * S:3 * S],
                                          in_=t["stat"][0:1, 4 * S:5 * S],
                                          func=AF.Sqrt, bias=t["cst"][0:1, 0:1]),
                   inc="sc3", event=ev)

        for jt in range(NJT):
            sq0(0, jt)
        sd(0, ("var0", 0), ("sd0", 0), None)

        for tc in range(NCH):
            nxt = tc + 1 < NCH
            for jt2 in range(NHT):
                p.wait(sc_, "mm3", ("C", tc, jt2))
                p.emit(lambda jt2=jt2: sc_.activation(
                    out=t["g1"][:, jt2, :], in_=ps[cbank(jt2)][:, :], func=AF.Gelu,
                    bias=t["b1"][:, jt2:jt2 + 1]),
                    inc="sc3", event=("gelu", tc, jt2))
                if jt2 >= 2:
                    p.wait(sc_, "mm3", ("Dsum", tc, jt2 - 2))
                p.wait(sc_, "sc3", value=p.cur("sc3"))
                p.emit(lambda jt2=jt2: sc_.activation(out=t["sq"][:, jt2 % 2, :],
                                                      in_=t["g1"][:, jt2, :], func=AF.Square),
                       inc="sc3", event=("sq2", tc, jt2))
                if nxt:
                    if jt2 % 2 == 1 and jt2 // 2 < NJT:
                        sq0(tc + 1, jt2 // 2)
                    if jt2 == 19:
                        sd(tc + 1, ("var0", tc + 1), ("sd0", tc + 1),
                           ("rb1", 0) if tc == 0 else ("m23b", tc - 1))
            sd(tc, ("var2", tc), ("sd2", tc), ("rb1", tc + 1) if nxt else ("rb1", tc))
            p.wait(sc_, "v3", ("Gap", tc, NJT - 1))
            if nxt:
                p.wait(sc_, "v3", ("q1", tc + 1))
            if tc >= 1:
                p.wait(sc_, "mm3", ("Gsum", tc - 1, NJT - 1))
            p.emit(lambda: sc_.copy(out=t["rbf"][:, :, :], in_=t["r3"][:, :, :]),
                   inc="sc3", event=("r3bf", tc, NJT - 1))
            p.wait(sc_, "sc3", value=p.cur("sc3"))
            p.emit(lambda: sc_.activation(out=t["q2"][:, :, :], in_=t["rbf"][:, :, :],
                                          func=AF.Square),
                   inc="sc3", event=("sq3", tc, NJT - 1))
            sd(tc, ("var3", tc), ("sd3", tc), ("rb2", tc))

    def stream_vector(v_):
        def vw(val=None):
            p.wait(v_, "v3", value=val if val is not None else p.cur("v3"))

        for i in range(16):
            wi, kt = i // 8, i % 8
            p.wait(v_, "q%d" % (i % 2), ("wcast", i))
            p.emit(lambda i=i, wi=wi, kt=kt: v_.reduce_sum(
                out=t["red"][:, wi * 8 + kt:wi * 8 + kt + 1], in_=wqa[i % 2](),
                axis=mybir.AxisListType.X, apply_absolute_value=True),
                inc="v3", event=("wred", i))
            p.events[("v3", ("wfree", i))] = p.cur("v3")
        for wi in range(2):
            p.wait(v_, "mm3", ("wtot", wi))
            p.emit(lambda wi=wi: v_.reduce_sum(out=t["scl"][0:1, 16 + wi:17 + wi],
                                               in_=ps[6][0:1, wi * 8:wi * 8 + 8],
                                               axis=mybir.AxisListType.X), inc="v3")
            vw()
            N = 1024 * 4096.0
            p.emit(lambda wi=wi, N=N: v_.tensor_scalar(
                out=t["scl"][0:1, 4 + wi:5 + wi], in0=t["scl"][0:1, 16 + wi:17 + wi],
                scalar1=1.0 / N, scalar2=1e-5, op0=ALU.mult, op1=ALU.max), inc="v3")
            vw()
            p.emit(lambda wi=wi: v_.tensor_scalar_mul(
                out=t["scl"][0:1, 0 + wi:1 + wi], in0=t["scl"][0:1, 4 + wi:5 + wi],
                scalar1=0.5), inc="v3")
            vw()
            p.emit(lambda wi=wi: v_.tensor_scalar_mul(
                out=t["scl"][0:1, 2 + wi:3 + wi], in0=t["scl"][0:1, 0 + wi:1 + wi],
                scalar1=-1.0), inc="v3", event=("thr",) if wi == 1 else None)
        p.wait(v_, "mm3", ("swcb",))
        p.emit(lambda: v_.tensor_copy(out=t["swc"][:, 0:6], in_=ps[7][:, 0:6]),
               inc="v3", event=("swcev",))
        for i in range(16, 32):
            wi, kt = (i - 16) // 8, (i - 16) % 8
            qc1 = qc1s[i % 2]
            p.wait(v_, "q%d" % (i % 2), ("wcast", i))
            if i >= 18:
                p.wait(v_, "sw%d" % (i % 2), ("wqst", i - 18))
            vw()
            p.emit(lambda i=i, wi=wi, qc1=qc1: v_.tensor_scalar(
                out=qc1(), in0=wqa[i % 2](), scalar1=t["swc"][:, 0 + wi:1 + wi],
                scalar2=None, op0=ALU.is_ge), inc="v3")
            vw()
            p.emit(lambda i=i, wi=wi: v_.tensor_scalar(
                out=qc2(), in0=wqa[i % 2](), scalar1=t["swc"][:, 2 + wi:3 + wi],
                scalar2=None, op0=ALU.is_le), inc="v3")
            vw()
            p.emit(lambda qc1=qc1: v_.tensor_sub(out=qc1(), in0=qc1(), in1=qc2()),
                   inc="v3", event=("wq", i))
            p.events[("v3", ("wfree", i))] = p.cur("v3")

        def statmath(Nf, banks, sumev, varev, sdev, statev, rowfree, sdfree):
            bs, bq = banks
            p.wait(v_, "mm3", sumev)
            p.emit(lambda bs=bs: v_.tensor_copy(out=t["stat"][0:1, 3 * S:4 * S],
                                                in_=ps[bs][0:1, :]), inc="v3")
            vw()
            p.emit(lambda: v_.tensor_mul(out=t["stat"][0:1, 2 * S:3 * S],
                                         in0=t["stat"][0:1, 3 * S:4 * S],
                                         in1=t["stat"][0:1, 3 * S:4 * S]), inc="v3")
            vw()
            if sdfree is not None:
                p.wait(v_, "sc3", sdfree)
            p.emit(lambda bq=bq: v_.tensor_tensor(out=t["stat"][0:1, 4 * S:5 * S],
                                                  in0=ps[bq][0:1, :],
                                                  in1=t["stat"][0:1, 2 * S:3 * S],
                                                  op=ALU.subtract),
                   inc="v3", event=varev)
            p.wait(v_, "sc3", sdev)
            if rowfree is not None:
                p.wait(v_, "mm3", rowfree)
            p.emit(lambda: v_.reciprocal(out=t["stat"][0:1, 0:S],
                                         in_=t["stat"][0:1, 2 * S:3 * S]), inc="v3")
            vw()
            p.emit(lambda: v_.tensor_mul(out=t["stat"][0:1, S:2 * S],
                                         in0=t["stat"][0:1, 3 * S:4 * S],
                                         in1=t["stat"][0:1, 0:S]),
                   inc="v3", event=statev)

        def scalerow(mxb, sTb, scev, rowev, wi, sxfree, mxTev):
            p.wait(v_, "mm3", mxTev)
            p.emit(lambda mxb=mxb: v_.reduce_max(out=t["smax"][:, 0:4],
                                                 in_=ps[mxb].ap().rearrange("p (a b) -> p a b", a=4),
                                                 axis=mybir.AxisListType.X), inc="v3")
            vw()
            p.emit(lambda: v_.tensor_scalar_max(out=t["smax"][:, 0:4], in0=t["smax"][:, 0:4],
                                                scalar1=1e-5), inc="v3")
            vw()
            p.emit(lambda wi=wi: v_.tensor_scalar(
                out=t["smax"][:, 4:8], in0=t["smax"][:, 0:4],
                scalar1=t["swc"][:, 4 + wi:5 + wi], scalar2=1.0 / 127.0,
                op0=ALU.mult, op1=ALU.mult), inc="v3")
            vw()
            p.emit(lambda: v_.reciprocal(out=t["smax"][:, 0:4], in_=t["smax"][:, 0:4]), inc="v3")
            vw()
            p.emit(lambda: v_.tensor_scalar_mul(out=t["smax"][:, 0:4], in0=t["smax"][:, 0:4],
                                                scalar1=127.0), inc="v3", event=scev)
            p.wait(v_, "mm3", sTev_holder[0])
            if sxfree is not None:
                p.wait(v_, "sx", sxfree)
            p.emit(lambda sTb=sTb: v_.tensor_copy(out=t["sflat"][0:8, 0:128], in_=ps[sTb][0:8, 0:128]),
                   inc="v3", event=rowev)

        sTev_holder = [None]

        def front(tcn):
            for jt in range(NJT):
                p.wait(v_, "mm3", ("A", tcn, jt))
                if jt == 0:
                    p.wait(v_, "qx", ("xc", tcn))
                    if tcn == 0:
                        p.wait(v_, "sw0", value=8 * 16)
                        p.wait(v_, "sw1", value=8 * 16)
                    if tcn >= 2:
                        p.wait(v_, "mm3", ("Gsum", tcn - 2, NJT - 1))
                p.emit(lambda tcn=tcn, jt=jt: v_.tensor_tensor(
                    out=t["rbf"][:, jt, :], in0=ps[4 + jt % 2][:, :], in1=xs(tcn)[:, jt, :],
                    op=ALU.add), inc="v3", event=("Aev", tcn, jt))
            if tcn == 0:
                rowfree, sdfree = None, None
            elif tcn == 1:
                rowfree, sdfree = ("rb1", 0), ("sd0", 0)
            else:
                rowfree, sdfree = ("m23b", tcn - 2), ("sd3", tcn - 2)
            statmath(1024.0, (6, 7), ("Bsum", tcn, NJT - 1), ("var0", tcn), ("sd0", tcn),
                     ("stat0", tcn), rowfree, sdfree)
            p.wait(v_, "mm3", ("rs0b", tcn))
            p.wait(v_, "mm3", ("m20b", tcn))
            if tcn >= 2:
                p.wait(v_, "st3", ("outst", tcn - 2, NJT - 1))
            vw()
            p.emit(lambda: v_.tensor_copy(out=t["v"][0][:], in_=ps[6][:, :]), inc="v3")
            vw()
            p.emit(lambda: v_.tensor_copy(out=t["v"][1][:], in_=ps[7][:, :]), inc="v3")
            vw()
            p.emit(lambda: v_.tensor_tensor(out=t["r3"][:, :, :], in0=t["rbf"][:, :, :],
                                            in1=rep0(t["v"][0][:], NJT), op=ALU.mult), inc="v3")
            vw()
            p.emit(lambda tcn=tcn: v_.tensor_tensor(out=x1b(tcn)[:, :, :], in0=t["r3"][:, :, :],
                                                    in1=rep0(t["v"][1][:], NJT), op=ALU.subtract),
                   inc="v3", event=("x1", tcn, NJT - 1))
            for jt in range(NJT):
                vw()
                p.emit(lambda tcn=tcn, jt=jt: v_.tensor_scalar(
                    out=t["rbf"][:, jt, :], in0=x1b(tcn)[:, jt, :],
                    scalar1=t["lng"][:, jt:jt + 1], scalar2=t["lnb"][:, jt:jt + 1],
                    op0=ALU.mult, op1=ALU.add), inc="v3", event=("l1o", tcn, jt))
            vw()
            p.emit(lambda: v_.tensor_scalar_mul(out=r3h()[:, 0:8, :], in0=t["rbf"][:, :, :],
                                                scalar1=-1.0), inc="v3")
            vw()
            p.emit(lambda: v_.tensor_tensor(out=r3h()[:, 0:8, :], in0=t["rbf"][:, :, :],
                                            in1=r3h()[:, 0:8, :], op=ALU.max), inc="v3")
            for w in (4, 2, 1):
                vw()
                p.emit(lambda w=w: v_.tensor_tensor(out=r3h()[:, 0:w, :], in0=r3h()[:, 0:w, :],
                                                    in1=r3h()[:, w:2 * w, :], op=ALU.max), inc="v3")
            vw()
            p.emit(lambda: v_.tensor_copy(out=t["v"][0][:], in_=r3h()[:, 0, :]),
                   inc="v3", event=("mx1", tcn))
            sTev_holder[0] = ("sT1", tcn)
            scalerow(6, 7, ("s1c", tcn), ("sfl", tcn, 1), 0,
                     ("sxd", tcn - 2, 2) if tcn >= 2 else None, ("mxT1", tcn))
            p.wait(v_, "mm3", ("sb1", tcn))
            p.emit(lambda: v_.tensor_copy(out=t["sbc"][:, :], in_=ps[6][:, :]), inc="v3")
            p.wait(v_, "mm3", ("rb1", tcn))
            vw()
            p.emit(lambda: v_.tensor_copy(out=t["rbc"][:, :], in_=ps[7][:, :]),
                   inc="v3", event=("sbc1ev", tcn))
            def msksub1(j):
                vw()
                p.emit(lambda j=j: v_.scalar_tensor_tensor(
                    out=t["sg"][j % 2][:], in0=qi1(j)[:, :], scalar=-0.5,
                    in1=t["sg"][j % 2][:], op0=ALU.add, op1=ALU.is_gt), inc="v3")
                vw()
                p.emit(lambda j=j: v_.tensor_tensor(out=t["q1"][:, j, :],
                                                    in0=qi1(j)[:, :],
                                                    in1=t["sg"][j % 2][:], op=ALU.subtract),
                       inc="v3", event=("q1t", tcn, j))

            for jt in range(NJT):
                vw()
                if jt == 0 and tcn >= 1:
                    p.wait(v_, "mm3", ("slotfree", tcn - 1, 2))
                p.emit(lambda jt=jt: v_.tensor_mul(out=t["sg"][jt % 2][:], in0=t["rbf"][:, jt, :],
                                                   in1=t["sbc"][:, :]), inc="v3")
                if jt >= 1:
                    msksub1(jt - 1)
                vw()
                p.emit(lambda jt=jt: v_.tensor_scalar_add(out=qi1(jt)[:, :], in0=t["sg"][jt % 2][:],
                                                          scalar1=0.5),
                       inc="v3", event=("qiev1", tcn, jt))
            msksub1(NJT - 1)
            vw()
            p.emit(lambda: v_.tensor_tensor(out=t["q1"][:, :, :], in0=t["q1"][:, :, :],
                                            in1=rep0(t["rbc"][:, :], NJT), op=ALU.mult),
                   inc="v3", event=("q1", tcn))

        def back(tc):
            nxt = tc + 1 < NCH
            statmath(4096.0, (2, 3), ("Dsum", tc, NHT - 1), ("var2", tc), ("sd2", tc),
                     ("stat2", tc), ("rb1", tc + 1) if nxt else ("rb1", tc),
                     ("sd0", tc + 1) if nxt else ("sd0", tc))
            p.wait(v_, "mm3", ("rs2b", tc))
            p.wait(v_, "mm3", ("m22b", tc))
            if tc >= 1:
                p.wait(v_, "st3", ("outst", tc - 1, NJT - 1))
            if nxt:
                p.wait(v_, "mm3", ("mxT1", tc + 1))
            vw()
            p.emit(lambda: v_.tensor_copy(out=t["v"][0][:], in_=ps[4][:, :]),
                   inc="v3", event=("dv0", tc))
            vw()
            p.emit(lambda: v_.tensor_copy(out=t["v"][1][:], in_=ps[5][:, :]),
                   inc="v3", event=("dv1", tc))
            vw()
            p.emit(lambda: v_.tensor_copy(out=t["sg"][0][:], in_=t["v"][0][:]), inc="v3")
            vw()
            p.emit(lambda: v_.tensor_copy(out=t["sg"][1][:], in_=t["v"][1][:]), inc="v3")
            for b4 in range(NHT // 4):
                vw()
                p.emit(lambda b4=b4: v_.tensor_tensor(
                    out=r3h()[:, 0:4, :], in0=t["g1"][:, b4 * 4:b4 * 4 + 4, :],
                    in1=rep0(t["sg"][0][:], 4), op=ALU.mult), inc="v3")
                vw()
                p.emit(lambda b4=b4: v_.tensor_tensor(
                    out=r3h()[:, 0:4, :], in0=r3h()[:, 0:4, :],
                    in1=rep0(t["sg"][1][:], 4), op=ALU.subtract), inc="v3")
                for k4 in range(4):
                    kt2 = b4 * 4 + k4
                    vw()
                    p.emit(lambda kt2=kt2, k4=k4: v_.tensor_scalar(
                        out=t["g1"][:, kt2, :], in0=r3h()[:, k4, :],
                        scalar1=t["ln2g"][:, kt2:kt2 + 1], scalar2=t["ln2b"][:, kt2:kt2 + 1],
                        op0=ALU.mult, op1=ALU.add), inc="v3", event=("l2o", tc, kt2))
            for h2 in range(2):
                vw()
                p.emit(lambda h2=h2: v_.tensor_scalar_mul(
                    out=r3h()[:, 8 * h2:8 * h2 + 8, :],
                    in0=t["g1"][:, 16 * h2:16 * h2 + 8, :], scalar1=-1.0), inc="v3")
                vw()
                p.emit(lambda h2=h2: v_.tensor_tensor(
                    out=r3h()[:, 8 * h2:8 * h2 + 8, :], in0=t["g1"][:, 16 * h2:16 * h2 + 8, :],
                    in1=r3h()[:, 8 * h2:8 * h2 + 8, :], op=ALU.max), inc="v3")
                vw()
                p.emit(lambda h2=h2: v_.tensor_scalar_mul(
                    out=t["rbf"][:, :, :],
                    in0=t["g1"][:, 16 * h2 + 8:16 * h2 + 16, :], scalar1=-1.0), inc="v3")
                vw()
                p.emit(lambda h2=h2: v_.tensor_tensor(
                    out=t["rbf"][:, :, :], in0=t["g1"][:, 16 * h2 + 8:16 * h2 + 16, :],
                    in1=t["rbf"][:, :, :], op=ALU.max), inc="v3")
                vw()
                p.emit(lambda h2=h2: v_.tensor_tensor(
                    out=r3h()[:, 8 * h2:8 * h2 + 8, :], in0=r3h()[:, 8 * h2:8 * h2 + 8, :],
                    in1=t["rbf"][:, :, :], op=ALU.max), inc="v3")
            vw()
            p.emit(lambda: v_.tensor_tensor(out=r3h()[:, 0:8, :], in0=r3h()[:, 0:8, :],
                                            in1=r3h()[:, 8:16, :], op=ALU.max), inc="v3")
            for w in (4, 2, 1):
                vw()
                p.emit(lambda w=w: v_.tensor_tensor(out=r3h()[:, 0:w, :], in0=r3h()[:, 0:w, :],
                                                    in1=r3h()[:, w:2 * w, :], op=ALU.max), inc="v3")
            vw()
            if nxt:
                p.wait(v_, "mm3", ("mxT1", tc + 1))
            p.emit(lambda: v_.tensor_copy(out=t["v"][0][:], in_=r3h()[:, 0, :]),
                   inc="v3", event=("mx2", tc))
            sTev_holder[0] = ("sT2", tc)
            if nxt:
                p.wait(v_, "mm3", ("sT1", tc + 1))
            scalerow(4, 5, ("s2c", tc), ("sfl", tc, 2), 1,
                     ("sxd", tc + 1, 1) if nxt else None, ("mxT2", tc))
            p.wait(v_, "mm3", ("sb2", tc))
            p.emit(lambda: v_.tensor_copy(out=t["sbc"][:, :], in_=ps[4][:, :]), inc="v3")
            p.wait(v_, "mm3", ("rb2", tc))
            vw()
            p.emit(lambda: v_.tensor_copy(out=t["rbc"][:, :], in_=ps[5][:, :]),
                   inc="v3", event=("sbc2ev", tc))
            def msksubF(b):
                vw()
                p.emit(lambda b=b: v_.scalar_tensor_tensor(
                    out=vbF(b), in0=qiB()[:, :, :], scalar=-0.5,
                    in1=vbF(b), op0=ALU.add, op1=ALU.is_gt), inc="v3")
                vw()
                half = (b % 2) * 4
                p.emit(lambda b=b, half=half: v_.tensor_tensor(
                    out=t["q2"][:, half:half + 4, :], in0=qiB()[:, :, :],
                    in1=vbF(b), op=ALU.subtract),
                    inc="v3", event=("q2b", tc, b))

            for b4 in range(NHT // 4):
                vw()
                if b4 == 0:
                    p.wait(v_, "mm3", ("C", tc, NHT - 1))
                if b4 >= 2:
                    p.wait(v_, "mm3", ("F", tc, (b4 - 2) * 4 + 3))
                p.emit(lambda b4=b4: v_.tensor_tensor(
                    out=vbF(b4), in0=t["g1"][:, b4 * 4:b4 * 4 + 4, :],
                    in1=rep0(t["sbc"][:, :], 4), op=ALU.mult), inc="v3")
                if b4 >= 1:
                    msksubF(b4 - 1)
                vw()
                p.emit(lambda b4=b4: v_.tensor_scalar_add(out=qiB()[:, :, :], in0=vbF(b4),
                                                          scalar1=0.5),
                       inc="v3", event=("qiev2", tc, b4))
            msksubF(NHT // 4 - 1)
            for jt in range(NJT):
                if jt == 0:
                    p.wait(v_, "mm3", ("F", tc, NHT - 1))
                vw()
                p.emit(lambda jt=jt: v_.tensor_mul(out=t["v"][jt % 2][:], in0=ps[jt][:, :],
                                                   in1=t["rbc"][:, :]),
                       inc="v3", event=("gev", tc, jt))
                vw()
                p.emit(lambda tc=tc, jt=jt: v_.scalar_tensor_tensor(
                    out=t["r3"][:, jt, :], in0=t["v"][jt % 2][:],
                    scalar=t["b2"][:, jt:jt + 1], in1=x1b(tc)[:, jt, :],
                    op0=ALU.add, op1=ALU.add),
                    inc="v3", event=("Gap", tc, jt))
            statmath(1024.0, (6, 7), ("Gsum", tc, NJT - 1), ("var3", tc), ("sd3", tc),
                     ("stat3", tc), ("rb2", tc), ("sd2", tc))
            p.wait(v_, "mm3", ("rs3b", tc))
            p.wait(v_, "mm3", ("m23b", tc))
            vw()
            p.emit(lambda: v_.tensor_copy(out=t["v"][0][:], in_=ps[4][:, :]), inc="v3")
            vw()
            p.emit(lambda: v_.tensor_copy(out=t["v"][1][:], in_=ps[5][:, :]),
                   inc="v3", event=("dv3", tc))
            vw()
            p.emit(lambda: v_.tensor_tensor(out=t["r3"][:, :, :], in0=t["r3"][:, :, :],
                                            in1=rep0(t["v"][0][:], NJT), op=ALU.mult), inc="v3")
            vw()
            p.emit(lambda: v_.tensor_tensor(out=t["r3"][:, :, :], in0=t["r3"][:, :, :],
                                            in1=rep0(t["v"][1][:], NJT), op=ALU.subtract),
                   inc="v3", event=("outj", tc, NJT - 1))

        front(0)
        for tc in range(NCH):
            if tc + 1 < NCH:
                front(tc + 1)
            back(tc)

    p.begin("gpsimd"); stream_gpsimd(E.get("gpsimd"))
    p.begin("sync"); stream_sync(E.get("sync"))
    p.begin("tensor"); stream_tensor(E.get("tensor"))
    p.begin("scalar"); stream_scalar(E.get("scalar"))
    p.begin("vector"); stream_vector(E.get("vector"))


L3_SEMS2 = ["ls", "ly", "lw", "q0", "q1", "qo", "qx", "sx", "sw0", "sw1", "mm3", "sc3", "v3", "st3"]


def l3_alloc2(nc, E, T3=2048):
    TCH = 512
    t = {}
    t["wbuf"] = E(nc.sbuf_tensor("l3_wbuf", [128, 32, 1024], BF16))
    t["x"] = E(nc.sbuf_tensor("l3_x", [128, 8, TCH], BF16))
    t["y2"] = E(nc.sbuf_tensor("l3_y2", [128, 16, TCH], BF16))
    t["x1"] = [E(nc.sbuf_tensor(f"l3_x1{i}", [128, 8, TCH], BF16)) for i in range(2)]
    t["rbf"] = E(nc.sbuf_tensor("l3_rbf", [128, 8, TCH], BF16))
    t["q1"] = E(nc.sbuf_tensor("l3_q1", [128, 8, TCH], BF16))
    t["q2"] = E(nc.sbuf_tensor("l3_q2", [128, 8, TCH], BF16))
    t["g1"] = E(nc.sbuf_tensor("l3_g1", [128, 32, TCH], BF16))
    t["sq"] = E(nc.sbuf_tensor("l3_sq", [128, 2, TCH], BF16))
    t["sqA"] = E(nc.sbuf_tensor("l3_sqA", [128, 1, TCH], BF16))
    t["v"] = [E(nc.sbuf_tensor(f"l3_v{i}", [128, TCH], F32)) for i in range(2)]
    t["sg"] = [E(nc.sbuf_tensor(f"l3_sg{i}", [128, TCH], BF16)) for i in range(2)]
    t["mx"] = t["sg"][0]
    t["mskB"] = E(nc.sbuf_tensor("l3_mskB", [128, 4, TCH], BF16))
    t["r3"] = E(nc.sbuf_tensor("l3_r3", [128, 8, TCH], F32))
    t["stat"] = E(nc.sbuf_tensor("l3_stat", [1, TCH * 5], F32))
    t["smax"] = E(nc.sbuf_tensor("l3_smax", [128, 8], F32))
    t["sflat"] = E(nc.sbuf_tensor("l3_sflat", [8, 128], F32))
    t["sbc"] = E(nc.sbuf_tensor("l3_sbc", [128, TCH], BF16))
    t["rbc"] = E(nc.sbuf_tensor("l3_rbc", [128, TCH], BF16))
    t["onesb"] = E(nc.sbuf_tensor("l3_onesb", [128, 2], BF16))
    t["onesf"] = E(nc.sbuf_tensor("l3_onesf", [128, 1], F32))
    t["onesrow"] = E(nc.sbuf_tensor("l3_onesrow", [1, 128], F32))
    t["identf"] = E(nc.sbuf_tensor("l3_identf", [128, 128], F32))
    t["lng"] = E(nc.sbuf_tensor("l3_lng", [128, 8], F32))
    t["lnb"] = E(nc.sbuf_tensor("l3_lnb", [128, 8], F32))
    t["ln2g"] = E(nc.sbuf_tensor("l3_ln2g", [128, 32], F32))
    t["ln2b"] = E(nc.sbuf_tensor("l3_ln2b", [128, 32], F32))
    t["b1"] = E(nc.sbuf_tensor("l3_b1", [128, 32], F32))
    t["b2"] = E(nc.sbuf_tensor("l3_b2", [128, 8], F32))
    t["swc"] = E(nc.sbuf_tensor("l3_swc", [128, 8], F32))
    t["red"] = E(nc.sbuf_tensor("l3_red", [128, 16], F32))
    t["scl"] = E(nc.sbuf_tensor("l3_scl", [1, 64], F32))
    t["cst"] = E(nc.sbuf_tensor("l3_cst", [1, 8], F32))
    t["ps"] = [E(nc.psum_tensor(f"l3_ps{i}", [128, TCH], F32)) for i in range(8)]
    return t


TT = 16384
TC = 2048
NCHUNK = TT // TC
NB_CHUNK = NCHUNK // 2
NG = 32
SPAN = min(512, TC)
NSPAN = TC // SPAN

def build_l2():
    nc = bass.Bass("TRN2", target_bir_lowering=False, debug=False)
    drep = nc.dram_tensor("drep", [NG, 128, TT], BF16, kind="ExternalInput").ap()
    durep = nc.dram_tensor("durep", [NG, 128, TT], BF16, kind="ExternalInput").ap()
    u_cm = nc.dram_tensor("u_cm", [2, 128, TT], BF16, kind="ExternalInput").ap()
    gs_cm = nc.dram_tensor("gs_cm", [2, 128, TT], BF16, kind="ExternalInput").ap()
    bdn = nc.dram_tensor("bdn", [128, TT], BF16, kind="ExternalInput").ap()
    cdn = nc.dram_tensor("cdn", [128, TT], BF16, kind="ExternalInput").ap()
    alog_dn = nc.dram_tensor("alog_dn", [128, NG], F32, kind="ExternalInput").ap()
    wsel = nc.dram_tensor("wsel", [128, 16, 128], BF16, kind="ExternalInput").ap()
    wdiag = nc.dram_tensor("wdiag", [128, 2, 128], F32, kind="ExternalInput").ap()
    y2 = nc.dram_tensor("y2", [2, 128, TT], BF16, kind="ExternalOutput").ap()

    iters = [(ch, ct, g16) for ch in range(NCHUNK) for ct in range(2) for g16 in range(16)]

    es = contextlib.ExitStack()
    with es:
        E = es.enter_context
        t_drep = [E(nc.sbuf_tensor(f"t_drep{i}", [128, TC], BF16)) for i in range(2)]
        t_durep = [E(nc.sbuf_tensor(f"t_durep{i}", [128, TC], BF16)) for i in range(2)]
        t_bdn = [E(nc.sbuf_tensor(f"t_bdn{i}", [128, TC], BF16)) for i in range(2)]
        t_cdn = [E(nc.sbuf_tensor(f"t_cdn{i}", [128, TC], BF16)) for i in range(2)]
        t_u = [E(nc.sbuf_tensor(f"t_u{i}", [128, 2, TC], BF16)) for i in range(2)]
        t_gs = [E(nc.sbuf_tensor(f"t_gs{i}", [128, 2, TC], BF16)) for i in range(2)]
        t_dA = [E(nc.sbuf_tensor(f"t_dA{i}", [128, TC], F32)) for i in range(2)]
        t_dBu = [E(nc.sbuf_tensor(f"t_dBu{i}", [128, TC], BF16)) for i in range(2)]
        t_hs = [E(nc.sbuf_tensor(f"t_hs{i}", [128, TC], BF16)) for i in range(2)]
        t_prod = [E(nc.sbuf_tensor(f"t_prod{i}", [128, TC], BF16)) for i in range(2)]
        t_y2 = [E(nc.sbuf_tensor(f"t_y2{i}", [128, TC], BF16)) for i in range(2)]
        t_wsel = E(nc.sbuf_tensor("t_wsel", [128, 16, 128], BF16))
        t_wdiagf = E(nc.sbuf_tensor("t_wdiagf", [128, 2, 128], F32))
        t_wdiag = E(nc.sbuf_tensor("t_wdiag", [128, 2, 128], BF16))
        t_alog = E(nc.sbuf_tensor("t_alog", [128, NG], F32))
        t_expA = E(nc.sbuf_tensor("t_expA", [128, NG], F32))
        t_negA = E(nc.sbuf_tensor("t_negA", [128, NG], F32))
        t_state = E(nc.sbuf_tensor("t_state", [128, NG], F32))
        psum = [E(nc.psum_tensor(f"ps{i}", [128, SPAN], F32)) for i in range(8)]

        s_drep = [E(nc.semaphore(f"s_drep{i}")) for i in range(2)]
        s_durep = [E(nc.semaphore(f"s_durep{i}")) for i in range(2)]
        s_bdn = [E(nc.semaphore(f"s_bdn{i}")) for i in range(2)]
        s_cdn = [E(nc.semaphore(f"s_cdn{i}")) for i in range(2)]
        s_u = [E(nc.semaphore(f"s_u{i}")) for i in range(2)]
        s_gs = [E(nc.semaphore(f"s_gs{i}")) for i in range(2)]
        s_st = [E(nc.semaphore(f"s_st{i}")) for i in range(2)]
        s_pre = E(nc.semaphore("s_pre"))
        s_dA = E(nc.semaphore("s_dA"))
        s_v = E(nc.semaphore("s_v"))
        s_mm = E(nc.semaphore("s_mm"))
        s_ev = E(nc.semaphore("s_ev"))
        s_sc = E(nc.semaphore("s_sc"))
        s_vA = E(nc.semaphore("s_vA"))
        vch = E(nc.semaphore("vch"))
        block = E(nc.Block())

        @block.sync
        def _(sync):
            sync.dma_start(out=t_wsel[:], in_=wsel).then_inc(s_pre, 16)
            sync.dma_start(out=t_wdiagf[:], in_=wdiag).then_inc(s_pre, 16)
            sync.dma_start(out=t_alog[:], in_=alog_dn).then_inc(s_pre, 16)
            for ch in range(NCHUNK):
                t0 = ch * TC
                sl = ch % 2
                if ch >= 2:
                    sync.wait_ge(s_v, (ch - 1) * 32)
                    sync.wait_ge(s_ev, (ch - 1) * 2)
                    sync.wait_ge(s_mm, (ch - 1) * 32)
                sync.dma_start(out=t_bdn[sl][:], in_=bdn[:, t0:t0 + TC]).then_inc(s_bdn[sl], 16)
                sync.dma_start(out=t_cdn[sl][:], in_=cdn[:, t0:t0 + TC]).then_inc(s_cdn[sl], 16)
                for c_ in range(2):
                    sync.dma_start(out=t_u[sl][:, c_, :], in_=u_cm[c_, :, t0:t0 + TC]).then_inc(s_u[sl], 16)
                    sync.dma_start(out=t_gs[sl][:, c_, :], in_=gs_cm[c_, :, t0:t0 + TC]).then_inc(s_gs[sl], 16)
                for ct in range(2):
                    for g16 in range(16):
                        it = ch * 32 + ct * 16 + g16
                        g = ct * 16 + g16
                        buf = it % 2
                        if it >= 2:
                            sync.wait_ge(s_dA, it - 1)
                            sync.wait_ge(s_v, it - 1)
                        sync.dma_start(out=t_drep[buf][:], in_=drep[g, :, t0:t0 + TC]).then_inc(s_drep[buf], 16)
                        sync.dma_start(out=t_durep[buf][:], in_=durep[g, :, t0:t0 + TC]).then_inc(s_durep[buf], 16)

        @block.scalar
        def _(scalar):
            scalar.wait_ge(s_pre, 48)
            scalar.activation(out=t_expA[:], in_=t_alog[:], func=AF.Exp).then_inc(s_sc, 1)
            scalar.copy(out=t_wdiag[:], in_=t_wdiagf[:]).then_inc(s_sc, 1)
            for it, (ch, ct, g16) in enumerate(iters):
                g = ct * 16 + g16
                buf = it % 2
                if it == 0:
                    scalar.wait_ge(s_vA, 1)
                if g16 == 0 and ch >= 1:
                    ctset_prev = (ch - 1) * 2 + ct
                    scalar.wait_ge(s_ev, ctset_prev + 1)
                    scalar.dma_start(out=y2[ct, :, (ch - 1) * TC:ch * TC],
                                     in_=t_y2[ct][:]).then_inc(s_st[ct], 16)
                scalar.wait_ge(s_drep[buf], 16 * (it // 2 + 1))
                if it >= 2:
                    scalar.wait_ge(s_v, it - 1)
                scalar.activation(out=t_dA[buf][:], in_=t_drep[buf][:], func=AF.Exp,
                                  scale=t_negA[:, g:g + 1]).then_inc(s_dA, 1)
            for ct in range(2):
                ctset = (NCHUNK - 1) * 2 + ct
                scalar.wait_ge(s_ev, ctset + 1)
                scalar.dma_start(out=y2[ct, :, (NCHUNK - 1) * TC:NCHUNK * TC],
                                 in_=t_y2[ct][:]).then_inc(s_st[ct], 16)
            scalar.wait_ge(s_st[0], NCHUNK * 16)
            scalar.wait_ge(s_st[1], NCHUNK * 16)

        @block.vector
        def _(vector):
            vector.wait_ge(s_sc, 1)
            vector.tensor_scalar_mul(out=t_negA[:], in0=t_expA[:], scalar1=-1.0).then_inc(s_vA, 1)
            vchn = [0]

            def vinc(ins):
                vchn[0] += 1
                ins.then_inc(vch, 1)
                return vchn[0]

            for it, (ch, ct, g16) in enumerate(iters):
                g = ct * 16 + g16
                buf = it % 2
                sl = ch % 2
                ctset = ch * 2 + ct
                vector.wait_ge(s_durep[buf], 16 * (it // 2 + 1))
                vector.wait_ge(s_bdn[sl], 16 * (ch // 2 + 1))
                vector.wait_ge(s_dA, it + 1)
                k_dbu = vinc(vector.tensor_tensor(out=t_dBu[buf][:], in0=t_durep[buf][:],
                                                  in1=t_bdn[sl][:], op=ALU.mult))
                vector.wait_ge(vch, k_dbu)
                init = 0.0 if ch % NB_CHUNK == 0 else t_state[:, g:g + 1]
                k_scan = vinc(vector.tensor_tensor_scan(
                    out=t_hs[buf][:], data0=t_dA[buf][:], data1=t_dBu[buf][:],
                    initial=init, op0=ALU.mult, op1=ALU.add))
                vector.wait_ge(vch, k_scan)
                vinc(vector.tensor_copy(out=t_state[:, g:g + 1], in_=t_hs[buf][:, TC - 1:TC]))
                if it >= 2:
                    vector.wait_ge(s_mm, it - 1)
                vector.wait_ge(s_cdn[sl], 16 * (ch // 2 + 1))
                vector.tensor_tensor(out=t_prod[buf][:], in0=t_hs[buf][:],
                                     in1=t_cdn[sl][:], op=ALU.mult).then_inc(s_v, 1)
                if g16 == 15:
                    vector.wait_ge(s_mm, it + 1)
                    if ch >= 1:
                        vector.wait_ge(s_st[ct], 16 * ch)
                    vector.wait_ge(s_gs[sl], 32 * (ch // 2 + 1))
                    pb = (ct % 2) * 4
                    for sp in range(NSPAN):
                        ins = vector.tensor_tensor(
                            out=t_y2[ct][:, sp * SPAN:(sp + 1) * SPAN],
                            in0=psum[pb + sp][:, :],
                            in1=t_gs[sl][:, ct, sp * SPAN:(sp + 1) * SPAN],
                            op=ALU.mult)
                        if sp == NSPAN - 1:
                            ins.then_inc(s_ev, 1)

        @block.tensor
        def _(tensor):
            tensor.wait_ge(s_pre, 48)
            tensor.wait_ge(s_sc, 2)
            for it, (ch, ct, g16) in enumerate(iters):
                buf = it % 2
                sl = ch % 2
                ctset = ch * 2 + ct
                tensor.wait_ge(s_v, it + 1)
                if g16 == 0:
                    if ctset >= 2:
                        tensor.wait_ge(s_ev, ctset - 1)
                    tensor.wait_ge(s_u[sl], 32 * (ch // 2 + 1))
                pb = (ct % 2) * 4
                last = None
                for sp in range(NSPAN):
                    last = tensor.matmul(psum[pb + sp][:, :], t_wsel[:, g16, :],
                                         t_prod[buf][:, sp * SPAN:(sp + 1) * SPAN],
                                         start=(g16 == 0), stop=False)
                if g16 == 15:
                    for sp in range(NSPAN):
                        last = tensor.matmul(psum[pb + sp][:, :], t_wdiag[:, ct, :],
                                             t_u[sl][:, ct, sp * SPAN:(sp + 1) * SPAN],
                                             start=False, stop=(sp == NSPAN - 1))
                last.then_inc(s_mm, 1)
    return nc










T1 = 2048
T3 = 2048


def _engs(g=None, s=None, te=None, sc=None, v=None):
    return {"gpsimd": g, "sync": s, "tensor": te, "scalar": sc, "vector": v}


def build_l1():
    nc = bass.Bass("TRN2", target_bir_lowering=False, debug=False)
    d = {}
    d["xT"] = nc.dram_tensor("xT", [T1 // 512, 8, 128, 515], F32, kind="ExternalInput").ap()
    d["w_inT"] = nc.dram_tensor("w_inT", [1024, 4096], F32, kind="ExternalInput").ap()
    d["conv_w_cm"] = nc.dram_tensor("conv_w_cm", [128, 16, 4], F32, kind="ExternalInput").ap()
    d["conv_b_cm"] = nc.dram_tensor("conv_b_cm", [128, 16], F32, kind="ExternalInput").ap()
    d["w_xT"] = nc.dram_tensor("w_xT", [2048, 96], F32, kind="ExternalInput").ap()
    d["w_dtT"] = nc.dram_tensor("w_dtT", [64, 16, 128], F32, kind="ExternalInput").ap()
    d["dt_bias_cm"] = nc.dram_tensor("dt_bias_cm", [128, 16], F32, kind="ExternalInput").ap()
    NCH1 = T1 // 512
    d["o_u"] = nc.dram_tensor("o_u", [NCH1, 128, 16, 512], BF16, kind="ExternalOutput").ap()
    d["o_du"] = nc.dram_tensor("o_du", [NCH1, 128, 16, 512], BF16, kind="ExternalOutput").ap()
    d["o_dl"] = nc.dram_tensor("o_dl", [NCH1, 128, 16, 512], BF16, kind="ExternalOutput").ap()
    d["o_gs"] = nc.dram_tensor("o_gs", [NCH1, 128, 16, 512], BF16, kind="ExternalOutput").ap()
    d["o_bc"] = nc.dram_tensor("o_bc", [NCH1, 32, 512], BF16, kind="ExternalOutput").ap()
    es = contextlib.ExitStack()
    with es:
        E = es.enter_context
        t = l1_alloc(nc, E, T1)
        sems = {n: E(nc.semaphore("s_" + n)) for n in L1_SEMS}
        events = {}
        l1_streams(Pass(True, None, events), t, d, None, T1)
        block = E(nc.Block())

        @block.gpsimd
        def _(g):
            l1_streams(Pass(False, sems, events, only="gpsimd"), t, d, _engs(g=g), T1)

        @block.sync
        def _(s):
            l1_streams(Pass(False, sems, events, only="sync"), t, d, _engs(s=s), T1)

        @block.tensor
        def _(te):
            l1_streams(Pass(False, sems, events, only="tensor"), t, d, _engs(te=te), T1)

        @block.scalar
        def _(sc):
            l1_streams(Pass(False, sems, events, only="scalar"), t, d, _engs(sc=sc), T1)

        @block.vector
        def _(v):
            l1_streams(Pass(False, sems, events, only="vector"), t, d, _engs(v=v), T1)
    return nc


def build_l3():
    nc = bass.Bass("TRN2", target_bir_lowering=False, debug=False)
    d = {}
    d["xT_s"] = nc.dram_tensor("xT_s", [128, 8, T3], F32, kind="ExternalInput").ap()
    d["y2T"] = nc.dram_tensor("y2T", [128, 16, T3], BF16, kind="ExternalInput").ap()
    d["w_outT_s"] = nc.dram_tensor("w_outT_s", [128, 16, 1024], F32, kind="ExternalInput").ap()
    d["w1T_s"] = nc.dram_tensor("w1T_s", [128, 8, 4096], F32, kind="ExternalInput").ap()
    d["w2T_s"] = nc.dram_tensor("w2T_s", [128, 8, 4096], F32, kind="ExternalInput").ap()
    for nm, c in (("lng", 8), ("lnb", 8), ("ln2g", 32), ("ln2b", 32), ("b1", 32), ("b2", 8)):
        d[nm] = nc.dram_tensor(nm, [128, c], F32, kind="ExternalInput").ap()
    d["ident"] = nc.dram_tensor("ident", [128, 128], BF16, kind="ExternalInput").ap()
    d["identf"] = nc.dram_tensor("identf", [128, 128], F32, kind="ExternalInput").ap()
    d["onesb"] = nc.dram_tensor("onesb", [128, 2], BF16, kind="ExternalInput").ap()
    d["onesf"] = nc.dram_tensor("onesf", [128, 1], F32, kind="ExternalInput").ap()
    d["onesrow"] = nc.dram_tensor("onesrow", [1, 128], F32, kind="ExternalInput").ap()
    d["cst"] = nc.dram_tensor("cst", [1, 8], F32, kind="ExternalInput").ap()
    d["w1q_d"] = nc.dram_tensor("w1q_d", [128, 8, 4096], BF16, kind="Internal").ap()
    d["w2q_d"] = nc.dram_tensor("w2q_d", [128, 8, 4096], BF16, kind="Internal").ap()
    d["outT_s"] = nc.dram_tensor("outT_s", [128, 8, T3], F32, kind="ExternalOutput").ap()
    es = contextlib.ExitStack()
    with es:
        E = es.enter_context
        t = l3_alloc2(nc, E, T3)
        sems = {n: E(nc.semaphore("s_" + n)) for n in L3_SEMS2}
        events = {}
        l3_streams(Pass(True, None, events), t, d, None, T3)
        block = E(nc.Block())

        @block.gpsimd
        def _(g):
            l3_streams(Pass(False, sems, events, only="gpsimd"), t, d, _engs(g=g), T3)

        @block.sync
        def _(s):
            l3_streams(Pass(False, sems, events, only="sync"), t, d, _engs(s=s), T3)

        @block.tensor
        def _(te):
            l3_streams(Pass(False, sems, events, only="tensor"), t, d, _engs(te=te), T3)

        @block.scalar
        def _(sc):
            l3_streams(Pass(False, sems, events, only="scalar"), t, d, _engs(sc=sc), T3)

        @block.vector
        def _(v):
            l3_streams(Pass(False, sems, events, only="vector"), t, d, _engs(v=v), T3)
    return nc


_P = np.arange(128)


def _pack_l1(x, W_in, conv_w, conv_b, W_x, W_dt, dt_bias):
    cw = np.ascontiguousarray(conv_w.reshape(16, 128, 4).transpose(1, 0, 2))
    cb = np.ascontiguousarray(conv_b.reshape(16, 128).T)
    db = np.ascontiguousarray(dt_bias.reshape(16, 128).T)
    w_inT = np.ascontiguousarray(W_in.T)
    w_xT = np.ascontiguousarray(W_x.T)
    wdt = np.ascontiguousarray(W_dt.reshape(16, 128, 64).transpose(2, 0, 1))
    maps = []
    for c in range(8):
        b, q = c // 4, c % 4
        t0 = q * 2048
        xs = np.zeros((2048 + 3, 1024), np.float32)
        lo = max(0, t0 - 3)
        xs[3 - (t0 - lo):] = x[b, lo:t0 + 2048]
        xTf = np.ascontiguousarray(xs.T)
        xc = np.empty((4, 8, 128, 515), np.float32)
        for tc in range(4):
            for kt in range(8):
                xc[tc, kt] = xTf[kt * 128:(kt + 1) * 128, tc * 512:tc * 512 + 515]
        maps.append({"xT": xc, "w_inT": w_inT,
                     "conv_w_cm": cw, "conv_b_cm": cb, "dt_bias_cm": db,
                     "w_xT": w_xT, "w_dtT": wdt})
    return maps


def _pack_l2(r1, A_log, D_skip):
    def full(key):
        return np.concatenate(
            [np.ascontiguousarray(r[key]).transpose(2, 1, 0, 3).reshape(2048, 2048)
             for r in r1], axis=1)

    dl = full("o_dl")
    du = full("o_du")
    uu = full("o_u")
    gs = full("o_gs")
    bc = np.concatenate(
        [np.ascontiguousarray(r["o_bc"]).transpose(1, 0, 2).reshape(32, 2048) for r in r1], axis=1)
    B, C = bc[0:16], bc[16:32]
    n_of_p = _P // 8
    ci_of_p = _P % 8
    wsel = np.zeros((128, 16, 128), bfd)
    for g16 in range(16):
        wsel[_P, g16, g16 * 8 + ci_of_p] = 1
    maps = []
    for k in range(8):
        ch0 = 256 * k
        gmap = ch0 + 8 * np.arange(32)[:, None] + ci_of_p[None, :]
        drep = dl[gmap]
        durep = du[gmap]
        u_cm = uu[ch0:ch0 + 256].reshape(2, 128, TT)
        gs_cm = gs[ch0:ch0 + 256].reshape(2, 128, TT)
        bdn = B[n_of_p]
        cdn = C[n_of_p]
        alog_dn = np.ascontiguousarray(
            A_log[ch0 + 8 * np.arange(32)[None, :] + ci_of_p[:, None], n_of_p[:, None]]
        ).astype(np.float32)
        wdiag = np.zeros((128, 2, 128), np.float32)
        for ct in range(2):
            wdiag[_P, ct, _P] = D_skip[ch0 + ct * 128 + _P]
        maps.append({"drep": np.ascontiguousarray(drep),
                     "durep": np.ascontiguousarray(durep),
                     "u_cm": np.ascontiguousarray(u_cm),
                     "gs_cm": np.ascontiguousarray(gs_cm),
                     "bdn": np.ascontiguousarray(bdn),
                     "cdn": np.ascontiguousarray(cdn),
                     "alog_dn": alog_dn, "wsel": wsel, "wdiag": wdiag})
    return maps


def _pack_l3(x, r2, W_out, w1, b1, w2, b2, lng, lnb, ln2g, ln2b):
    y2_full = np.concatenate(
        [np.ascontiguousarray(r["y2"]).reshape(256, TT) for r in r2], axis=0)
    woT = np.ascontiguousarray(W_out.T.reshape(16, 128, 1024).transpose(1, 0, 2))
    w1T = np.ascontiguousarray(w1.T.reshape(8, 128, 4096).transpose(1, 0, 2))
    w2l = np.ascontiguousarray(w2.T.reshape(32, 128, 1024).transpose(1, 0, 2))
    w2T = np.ascontiguousarray(w2l.reshape(128, 8, 4096))
    small = {
        "lng": np.ascontiguousarray(lng.reshape(8, 128).T),
        "lnb": np.ascontiguousarray(lnb.reshape(8, 128).T),
        "ln2g": np.ascontiguousarray(ln2g.reshape(32, 128).T),
        "ln2b": np.ascontiguousarray(ln2b.reshape(32, 128).T),
        "b1": np.ascontiguousarray(b1.reshape(32, 128).T),
        "b2": np.ascontiguousarray(b2.reshape(8, 128).T),
        "ident": np.eye(128, dtype=bfd),
        "identf": np.eye(128, dtype=np.float32),
        "onesb": np.broadcast_to(np.array([1.0 / 1024, 1.0 / 4096], bfd), (128, 2)).copy(),
        "onesf": np.ones((128, 1), np.float32),
        "onesrow": np.ones((1, 128), np.float32),
        "cst": np.array([[1e-5, 0, 0, 0, 0, 0, 0, 0]], np.float32),
    }
    maps = []
    for c in range(8):
        b, q = c // 4, c % 4
        t0 = q * 2048
        xT = np.ascontiguousarray(x[b, t0:t0 + 2048].T)
        xT_s = np.ascontiguousarray(xT.reshape(8, 128, 2048).transpose(1, 0, 2))
        y2T = np.ascontiguousarray(
            y2_full[:, c * 2048:(c + 1) * 2048].reshape(16, 128, 2048).transpose(1, 0, 2))
        m = {"xT_s": xT_s, "y2T": y2T, "w_outT_s": woT, "w1T_s": w1T, "w2T_s": w2T}
        m.update(small)
        maps.append(m)
    return maps


_NCS = {}


def _get(name, builder):
    if name not in _NCS:
        _NCS[name] = builder()
    return _NCS[name]


def kernel(L0_attn_hidden_states, L0_attn_W_in, L0_attn_conv_w, L0_attn_conv_b,
           L0_attn_W_x, L0_attn_W_dt, L0_attn_dt_bias, L0_attn_A_log, L0_attn_D_skip,
           L0_attn_W_out, L0_ffn_ln1_g, L0_ffn_ln1_b, L0_ffn_w1, L0_ffn_b1,
           L0_ffn_ln2_g, L0_ffn_ln2_b, L0_ffn_w2, L0_ffn_b2):
    x = np.ascontiguousarray(np.asarray(L0_attn_hidden_states, np.float32))
    m1 = _pack_l1(x, np.asarray(L0_attn_W_in, np.float32),
                  np.asarray(L0_attn_conv_w, np.float32),
                  np.asarray(L0_attn_conv_b, np.float32),
                  np.asarray(L0_attn_W_x, np.float32),
                  np.asarray(L0_attn_W_dt, np.float32),
                  np.asarray(L0_attn_dt_bias, np.float32))
    nc1 = _get("l1", build_l1)
    r1 = run_bass_kernel_spmd(nc1, m1, core_ids=list(range(8))).results

    m2 = _pack_l2(r1, np.asarray(L0_attn_A_log, np.float32),
                  np.asarray(L0_attn_D_skip, np.float32))
    nc2 = _get("l2", build_l2)
    r2 = run_bass_kernel_spmd(nc2, m2, core_ids=list(range(8))).results

    m3 = _pack_l3(x, r2, np.asarray(L0_attn_W_out, np.float32),
                  np.asarray(L0_ffn_w1, np.float32), np.asarray(L0_ffn_b1, np.float32),
                  np.asarray(L0_ffn_w2, np.float32), np.asarray(L0_ffn_b2, np.float32),
                  np.asarray(L0_ffn_ln1_g, np.float32), np.asarray(L0_ffn_ln1_b, np.float32),
                  np.asarray(L0_ffn_ln2_g, np.float32), np.asarray(L0_ffn_ln2_b, np.float32))
    nc3 = _get("l3", build_l3)
    r3 = run_bass_kernel_spmd(nc3, m3, core_ids=list(range(8))).results

    out = np.zeros((2, 8192, 1024), np.float32)
    for c in range(8):
        b, q = c // 4, c % 4
        o = np.asarray(r3[c]["outT_s"], np.float32)
        out[b, q * 2048:(q + 1) * 2048] = o.transpose(2, 1, 0).reshape(2048, 1024)
    return out
